# version 01
# baseline (speedup 1.0000x reference)
import numpy as np
from contextlib import ExitStack
import concourse.bass as bass
import concourse.mybir as mybir
from concourse.bass_utils import run_bass_kernel_spmd

F32 = mybir.dt.float32
BF16 = mybir.dt.bfloat16
AF = mybir.ActivationFunctionType
ALU = mybir.AluOpType

C = 1024
T = 4096
NCT = 8
H = 4096
NHT = 32
NCORE = 8
EPS = 1e-5
NB_HEADS = 16
DH = 64


def _t(a):
    return np.ascontiguousarray(np.asarray(a, np.float32).T)


def _pp(v, n):
    return np.ascontiguousarray(np.asarray(v, np.float32).reshape(n, 128).T)


_SV = {}
_SV_W = [0]


def _sv_slot(name, width):
    if name not in _SV:
        _SV[name] = (_SV_W[0], width)
        _SV_W[0] += width
    return _SV[name]


for _n in ["ln0_w", "ln0_b", "ln1_w", "ln1_b", "ln2_w", "ln2_b",
           "tmk", "tmv", "tmr", "cmk", "cmr", "tdecay", "tfirst",
           "l0f_b2", "l1_bq", "l1_bk", "l1_bo", "l1_b1", "l1_b2", "l1f_s3"]:
    _sv_slot(_n, 8)
for _n in ["l0f_b1", "l1f_s1", "l1f_s2", "l1f_hs1", "l1f_hs2"]:
    _sv_slot(_n, 32)
SVW = _SV_W[0]

_RV = {"bv_row": (0, 1024), "eps": (1024, 1), "one": (1025, 1)}
RVW = 1026


def _relpos_bucket(rel, num_buckets=32, max_distance=128):
    import math
    n = -rel
    nb = num_buckets // 2
    ret = (n < 0).astype(np.int64) * nb
    n = np.abs(n)
    max_exact = nb // 2
    is_small = n < max_exact
    val_if_large = max_exact + (
        np.log(np.maximum(n, 1).astype(np.float64) / max_exact)
        / math.log(max_distance / max_exact) * (nb - max_exact)
    ).astype(np.int64)
    val_if_large = np.minimum(val_if_large, nb - 1)
    return ret + np.where(is_small, n, val_if_large)


def prep_inputs(inp):
    d = {}
    x = np.asarray(inp["L0_attn_x"], np.float32)
    d["x_fm"] = [np.ascontiguousarray(x[b].T) for b in range(NCORE)]

    for src, dst in [("L0_attn_Wk", "wk_t"), ("L0_attn_Wv", "wv_t"),
                     ("L0_attn_Wr", "wr_t"), ("L0_attn_Wo", "wo_t"),
                     ("L0_attn_Wfk", "wfk_t"), ("L0_attn_Wfr", "wfr_t"),
                     ("L0_attn_Wfv", "wfv_t"),
                     ("L1_attn_Wq", "l1wq_t"), ("L1_attn_Wk", "l1wk_t"),
                     ("L1_attn_Wv", "l1wv_t"), ("L1_attn_Wo", "l1wo_t"),
                     ("L1_attn_W1", "l1w1_t"), ("L1_attn_W2", "l1w2_t"),
                     ("L1_ffn_w1_w", "f1_t"), ("L1_ffn_w2_w", "f2_t"),
                     ("L1_ffn_h1_w", "fh1_t"), ("L1_ffn_h2_w", "fh2_t"),
                     ("L1_ffn_w3_w", "f3_t")]:
        d[dst] = _t(inp[src])
    d["l0f_w1"] = np.ascontiguousarray(np.asarray(inp["L0_ffn_w1"], np.float32))
    d["l0f_w2"] = np.ascontiguousarray(np.asarray(inp["L0_ffn_w2"], np.float32))

    sv = np.zeros((128, SVW), np.float32)

    def put(name, val, n):
        off, w = _SV[name]
        assert w == n
        sv[:, off:off + n] = _pp(val, n)

    put("ln0_w", inp["L0_attn_ln0_w"], 8); put("ln0_b", inp["L0_attn_ln0_b"], 8)
    put("ln1_w", inp["L0_attn_ln1_w"], 8); put("ln1_b", inp["L0_attn_ln1_b"], 8)
    put("ln2_w", inp["L0_attn_ln2_w"], 8); put("ln2_b", inp["L0_attn_ln2_b"], 8)
    put("tmk", inp["L0_attn_tmk"], 8); put("tmv", inp["L0_attn_tmv"], 8)
    put("tmr", inp["L0_attn_tmr"], 8)
    put("cmk", inp["L0_attn_cmk"], 8); put("cmr", inp["L0_attn_cmr"], 8)
    put("tdecay", inp["L0_attn_time_decay"], 8)
    put("tfirst", inp["L0_attn_time_first"], 8)
    put("l0f_b1", inp["L0_ffn_b1"], 32); put("l0f_b2", inp["L0_ffn_b2"], 8)
    put("l1_bq", inp["L1_attn_bq"], 8); put("l1_bk", inp["L1_attn_bk"], 8)
    put("l1_bo", inp["L1_attn_bo"], 8)
    put("l1_b1", inp["L1_attn_b1"], 8); put("l1_b2", inp["L1_attn_b2"], 8)
    put("l1f_s1", inp["L1_ffn_w1_s"], 32); put("l1f_s2", inp["L1_ffn_w2_s"], 32)
    put("l1f_hs1", inp["L1_ffn_h1_s"], 32); put("l1f_hs2", inp["L1_ffn_h2_s"], 32)
    put("l1f_s3", inp["L1_ffn_w3_s"], 8)
    d["smallvecs"] = sv

    rv = np.zeros((1, RVW), np.float32)
    rv[0, _RV["eps"][0]] = EPS
    rv[0, _RV["bv_row"][0]:_RV["bv_row"][0] + C] = np.asarray(inp["L1_attn_bv"], np.float32)
    rv[0, _RV["one"][0]] = 1.0
    d["rowvecs"] = rv

    rel_bias = np.asarray(inp["L1_attn_rel_bias"], np.float32)
    n = np.arange(8191)
    buckets = _relpos_bucket(4095 - n)
    d["g_master"] = np.ascontiguousarray(rel_bias[buckets, :].T)
    return d


WEIGHTS = {
    "wk_t": ([C, C], "A1"), "wv_t": ([C, C], "A1"), "wr_t": ([C, C], "A1"),
    "wo_t": ([C, C], "A1"),
    "wfk_t": ([C, H], "A2"), "wfv_t": ([H, C], "A2"), "wfr_t": ([C, C], "A2"),
    "l0f_w1": ([C, H], "B"), "l0f_w2": ([H, C], "B"),
    "l1wq_t": ([C, C], "C1"), "l1wk_t": ([C, C], "C1"), "l1wv_t": ([C, C], "C1"),
    "l1wo_t": ([C, C], "C3"), "l1w1_t": ([C, C], "C4"), "l1w2_t": ([C, C], "C4"),
    "f1_t": ([C, H], "D"), "f2_t": ([C, H], "D"),
    "fh1_t": ([H, H], "D"), "fh2_t": ([H, H], "D"), "f3_t": ([H, C], "D"),
    "g_master": ([16, 8191], "C0"),
}
ACTS = {
    "x_fm":    ([C, T], F32, None, {"A1", "A2"}),
    "x2_d":    ([C, T], F32, "A1", {"A2"}),
    "xsum_d":  ([C, T], F32, "A1", {"A2"}),
    "res1_d":  ([C, T], F32, "A2", {"B"}),
    "wfk16_d": ([C, H], BF16, "A2", {"A2"}),
    "wfv16_d": ([H, C], BF16, "A2", {"A2"}),
    "res2_d":  ([C, T], F32, "B", {"C0", "C4"}),
    "xatt_d":  ([C, T], F32, "C0", {"C3"}),
    "xatt16_d": ([C, T], BF16, "C0", {"C1"}),
    "g16_d":   ([16, 8191], BF16, "C0", {"C1"}),
    "att_tok_d": ([T, C], BF16, "C1", {"C3"}),
    "v_d":     ([T, C], BF16, "C1", {"C1"}),
    "ab_d":    ([C, T], F32, "C3", {"C4"}),
    "ab16_d":  ([C, T], BF16, "C3", {"C4"}),
    "res3_d":  ([C, T], F32, "C4", {"D"}),
    "f116_d":  ([C, H], BF16, "D", {"D"}),
    "f216_d":  ([C, H], BF16, "D", {"D"}),
    "fh116_d": ([H, H], BF16, "D", {"D"}),
    "fh216_d": ([H, H], BF16, "D", {"D"}),
    "f316_d":  ([H, C], BF16, "D", {"D"}),
    "y_fm":    ([C, T], F32, "D", set()),
}
ALL_PHASES = ["A1", "A2", "B", "C0", "C1", "C3", "C4", "D"]



class P:

    def __init__(self, nc, block, ctx, dram, phases):
        self.nc = nc
        self.block = block
        self.ctx = ctx
        self.dram = dram
        self.phases = phases
        self.cnt = {}
        self.sem = {}
        self.ev = {}

    def s(self, name):
        if name not in self.sem:
            self.sem[name] = self.ctx.enter_context(self.nc.semaphore(name))
            self.cnt[name] = 0
        return self.sem[name]

    def inc(self, inst, semname, n=1, event=None):
        sem = self.s(semname)
        inst.then_inc(sem, n)
        self.cnt[semname] += n
        if event is not None:
            self.ev[event] = (semname, self.cnt[semname])
        return self.cnt[semname]

    def mark_dma(self, inst, semname, event=None):
        return self.inc(inst, semname, 16, event)

    def reserve(self, semname, event, n=1):
        self.s(semname)
        self.cnt[semname] += n
        self.ev[event] = (semname, self.cnt[semname])

    def inc_reserved(self, inst, event):
        semname, _ = self.ev[event]
        inst.then_inc(self.sem[semname], 1)

    def wait(self, eng, event):
        semname, v = self.ev[event]
        eng.wait_ge(self.sem[semname], v)

    def wait_maybe(self, eng, event):
        if event in self.ev:
            self.wait(eng, event)

    def sb(self, name, shape, dtype):
        return self.ctx.enter_context(self.nc.sbuf_tensor(name, shape, dtype))

    def ps(self, name, shape, dtype=F32):
        return self.ctx.enter_context(self.nc.psum_tensor(name, shape, dtype))

    def barrier(self, tag):
        b = self.block
        names = ["sync", "vector", "scalar", "tensor", "gpsimd"]
        sem = self.s("bar")
        base = self.cnt["bar"]

        for n in names:
            def emit(eng, n=n):
                eng.sem_inc(sem, 1)
            getattr(b, n)(emit)
        self.cnt["bar"] += 5
        tgt = self.cnt["bar"]

        for n in names:
            def emit(eng):
                eng.wait_ge(sem, tgt)
            getattr(b, n)(emit)


    def get_stg(self, cols=8192):
        if not hasattr(self, "stg") or self.stg_cols < cols:
            assert not hasattr(self, "stg"), "allocate biggest staging first"
            self.stg = self.sb("wstg", [128, cols], F32)
            self.stg_cols = cols
            self.stg_users = 0
        return self.stg

    def load_weight_cast(self, tag, w_ap, n_in_t, n_out, dma_sem, stage_cols=8192,
                         cast_eng="vector", w16=None, halt_event=None,
                         dma_eng="sync"):
        assert stage_cols % n_out == 0
        tps = stage_cols // n_out
        nstage = (n_in_t + tps - 1) // tps
        if w16 is None:
            w16 = self.sb(f"{tag}_16", [128, n_in_t * n_out], BF16)
        stg = self.get_stg(stage_cols)
        src = w_ap.rearrange("(ct p) o -> p ct o", p=128)
        ev_done = f"{tag}_cast_done"
        csem = "v" if cast_eng == "vector" else "g"
        prev_user = getattr(self, "stg_last_ev", None)
        for j in range(nstage):
            self.reserve(csem, f"{tag}_cast_{j}" if j < nstage - 1 else ev_done)
        self.stg_last_ev = ev_done

        def emit_sync(sync):
            for j in range(nstage):
                c0 = j * tps
                c1 = min(n_in_t, c0 + tps)
                if j == 0:
                    if prev_user is not None:
                        self.wait(sync, prev_user)
                    if halt_event is not None:
                        self.wait(sync, halt_event)
                else:
                    self.wait(sync, f"{tag}_cast_{j-1}")
                self.mark_dma(
                    sync.dma_start(
                        out=stg.ap()[:, 0:(c1 - c0) * n_out].rearrange(
                            "p (ct o) -> p ct o", ct=c1 - c0),
                        in_=src[:, c0:c1, :]),
                    dma_sem, f"{tag}_ld_{j}")
        getattr(self.block, dma_eng)(emit_sync)

        def emit_cast(eng):
            for j in range(nstage):
                c0 = j * tps
                c1 = min(n_in_t, c0 + tps)
                self.wait(eng, f"{tag}_ld_{j}")
                ev = f"{tag}_cast_{j}" if j < nstage - 1 else ev_done
                self.inc_reserved(
                    eng.tensor_copy(
                        out=w16[:, c0 * n_out:c1 * n_out],
                        in_=stg[:, 0:(c1 - c0) * n_out]),
                    ev)
        getattr(self.block, cast_eng)(emit_cast)
        return w16, ev_done


def declare_dram(nc, phases):
    dram = {}
    pset = set(phases)
    for name, (shape, prod) in WEIGHTS.items():
        if prod in pset:
            dram[name] = nc.dram_tensor(name, shape, F32, kind="ExternalInput").ap()
    for name, (shape, dt, prod, cons) in ACTS.items():
        needed_in = bool(cons & pset) or (name == "x_fm" and bool(pset))
        produced = prod in pset
        if name == "x_fm":
            if pset & {"A1", "A2"}:
                dram[name] = nc.dram_tensor(name, shape, dt, kind="ExternalInput").ap()
            continue
        if produced and cons and cons <= {prod}:
            dram[name] = nc.dram_tensor(name, shape, dt, kind="Internal").ap()
        elif produced:
            dram[name] = nc.dram_tensor(name, shape, dt, kind="ExternalOutput").ap()
        elif cons & pset:
            dram[name] = nc.dram_tensor(name, shape, dt, kind="ExternalInput").ap()
    if pset:
        dram["smallvecs"] = nc.dram_tensor("smallvecs", [128, SVW], F32,
                                           kind="ExternalInput").ap()
        dram["rowvecs"] = nc.dram_tensor("rowvecs", [1, RVW], F32,
                                         kind="ExternalInput").ap()
    return dram


def fm_view(ap):
    return ap.rearrange("(ct p) t -> p ct t", p=128)



def emit_stats(p, tag, src32, src16, sq16, nct, tc, psA, psB, rowbuf,
               prep_ps, dep_events, double=False, no_eps=False):
    cs = [rowbuf[:, k * tc:(k + 1) * tc] for k in range(6)]

    def vec_cast(vector):
        for e in dep_events:
            p.wait(vector, e)
        p.inc(vector.tensor_copy(out=src16[:], in_=src32[:]), "v", 1, f"{tag}_c16")
    p.block.vector(vec_cast)

    def sc_sq(scalar):
        for e in dep_events:
            p.wait(scalar, e)
        p.inc(scalar.activation(out=sq16[:], in_=src32[:], func=AF.Square),
              "s", 1, f"{tag}_sq")
    p.block.scalar(sc_sq)

    def tn_stats(tensor):
        p.wait(tensor, f"{tag}_c16")
        p.wait(tensor, f"{tag}_sq")
        s16v = src16.ap().rearrange("p (ct t) -> p ct t", ct=nct)
        q16v = sq16.ap().rearrange("p (ct t) -> p ct t", ct=nct)
        for ct in range(nct):
            tensor.matmul(psA[:], p.ones16[:], s16v[:, ct, :],
                          start=(ct == 0), stop=(ct == nct - 1))
        for ct in range(nct):
            inst = tensor.matmul(psB[:], p.ones16[:], q16v[:, ct, :],
                                 start=(ct == 0), stop=(ct == nct - 1))
        p.inc(inst, "t", 1, f"{tag}_stats")
    p.block.tensor(tn_stats)

    def vec_rows(vector):
        p.wait(vector, f"{tag}_stats")
        vector.tensor_scalar_mul(cs[0], psA[:], 1.0 / C)
        vector.tensor_scalar_mul(cs[1], psB[:], 1.0 / C)
        vector.tensor_tensor(out=cs[2], in0=cs[0], in1=cs[0], op=ALU.mult)
        inst = vector.tensor_tensor(out=cs[1], in0=cs[1], in1=cs[2],
                                    op=ALU.subtract)
        if double:
            vector.tensor_scalar_add(cs[3], cs[1], EPS)
            vector.reciprocal(cs[4], cs[3])
            inst = vector.tensor_tensor(out=cs[2], in0=cs[1], in1=cs[4],
                                        op=ALU.mult)
        p.inc(inst, "v", 1, f"{tag}_var")
    p.block.vector(vec_rows)

    def sc_rstd(scalar):
        p.wait(scalar, f"{tag}_var")
        if double:
            scalar.activation(out=cs[3], in_=cs[3], func=AF.Ln)
            scalar.activation(out=cs[3], in_=cs[3], func=AF.Exp, scale=-0.5)
            scalar.activation(out=cs[5], in_=cs[2], func=AF.Ln, bias=p.eps_ap)
            inst = scalar.activation(out=cs[5], in_=cs[5], func=AF.Exp, scale=-0.5)
        else:
            scalar.activation(out=cs[2], in_=cs[1], func=AF.Ln,
                              bias=0.0 if no_eps else p.eps_ap)
            inst = scalar.activation(out=cs[1], in_=cs[2], func=AF.Exp, scale=-0.5)
        p.inc(inst, "s", 1, f"{tag}_rstd")
    p.block.scalar(sc_rstd)

    if double:
        def vec_r12(vector):
            p.wait(vector, f"{tag}_rstd")
            p.inc(vector.tensor_tensor(out=cs[1], in0=cs[3], in1=cs[5], op=ALU.mult),
                  "v", 1, f"{tag}_r12")
        p.block.vector(vec_r12)

    def tn_rep(tensor):
        p.wait(tensor, f"{tag}_r12" if double else f"{tag}_rstd")
        for o in range(0, 2 * tc, 512):
            e = min(2 * tc, o + 512)
            inst = tensor.matmul(prep_ps[:, o:e], p.ones32[:], rowbuf[:, o:e],
                                 start=True, stop=True)
        p.inc(inst, "t", 1, f"{tag}_prep")
    p.block.tensor(tn_rep)
    return f"{tag}_prep"



def emit_C0(p):
    TC = 512
    NCH = T // TC
    res2 = fm_view(p.dram["res2_d"])
    xatt = fm_view(p.dram["xatt_d"])
    xatt16 = fm_view(p.dram["xatt16_d"])

    sx = p.sb("c0_sx", [128, NCT * TC], F32)
    tmp32 = p.sb("c0_tmp", [128, NCT * TC], F32)
    sc16 = p.sb("c0_sc16", [128, NCT * TC], BF16)
    sq16 = p.sb("c0_sq16", [128, NCT * TC], BF16)
    o16 = p.sb("c0_o16", [128, NCT * TC], BF16)
    gm32 = p.sb("c0_gm32", [16, 8191], F32)
    gm16 = p.sb("c0_gm16", [16, 8191], BF16)
    rowbuf = p.sb("c0_rows", [1, 6 * TC], F32)

    psA = p.ps("c0_psA", [1, TC])
    psB = p.ps("c0_psB", [1, TC])
    prep = p.ps("c0_prep", [128, 2 * TC])

    def r3(buf):
        return buf.ap().rearrange("p (ct t) -> p ct t", ct=NCT)

    def gld(gpsimd):
        p.mark_dma(gpsimd.dma_start(out=gm32[:], in_=p.dram["g_master"][:]),
                   "dma_c0g", "c0_gld")
    p.block.gpsimd(gld)

    def gcast(vector):
        p.wait(vector, "c0_gld")
        p.inc(vector.tensor_copy(out=gm16[:], in_=gm32[:]), "v", 1, "c0_gcast")
    p.block.vector(gcast)

    def gst(gpsimd):
        p.wait(gpsimd, "c0_gcast")
        p.mark_dma(gpsimd.dma_start(out=p.dram["g16_d"][:], in_=gm16[:]),
                   "dma_c0g", "c0_gst")
    p.block.gpsimd(gst)

    for i in range(NCH):
        tsl = slice(i * TC, (i + 1) * TC)

        def ld(sync, i=i, tsl=tsl):
            if i > 0:
                p.wait(sync, f"c0_app_{i-1}")
            p.mark_dma(sync.dma_start(out=r3(sx), in_=res2[:, :, tsl]),
                       "dma_c0", f"c0_x_{i}")
        p.block.sync(ld)

        emit_stats(p, f"c0l{i}", sx, sc16, sq16, NCT, TC, psA, psB,
                   rowbuf, prep, [f"c0_x_{i}"])

        def vec_app(vector, i=i):
            p.wait(vector, f"c0l{i}_prep")
            if i > 0:
                p.wait(vector, f"c0_st_{i-1}")
            Mb = prep.ap()[:, 0:TC].unsqueeze(1).broadcast_to([128, NCT, TC])
            Rb = prep.ap()[:, TC:2 * TC].unsqueeze(1).broadcast_to([128, NCT, TC])
            vector.tensor_tensor(out=r3(tmp32), in0=r3(sx), in1=Mb, op=ALU.subtract)
            vector.tensor_tensor(out=r3(tmp32), in0=r3(tmp32), in1=Rb, op=ALU.mult)
            p.inc(vector.tensor_copy(out=o16[:], in_=tmp32[:]), "v", 1, f"c0_app_{i}")
        p.block.vector(vec_app)

        def st(sync, i=i, tsl=tsl):
            p.wait(sync, f"c0_app_{i}")
            p.mark_dma(sync.dma_start(out=xatt[:, :, tsl], in_=r3(tmp32)),
                       "dma_c0s", None)
            p.mark_dma(sync.dma_start(out=xatt16[:, :, tsl], in_=r3(o16)),
                       "dma_c0s", f"c0_st_{i}")
        p.block.sync(st)

    def fin(sync):
        p.wait(sync, f"c0_st_{NCH-1}")
        p.wait(sync, "c0_gst")
    p.block.sync(fin)



C1_DEBUG = 0
C1_NTC = 8

def emit_C1(p):
    GW = 8064
    NST = 8
    xatt16 = fm_view(p.dram["xatt16_d"])
    attTok = p.dram["att_tok_d"]
    vd = p.dram["v_d"]
    g16d = p.dram["g16_d"]
    sv = p.sv32

    qF = p.sb("c1_qF", [64, T], BF16)
    kF = p.sb("c1_kF", [64, T], BF16)
    vones = p.sb("c1_vones", [128, 32 * 66], BF16)
    att_s = p.sb("c1_att_s", [64, 2 * 512], BF16)
    atok = p.sb("c1_atok", [128, 2 * 4 * 64], BF16)
    Hb = p.sb("c1_H", [128, GW], BF16)
    G = p.sb("c1_G", [128, GW], BF16)
    Efull = p.sb("c1_Efull", [128, 32 * 512], BF16)
    ones64 = p.sb("c1_ones64", [1, 64], F32)
    xblk = p.sb("c1_xblk", [128, NCT * 256], BF16)
    xtt = p.sb("c1_xtt", [128, NCT * 128], BF16)
    wq16 = p.sb("c1_wq16", [128, NCT * C], BF16)
    wk16 = p.sb("c1_wk16", [128, NCT * C], BF16)
    wv16 = p.sb("c1_wv16", [128, NCT * C], BF16)
    vst = p.sb("c1_vst", [128, 2 * C], BF16)
    vb32 = p.sb("c1_vb32", [128, C], F32)
    r32 = p.sb("c1_r32", [1, 512], F32)
    Rr32 = p.sb("c1_Rr32", [64, 512], F32)
    tE32 = p.sb("c1_tE32", [64, 512], F32)

    pp = [p.ps(f"c1_pp{j}", [128, 512]) for j in range(4)]
    pE = p.ps("c1_pE", [65, 512])
    pB = p.ps("c1_pB", [64, 512])
    prep2 = p.ps("c1_prep2", [64, 512])
    vbr = p.ps("c1_vbr", [128, 512])

    def vinit(vector):
        vector.memset(ones64[:], 1.0)
        p.inc(vector.memset(vones[:], 1.0), "v", 1, "c1_vones_init")
    p.block.vector(vinit)

    wq16, wqev = p.load_weight_cast("c1_wq", p.dram["l1wq_t"], NCT, C, "dma_c1w",
                                    stage_cols=4096, dma_eng="gpsimd", w16=wq16)
    wk16, wkev = p.load_weight_cast("c1_wk", p.dram["l1wk_t"], NCT, C, "dma_c1w",
                                    stage_cols=4096, dma_eng="gpsimd", w16=wk16)
    wv16, wvev = p.load_weight_cast("c1_wv", p.dram["l1wv_t"], NCT, C, "dma_c1w",
                                    stage_cols=4096, dma_eng="gpsimd", w16=wv16)

    def tn_bv(tensor):
        p.wait(tensor, "pre_ready")
        bvo = _RV["bv_row"][0]
        p.inc(tensor.matmul(vbr[:], p.ones32[:], p.rv32[:, bvo:bvo + 512],
                            start=True, stop=True), "t", 1, "c1_bv0")
    p.block.tensor(tn_bv)

    def vec_bv0(vector):
        p.wait(vector, "c1_bv0")
        p.inc(vector.tensor_copy(out=vb32[:, 0:512], in_=vbr[:]), "v", 1, "c1_bve0")
    p.block.vector(vec_bv0)

    def tn_bv1(tensor):
        p.wait(tensor, "c1_bve0")
        bvo = _RV["bv_row"][0]
        p.inc(tensor.matmul(vbr[:], p.ones32[:], p.rv32[:, bvo + 512:bvo + 1024],
                            start=True, stop=True), "t", 1, "c1_bv1")
    p.block.tensor(tn_bv1)

    def vec_bv1(vector):
        p.wait(vector, "c1_bv1")
        p.inc(vector.tensor_copy(out=vb32[:, 512:1024], in_=vbr[:]), "v", 1, "c1_bv")
    p.block.vector(vec_bv1)

    for tt in range(32):
        p.reserve("v", f"c1_vev_{tt}")

        def ldv(sync, tt=tt):
            if tt:
                p.wait(sync, f"c1_vmm_{tt-1}")
            p.mark_dma(sync.dma_start(
                out=xtt.ap().rearrange("p (ct t) -> p ct t", ct=NCT),
                in_=xatt16[:, :, tt * 128:(tt + 1) * 128]),
                "dma_c1x", f"c1_xtt_{tt}")
        p.block.sync(ldv)

        def tn_v(tensor, tt=tt):
            p.wait(tensor, f"c1_xtt_{tt}")
            if tt == 0:
                p.wait(tensor, wvev)
            if tt >= 2:
                p.wait(tensor, f"c1_vev_{tt-2}")
            xv = xtt.ap().rearrange("p (ct t) -> p ct t", ct=NCT)
            wvw = wv16.ap().rearrange("p (ct o) -> p ct o", ct=NCT)
            for half in range(2):
                for ct in range(NCT):
                    inst = tensor.matmul(pp[2 * (tt % 2) + half][:], xv[:, ct, :],
                                         wvw[:, ct, half * 512:(half + 1) * 512],
                                         start=(ct == 0), stop=(ct == NCT - 1))
                p.inc(inst, "t", 1, f"c1_vmmh_{tt}_{half}")
            p.ev[f"c1_vmm_{tt}"] = p.ev[f"c1_vmmh_{tt}_{1}"]
        p.block.tensor(tn_v)

        def vec_v(vector, tt=tt):
            if tt == 0:
                p.wait(vector, "c1_bv")
            if tt >= 2:
                p.wait(vector, f"c1_vst_{tt-2}")
            for half in range(2):
                p.wait(vector, f"c1_vmmh_{tt}_{half}")
                inst = vector.tensor_tensor(
                    out=vst[:, (tt % 2) * C + half * 512:(tt % 2) * C + (half + 1) * 512],
                    in0=pp[2 * (tt % 2) + half][:],
                    in1=vb32[:, half * 512:(half + 1) * 512], op=ALU.add)
            p.inc_reserved(inst, f"c1_vev_{tt}")
        p.block.vector(vec_v)

        def stv(sync, tt=tt):
            p.wait(sync, f"c1_vev_{tt}")
            p.mark_dma(sync.dma_start(
                out=vd[tt * 128:(tt + 1) * 128, :],
                in_=vst[:, (tt % 2) * C:(tt % 2 + 1) * C]),
                "dma_c1s", f"c1_vst_{tt}")
        p.block.sync(stv)

    for j in range(16):
        bq0 = f"c1b{j}"

        def ldxb(sync, j=j):
            if j:
                p.wait(sync, f"c1b{j-1}_qkf")
            p.mark_dma(sync.dma_start(
                out=xblk.ap().rearrange("p (ct t) -> p ct t", ct=NCT),
                in_=xatt16[:, :, j * 256:(j + 1) * 256]),
                "dma_c1x", f"{bq0}_xb")
        p.block.sync(ldxb)

        def ldvf(sync, j=j):
            p.wait(sync, f"c1_vst_{31}")
            if j:
                p.wait(sync, f"c1b{j-1}_avdone")
            p.mark_dma(sync.dma_start(
                out=bass.AP(vones, 0, [[66 * 32, 128], [66, 32], [1, 64]]),
                in_=bass.AP(vd.tensor, j * 256 * C,
                            [[C, 8], [64, 16], [8 * C, 32], [1, 64]])),
                "dma_c1x", f"{bq0}_vf")
        p.block.sync(ldvf)

        for ck in range(16):
            p.reserve("s", f"{bq0}_qkev_{ck}")
        for ck in range(16):
            p.reserve("s", f"{bq0}_kkev_{ck}")

        def tn_qkf(tensor, j=j):
            p.wait(tensor, f"{bq0}_xb")
            if j == 0:
                p.wait(tensor, wqev)
                p.wait(tensor, wkev)
            xv = xblk.ap().rearrange("p (ct t) -> p ct t", ct=NCT)
            for jq, w16b in ((0, wq16), (1, wk16)):
                wvw = w16b.ap().rearrange("p (ct o) -> p ct o", ct=NCT)
                for ck in range(16):
                    oi = jq * 16 + ck
                    if oi >= 2:
                        pj, pck = divmod(oi - 2, 16)
                        p.wait(tensor, f"{bq0}_qkev_{pck}" if pj == 0
                               else f"{bq0}_kkev_{pck}")
                    elif j > 0:
                        p.wait(tensor, f"c1h{j-1}s{NST-1}_ex_{15}")
                    else:
                        p.wait(tensor, f"c1_vev_{31}")
                    for ct in range(NCT):
                        inst = tensor.matmul(
                            pp[oi % 2].ap()[0:64, 0:256],
                            wvw[:, ct, ck * 64:(ck + 1) * 64],
                            xv[:, ct, :],
                            start=(ct == 0), stop=(ct == NCT - 1))
                    p.inc(inst, "t", 1,
                          f"{bq0}_qkmm_{ck}" if jq == 0 else f"{bq0}_kkmm_{ck}")
        p.block.tensor(tn_qkf)

        def sc_qkf(scalar, j=j):
            bqo, _ = _SV["l1_bq"]
            bko, _ = _SV["l1_bk"]
            for jq, dstF, bo in ((0, qF, bqo), (1, kF, bko)):
                for ck in range(16):
                    p.wait(scalar, f"{bq0}_qkmm_{ck}" if jq == 0 else f"{bq0}_kkmm_{ck}")
                    oi = jq * 16 + ck
                    dst = bass.AP(dstF, ck, [[T, 64], [16, 256]])
                    bias_ap = sv[(ck % 2) * 64:(ck % 2) * 64 + 64,
                                 (bqo if jq == 0 else bko) + ck // 2:
                                 (bqo if jq == 0 else bko) + ck // 2 + 1]
                    p.inc_reserved(
                        scalar.activation(out=dst, in_=pp[oi % 2].ap()[0:64, 0:256],
                                          func=AF.Identity, bias=bias_ap),
                        f"{bq0}_qkev_{ck}" if jq == 0 else f"{bq0}_kkev_{ck}")
            p.ev[f"{bq0}_qkf"] = p.ev[f"{bq0}_kkev_{15}"]
        p.block.scalar(sc_qkf)

        def ldH(sync, j=j):
            if j > 0:
                p.wait(sync, f"c1_Gbuilt_{j-1}")
            if "c0_gst" in p.ev:
                p.wait(sync, "c0_gst")
            p.mark_dma(sync.dma_start(
                out=Hb[:],
                in_=bass.AP(g16d.tensor, j * 8191, [[1, 128], [1, GW]])),
                "dma_c1g", f"c1_H_{j}")
        p.block.sync(ldH)

        def vec_G(vector, j=j):
            p.wait(vector, f"c1_H_{j}")
            if j >= 1:
                p.wait(vector, f"c1_Gfree_{j-1}")
            p.inc(vector.tensor_copy(out=G[:], in_=Hb[:, ::-1]),
                  "v", 1, f"c1_Gbuilt_{j}")
        p.block.vector(vec_G)

        vt3 = vones.ap().rearrange("p (kt s) -> p kt s", kt=32)

        for st_i in range(NST):
            q0 = st_i * 512
            hq = f"c1h{j}s{st_i}"
            NKB = 16
            for kb in range(NKB):
                p.reserve("s", f"{hq}_ex_{kb}")
            p.reserve("v", f"{hq}_rc")
            p.reserve("v", f"{hq}_fin")
            p.reserve("v", f"{hq}_tr")

            def tn_att(tensor, j=j, st_i=st_i, q0=q0, hq=hq, NKB=NKB):
                if st_i == 0:
                    p.wait(tensor, f"c1_Gbuilt_{j}")
                    p.wait(tensor, f"{bq0}_vf")
                    p.wait(tensor, f"{bq0}_qkf")
                if j + st_i > 0:
                    prev = f"c1h{j}s{st_i-1}_fin" if st_i else \
                        f"c1h{j-1}s{NST-1}_fin"
                    p.wait(tensor, prev)

                for kb in range(NKB):
                    if kb >= 2:
                        p.wait(tensor, f"{hq}_ex_{kb-2}")
                    elif st_i == 0 and kb < 2 and j + st_i > 0:
                        pass
                    for kt in (2 * kb, 2 * kb + 1):
                        inst = tensor.matmul(
                            pp[kt % 4][:],
                            kF.ap()[:, kt * 128:(kt + 1) * 128],
                            qF.ap()[:, q0:q0 + 512],
                            start=True, stop=True)
                    p.inc(inst, "t", 1, f"{hq}_lgb_{kb}")
                p.wait(tensor, f"{hq}_ex_{NKB-1}")
                ev = Efull.ap().rearrange("p (kt t) -> p kt t", kt=32)
                for kt in range(32):
                    tensor.matmul(pE[:], vt3[:, kt, 0:65], ev[:, kt, :],
                                  start=(kt == 0), stop=(kt == 31))
                    s = q0 - kt * 128 + 3968
                    inst = tensor.matmul(pB[:], vt3[:, kt, 0:64],
                                         G[:, s:s + 512],
                                         start=(kt == 0), stop=(kt == 31))
                p.inc(inst, "t", 1, f"{hq}_avd")
            p.block.tensor(tn_att)

            def sc_exp(scalar, hq=hq, NKB=NKB):
                ev = Efull.ap().rearrange("p (kt t) -> p kt t", kt=32)
                for kb in range(NKB):
                    p.wait(scalar, f"{hq}_lgb_{kb}")
                    for kt in (2 * kb, 2 * kb + 1):
                        inst = scalar.activation(out=ev[:, kt, :],
                                                 in_=pp[kt % 4][:], func=AF.Exp)
                    p.inc_reserved(inst, f"{hq}_ex_{kb}")
            p.block.scalar(sc_exp)

            def vec_rc(vector, hq=hq):
                p.wait(vector, f"{hq}_avd")
                p.inc_reserved(vector.reciprocal(r32[:], pE[64:65, :]), f"{hq}_rc")
            p.block.vector(vec_rc)

            def tn_rep(tensor, hq=hq):
                p.wait(tensor, f"{hq}_rc")
                p.inc(tensor.matmul(prep2[:], ones64[:], r32[:],
                                    start=True, stop=True),
                      "t", 1, f"{hq}_rp")
            p.block.tensor(tn_rep)

            def vec_att(vector, j=j, st_i=st_i, hq=hq, q0=q0):
                p.wait(vector, f"{hq}_rp")
                k = j * NST + st_i
                if k >= 2:
                    p.wait(vector, f"c1_ast_{k-2}")
                vector.tensor_copy(out=Rr32[:], in_=prep2[:])
                vector.tensor_tensor(out=tE32[:], in0=pE[0:64, :],
                                     in1=Rr32[:], op=ALU.mult)
                si = (st_i % 2) * 512
                p.inc_reserved(
                    vector.tensor_tensor(out=att_s[:, si:si + 512],
                                         in0=tE32[:], in1=pB[:], op=ALU.add),
                    f"{hq}_fin")
                ai = (st_i % 2) * 256
                for br in range(2):
                    for bc in range(16):
                        pbase2 = 32 * (bc % 4)
                        col = ai + (bc // 4) * 64 + br * 32
                        inst = vector.transpose(
                            out=atok.ap()[pbase2:pbase2 + 32, col:col + 32],
                            in_=att_s.ap()[br * 32:(br + 1) * 32,
                                           si + bc * 32:si + bc * 32 + 32])
                p.inc_reserved(inst, f"{hq}_tr")
            p.block.vector(vec_att)

            def st_att(sync, j=j, st_i=st_i, hq=hq, q0=q0):
                p.wait(sync, f"{hq}_tr")
                k = j * NST + st_i
                ai = (st_i % 2) * 256
                p.mark_dma(sync.dma_start(
                    out=bass.AP(attTok.tensor, j * 256 * C + q0 * 64,
                                [[64, 128], [8192, 4], [1, 64]]),
                    in_=bass.AP(atok, ai, [[512, 128], [64, 4], [1, 64]])),
                    "dma_c1s", f"c1_ast_{k}")
            p.block.sync(st_att)

        p.ev[f"c1_Gfree_{j}"] = p.ev[f"c1h{j}s{NST-1}_fin"]
        p.ev[f"{bq0}_avdone"] = p.ev[f"c1h{j}s{NST-1}_avd"]

    def fin(sync):
        p.wait(sync, f"c1_ast_{16*NST-1}")
    p.block.sync(fin)



def emit_C3(p):
    TC = 512
    NCH = T // TC
    attTok = p.dram["att_tok_d"]
    xatt = fm_view(p.dram["xatt_d"])
    abd = fm_view(p.dram["ab_d"])
    ab16d = fm_view(p.dram["ab16_d"])
    sv = p.sv32

    wo16, woev = p.load_weight_cast("c3_wo", p.dram["l1wo_t"], NCT, C, "dma_c3w",
                                    stage_cols=4096, dma_eng="gpsimd")

    atokc = p.sb("c3_atok", [128, 4 * C], BF16)
    ident = p.sb("c3_ident", [128, 128], BF16)
    sa16 = p.sb("c3_sa16", [128, NCT * TC], BF16)
    sx = p.sb("c3_sx", [128, NCT * TC], F32)
    t32 = p.sb("c3_t32", [128, NCT * TC], F32)
    tmp32 = p.sb("c3_tmp", [128, NCT * TC], F32)
    sc16 = p.sb("c3_sc16", [128, NCT * TC], BF16)
    sq16 = p.sb("c3_sq16", [128, NCT * TC], BF16)
    o16 = p.sb("c3_o16", [128, NCT * TC], BF16)
    rowbuf = p.sb("c3_rows", [1, 6 * TC], F32)

    psA = p.ps("c3_psA", [1, TC])
    psB = p.ps("c3_psB", [1, TC])
    prep = p.ps("c3_prep", [128, 2 * TC])
    po = [p.ps(f"c3_po{j}", [128, TC]) for j in range(2)]
    pt16 = [p.ps(f"c3_pt16{j}", [128, 128], BF16) for j in range(2)]

    def r3(buf):
        return buf.ap().rearrange("p (ct t) -> p ct t", ct=NCT)

    def gid(gpsimd):
        gpsimd.memset(ident[:], 0.0)
        import concourse.mybir as _mb
        inst = gpsimd.affine_select(out=ident[:], in_=ident[:],
                                    compare_op=_mb.AluOpType.not_equal,
                                    fill=1.0, base=0, pattern=[[-1, 128]],
                                    channel_multiplier=1)
        p.inc(inst, "g", 1, "c3_id")
    p.block.gpsimd(gid)

    for i in range(NCH):
        tsl = slice(i * TC, (i + 1) * TC)

        def ld(sync, i=i, tsl=tsl):
            if i > 0:
                p.wait(sync, f"c3_tp_{i-1}_{31}")
                p.wait(sync, f"c3_t_{i-1}")
            p.mark_dma(sync.dma_start(
                out=atokc.ap().rearrange("p (tt c) -> p tt c", tt=4),
                in_=attTok[i * TC:(i + 1) * TC, :].rearrange(
                    "(tt p) c -> p tt c", p=128)),
                "dma_c3", None)
            p.mark_dma(sync.dma_start(out=r3(sx), in_=xatt[:, :, tsl]),
                       "dma_c3", f"c3_x_{i}")
        p.block.sync(ld)

        for z in range(32):
            p.reserve("v", f"c3_tev_{i}_{z}")

        def tn_tp(tensor, i=i):
            p.wait(tensor, f"c3_x_{i}")
            if i == 0:
                p.wait(tensor, "c3_id")
            av = atokc.ap().rearrange("p (tt c) -> p tt c", tt=4)
            z = 0
            for tt in range(4):
                for ct in range(NCT):
                    if z >= 2:
                        p.wait(tensor, f"c3_tev_{i}_{z-2}")
                    elif i > 0:
                        p.wait(tensor, f"c3_tev_{i-1}_{30+z}")
                    inst = tensor.transpose(
                        pt16[z % 2][:],
                        av[:, tt, ct * 128:(ct + 1) * 128],
                        ident[:])
                    p.inc(inst, "t", 1, f"c3_tp_{i}_{z}")
                    z += 1
        p.block.tensor(tn_tp)

        def vec_tp(vector, i=i):
            sa = r3(sa16)
            z = 0
            for tt in range(4):
                for ct in range(NCT):
                    p.wait(vector, f"c3_tp_{i}_{z}")
                    p.inc_reserved(
                        vector.tensor_copy(
                            out=sa[:, ct, tt * 128:(tt + 1) * 128],
                            in_=pt16[z % 2][:]),
                        f"c3_tev_{i}_{z}")
                    z += 1
        p.block.vector(vec_tp)

        for ot in range(NCT):
            p.reserve("s", f"c3_ev_{i}_{ot}")

        def tn_mm(tensor, i=i):
            if i == 0:
                p.wait(tensor, woev)
            p.wait(tensor, f"c3_tev_{i}_{31}")
            wv = wo16.ap().rearrange("p (ct o) -> p ct o", ct=NCT)
            av = r3(sa16)
            for ot in range(NCT):
                if ot >= 2:
                    p.wait(tensor, f"c3_ev_{i}_{ot-2}")
                elif i > 0:
                    p.wait(tensor, f"c3_ev_{i-1}_{NCT-2+ot}")
                for ct in range(NCT):
                    inst = tensor.matmul(po[ot % 2][:], wv[:, ct, ot * 128:(ot + 1) * 128],
                                         av[:, ct, :],
                                         start=(ct == 0), stop=(ct == NCT - 1))
                p.inc(inst, "t", 1, f"c3_mm_{i}_{ot}")
        p.block.tensor(tn_mm)

        def sc_ev(scalar, i=i):
            bo, _ = _SV["l1_bo"]
            t3 = r3(t32)
            for ot in range(NCT):
                p.wait(scalar, f"c3_mm_{i}_{ot}")
                p.inc_reserved(scalar.activation(out=t3[:, ot, :], in_=po[ot % 2][:],
                                                 func=AF.Identity,
                                                 bias=sv[:, bo + ot:bo + ot + 1]),
                               f"c3_ev_{i}_{ot}")
        p.block.scalar(sc_ev)

        def vec_t(vector, i=i):
            p.wait(vector, f"c3_ev_{i}_{NCT-1}")
            p.inc(vector.tensor_tensor(out=t32[:], in0=t32[:], in1=sx[:], op=ALU.add),
                  "v", 1, f"c3_t_{i}")
        p.block.vector(vec_t)

        emit_stats(p, f"c3l{i}", t32, sc16, sq16, NCT, TC, psA, psB,
                   rowbuf, prep, [f"c3_t_{i}"], no_eps=True)

        def vec_app(vector, i=i):
            p.wait(vector, f"c3l{i}_prep")
            if i > 0:
                p.wait(vector, f"c3_st_{i-1}")
            Mb = prep.ap()[:, 0:TC].unsqueeze(1).broadcast_to([128, NCT, TC])
            Rb = prep.ap()[:, TC:2 * TC].unsqueeze(1).broadcast_to([128, NCT, TC])
            vector.tensor_tensor(out=r3(tmp32), in0=r3(t32), in1=Mb, op=ALU.subtract)
            vector.tensor_tensor(out=r3(tmp32), in0=r3(tmp32), in1=Rb, op=ALU.mult)
            p.inc(vector.tensor_copy(out=o16[:], in_=tmp32[:]), "v", 1, f"c3_app_{i}")
        p.block.vector(vec_app)

        def st(sync, i=i, tsl=tsl):
            p.wait(sync, f"c3_app_{i}")
            p.mark_dma(sync.dma_start(out=abd[:, :, tsl], in_=r3(tmp32)),
                       "dma_c3s", None)
            p.mark_dma(sync.dma_start(out=ab16d[:, :, tsl], in_=r3(o16)),
                       "dma_c3s", f"c3_st_{i}")
        p.block.sync(st)

    def fin(sync):
        p.wait(sync, f"c3_st_{NCH-1}")
    p.block.sync(fin)



def emit_C4(p):
    TC = 512
    NCH = T // TC
    ab16d = fm_view(p.dram["ab16_d"])
    abd = fm_view(p.dram["ab_d"])
    res2 = fm_view(p.dram["res2_d"])
    res3 = fm_view(p.dram["res3_d"])
    sv = p.sv32

    w116, w1ev = p.load_weight_cast("c4_w1", p.dram["l1w1_t"], NCT, C, "dma_c4w",
                                    stage_cols=4096, dma_eng="gpsimd")
    w216, w2ev = p.load_weight_cast("c4_w2", p.dram["l1w2_t"], NCT, C, "dma_c4w",
                                    stage_cols=4096, dma_eng="gpsimd")

    sa16 = p.sb("c4_sa16", [128, NCT * TC], BF16)
    sab = p.sb("c4_sab", [128, NCT * TC], F32)
    sr2 = p.sb("c4_sr2", [128, NCT * TC], F32)
    f116 = p.sb("c4_f116", [128, NCT * TC], BF16)
    t32 = p.sb("c4_t32", [128, NCT * TC], F32)
    tmp32 = p.sb("c4_tmp", [128, NCT * TC], F32)
    sc16 = p.sb("c4_sc16", [128, NCT * TC], BF16)
    sq16 = p.sb("c4_sq16", [128, NCT * TC], BF16)
    rowbuf = p.sb("c4_rows", [1, 6 * TC], F32)

    psA = p.ps("c4_psA", [1, TC])
    psB = p.ps("c4_psB", [1, TC])
    prep = p.ps("c4_prep", [128, 2 * TC])
    po = [p.ps(f"c4_po{j}", [128, TC]) for j in range(2)]

    def r3(buf):
        return buf.ap().rearrange("p (ct t) -> p ct t", ct=NCT)

    for i in range(NCH):
        tsl = slice(i * TC, (i + 1) * TC)

        def ld(sync, i=i, tsl=tsl):
            if i > 0:
                p.wait(sync, f"c4_f1mm_{i-1}_{NCT-1}")
                p.wait(sync, f"c4_out_{i-1}")
            p.mark_dma(sync.dma_start(out=r3(sa16), in_=ab16d[:, :, tsl]),
                       "dma_c4", None)
            p.mark_dma(sync.dma_start(out=r3(sab), in_=abd[:, :, tsl]),
                       "dma_c4", None)
            p.mark_dma(sync.dma_start(out=r3(sr2), in_=res2[:, :, tsl]),
                       "dma_c4", f"c4_x_{i}")
        p.block.sync(ld)

        for ot in range(NCT):
            p.reserve("s", f"c4_f1ev_{i}_{ot}")

        def tn_f1(tensor, i=i):
            p.wait(tensor, f"c4_x_{i}")
            if i == 0:
                p.wait(tensor, w1ev)
            wv = w116.ap().rearrange("p (ct o) -> p ct o", ct=NCT)
            av = r3(sa16)
            for ot in range(NCT):
                if ot >= 2:
                    p.wait(tensor, f"c4_f1ev_{i}_{ot-2}")
                elif i > 0:
                    p.wait(tensor, f"c4_f1ev_{i-1}_{NCT-2+ot}")
                for ct in range(NCT):
                    inst = tensor.matmul(po[ot % 2][:], wv[:, ct, ot * 128:(ot + 1) * 128],
                                         av[:, ct, :],
                                         start=(ct == 0), stop=(ct == NCT - 1))
                p.inc(inst, "t", 1, f"c4_f1mm_{i}_{ot}")
        p.block.tensor(tn_f1)

        def sc_f1(scalar, i=i):
            b1, _ = _SV["l1_b1"]
            fv = r3(f116)
            for ot in range(NCT):
                p.wait(scalar, f"c4_f1mm_{i}_{ot}")
                p.inc_reserved(scalar.activation(out=fv[:, ot, :], in_=po[ot % 2][:],
                                                 func=AF.Identity,
                                                 bias=sv[:, b1 + ot:b1 + ot + 1]),
                               f"c4_f1ev_{i}_{ot}")
        p.block.scalar(sc_f1)

        for ot in range(NCT):
            p.reserve("v", f"c4_f2ev_{i}_{ot}")

        def tn_f2(tensor, i=i):
            if i == 0:
                p.wait(tensor, w2ev)
            p.wait(tensor, f"c4_f1ev_{i}_{NCT-1}")
            wv = w216.ap().rearrange("p (ct o) -> p ct o", ct=NCT)
            fv = r3(f116)
            for ot in range(NCT):
                if ot >= 2:
                    p.wait(tensor, f"c4_f2ev_{i}_{ot-2}")
                elif i > 0:
                    p.wait(tensor, f"c4_f2ev_{i-1}_{NCT-2+ot}")
                for ct in range(NCT):
                    inst = tensor.matmul(po[ot % 2][:], wv[:, ct, ot * 128:(ot + 1) * 128],
                                         fv[:, ct, :],
                                         start=(ct == 0), stop=(ct == NCT - 1))
                p.inc(inst, "t", 1, f"c4_f2mm_{i}_{ot}")
        p.block.tensor(tn_f2)

        def vec_f2(vector, i=i):
            b2, _ = _SV["l1_b2"]
            t3 = r3(t32)
            ab3 = r3(sab)
            for ot in range(NCT):
                p.wait(vector, f"c4_f2mm_{i}_{ot}")
                p.inc_reserved(
                    vector.scalar_tensor_tensor(out=t3[:, ot, :], in0=po[ot % 2][:],
                                                scalar=sv[:, b2 + ot:b2 + ot + 1],
                                                in1=ab3[:, ot, :],
                                                op0=ALU.add, op1=ALU.add),
                    f"c4_f2ev_{i}_{ot}")
            p.ev[f"c4_t_{i}"] = p.ev[f"c4_f2ev_{i}_{NCT-1}"]
        p.block.vector(vec_f2)

        emit_stats(p, f"c4l{i}", t32, sc16, sq16, NCT, TC, psA, psB,
                   rowbuf, prep, [f"c4_t_{i}"], no_eps=True)

        def vec_app(vector, i=i):
            p.wait(vector, f"c4l{i}_prep")
            if i > 0:
                p.wait(vector, f"c4_st_{i-1}")
            Mb = prep.ap()[:, 0:TC].unsqueeze(1).broadcast_to([128, NCT, TC])
            Rb = prep.ap()[:, TC:2 * TC].unsqueeze(1).broadcast_to([128, NCT, TC])
            vector.tensor_tensor(out=r3(tmp32), in0=r3(t32), in1=Mb, op=ALU.subtract)
            vector.tensor_tensor(out=r3(tmp32), in0=r3(tmp32), in1=Rb, op=ALU.mult)
            p.inc(vector.tensor_tensor(out=tmp32[:], in0=tmp32[:], in1=sr2[:],
                                       op=ALU.add),
                  "v", 1, f"c4_out_{i}")
        p.block.vector(vec_app)

        def st(sync, i=i, tsl=tsl):
            p.wait(sync, f"c4_out_{i}")
            p.mark_dma(sync.dma_start(out=res3[:, :, tsl], in_=r3(tmp32)),
                       "dma_c4s", f"c4_st_{i}")
        p.block.sync(st)

    def fin(sync):
        p.wait(sync, f"c4_st_{NCH-1}")
    p.block.sync(fin)



DV = {"om_tmk": 0, "om_tmv": 8, "om_tmr": 16, "om_cmk": 24, "om_cmr": 32,
      "lam": 40, "eu": 48, "scr": 56}


def emit_derived(p):
    if getattr(p, "derived_done", False):
        return
    p.derived_done = True
    sv = p.sv32
    O = SVW

    def vec(vector):
        p.wait(vector, "pre_ready")
        for nm, src in [("om_tmk", "tmk"), ("om_tmv", "tmv"), ("om_tmr", "tmr"),
                        ("om_cmk", "cmk"), ("om_cmr", "cmr")]:
            o1, _ = _SV[src]
            inst = vector.tensor_scalar(out=sv[:, O + DV[nm]:O + DV[nm] + 8],
                                        in0=sv[:, o1:o1 + 8], scalar1=-1.0,
                                        scalar2=1.0, op0=ALU.mult, op1=ALU.add)
        p.inc(inst, "v", 1, "derived_v")
    p.block.vector(vec)

    def sc(scalar):
        p.wait(scalar, "pre_ready")
        td, _ = _SV["tdecay"]
        tf, _ = _SV["tfirst"]
        scalar.activation(out=sv[:, O + DV["scr"]:O + DV["scr"] + 8],
                          in_=sv[:, td:td + 8], func=AF.Exp)
        scalar.activation(out=sv[:, O + DV["lam"]:O + DV["lam"] + 8],
                          in_=sv[:, O + DV["scr"]:O + DV["scr"] + 8],
                          func=AF.Exp, scale=-1.0)
        inst = scalar.activation(out=sv[:, O + DV["eu"]:O + DV["eu"] + 8],
                                 in_=sv[:, tf:tf + 8], func=AF.Exp)
        p.inc(inst, "s", 1, "derived_s")
    p.block.scalar(sc)



def emit_A1(p):
    TC = 256
    NCH = T // TC
    xfm = fm_view(p.dram["x_fm"])
    x2v = fm_view(p.dram["x2_d"])
    xsv = fm_view(p.dram["xsum_d"])
    emit_derived(p)
    sv = p.sv32
    O = SVW

    wk16, wkev = p.load_weight_cast("a1_wk", p.dram["wk_t"], NCT, C, "dma_a1w",
                                    stage_cols=4096, dma_eng="gpsimd")
    wv16, wvev = p.load_weight_cast("a1_wv", p.dram["wv_t"], NCT, C, "dma_a1w",
                                    stage_cols=4096, dma_eng="gpsimd")
    wr16, wrev = p.load_weight_cast("a1_wr", p.dram["wr_t"], NCT, C, "dma_a1w",
                                    stage_cols=4096, dma_eng="gpsimd")
    wo16, woev = p.load_weight_cast("a1_wo", p.dram["wo_t"], NCT, C, "dma_a1w",
                                    stage_cols=4096, dma_eng="gpsimd")

    sx = p.sb("a1_sx", [128, NCT * TC], F32)
    tmp32 = p.sb("a1_tmp", [128, NCT * TC], F32)
    sc16 = p.sb("a1_sc16", [128, NCT * TC], BF16)
    sq16 = p.sb("a1_sq16", [128, NCT * TC], BF16)
    x1 = p.sb("a1_x1", [128, NCT * TC], F32)
    hpad = p.sb("a1_hpad", [128, NCT * (TC + 1)], F32)
    mixk = p.sb("a1_mixk", [128, NCT * TC], BF16)
    mixv = p.sb("a1_mixv", [128, NCT * TC], BF16)
    mixr = p.sb("a1_mixr", [128, NCT * TC], BF16)
    ek = p.sb("a1_ek", [128, NCT * TC], F32)
    v32 = p.sb("a1_v32", [128, NCT * TC], F32)
    sr = p.sb("a1_sr", [128, NCT * TC], F32)
    out32 = p.sb("a1_out", [128, NCT * TC], F32)
    Apad = p.sb("a1_Apad", [128, NCT * (TC + 1)], F32)
    Bpad = p.sb("a1_Bpad", [128, NCT * (TC + 1)], F32)
    rowbuf = p.sb("a1_rows", [1, 6 * TC], F32)

    psA = p.ps("a1_psA", [1, TC])
    psB = p.ps("a1_psB", [1, TC])
    prep = p.ps("a1_prep", [128, 2 * TC])
    pkv = p.ps("a1_pkv", [128, NCT * TC])

    hp = hpad.ap().rearrange("p (ct t) -> p ct t", ct=NCT)
    Av = Apad.ap().rearrange("p (ct t) -> p ct t", ct=NCT)
    Bv = Bpad.ap().rearrange("p (ct t) -> p ct t", ct=NCT)

    def init(vector):
        vector.memset(hpad[:], 0.0)
        vector.memset(Apad[:], 0.0)
        p.inc(vector.memset(Bpad[:], 0.0), "v", 1, "a1_init")
    p.block.vector(init)

    def r3(buf):
        return buf.ap().rearrange("p (ct t) -> p ct t", ct=NCT)

    for i in range(NCH):
        tsl = slice(i * TC, (i + 1) * TC)

        def ld(sync, i=i, tsl=tsl):
            if i > 0:
                p.wait(sync, f"a1_xs_{i-1}")
            p.mark_dma(sync.dma_start(out=r3(sx), in_=xfm[:, :, tsl]),
                       "dma_a1x", f"a1_x_{i}")
        p.block.sync(ld)

        emit_stats(p, f"a1o{i}", sx, sc16, sq16, NCT, TC, psA, psB,
                   rowbuf, prep, [f"a1_x_{i}"] + (["a1_init", "derived_v", "derived_s"]
                                                  if i == 0 else []),
                   double=True)

        def vec_x1(vector, i=i):
            p.wait(vector, f"a1o{i}_prep")
            Mb = prep.ap()[:, 0:TC].unsqueeze(1).broadcast_to([128, NCT, TC])
            Rb = prep.ap()[:, TC:2 * TC].unsqueeze(1).broadcast_to([128, NCT, TC])
            vector.tensor_tensor(out=r3(tmp32), in0=r3(sx), in1=Mb, op=ALU.subtract)
            vector.tensor_tensor(out=r3(tmp32), in0=r3(tmp32), in1=Rb, op=ALU.mult)
            g0, _ = _SV["ln0_w"]
            b0, _ = _SV["ln0_b"]
            x1v = r3(x1)
            t3 = r3(tmp32)
            for ct in range(NCT):
                inst = vector.tensor_scalar(out=x1v[:, ct, :], in0=t3[:, ct, :],
                                            scalar1=sv[:, g0 + ct:g0 + ct + 1],
                                            scalar2=sv[:, b0 + ct:b0 + ct + 1],
                                            op0=ALU.mult, op1=ALU.add)
            p.inc(inst, "v", 1, f"a1_x1_{i}")
        p.block.vector(vec_x1)

        emit_stats(p, f"a1l{i}", x1, sc16, sq16, NCT, TC, psA, psB,
                   rowbuf, prep, [f"a1_x1_{i}"])

        def vec_h(vector, i=i):
            p.wait(vector, f"a1l{i}_prep")
            Mb = prep.ap()[:, 0:TC].unsqueeze(1).broadcast_to([128, NCT, TC])
            Rb = prep.ap()[:, TC:2 * TC].unsqueeze(1).broadcast_to([128, NCT, TC])
            vector.tensor_tensor(out=r3(tmp32), in0=r3(x1), in1=Mb, op=ALU.subtract)
            vector.tensor_tensor(out=r3(tmp32), in0=r3(tmp32), in1=Rb, op=ALU.mult)
            g1, _ = _SV["ln1_w"]
            b1, _ = _SV["ln1_b"]
            t3 = r3(tmp32)
            for ct in range(NCT):
                vector.tensor_scalar(out=hp[:, ct, 1:TC + 1], in0=t3[:, ct, :],
                                     scalar1=sv[:, g1 + ct:g1 + ct + 1],
                                     scalar2=sv[:, b1 + ct:b1 + ct + 1],
                                     op0=ALU.mult, op1=ALU.add)
            for mbuf, tm, om in [(mixk, "tmk", "om_tmk"), (mixv, "tmv", "om_tmv"),
                                 (mixr, "tmr", "om_tmr")]:
                to, _ = _SV[tm]
                oo = O + DV[om]
                mv = r3(mbuf)
                for ct in range(NCT):
                    vector.tensor_scalar(out=t3[:, ct, :], in0=hp[:, ct, 0:TC],
                                         scalar1=sv[:, oo + ct:oo + ct + 1],
                                         scalar2=None, op0=ALU.mult)
                    vector.scalar_tensor_tensor(out=mv[:, ct, :], in0=hp[:, ct, 1:TC + 1],
                                                scalar=sv[:, to + ct:to + ct + 1],
                                                in1=t3[:, ct, :],
                                                op0=ALU.mult, op1=ALU.add)
            inst = vector.tensor_copy(out=hp[:, :, 0:1], in_=hp[:, :, TC:TC + 1])
            p.inc(inst, "v", 1, f"a1_mix_{i}")
        p.block.vector(vec_h)

        p.reserve("s", f"a1_ek_{i}")
        p.reserve("s", f"a1_v_{i}")
        p.reserve("s", f"a1_sr_{i}")

        def proj(tensor, lhsT16, rhs_buf, wait_evs, ev, i=i):
            for e in wait_evs:
                p.wait(tensor, e)
            wv = lhsT16.ap().rearrange("p (ct o) -> p ct o", ct=NCT)
            rv = r3(rhs_buf)
            pv = pkv.ap().rearrange("p (ot t) -> p ot t", ot=NCT)
            for ot in range(NCT):
                for ct in range(NCT):
                    inst = tensor.matmul(pv[:, ot, :], wv[:, ct, ot * 128:(ot + 1) * 128],
                                         rv[:, ct, :],
                                         start=(ct == 0), stop=(ct == NCT - 1))
            p.inc(inst, "t", 1, ev)

        def tn_k(tensor, i=i):
            evs = [f"a1_mix_{i}"]
            if i == 0:
                evs.append(wkev)
            else:
                evs.append(f"a1_x2_{i-1}")
            proj(tensor, wk16, mixk, evs, f"a1_kmm_{i}", i)
        p.block.tensor(tn_k)

        def sc_k(scalar, i=i):
            p.wait(scalar, f"a1_kmm_{i}")
            p.inc_reserved(scalar.activation(out=ek[:], in_=pkv[:], func=AF.Exp),
                           f"a1_ek_{i}")
        p.block.scalar(sc_k)

        def tn_v(tensor, i=i):
            evs = [f"a1_ek_{i}"] + ([wvev] if i == 0 else [])
            proj(tensor, wv16, mixv, evs, f"a1_vmm_{i}", i)
        p.block.tensor(tn_v)

        def sc_v(scalar, i=i):
            p.wait(scalar, f"a1_vmm_{i}")
            p.inc_reserved(scalar.copy(out=v32[:], in_=pkv[:]), f"a1_v_{i}")
        p.block.scalar(sc_v)

        def tn_r(tensor, i=i):
            evs = [f"a1_v_{i}"] + ([wrev] if i == 0 else [])
            proj(tensor, wr16, mixr, evs, f"a1_rmm_{i}", i)
        p.block.tensor(tn_r)

        def sc_r(scalar, i=i):
            p.wait(scalar, f"a1_rmm_{i}")
            p.inc_reserved(scalar.activation(out=sr[:], in_=pkv[:], func=AF.Sigmoid),
                           f"a1_sr_{i}")
        p.block.scalar(sc_r)

        def vec_wkv(vector, i=i):
            p.wait(vector, f"a1_ek_{i}")
            p.wait(vector, f"a1_v_{i}")
            if i > 0:
                p.wait(vector, f"a1_st_{i-1}")
            vector.tensor_tensor(out=out32[:], in0=ek[:], in1=v32[:], op=ALU.mult)
            lam = O + DV["lam"]
            eu = O + DV["eu"]
            o3 = r3(out32)
            e3 = r3(ek)
            for ct in range(NCT):
                vector.tensor_tensor_scan(
                    out=Av[:, ct, 1:TC + 1],
                    data0=sv[:, lam + ct:lam + ct + 1].broadcast_to([128, TC]),
                    data1=o3[:, ct, :], initial=Av[:, ct, 0:1],
                    op0=ALU.mult, op1=ALU.add)
            for ct in range(NCT):
                vector.tensor_tensor_scan(
                    out=Bv[:, ct, 1:TC + 1],
                    data0=sv[:, lam + ct:lam + ct + 1].broadcast_to([128, TC]),
                    data1=e3[:, ct, :], initial=Bv[:, ct, 0:1],
                    op0=ALU.mult, op1=ALU.add)
            n3 = r3(v32)
            for ct in range(NCT):
                vector.scalar_tensor_tensor(out=n3[:, ct, :], in0=o3[:, ct, :],
                                            scalar=sv[:, eu + ct:eu + ct + 1],
                                            in1=Av[:, ct, 0:TC],
                                            op0=ALU.mult, op1=ALU.add)
            t3 = r3(tmp32)
            for ct in range(NCT):
                inst = vector.scalar_tensor_tensor(out=t3[:, ct, :], in0=e3[:, ct, :],
                                                   scalar=sv[:, eu + ct:eu + ct + 1],
                                                   in1=Bv[:, ct, 0:TC],
                                                   op0=ALU.mult, op1=ALU.add)
            p.inc(inst, "v", 1, f"a1_den_{i}")
        p.block.vector(vec_wkv)

        def sc_rden(scalar, i=i):
            p.wait(scalar, f"a1_den_{i}")
            scalar.activation(out=tmp32[:], in_=tmp32[:], func=AF.Ln)
            p.inc(scalar.activation(out=tmp32[:], in_=tmp32[:], func=AF.Exp,
                                    scale=-1.0),
                  "s", 1, f"a1_rden_{i}")
        p.block.scalar(sc_rden)

        def vec_y(vector, i=i):
            p.wait(vector, f"a1_rden_{i}")
            p.wait(vector, f"a1_sr_{i}")
            vector.tensor_tensor(out=ek[:], in0=v32[:], in1=tmp32[:], op=ALU.mult)
            vector.tensor_tensor(out=mixk[:], in0=sr[:], in1=ek[:], op=ALU.mult)
            vector.tensor_copy(out=Av[:, :, 0:1], in_=Av[:, :, TC:TC + 1])
            p.inc(vector.tensor_copy(out=Bv[:, :, 0:1], in_=Bv[:, :, TC:TC + 1]),
                  "v", 1, f"a1_satt_{i}")
        p.block.vector(vec_y)

        def tn_wo(tensor, i=i):
            evs = [f"a1_satt_{i}", f"a1_sr_{i}"] + ([woev] if i == 0 else [])
            proj(tensor, wo16, mixk, evs, f"a1_womm_{i}", i)
        p.block.tensor(tn_wo)

        def vec_x2(vector, i=i):
            p.wait(vector, f"a1_womm_{i}")
            vector.tensor_tensor(out=out32[:], in0=pkv[:], in1=x1[:], op=ALU.add)
            p.inc(vector.tensor_tensor(out=ek[:], in0=out32[:], in1=sx[:], op=ALU.add),
                  "v", 1, f"a1_xs_{i}")
            p.ev[f"a1_x2_{i}"] = p.ev[f"a1_xs_{i}"]
        p.block.vector(vec_x2)

        def st(sync, i=i, tsl=tsl):
            p.wait(sync, f"a1_xs_{i}")
            p.mark_dma(sync.dma_start(out=x2v[:, :, tsl], in_=r3(out32)),
                       "dma_a1s", None)
            p.mark_dma(sync.dma_start(out=xsv[:, :, tsl], in_=r3(ek)),
                       "dma_a1s", f"a1_st_{i}")
        p.block.sync(st)

    def fin(sync):
        p.wait(sync, f"a1_st_{NCH-1}")
    p.block.sync(fin)



def emit_A2(p):
    TC = 512
    NCH = T // TC
    NQ = 4
    QH = NHT // NQ
    x2v = fm_view(p.dram["x2_d"])
    xsv = fm_view(p.dram["xsum_d"])
    res1 = fm_view(p.dram["res1_d"])
    wfk16d = p.dram["wfk16_d"]
    wfv16d = p.dram["wfv16_d"]
    emit_derived(p)
    sv = p.sv32
    O = SVW

    wfr16, wfrev = p.load_weight_cast("a2_wfr", p.dram["wfr_t"], NCT, C, "dma_a2w",
                                      stage_cols=4096, dma_eng="gpsimd")

    x2c = p.sb("a2_x2c", [128, NCT * TC], F32)
    tmp32 = p.sb("a2_tmp", [128, NCT * TC], F32)
    sc16 = p.sb("a2_sc16", [128, NCT * TC], BF16)
    sq16 = p.sb("a2_sq16", [128, NCT * TC], BF16)
    h2pad = p.sb("a2_h2pad", [128, NCT * (TC + 1)], F32)
    mixk2 = p.sb("a2_mixk2", [128, NCT * TC], BF16)
    mixr2 = p.sb("a2_mixr2", [128, NCT * TC], BF16)
    kf16 = p.sb("a2_kf16", [128, QH * TC], BF16)
    kv32 = p.sb("a2_kv32", [128, NCT * TC], F32)
    xs = p.sb("a2_xs", [128, NCT * TC], F32)
    wfkq = p.sb("a2_wfkq", [128, NCT * 1024], BF16)
    wfvq = p.sb("a2_wfvq", [128, QH * 1024], BF16)
    rowbuf = p.sb("a2_rows", [1, 6 * TC], F32)

    psA = p.ps("a2_psA", [1, TC])
    psB = p.ps("a2_psB", [1, TC])
    prep = p.ps("a2_prep", [128, 2 * TC])
    pkf = [p.ps(f"a2_pkf{j}", [128, TC]) for j in range(2)]
    pkv = [p.ps(f"a2_pkv{j}", [128, TC]) for j in range(2)]

    hp = h2pad.ap().rearrange("p (ct t) -> p ct t", ct=NCT)

    def r3(buf, n=NCT):
        return buf.ap().rearrange("p (ct t) -> p ct t", ct=n)

    def init(vector):
        p.inc(vector.memset(h2pad[:], 0.0), "v", 1, "a2_init")
    p.block.vector(init)

    def emit_qload(i, q):
        if i == 0:
            _, ev1 = p.load_weight_cast(f"a2fk{q}", p.dram["wfk_t"][:, q * 1024:(q + 1) * 1024],
                                        NCT, 1024, "dma_a2w", stage_cols=4096,
                                        dma_eng="gpsimd", w16=wfkq,
                                        halt_event=f"a2_kfmm_0_{q-1}" if q else None)
            _, ev2 = p.load_weight_cast(f"a2fv{q}", p.dram["wfv_t"][q * 1024:(q + 1) * 1024, :],
                                        QH, C, "dma_a2w", stage_cols=4096,
                                        dma_eng="gpsimd", w16=wfvq,
                                        halt_event=f"a2_kvmm_0_{q-1}" if q else None)
            p.ev[f"a2_fkld_{i}_{q}"] = p.ev[ev1]
            p.ev[f"a2_fvld_{i}_{q}"] = p.ev[ev2]

            def cst(gpsimd, q=q):
                p.wait(gpsimd, ev1)
                p.mark_dma(gpsimd.dma_start(out=wfk16d[:, q * 1024:(q + 1) * 1024]
                                            .rearrange("(ct p) o -> p ct o", p=128),
                                            in_=r3(wfkq)),
                           "dma_a2c", f"a2_fkcst_{q}")
                p.wait(gpsimd, ev2)
                p.mark_dma(gpsimd.dma_start(out=wfv16d[q * 1024:(q + 1) * 1024, :]
                                            .rearrange("(ct p) o -> p ct o", p=128),
                                            in_=r3(wfvq, QH)),
                           "dma_a2c", f"a2_fvcst_{q}")
            p.block.gpsimd(cst)
        else:
            def ldq(gpsimd, i=i, q=q):
                p.wait(gpsimd, f"a2_fkcst_{q}")
                p.wait(gpsimd, f"a2_fvcst_{q}")
                prev = (i, q - 1) if q > 0 else (i - 1, NQ - 1)
                p.wait(gpsimd, f"a2_kfmm_{prev[0]}_{prev[1]}")
                p.mark_dma(gpsimd.dma_start(
                    out=r3(wfkq),
                    in_=wfk16d[:, q * 1024:(q + 1) * 1024].rearrange(
                        "(ct p) o -> p ct o", p=128)),
                    "dma_a2w", f"a2_fkld_{i}_{q}")
                p.wait(gpsimd, f"a2_kvmm_{prev[0]}_{prev[1]}")
                p.mark_dma(gpsimd.dma_start(
                    out=r3(wfvq, QH),
                    in_=wfv16d[q * 1024:(q + 1) * 1024, :].rearrange(
                        "(ct p) o -> p ct o", p=128)),
                    "dma_a2w", f"a2_fvld_{i}_{q}")
            p.block.gpsimd(ldq)

    for i in range(NCH):
        tsl = slice(i * TC, (i + 1) * TC)

        def ld(sync, i=i, tsl=tsl):
            if i > 0:
                p.wait(sync, f"a2_mix_{i-1}")
                p.wait(sync, f"a2_st_{i-1}")
            p.mark_dma(sync.dma_start(out=r3(x2c), in_=x2v[:, :, tsl]),
                       "dma_a2x", f"a2_x2_{i}")
            if i > 0:
                p.wait(sync, f"a2_out_{i-1}")
            p.mark_dma(sync.dma_start(out=r3(xs), in_=xsv[:, :, tsl]),
                       "dma_a2x", f"a2_xs_{i}")
        p.block.sync(ld)

        emit_stats(p, f"a2l{i}", x2c, sc16, sq16, NCT, TC, psA, psB,
                   rowbuf, prep, [f"a2_x2_{i}"] + (["a2_init", "derived_v",
                                                    "derived_s"] if i == 0 else []))

        def vec_h(vector, i=i):
            p.wait(vector, f"a2l{i}_prep")
            Mb = prep.ap()[:, 0:TC].unsqueeze(1).broadcast_to([128, NCT, TC])
            Rb = prep.ap()[:, TC:2 * TC].unsqueeze(1).broadcast_to([128, NCT, TC])
            vector.tensor_tensor(out=r3(tmp32), in0=r3(x2c), in1=Mb, op=ALU.subtract)
            vector.tensor_tensor(out=r3(tmp32), in0=r3(tmp32), in1=Rb, op=ALU.mult)
            g2, _ = _SV["ln2_w"]
            b2, _ = _SV["ln2_b"]
            t3 = r3(tmp32)
            for ct in range(NCT):
                vector.tensor_scalar(out=hp[:, ct, 1:TC + 1], in0=t3[:, ct, :],
                                     scalar1=sv[:, g2 + ct:g2 + ct + 1],
                                     scalar2=sv[:, b2 + ct:b2 + ct + 1],
                                     op0=ALU.mult, op1=ALU.add)
            for mbuf, tm, om in [(mixk2, "cmk", "om_cmk"), (mixr2, "cmr", "om_cmr")]:
                to, _ = _SV[tm]
                oo = O + DV[om]
                mv = r3(mbuf)
                for ct in range(NCT):
                    vector.tensor_scalar(out=t3[:, ct, :], in0=hp[:, ct, 0:TC],
                                         scalar1=sv[:, oo + ct:oo + ct + 1],
                                         scalar2=None, op0=ALU.mult)
                    vector.scalar_tensor_tensor(out=mv[:, ct, :], in0=hp[:, ct, 1:TC + 1],
                                                scalar=sv[:, to + ct:to + ct + 1],
                                                in1=t3[:, ct, :],
                                                op0=ALU.mult, op1=ALU.add)
            inst = vector.tensor_copy(out=hp[:, :, 0:1], in_=hp[:, :, TC:TC + 1])
            p.inc(inst, "v", 1, f"a2_mix_{i}")
        p.block.vector(vec_h)

        for q in range(NQ):
            emit_qload(i, q)
            for ht in range(QH):
                p.reserve("s", f"a2_kfev_{i}_{q}_{ht}")

            def tn_kf(tensor, i=i, q=q):
                p.wait(tensor, f"a2_mix_{i}")
                p.wait(tensor, f"a2_fkld_{i}_{q}")
                wv = r3(wfkq)
                mv = r3(mixk2)
                for ht in range(QH):
                    if ht >= 2:
                        p.wait(tensor, f"a2_kfev_{i}_{q}_{ht-2}")
                    elif (i, q) != (0, 0):
                        pi, pq = (i, q - 1) if q else (i - 1, NQ - 1)
                        p.wait(tensor, f"a2_kfev_{pi}_{pq}_{QH-2+ht}")
                    for ct in range(NCT):
                        inst = tensor.matmul(pkf[ht % 2][:],
                                             wv[:, ct, ht * 128:(ht + 1) * 128],
                                             mv[:, ct, :],
                                             start=(ct == 0), stop=(ct == NCT - 1))
                    p.inc(inst, "t", 1, f"a2_kfg_{i}_{q}_{ht}")
                p.ev[f"a2_kfmm_{i}_{q}"] = p.ev[f"a2_kfg_{i}_{q}_{QH-1}"]
            p.block.tensor(tn_kf)

            def sc_kf(scalar, i=i, q=q):
                kv = r3(kf16, QH)
                for ht in range(QH):
                    p.wait(scalar, f"a2_kfg_{i}_{q}_{ht}")
                    scalar.activation(out=kv[:, ht, :], in_=pkf[ht % 2][:],
                                      func=AF.Relu)
                    p.inc_reserved(scalar.activation(out=kv[:, ht, :], in_=kv[:, ht, :],
                                                     func=AF.Square),
                                   f"a2_kfev_{i}_{q}_{ht}")
            p.block.scalar(sc_kf)

            for ct in range(NCT):
                p.reserve("v", f"a2_kvev_{i}_{q}_{ct}")

            def tn_kv(tensor, i=i, q=q):
                p.wait(tensor, f"a2_kfev_{i}_{q}_{QH-1}")
                p.wait(tensor, f"a2_fvld_{i}_{q}")
                wv = r3(wfvq, QH)
                kv = r3(kf16, QH)
                for ct in range(NCT):
                    if ct >= 2:
                        p.wait(tensor, f"a2_kvev_{i}_{q}_{ct-2}")
                    elif (i, q) != (0, 0):
                        pi, pq = (i, q - 1) if q else (i - 1, NQ - 1)
                        p.wait(tensor, f"a2_kvev_{pi}_{pq}_{NCT-2+ct}")
                    for ht in range(QH):
                        inst = tensor.matmul(pkv[ct % 2][:],
                                             wv[:, ht, ct * 128:(ct + 1) * 128],
                                             kv[:, ht, :],
                                             start=(ht == 0), stop=(ht == QH - 1))
                    p.inc(inst, "t", 1, f"a2_kvg_{i}_{q}_{ct}")
                p.ev[f"a2_kvmm_{i}_{q}"] = p.ev[f"a2_kvg_{i}_{q}_{NCT-1}"]
            p.block.tensor(tn_kv)

            def vec_kv(vector, i=i, q=q):
                k3 = r3(kv32)
                for ct in range(NCT):
                    p.wait(vector, f"a2_kvg_{i}_{q}_{ct}")
                    if q == 0:
                        inst = vector.tensor_copy(out=k3[:, ct, :], in_=pkv[ct % 2][:])
                    else:
                        inst = vector.tensor_tensor(out=k3[:, ct, :], in0=pkv[ct % 2][:],
                                                    in1=k3[:, ct, :], op=ALU.add)
                    p.inc_reserved(inst, f"a2_kvev_{i}_{q}_{ct}")
            p.block.vector(vec_kv)

        for ot in range(NCT):
            p.reserve("s", f"a2_srev_{i}_{ot}")

        def tn_r(tensor, i=i):
            p.wait(tensor, f"a2_mix_{i}")
            if i == 0:
                p.wait(tensor, wfrev)
            wv = r3(wfr16)
            mv = r3(mixr2)
            for ot in range(NCT):
                if ot >= 2:
                    p.wait(tensor, f"a2_srev_{i}_{ot-2}")
                else:
                    p.wait(tensor, f"a2_kfev_{i}_{NQ-1}_{QH-1}")
                for ct in range(NCT):
                    inst = tensor.matmul(pkf[ot % 2][:], wv[:, ct, ot * 128:(ot + 1) * 128],
                                         mv[:, ct, :],
                                         start=(ct == 0), stop=(ct == NCT - 1))
                p.inc(inst, "t", 1, f"a2_rmm_{i}_{ot}")
        p.block.tensor(tn_r)

        def sc_sr(scalar, i=i):
            t3 = r3(tmp32)
            for ot in range(NCT):
                p.wait(scalar, f"a2_rmm_{i}_{ot}")
                p.inc_reserved(scalar.activation(out=t3[:, ot, :], in_=pkf[ot % 2][:],
                                                 func=AF.Sigmoid),
                               f"a2_srev_{i}_{ot}")
        p.block.scalar(sc_sr)

        def vec_out(vector, i=i):
            p.wait(vector, f"a2_srev_{i}_{NCT-1}")
            p.wait(vector, f"a2_kvev_{i}_{NQ-1}_{NCT-1}")
            p.wait(vector, f"a2_xs_{i}")
            vector.tensor_tensor(out=kv32[:], in0=kv32[:], in1=tmp32[:], op=ALU.mult)
            p.inc(vector.tensor_tensor(out=x2c[:], in0=kv32[:], in1=xs[:], op=ALU.add),
                  "v", 1, f"a2_out_{i}")
        p.block.vector(vec_out)

        def st(sync, i=i, tsl=tsl):
            p.wait(sync, f"a2_out_{i}")
            p.mark_dma(sync.dma_start(out=res1[:, :, tsl], in_=r3(x2c)),
                       "dma_a2s", f"a2_st_{i}")
        p.block.sync(st)

    def fin(sync):
        p.wait(sync, f"a2_st_{NCH-1}")
    p.block.sync(fin)



def emit_B(p):
    nc = p.nc
    TC = 256
    NCH = T // TC
    res1 = fm_view(p.dram["res1_d"])
    res2 = fm_view(p.dram["res2_d"])

    w116, w1ev = p.load_weight_cast("b_w1", p.dram["l0f_w1"], NCT, H, "dma_b",
                                    stage_cols=4096)
    w216, w2ev = p.load_weight_cast("b_w2", p.dram["l0f_w2"], NHT, C, "dma_b",
                                    stage_cols=4096)

    sx = p.sb("b_sx", [128, NCT * TC], F32)
    tmp32 = p.sb("b_tmp", [128, NCT * TC], F32)
    sc16 = p.sb("b_sc16", [128, NCT * TC], BF16)
    sq16 = p.sb("b_sq16", [128, NCT * TC], BF16)
    h16 = p.sb("b_h16", [128, NHT * TC], BF16)
    rowbuf = p.sb("b_rows", [1, 6 * TC], F32)
    out32 = p.sb("b_out", [128, NCT * TC], F32)

    psA = p.ps("b_psA", [1, TC])
    psB = p.ps("b_psB", [1, TC])
    prep = p.ps("b_prep", [128, 2 * TC])
    ph = [p.ps(f"b_ph{i}", [128, TC]) for i in range(2)]
    po = [p.ps(f"b_po{i}", [128, TC]) for i in range(2)]

    sv16 = p.sv16
    sv = p.sv32

    for i in range(NCH):
        tsl = slice(i * TC, (i + 1) * TC)

        def ld(sync, i=i, tsl=tsl):
            if i >= 1:
                p.wait(sync, f"b_res_done_{i-1}")
            p.mark_dma(sync.dma_start(out=sx.ap().rearrange("p (ct t) -> p ct t", ct=NCT),
                                      in_=res1[:, :, tsl]),
                       "dma_b", f"b_x_{i}")
        p.block.sync(ld)

        if i == 0:
            pass

        prep_ev = emit_stats(p, f"bln{i}", sx, sc16, sq16, NCT, TC,
                             psA, psB, rowbuf, prep, [f"b_x_{i}"])

        def vec_ln(vector, i=i):
            p.wait(vector, f"bln{i}_prep")
            Mb = prep.ap()[:, 0:TC].unsqueeze(1).broadcast_to([128, NCT, TC])
            Rb = prep.ap()[:, TC:2 * TC].unsqueeze(1).broadcast_to([128, NCT, TC])
            xv = sx.ap().rearrange("p (ct t) -> p ct t", ct=NCT)
            tv = tmp32.ap().rearrange("p (ct t) -> p ct t", ct=NCT)
            ov = sq16.ap().rearrange("p (ct t) -> p ct t", ct=NCT)
            vector.tensor_tensor(out=tv, in0=xv, in1=Mb, op=ALU.subtract)
            p.inc(vector.tensor_tensor(out=ov, in0=tv, in1=Rb, op=ALU.mult),
                  "v", 1, f"b_xln_{i}")
        p.block.vector(vec_ln)

        for ht in range(NHT):
            p.reserve("s", f"b_hev_{i}_{ht}")
        for ct in range(NCT):
            p.reserve("v", f"b_oev_{i}_{ct}")
        p.ev[f"b_res_done_{i}"] = p.ev[f"b_oev_{i}_{NCT-1}"]

        def tn_h(tensor, i=i):
            p.wait(tensor, f"b_xln_{i}")
            if i == 0:
                p.wait(tensor, w1ev)
            xl = sq16.ap().rearrange("p (ct t) -> p ct t", ct=NCT)
            w1v = w116.ap().rearrange("p (ct h) -> p ct h", ct=NCT)
            for ht in range(NHT):
                if ht >= 2:
                    p.wait(tensor, f"b_hev_{i}_{ht-2}")
                elif i > 0:
                    p.wait(tensor, f"b_hev_{i-1}_{NHT-2+ht}")
                pb = ph[ht % 2]
                for ct in range(NCT):
                    inst = tensor.matmul(pb[:], w1v[:, ct, ht * 128:(ht + 1) * 128],
                                         xl[:, ct, :],
                                         start=(ct == 0), stop=(ct == NCT - 1))
                p.inc(inst, "t", 1, f"b_hmm_{i}_{ht}")
        p.block.tensor(tn_h)

        def sc_h(scalar, i=i):
            b1off = _SV["l0f_b1"][0]
            hv = h16.ap().rearrange("p (ht t) -> p ht t", ht=NHT)
            for ht in range(NHT):
                p.wait(scalar, f"b_hmm_{i}_{ht}")
                p.inc_reserved(scalar.activation(out=hv[:, ht, :], in_=ph[ht % 2][:],
                                                 func=AF.Relu,
                                                 bias=sv[:, b1off + ht:b1off + ht + 1]),
                               f"b_hev_{i}_{ht}")
        p.block.scalar(sc_h)

        def tn_o(tensor, i=i):
            if i == 0:
                p.wait(tensor, w2ev)
            hv = h16.ap().rearrange("p (ht t) -> p ht t", ht=NHT)
            w2v = w216.ap().rearrange("p (ht c) -> p ht c", ht=NHT)
            p.wait(tensor, f"b_hev_{i}_{NHT-1}")
            for ct in range(NCT):
                if ct >= 2:
                    p.wait(tensor, f"b_oev_{i}_{ct-2}")
                elif i > 0:
                    p.wait(tensor, f"b_oev_{i-1}_{NCT-2+ct}")
                pb = po[ct % 2]
                for ht in range(NHT):
                    inst = tensor.matmul(pb[:], w2v[:, ht, ct * 128:(ct + 1) * 128],
                                         hv[:, ht, :],
                                         start=(ht == 0), stop=(ht == NHT - 1))
                p.inc(inst, "t", 1, f"b_omm_{i}_{ct}")
        p.block.tensor(tn_o)

        def vec_o(vector, i=i):
            b2off = _SV["l0f_b2"][0]
            xv = sx.ap().rearrange("p (ct t) -> p ct t", ct=NCT)
            ov = out32.ap().rearrange("p (ct t) -> p ct t", ct=NCT)
            if i > 0:
                p.wait(vector, f"b_st_{i-1}")
            for ct in range(NCT):
                p.wait(vector, f"b_omm_{i}_{ct}")
                inst = vector.scalar_tensor_tensor(
                    out=ov[:, ct, :], in0=po[ct % 2][:],
                    scalar=sv[:, b2off + ct:b2off + ct + 1],
                    in1=xv[:, ct, :], op0=ALU.add, op1=ALU.add)
                p.inc_reserved(inst, f"b_oev_{i}_{ct}")
        p.block.vector(vec_o)

        def st(gpsimd, i=i, tsl=tsl):
            p.wait(gpsimd, f"b_res_done_{i}")
            p.mark_dma(gpsimd.dma_start(
                out=res2[:, :, tsl],
                in_=out32.ap().rearrange("p (ct t) -> p ct t", ct=NCT)),
                "dma_bs", f"b_st_{i}")
            p.ev[f"b_out_used_{i}"] = p.ev[f"b_st_{i}"]
        p.block.gpsimd(st)

    def fin(gpsimd):
        p.wait(gpsimd, f"b_st_{NCH-1}")
    p.block.gpsimd(fin)



def emit_D(p):
    TC = 512
    NCH = T // TC
    res3 = fm_view(p.dram["res3_d"])
    yfm = fm_view(p.dram["y_fm"])
    sv = p.sv32

    xc = p.sb("d_xc", [128, NCT * TC], F32)
    tmp32 = p.sb("d_tmp", [128, NCT * TC], F32)
    sc16 = p.sb("d_sc16", [128, NCT * TC], BF16)
    sq16 = p.sb("d_sq16", [128, NCT * TC], BF16)
    xln16 = sq16
    t216 = p.sb("d_t216", [128, NCT * TC], BF16)
    hA = p.sb("d_hA", [128, NHT * TC], BF16)
    hB = p.sb("d_hB", [128, NHT * TC], BF16)
    wbuf = p.sb("d_wbuf", [128, 16384], BF16)
    out32 = tmp32
    rowbuf = p.sb("d_rows", [1, 6 * TC], F32)

    psA = p.ps("d_psA", [1, TC])
    psB = p.ps("d_psB", [1, TC])
    prep = p.ps("d_prep", [128, 2 * TC])
    pm = [p.ps(f"d_pm{j}", [128, TC]) for j in range(2)]

    def r3(buf, n=NCT):
        return buf.ap().rearrange("p (ct t) -> p ct t", ct=n)

    BLOCKS = []
    for ob in range(4):
        BLOCKS.append(("f1", "f1_t", "f116_d", NCT, 1024, ob * 1024))
        BLOCKS.append(("f2", "f2_t", "f216_d", NCT, 1024, ob * 1024))
    for ob in range(8):
        BLOCKS.append(("h1", "fh1_t", "fh116_d", NHT, 512, ob * 512))
    for ob in range(8):
        BLOCKS.append(("h2", "fh2_t", "fh216_d", NHT, 512, ob * 512))
    for ob in range(2):
        BLOCKS.append(("w3", "f3_t", "f316_d", NHT, 512, ob * 512))
    NBLK = len(BLOCKS)

    evac_seq = {}

    def mm_bank_wait(tensor, i, g):
        if g >= 2:
            p.wait(tensor, evac_seq[i][g - 2])
        elif i > 0:
            p.wait(tensor, evac_seq[i - 1][len(evac_seq[i - 1]) - 2 + g])
        return pm[g % 2]

    def emit_wload(i, b):
        nm, src, cache, nin, nout, o0 = BLOCKS[b]
        ldev = f"d_wld_{i}_{b}"
        halt = (f"d_wmm_{i}_{b-1}" if b else f"d_wmm_{i-1}_{NBLK-1}") \
            if (i, b) != (0, 0) else None
        if i == 0:
            src_ap = p.dram[src][:, o0:o0 + nout]
            _, ev = p.load_weight_cast(f"d_{nm}{o0}", src_ap, nin, nout, "dma_dw",
                                       stage_cols=4096, dma_eng="gpsimd",
                                       w16=wbuf, halt_event=halt)
            p.ev[ldev] = p.ev[ev]

            def cst(gpsimd, b=b, cache=cache, nin=nin, nout=nout, o0=o0, ev=ev):
                p.wait(gpsimd, ev)
                p.mark_dma(gpsimd.dma_start(
                    out=p.dram[cache][:, o0:o0 + nout].rearrange(
                        "(ct p) o -> p ct o", p=128),
                    in_=wbuf.ap()[:, 0:nin * nout].rearrange(
                        "p (ct o) -> p ct o", ct=nin)),
                    "dma_dc", f"d_cst_{b}")
            p.block.gpsimd(cst)
        else:
            def ldq(gpsimd, i=i, b=b, cache=cache, nin=nin, nout=nout, o0=o0, halt=halt):
                p.wait(gpsimd, f"d_cst_{b}")
                p.wait(gpsimd, halt)
                p.mark_dma(gpsimd.dma_start(
                    out=wbuf.ap()[:, 0:nin * nout].rearrange(
                        "p (ct o) -> p ct o", ct=nin),
                    in_=p.dram[cache][:, o0:o0 + nout].rearrange(
                        "(ct p) o -> p ct o", p=128)),
                    "dma_dw", ldev)
            p.block.gpsimd(ldq)

    for i in range(NCH):
        tsl = slice(i * TC, (i + 1) * TC)
        evac_seq[i] = []
        gctr = [0]

        def ld(sync, i=i, tsl=tsl):
            if i > 0:
                p.wait(sync, f"d_out_{i-1}")
            p.mark_dma(sync.dma_start(out=r3(xc), in_=res3[:, :, tsl]),
                       "dma_d", f"d_x_{i}")
        p.block.sync(ld)

        emit_stats(p, f"dl{i}", xc, sc16, sq16, NCT, TC, psA, psB,
                   rowbuf, prep, [f"d_x_{i}"])

        def vec_app(vector, i=i):
            p.wait(vector, f"dl{i}_prep")
            if i > 0:
                p.wait(vector, f"d_st_{i-1}")
            Mb = prep.ap()[:, 0:TC].unsqueeze(1).broadcast_to([128, NCT, TC])
            Rb = prep.ap()[:, TC:2 * TC].unsqueeze(1).broadcast_to([128, NCT, TC])
            vector.tensor_tensor(out=r3(tmp32), in0=r3(xc), in1=Mb, op=ALU.subtract)
            vector.tensor_tensor(out=r3(tmp32), in0=r3(tmp32), in1=Rb, op=ALU.mult)
            p.inc(vector.tensor_copy(out=xln16[:], in_=tmp32[:]), "v", 1, f"d_xln_{i}")
        p.block.vector(vec_app)

        hAv = r3(hA, NHT)
        hBv = r3(hB, NHT)
        t2v = r3(t216, NCT)

        for ob in range(4):
            for half, nm in ((0, "f1"), (1, "f2")):
                b = ob * 2 + half
                emit_wload(i, b)
                for sub in range(8):
                    p.reserve("s", f"d_gev_{i}_{b}_{sub}")
                    evac_seq[i].append(f"d_gev_{i}_{b}_{sub}")

                def tn_g(tensor, i=i, b=b):
                    p.wait(tensor, f"d_wld_{i}_{b}")
                    p.wait(tensor, f"d_xln_{i}")
                    wv = wbuf.ap()[:, 0:NCT * 1024].rearrange("p (ct o) -> p ct o",
                                                              ct=NCT)
                    xv = r3(xln16)
                    for sub in range(8):
                        pb = mm_bank_wait(tensor, i, gctr[0]); gctr[0] += 1
                        for ct in range(NCT):
                            inst = tensor.matmul(
                                pb[:], wv[:, ct, sub * 128:(sub + 1) * 128],
                                xv[:, ct, :],
                                start=(ct == 0), stop=(ct == NCT - 1))
                        p.inc(inst, "t", 1, f"d_gmm_{i}_{b}_{sub}")
                    p.ev[f"d_wmm_{i}_{b}"] = p.ev[f"d_gmm_{i}_{b}_{7}"]
                p.block.tensor(tn_g)

                def sc_g(scalar, i=i, b=b, ob=ob, half=half):
                    s1o, _ = _SV["l1f_s1"]
                    s2o, _ = _SV["l1f_s2"]
                    for sub in range(8):
                        glob = ob * 8 + sub
                        p.wait(scalar, f"d_gmm_{i}_{b}_{sub}")
                        bank = (b * 8 + sub) % 2
                        if half == 0:
                            inst = scalar.activation(
                                out=hAv[:, glob, :], in_=pm[bank][:], func=AF.Silu,
                                scale=sv[:, s1o + glob:s1o + glob + 1])
                        else:
                            if i > 0 or ob > 0:
                                p.wait(scalar, f"d_hA_{i}_{(ob-1)*8+sub}" if ob
                                       else f"d_hA_{i-1}_{24+sub}")
                            inst = scalar.activation(
                                out=t2v[:, sub, :], in_=pm[bank][:], func=AF.Identity,
                                scale=sv[:, s2o + glob:s2o + glob + 1])
                        p.inc_reserved(inst, f"d_gev_{i}_{b}_{sub}")
                        if half == 1:
                            p.reserve("v", f"d_hA_{i}_{glob}")
                p.block.scalar(sc_g)

                if half == 1:
                    def vec_g(vector, i=i, b=b, ob=ob):
                        for sub in range(8):
                            glob = ob * 8 + sub
                            p.wait(vector, f"d_gev_{i}_{b}_{sub}")
                            p.inc_reserved(
                                vector.tensor_tensor(out=hAv[:, glob, :],
                                                     in0=hAv[:, glob, :],
                                                     in1=t2v[:, sub, :], op=ALU.mult),
                                f"d_hA_{i}_{glob}")
                    p.block.vector(vec_g)
        p.ev[f"d_hA_done_{i}"] = p.ev[f"d_hA_{i}_{NHT-1}"]

        for hph in range(2):
            srcv = hAv if hph == 0 else hBv
            dstv = hBv if hph == 0 else hAv
            svn = "l1f_hs1" if hph == 0 else "l1f_hs2"
            base_b = 8 + 8 * hph
            in_ev = f"d_hA_done_{i}" if hph == 0 else f"d_hB_done_{i}"
            for ob in range(8):
                b = base_b + ob
                emit_wload(i, b)
                for sub in range(4):
                    p.reserve("s", f"d_hev_{i}_{b}_{sub}")
                    evac_seq[i].append(f"d_hev_{i}_{b}_{sub}")

                def tn_h(tensor, i=i, b=b, srcv=srcv, in_ev=in_ev):
                    p.wait(tensor, f"d_wld_{i}_{b}")
                    p.wait(tensor, in_ev)
                    wv = wbuf.ap()[:, 0:NHT * 512].rearrange("p (ht o) -> p ht o",
                                                             ht=NHT)
                    for sub in range(4):
                        pb = mm_bank_wait(tensor, i, gctr[0]); gctr[0] += 1
                        for ht in range(NHT):
                            inst = tensor.matmul(
                                pb[:], wv[:, ht, sub * 128:(sub + 1) * 128],
                                srcv[:, ht, :],
                                start=(ht == 0), stop=(ht == NHT - 1))
                        p.inc(inst, "t", 1, f"d_hmm_{i}_{b}_{sub}")
                    p.ev[f"d_wmm_{i}_{b}"] = p.ev[f"d_hmm_{i}_{b}_{3}"]
                p.block.tensor(tn_h)

                def sc_h(scalar, i=i, b=b, ob=ob, dstv=dstv, svn=svn, base_b=base_b):
                    so, _ = _SV[svn]
                    for sub in range(4):
                        glob = ob * 4 + sub
                        p.wait(scalar, f"d_hmm_{i}_{b}_{sub}")
                        bank = (64 + (b - 8) * 4 + sub) % 2
                        p.inc_reserved(
                            scalar.activation(out=dstv[:, glob, :], in_=pm[bank][:],
                                              func=AF.Gelu,
                                              scale=sv[:, so + glob:so + glob + 1]),
                            f"d_hev_{i}_{b}_{sub}")
                p.block.scalar(sc_h)
            p.ev[f"d_hB_done_{i}" if hph == 0 else f"d_hC_done_{i}"] = \
                p.ev[f"d_hev_{i}_{base_b+7}_{3}"]

        for ob in range(2):
            b = 24 + ob
            emit_wload(i, b)
            for sub in range(4):
                p.reserve("v", f"d_w3ev_{i}_{ob*4+sub}")
                evac_seq[i].append(f"d_w3ev_{i}_{ob*4+sub}")

            def tn_w3(tensor, i=i, b=b):
                p.wait(tensor, f"d_wld_{i}_{b}")
                p.wait(tensor, f"d_hC_done_{i}")
                wv = wbuf.ap()[:, 0:NHT * 512].rearrange("p (ht o) -> p ht o", ht=NHT)
                for sub in range(4):
                    pb = mm_bank_wait(tensor, i, gctr[0]); gctr[0] += 1
                    for ht in range(NHT):
                        inst = tensor.matmul(
                            pb[:], wv[:, ht, sub * 128:(sub + 1) * 128],
                            hAv[:, ht, :],
                            start=(ht == 0), stop=(ht == NHT - 1))
                    p.inc(inst, "t", 1, f"d_w3mm_{i}_{(b-24)*4+sub}")
                p.ev[f"d_wmm_{i}_{b}"] = p.ev[f"d_w3mm_{i}_{(b-24)*4+3}"]
            p.block.tensor(tn_w3)

            def vec_w3(vector, i=i, b=b, ob=ob):
                s3o, _ = _SV["l1f_s3"]
                ov = r3(out32)
                xv = r3(xc)
                for sub in range(4):
                    glob = ob * 4 + sub
                    p.wait(vector, f"d_w3mm_{i}_{glob}")
                    if i > 0 and glob == 0:
                        p.wait(vector, f"d_st_{i-1}")
                    bank = (128 + glob) % 2
                    p.inc_reserved(
                        vector.scalar_tensor_tensor(out=ov[:, glob, :],
                                                    in0=pm[bank][:],
                                                    scalar=sv[:, s3o + glob:s3o + glob + 1],
                                                    in1=xv[:, glob, :],
                                                    op0=ALU.mult, op1=ALU.add),
                        f"d_w3ev_{i}_{glob}")
            p.block.vector(vec_w3)
        p.ev[f"d_out_{i}"] = p.ev[f"d_w3ev_{i}_{7}"]

        def st(sync, i=i, tsl=tsl):
            p.wait(sync, f"d_out_{i}")
            p.mark_dma(sync.dma_start(out=yfm[:, :, tsl], in_=r3(out32)),
                       "dma_ds", f"d_st_{i}")
        p.block.sync(st)

    def fin(sync):
        p.wait(sync, f"d_st_{NCH-1}")
    p.block.sync(fin)



def emit_prelude(p):
    nc = p.nc
    p.sv32 = p.sb("sv32", [128, SVW + 64], F32)
    p.sv16 = p.sb("sv16", [128, SVW], BF16)
    p.rv32 = p.sb("rv32", [1, RVW], F32)
    p.ones16 = p.sb("ones16", [128, 1], BF16)
    p.ones32 = p.sb("ones32", [1, 128], F32)

    def ld(sync):
        p.mark_dma(sync.dma_start(out=p.sv32[:, 0:SVW], in_=p.dram["smallvecs"][:]),
                   "dma_pre", "sv_ld")
        p.mark_dma(sync.dma_start(out=p.rv32[:], in_=p.dram["rowvecs"][:]),
                   "dma_pre", "rv_ld")
    p.block.sync(ld)

    def vec(vector):
        p.wait(vector, "sv_ld")
        p.wait(vector, "rv_ld")
        vector.memset(p.ones16[:], 1.0)
        vector.memset(p.ones32[:], 1.0)
        p.inc(vector.tensor_copy(out=p.sv16[:], in_=p.sv32[:, 0:SVW]), "v", 1, "sv16_ready")
        p.ev["pre_ready"] = ("v", p.cnt["v"])
    p.block.vector(vec)

    p.eps_ap = p.rv32.ap()[:, _RV["eps"][0]:_RV["eps"][0] + 1]

    def allwait(eng):
        p.wait(eng, "pre_ready")
    for n in ["sync", "vector", "scalar", "tensor", "gpsimd"]:
        getattr(p.block, n)(allwait)



def build_program(phases):
    nc = bass.Bass("TRN2", target_bir_lowering=False)
    ctx = ExitStack()
    dram = declare_dram(nc, phases)
    block = ctx.enter_context(nc.Block())
    p = P(nc, block, ctx, dram, phases)
    emit_prelude(p)
    for ph in phases:
        if ph == "A1":
            emit_A1(p)
        elif ph == "A2":
            emit_A2(p)
        elif ph == "B":
            emit_B(p)
        elif ph == "C0":
            emit_C0(p)
        elif ph == "C1":
            emit_C1(p)
        elif ph == "C3":
            emit_C3(p)
        elif ph == "C4":
            emit_C4(p)
        elif ph == "D":
            emit_D(p)
        else:
            raise NotImplementedError(ph)
    ctx.close()
    return nc


def run_phases(phase_groups, host, extra_outputs=()):
    acts = {}
    results = None
    for phases in phase_groups:
        nc = build_program(phases)
        in_map = {}
        pset = set(phases)
        for name, (shape, prod) in WEIGHTS.items():
            if prod in pset:
                in_map[name] = host[name]
        in_map["smallvecs"] = host["smallvecs"]
        in_map["rowvecs"] = host["rowvecs"]
        per_core = [dict(in_map) for _ in range(NCORE)]
        for name, (shape, dt, prod, cons) in ACTS.items():
            if name == "x_fm":
                if pset & {"A1", "A2"}:
                    for b in range(NCORE):
                        per_core[b]["x_fm"] = host["x_fm"][b]
                continue
            if (cons & pset) and prod not in pset:
                for b in range(NCORE):
                    per_core[b][name] = acts[name][b]
        res = run_bass_kernel_spmd(nc, per_core, core_ids=list(range(NCORE)))
        for b, r in enumerate(res.results):
            for k, v in r.items():
                acts.setdefault(k, [None] * NCORE)[b] = v
    return acts


def kernel(**inputs):
    host = prep_inputs(inputs)
    acts = run_phases([["A1"], ["A2"], ["B"], ["C0"], ["C1"], ["C3"], ["C4"], ["D"]], host)
    out = np.stack([acts["y_fm"][b].T for b in range(NCORE)], axis=0)
    return np.ascontiguousarray(out.astype(np.float32))


# version 02
# speedup vs baseline: 1.0014x; 1.0014x over previous
import numpy as np
from contextlib import ExitStack
import concourse.bass as bass
import concourse.mybir as mybir
from concourse.bass_utils import run_bass_kernel_spmd

F32 = mybir.dt.float32
BF16 = mybir.dt.bfloat16
AF = mybir.ActivationFunctionType
ALU = mybir.AluOpType

C = 1024
T = 4096
NCT = 8
H = 4096
NHT = 32
NCORE = 8
EPS = 1e-5
NB_HEADS = 16
DH = 64


def _t(a):
    return np.ascontiguousarray(np.asarray(a, np.float32).T)


def _pp(v, n):
    return np.ascontiguousarray(np.asarray(v, np.float32).reshape(n, 128).T)


_SV = {}
_SV_W = [0]


def _sv_slot(name, width):
    if name not in _SV:
        _SV[name] = (_SV_W[0], width)
        _SV_W[0] += width
    return _SV[name]


for _n in ["ln0_w", "ln0_b", "ln1_w", "ln1_b", "ln2_w", "ln2_b",
           "tmk", "tmv", "tmr", "cmk", "cmr", "tdecay", "tfirst",
           "l0f_b2", "l1_bq", "l1_bk", "l1_bo", "l1_b1", "l1_b2", "l1f_s3"]:
    _sv_slot(_n, 8)
for _n in ["l0f_b1", "l1f_s1", "l1f_s2", "l1f_hs1", "l1f_hs2"]:
    _sv_slot(_n, 32)
SVW = _SV_W[0]

_RV = {"bv_row": (0, 1024), "eps": (1024, 1), "one": (1025, 1)}
RVW = 1026


def _relpos_bucket(rel, num_buckets=32, max_distance=128):
    import math
    n = -rel
    nb = num_buckets // 2
    ret = (n < 0).astype(np.int64) * nb
    n = np.abs(n)
    max_exact = nb // 2
    is_small = n < max_exact
    val_if_large = max_exact + (
        np.log(np.maximum(n, 1).astype(np.float64) / max_exact)
        / math.log(max_distance / max_exact) * (nb - max_exact)
    ).astype(np.int64)
    val_if_large = np.minimum(val_if_large, nb - 1)
    return ret + np.where(is_small, n, val_if_large)


def prep_inputs(inp):
    d = {}
    x = np.asarray(inp["L0_attn_x"], np.float32)
    d["x_fm"] = [np.ascontiguousarray(x[b].T) for b in range(NCORE)]

    for src, dst in [("L0_attn_Wk", "wk_t"), ("L0_attn_Wv", "wv_t"),
                     ("L0_attn_Wr", "wr_t"), ("L0_attn_Wo", "wo_t"),
                     ("L0_attn_Wfk", "wfk_t"), ("L0_attn_Wfr", "wfr_t"),
                     ("L0_attn_Wfv", "wfv_t"),
                     ("L1_attn_Wq", "l1wq_t"), ("L1_attn_Wk", "l1wk_t"),
                     ("L1_attn_Wv", "l1wv_t"), ("L1_attn_Wo", "l1wo_t"),
                     ("L1_attn_W1", "l1w1_t"), ("L1_attn_W2", "l1w2_t"),
                     ("L1_ffn_w1_w", "f1_t"), ("L1_ffn_w2_w", "f2_t"),
                     ("L1_ffn_h1_w", "fh1_t"), ("L1_ffn_h2_w", "fh2_t"),
                     ("L1_ffn_w3_w", "f3_t")]:
        d[dst] = _t(inp[src])
    d["l0f_w1"] = np.ascontiguousarray(np.asarray(inp["L0_ffn_w1"], np.float32))
    d["l0f_w2"] = np.ascontiguousarray(np.asarray(inp["L0_ffn_w2"], np.float32))

    sv = np.zeros((128, SVW), np.float32)

    def put(name, val, n):
        off, w = _SV[name]
        assert w == n
        sv[:, off:off + n] = _pp(val, n)

    put("ln0_w", inp["L0_attn_ln0_w"], 8); put("ln0_b", inp["L0_attn_ln0_b"], 8)
    put("ln1_w", inp["L0_attn_ln1_w"], 8); put("ln1_b", inp["L0_attn_ln1_b"], 8)
    put("ln2_w", inp["L0_attn_ln2_w"], 8); put("ln2_b", inp["L0_attn_ln2_b"], 8)
    put("tmk", inp["L0_attn_tmk"], 8); put("tmv", inp["L0_attn_tmv"], 8)
    put("tmr", inp["L0_attn_tmr"], 8)
    put("cmk", inp["L0_attn_cmk"], 8); put("cmr", inp["L0_attn_cmr"], 8)
    put("tdecay", inp["L0_attn_time_decay"], 8)
    put("tfirst", inp["L0_attn_time_first"], 8)
    put("l0f_b1", inp["L0_ffn_b1"], 32); put("l0f_b2", inp["L0_ffn_b2"], 8)
    put("l1_bq", inp["L1_attn_bq"], 8); put("l1_bk", inp["L1_attn_bk"], 8)
    put("l1_bo", inp["L1_attn_bo"], 8)
    put("l1_b1", inp["L1_attn_b1"], 8); put("l1_b2", inp["L1_attn_b2"], 8)
    put("l1f_s1", inp["L1_ffn_w1_s"], 32); put("l1f_s2", inp["L1_ffn_w2_s"], 32)
    put("l1f_hs1", inp["L1_ffn_h1_s"], 32); put("l1f_hs2", inp["L1_ffn_h2_s"], 32)
    put("l1f_s3", inp["L1_ffn_w3_s"], 8)
    d["smallvecs"] = sv

    rv = np.zeros((1, RVW), np.float32)
    rv[0, _RV["eps"][0]] = EPS
    rv[0, _RV["bv_row"][0]:_RV["bv_row"][0] + C] = np.asarray(inp["L1_attn_bv"], np.float32)
    rv[0, _RV["one"][0]] = 1.0
    d["rowvecs"] = rv

    rel_bias = np.asarray(inp["L1_attn_rel_bias"], np.float32)
    n = np.arange(8191)
    buckets = _relpos_bucket(4095 - n)
    d["g_master"] = np.ascontiguousarray(rel_bias[buckets, :].T)
    return d


WEIGHTS = {
    "wk_t": ([C, C], "A1"), "wv_t": ([C, C], "A1"), "wr_t": ([C, C], "A1"),
    "wo_t": ([C, C], "A1"),
    "wfk_t": ([C, H], "A2"), "wfv_t": ([H, C], "A2"), "wfr_t": ([C, C], "A2"),
    "l0f_w1": ([C, H], "B"), "l0f_w2": ([H, C], "B"),
    "l1wq_t": ([C, C], "C1"), "l1wk_t": ([C, C], "C1"), "l1wv_t": ([C, C], "C1"),
    "l1wo_t": ([C, C], "C3"), "l1w1_t": ([C, C], "C4"), "l1w2_t": ([C, C], "C4"),
    "f1_t": ([C, H], "D"), "f2_t": ([C, H], "D"),
    "fh1_t": ([H, H], "D"), "fh2_t": ([H, H], "D"), "f3_t": ([H, C], "D"),
    "g_master": ([16, 8191], "C0"),
}
ACTS = {
    "x_fm":    ([C, T], F32, None, {"A1", "A2"}),
    "x2_d":    ([C, T], F32, "A1", {"A2"}),
    "xsum_d":  ([C, T], F32, "A1", {"A2"}),
    "res1_d":  ([C, T], F32, "A2", {"B"}),
    "wfk16_d": ([C, H], BF16, "A2", {"A2"}),
    "wfv16_d": ([H, C], BF16, "A2", {"A2"}),
    "res2_d":  ([C, T], F32, "B", {"C0", "C4"}),
    "xatt_d":  ([C, T], F32, "C0", {"C3"}),
    "xatt16_d": ([C, T], BF16, "C0", {"C1"}),
    "g16_d":   ([16, 8191], BF16, "C0", {"C1"}),
    "att_tok_d": ([T, C], BF16, "C1", {"C3"}),
    "v_d":     ([T, C], BF16, "C1", {"C1"}),
    "ab_d":    ([C, T], F32, "C3", {"C4"}),
    "ab16_d":  ([C, T], BF16, "C3", {"C4"}),
    "res3_d":  ([C, T], F32, "C4", {"D"}),
    "f116_d":  ([C, H], BF16, "D", {"D"}),
    "f216_d":  ([C, H], BF16, "D", {"D"}),
    "fh116_d": ([H, H], BF16, "D", {"D"}),
    "fh216_d": ([H, H], BF16, "D", {"D"}),
    "f316_d":  ([H, C], BF16, "D", {"D"}),
    "y_fm":    ([C, T], F32, "D", set()),
}
ALL_PHASES = ["A1", "A2", "B", "C0", "C1", "C3", "C4", "D"]



class P:

    def __init__(self, nc, block, ctx, dram, phases):
        self.nc = nc
        self.block = block
        self.ctx = ctx
        self.dram = dram
        self.phases = phases
        self.cnt = {}
        self.sem = {}
        self.ev = {}

    def s(self, name):
        if name not in self.sem:
            self.sem[name] = self.ctx.enter_context(self.nc.semaphore(name))
            self.cnt[name] = 0
        return self.sem[name]

    def inc(self, inst, semname, n=1, event=None):
        sem = self.s(semname)
        inst.then_inc(sem, n)
        self.cnt[semname] += n
        if event is not None:
            self.ev[event] = (semname, self.cnt[semname])
        return self.cnt[semname]

    def mark_dma(self, inst, semname, event=None):
        return self.inc(inst, semname, 16, event)

    def reserve(self, semname, event, n=1):
        self.s(semname)
        self.cnt[semname] += n
        self.ev[event] = (semname, self.cnt[semname])

    def inc_reserved(self, inst, event):
        semname, _ = self.ev[event]
        inst.then_inc(self.sem[semname], 1)

    def wait(self, eng, event):
        semname, v = self.ev[event]
        eng.wait_ge(self.sem[semname], v)

    def wait_maybe(self, eng, event):
        if event in self.ev:
            self.wait(eng, event)

    def sb(self, name, shape, dtype, scope=None):
        ctx = scope or getattr(self, "phase_ctx", None) or self.ctx
        return ctx.enter_context(self.nc.sbuf_tensor(name, shape, dtype))

    def ps(self, name, shape, dtype=F32):
        ctx = getattr(self, "phase_ctx", None) or self.ctx
        return ctx.enter_context(self.nc.psum_tensor(name, shape, dtype))

    def barrier(self, tag):
        b = self.block
        names = ["sync", "vector", "scalar", "tensor", "gpsimd"]
        sem = self.s("bar")
        base = self.cnt["bar"]

        for n in names:
            def emit(eng, n=n):
                eng.sem_inc(sem, 1)
            getattr(b, n)(emit)
        self.cnt["bar"] += 5
        tgt = self.cnt["bar"]

        for n in names:
            def emit(eng):
                eng.wait_ge(sem, tgt)
            getattr(b, n)(emit)


    def get_stg(self, cols=8192):
        if not hasattr(self, "stg") or self.stg_cols < cols:
            assert not hasattr(self, "stg"), "allocate biggest staging first"
            self.stg = self.sb("wstg", [128, cols], F32, scope=self.ctx)
            self.stg_cols = cols
            self.stg_users = 0
        return self.stg

    def load_weight_cast(self, tag, w_ap, n_in_t, n_out, dma_sem, stage_cols=8192,
                         cast_eng="vector", w16=None, halt_event=None,
                         dma_eng="sync"):
        assert stage_cols % n_out == 0
        tps = stage_cols // n_out
        nstage = (n_in_t + tps - 1) // tps
        if w16 is None:
            w16 = self.sb(f"{tag}_16", [128, n_in_t * n_out], BF16)
        stg = self.get_stg(stage_cols)
        src = w_ap.rearrange("(ct p) o -> p ct o", p=128)
        ev_done = f"{tag}_cast_done"
        csem = "v" if cast_eng == "vector" else "g"
        prev_user = getattr(self, "stg_last_ev", None)
        for j in range(nstage):
            self.reserve(csem, f"{tag}_cast_{j}" if j < nstage - 1 else ev_done)
        self.stg_last_ev = ev_done

        def emit_sync(sync):
            for j in range(nstage):
                c0 = j * tps
                c1 = min(n_in_t, c0 + tps)
                if j == 0:
                    if prev_user is not None:
                        self.wait(sync, prev_user)
                    if halt_event is not None:
                        self.wait(sync, halt_event)
                else:
                    self.wait(sync, f"{tag}_cast_{j-1}")
                self.mark_dma(
                    sync.dma_start(
                        out=stg.ap()[:, 0:(c1 - c0) * n_out].rearrange(
                            "p (ct o) -> p ct o", ct=c1 - c0),
                        in_=src[:, c0:c1, :]),
                    dma_sem, f"{tag}_ld_{j}")
        getattr(self.block, dma_eng)(emit_sync)

        def emit_cast(eng):
            for j in range(nstage):
                c0 = j * tps
                c1 = min(n_in_t, c0 + tps)
                self.wait(eng, f"{tag}_ld_{j}")
                ev = f"{tag}_cast_{j}" if j < nstage - 1 else ev_done
                self.inc_reserved(
                    eng.tensor_copy(
                        out=w16[:, c0 * n_out:c1 * n_out],
                        in_=stg[:, 0:(c1 - c0) * n_out]),
                    ev)
        getattr(self.block, cast_eng)(emit_cast)
        return w16, ev_done


def declare_dram(nc, phases):
    dram = {}
    pset = set(phases)
    for name, (shape, prod) in WEIGHTS.items():
        if prod in pset:
            dram[name] = nc.dram_tensor(name, shape, F32, kind="ExternalInput").ap()
    for name, (shape, dt, prod, cons) in ACTS.items():
        needed_in = bool(cons & pset) or (name == "x_fm" and bool(pset))
        produced = prod in pset
        if name == "x_fm":
            if pset & {"A1", "A2"}:
                dram[name] = nc.dram_tensor(name, shape, dt, kind="ExternalInput").ap()
            continue
        if produced and cons and cons <= {prod}:
            dram[name] = nc.dram_tensor(name, shape, dt, kind="Internal").ap()
        elif produced:
            dram[name] = nc.dram_tensor(name, shape, dt, kind="ExternalOutput").ap()
        elif cons & pset:
            dram[name] = nc.dram_tensor(name, shape, dt, kind="ExternalInput").ap()
    if pset:
        dram["smallvecs"] = nc.dram_tensor("smallvecs", [128, SVW], F32,
                                           kind="ExternalInput").ap()
        dram["rowvecs"] = nc.dram_tensor("rowvecs", [1, RVW], F32,
                                         kind="ExternalInput").ap()
    return dram


def fm_view(ap):
    return ap.rearrange("(ct p) t -> p ct t", p=128)



def emit_stats(p, tag, src32, src16, sq16, nct, tc, psA, psB, rowbuf,
               prep_ps, dep_events, double=False, no_eps=False):
    cs = [rowbuf[:, k * tc:(k + 1) * tc] for k in range(6)]

    def vec_cast(vector):
        for e in dep_events:
            p.wait(vector, e)
        p.inc(vector.tensor_copy(out=src16[:], in_=src32[:]), "v", 1, f"{tag}_c16")
    p.block.vector(vec_cast)

    def sc_sq(scalar):
        for e in dep_events:
            p.wait(scalar, e)
        p.inc(scalar.activation(out=sq16[:], in_=src32[:], func=AF.Square),
              "s", 1, f"{tag}_sq")
    p.block.scalar(sc_sq)

    def tn_stats(tensor):
        p.wait(tensor, f"{tag}_c16")
        p.wait(tensor, f"{tag}_sq")
        s16v = src16.ap().rearrange("p (ct t) -> p ct t", ct=nct)
        q16v = sq16.ap().rearrange("p (ct t) -> p ct t", ct=nct)
        for ct in range(nct):
            tensor.matmul(psA[:], p.ones16[:], s16v[:, ct, :],
                          start=(ct == 0), stop=(ct == nct - 1))
        for ct in range(nct):
            inst = tensor.matmul(psB[:], p.ones16[:], q16v[:, ct, :],
                                 start=(ct == 0), stop=(ct == nct - 1))
        p.inc(inst, "t", 1, f"{tag}_stats")
    p.block.tensor(tn_stats)

    def vec_rows(vector):
        p.wait(vector, f"{tag}_stats")
        vector.tensor_scalar_mul(cs[0], psA[:], 1.0 / C)
        vector.tensor_scalar_mul(cs[1], psB[:], 1.0 / C)
        vector.tensor_tensor(out=cs[2], in0=cs[0], in1=cs[0], op=ALU.mult)
        inst = vector.tensor_tensor(out=cs[1], in0=cs[1], in1=cs[2],
                                    op=ALU.subtract)
        if double:
            vector.tensor_scalar_add(cs[3], cs[1], EPS)
            vector.reciprocal(cs[4], cs[3])
            inst = vector.tensor_tensor(out=cs[2], in0=cs[1], in1=cs[4],
                                        op=ALU.mult)
        p.inc(inst, "v", 1, f"{tag}_var")
    p.block.vector(vec_rows)

    def sc_rstd(scalar):
        p.wait(scalar, f"{tag}_var")
        if double:
            scalar.activation(out=cs[3], in_=cs[3], func=AF.Ln)
            scalar.activation(out=cs[3], in_=cs[3], func=AF.Exp, scale=-0.5)
            scalar.activation(out=cs[5], in_=cs[2], func=AF.Ln, bias=p.eps_ap)
            inst = scalar.activation(out=cs[5], in_=cs[5], func=AF.Exp, scale=-0.5)
        else:
            scalar.activation(out=cs[2], in_=cs[1], func=AF.Ln,
                              bias=0.0 if no_eps else p.eps_ap)
            inst = scalar.activation(out=cs[1], in_=cs[2], func=AF.Exp, scale=-0.5)
        p.inc(inst, "s", 1, f"{tag}_rstd")
    p.block.scalar(sc_rstd)

    if double:
        def vec_r12(vector):
            p.wait(vector, f"{tag}_rstd")
            p.inc(vector.tensor_tensor(out=cs[1], in0=cs[3], in1=cs[5], op=ALU.mult),
                  "v", 1, f"{tag}_r12")
        p.block.vector(vec_r12)

    def tn_rep(tensor):
        p.wait(tensor, f"{tag}_r12" if double else f"{tag}_rstd")
        for o in range(0, 2 * tc, 512):
            e = min(2 * tc, o + 512)
            inst = tensor.matmul(prep_ps[:, o:e], p.ones32[:], rowbuf[:, o:e],
                                 start=True, stop=True)
        p.inc(inst, "t", 1, f"{tag}_prep")
    p.block.tensor(tn_rep)
    return f"{tag}_prep"



def emit_C0(p):
    TC = 512
    NCH = T // TC
    res2 = fm_view(p.dram["res2_d"])
    xatt = fm_view(p.dram["xatt_d"])
    xatt16 = fm_view(p.dram["xatt16_d"])

    sx = p.sb("c0_sx", [128, NCT * TC], F32)
    tmp32 = p.sb("c0_tmp", [128, NCT * TC], F32)
    sc16 = p.sb("c0_sc16", [128, NCT * TC], BF16)
    sq16 = p.sb("c0_sq16", [128, NCT * TC], BF16)
    o16 = p.sb("c0_o16", [128, NCT * TC], BF16)
    gm32 = p.sb("c0_gm32", [16, 8191], F32)
    gm16 = p.sb("c0_gm16", [16, 8191], BF16)
    rowbuf = p.sb("c0_rows", [1, 6 * TC], F32)

    psA = p.ps("c0_psA", [1, TC])
    psB = p.ps("c0_psB", [1, TC])
    prep = p.ps("c0_prep", [128, 2 * TC])

    def r3(buf):
        return buf.ap().rearrange("p (ct t) -> p ct t", ct=NCT)

    def gld(gpsimd):
        p.mark_dma(gpsimd.dma_start(out=gm32[:], in_=p.dram["g_master"][:]),
                   "dma_c0g", "c0_gld")
    p.block.gpsimd(gld)

    def gcast(vector):
        p.wait(vector, "c0_gld")
        p.inc(vector.tensor_copy(out=gm16[:], in_=gm32[:]), "v", 1, "c0_gcast")
    p.block.vector(gcast)

    def gst(gpsimd):
        p.wait(gpsimd, "c0_gcast")
        p.mark_dma(gpsimd.dma_start(out=p.dram["g16_d"][:], in_=gm16[:]),
                   "dma_c0g", "c0_gst")
    p.block.gpsimd(gst)

    for i in range(NCH):
        tsl = slice(i * TC, (i + 1) * TC)

        def ld(sync, i=i, tsl=tsl):
            if i > 0:
                p.wait(sync, f"c0_app_{i-1}")
            p.mark_dma(sync.dma_start(out=r3(sx), in_=res2[:, :, tsl]),
                       "dma_c0", f"c0_x_{i}")
        p.block.sync(ld)

        emit_stats(p, f"c0l{i}", sx, sc16, sq16, NCT, TC, psA, psB,
                   rowbuf, prep, [f"c0_x_{i}"])

        def vec_app(vector, i=i):
            p.wait(vector, f"c0l{i}_prep")
            if i > 0:
                p.wait(vector, f"c0_st_{i-1}")
            Mb = prep.ap()[:, 0:TC].unsqueeze(1).broadcast_to([128, NCT, TC])
            Rb = prep.ap()[:, TC:2 * TC].unsqueeze(1).broadcast_to([128, NCT, TC])
            vector.tensor_tensor(out=r3(tmp32), in0=r3(sx), in1=Mb, op=ALU.subtract)
            vector.tensor_tensor(out=r3(tmp32), in0=r3(tmp32), in1=Rb, op=ALU.mult)
            p.inc(vector.tensor_copy(out=o16[:], in_=tmp32[:]), "v", 1, f"c0_app_{i}")
        p.block.vector(vec_app)

        def st(sync, i=i, tsl=tsl):
            p.wait(sync, f"c0_app_{i}")
            p.mark_dma(sync.dma_start(out=xatt[:, :, tsl], in_=r3(tmp32)),
                       "dma_c0s", None)
            p.mark_dma(sync.dma_start(out=xatt16[:, :, tsl], in_=r3(o16)),
                       "dma_c0s", f"c0_st_{i}")
        p.block.sync(st)

    def fin(sync):
        p.wait(sync, f"c0_st_{NCH-1}")
        p.wait(sync, "c0_gst")
    p.block.sync(fin)



C1_DEBUG = 0
C1_NTC = 8

def emit_C1(p):
    GW = 8064
    NST = 8
    xatt16 = fm_view(p.dram["xatt16_d"])
    attTok = p.dram["att_tok_d"]
    vd = p.dram["v_d"]
    g16d = p.dram["g16_d"]
    sv = p.sv32

    qF = p.sb("c1_qF", [64, T], BF16)
    kF = p.sb("c1_kF", [64, T], BF16)
    vones = p.sb("c1_vones", [128, 32 * 66], BF16)
    att_s = p.sb("c1_att_s", [64, 2 * 512], BF16)
    atok = p.sb("c1_atok", [128, 2 * 4 * 64], BF16)
    Hb = p.sb("c1_H", [128, GW], BF16)
    G = p.sb("c1_G", [128, GW], BF16)
    Efull = p.sb("c1_Efull", [128, 32 * 512], BF16)
    ones64 = p.sb("c1_ones64", [1, 64], F32)
    xblk = p.sb("c1_xblk", [128, NCT * 256], BF16)
    xtt = p.sb("c1_xtt", [128, NCT * 128], BF16)
    wq16 = p.sb("c1_wq16", [128, NCT * C], BF16)
    wk16 = p.sb("c1_wk16", [128, NCT * C], BF16)
    wv16 = p.sb("c1_wv16", [128, NCT * C], BF16)
    vst = p.sb("c1_vst", [128, 2 * C], BF16)
    vb32 = p.sb("c1_vb32", [128, C], F32)
    r32 = p.sb("c1_r32", [1, 512], F32)
    Rr32 = p.sb("c1_Rr32", [64, 512], F32)
    tE32 = p.sb("c1_tE32", [64, 512], F32)

    pp = [p.ps(f"c1_pp{j}", [128, 512]) for j in range(4)]
    pE = p.ps("c1_pE", [65, 512])
    pB = p.ps("c1_pB", [64, 512])
    prep2 = p.ps("c1_prep2", [64, 512])
    vbr = p.ps("c1_vbr", [128, 512])

    def vinit(vector):
        vector.memset(ones64[:], 1.0)
        p.inc(vector.memset(vones[:], 1.0), "v", 1, "c1_vones_init")
    p.block.vector(vinit)

    wq16, wqev = p.load_weight_cast("c1_wq", p.dram["l1wq_t"], NCT, C, "dma_c1w",
                                    stage_cols=4096, dma_eng="gpsimd", w16=wq16)
    wk16, wkev = p.load_weight_cast("c1_wk", p.dram["l1wk_t"], NCT, C, "dma_c1w",
                                    stage_cols=4096, dma_eng="gpsimd", w16=wk16)
    wv16, wvev = p.load_weight_cast("c1_wv", p.dram["l1wv_t"], NCT, C, "dma_c1w",
                                    stage_cols=4096, dma_eng="gpsimd", w16=wv16)

    def tn_bv(tensor):
        p.wait(tensor, "pre_ready")
        bvo = _RV["bv_row"][0]
        p.inc(tensor.matmul(vbr[:], p.ones32[:], p.rv32[:, bvo:bvo + 512],
                            start=True, stop=True), "t", 1, "c1_bv0")
    p.block.tensor(tn_bv)

    def vec_bv0(vector):
        p.wait(vector, "c1_bv0")
        p.inc(vector.tensor_copy(out=vb32[:, 0:512], in_=vbr[:]), "v", 1, "c1_bve0")
    p.block.vector(vec_bv0)

    def tn_bv1(tensor):
        p.wait(tensor, "c1_bve0")
        bvo = _RV["bv_row"][0]
        p.inc(tensor.matmul(vbr[:], p.ones32[:], p.rv32[:, bvo + 512:bvo + 1024],
                            start=True, stop=True), "t", 1, "c1_bv1")
    p.block.tensor(tn_bv1)

    def vec_bv1(vector):
        p.wait(vector, "c1_bv1")
        p.inc(vector.tensor_copy(out=vb32[:, 512:1024], in_=vbr[:]), "v", 1, "c1_bv")
    p.block.vector(vec_bv1)

    for tt in range(32):
        p.reserve("v", f"c1_vev_{tt}")

        def ldv(sync, tt=tt):
            if tt:
                p.wait(sync, f"c1_vmm_{tt-1}")
            p.mark_dma(sync.dma_start(
                out=xtt.ap().rearrange("p (ct t) -> p ct t", ct=NCT),
                in_=xatt16[:, :, tt * 128:(tt + 1) * 128]),
                "dma_c1x", f"c1_xtt_{tt}")
        p.block.sync(ldv)

        def tn_v(tensor, tt=tt):
            p.wait(tensor, f"c1_xtt_{tt}")
            if tt == 0:
                p.wait(tensor, wvev)
            if tt >= 2:
                p.wait(tensor, f"c1_vev_{tt-2}")
            xv = xtt.ap().rearrange("p (ct t) -> p ct t", ct=NCT)
            wvw = wv16.ap().rearrange("p (ct o) -> p ct o", ct=NCT)
            for half in range(2):
                for ct in range(NCT):
                    inst = tensor.matmul(pp[2 * (tt % 2) + half][:], xv[:, ct, :],
                                         wvw[:, ct, half * 512:(half + 1) * 512],
                                         start=(ct == 0), stop=(ct == NCT - 1))
                p.inc(inst, "t", 1, f"c1_vmmh_{tt}_{half}")
            p.ev[f"c1_vmm_{tt}"] = p.ev[f"c1_vmmh_{tt}_{1}"]
        p.block.tensor(tn_v)

        def vec_v(vector, tt=tt):
            if tt == 0:
                p.wait(vector, "c1_bv")
            if tt >= 2:
                p.wait(vector, f"c1_vst_{tt-2}")
            for half in range(2):
                p.wait(vector, f"c1_vmmh_{tt}_{half}")
                inst = vector.tensor_tensor(
                    out=vst[:, (tt % 2) * C + half * 512:(tt % 2) * C + (half + 1) * 512],
                    in0=pp[2 * (tt % 2) + half][:],
                    in1=vb32[:, half * 512:(half + 1) * 512], op=ALU.add)
            p.inc_reserved(inst, f"c1_vev_{tt}")
        p.block.vector(vec_v)

        def stv(sync, tt=tt):
            p.wait(sync, f"c1_vev_{tt}")
            p.mark_dma(sync.dma_start(
                out=vd[tt * 128:(tt + 1) * 128, :],
                in_=vst[:, (tt % 2) * C:(tt % 2 + 1) * C]),
                "dma_c1s", f"c1_vst_{tt}")
        p.block.sync(stv)

    for j in range(16):
        bq0 = f"c1b{j}"

        def ldxb(sync, j=j):
            if j:
                p.wait(sync, f"c1b{j-1}_qkf")
            p.mark_dma(sync.dma_start(
                out=xblk.ap().rearrange("p (ct t) -> p ct t", ct=NCT),
                in_=xatt16[:, :, j * 256:(j + 1) * 256]),
                "dma_c1x", f"{bq0}_xb")
        p.block.sync(ldxb)

        def ldvf(sync, j=j):
            p.wait(sync, f"c1_vst_{31}")
            if j:
                p.wait(sync, f"c1b{j-1}_avdone")
            p.mark_dma(sync.dma_start(
                out=bass.AP(vones, 0, [[66 * 32, 128], [66, 32], [1, 64]]),
                in_=bass.AP(vd.tensor, j * 256 * C,
                            [[C, 8], [64, 16], [8 * C, 32], [1, 64]])),
                "dma_c1x", f"{bq0}_vf")
        p.block.sync(ldvf)

        for ck in range(16):
            p.reserve("s", f"{bq0}_qkev_{ck}")
        for ck in range(16):
            p.reserve("s", f"{bq0}_kkev_{ck}")

        def tn_qkf(tensor, j=j):
            p.wait(tensor, f"{bq0}_xb")
            if j == 0:
                p.wait(tensor, wqev)
                p.wait(tensor, wkev)
            xv = xblk.ap().rearrange("p (ct t) -> p ct t", ct=NCT)
            for jq, w16b in ((0, wq16), (1, wk16)):
                wvw = w16b.ap().rearrange("p (ct o) -> p ct o", ct=NCT)
                for ck in range(16):
                    oi = jq * 16 + ck
                    if oi >= 2:
                        pj, pck = divmod(oi - 2, 16)
                        p.wait(tensor, f"{bq0}_qkev_{pck}" if pj == 0
                               else f"{bq0}_kkev_{pck}")
                    elif j > 0:
                        p.wait(tensor, f"c1h{j-1}s{NST-1}_ex_{15}")
                    else:
                        p.wait(tensor, f"c1_vev_{31}")
                    for ct in range(NCT):
                        inst = tensor.matmul(
                            pp[oi % 2].ap()[0:64, 0:256],
                            wvw[:, ct, ck * 64:(ck + 1) * 64],
                            xv[:, ct, :],
                            start=(ct == 0), stop=(ct == NCT - 1))
                    p.inc(inst, "t", 1,
                          f"{bq0}_qkmm_{ck}" if jq == 0 else f"{bq0}_kkmm_{ck}")
        p.block.tensor(tn_qkf)

        def sc_qkf(scalar, j=j):
            bqo, _ = _SV["l1_bq"]
            bko, _ = _SV["l1_bk"]
            for jq, dstF, bo in ((0, qF, bqo), (1, kF, bko)):
                for ck in range(16):
                    p.wait(scalar, f"{bq0}_qkmm_{ck}" if jq == 0 else f"{bq0}_kkmm_{ck}")
                    oi = jq * 16 + ck
                    dst = bass.AP(dstF, ck, [[T, 64], [16, 256]])
                    bias_ap = sv[(ck % 2) * 64:(ck % 2) * 64 + 64,
                                 (bqo if jq == 0 else bko) + ck // 2:
                                 (bqo if jq == 0 else bko) + ck // 2 + 1]
                    p.inc_reserved(
                        scalar.activation(out=dst, in_=pp[oi % 2].ap()[0:64, 0:256],
                                          func=AF.Identity, bias=bias_ap),
                        f"{bq0}_qkev_{ck}" if jq == 0 else f"{bq0}_kkev_{ck}")
            p.ev[f"{bq0}_qkf"] = p.ev[f"{bq0}_kkev_{15}"]
        p.block.scalar(sc_qkf)

        def ldH(sync, j=j):
            if j > 0:
                p.wait(sync, f"c1_Gbuilt_{j-1}")
            if "c0_gst" in p.ev:
                p.wait(sync, "c0_gst")
            p.mark_dma(sync.dma_start(
                out=Hb[:],
                in_=bass.AP(g16d.tensor, j * 8191, [[1, 128], [1, GW]])),
                "dma_c1g", f"c1_H_{j}")
        p.block.sync(ldH)

        def vec_G(vector, j=j):
            p.wait(vector, f"c1_H_{j}")
            if j >= 1:
                p.wait(vector, f"c1_Gfree_{j-1}")
            p.inc(vector.tensor_copy(out=G[:], in_=Hb[:, ::-1]),
                  "v", 1, f"c1_Gbuilt_{j}")
        p.block.vector(vec_G)

        vt3 = vones.ap().rearrange("p (kt s) -> p kt s", kt=32)

        for st_i in range(NST):
            q0 = st_i * 512
            hq = f"c1h{j}s{st_i}"
            NKB = 16
            for kb in range(NKB):
                p.reserve("s", f"{hq}_ex_{kb}")
            p.reserve("v", f"{hq}_rc")
            p.reserve("v", f"{hq}_fin")
            p.reserve("v", f"{hq}_tr")

            def tn_att(tensor, j=j, st_i=st_i, q0=q0, hq=hq, NKB=NKB):
                if st_i == 0:
                    p.wait(tensor, f"c1_Gbuilt_{j}")
                    p.wait(tensor, f"{bq0}_vf")
                    p.wait(tensor, f"{bq0}_qkf")
                if j + st_i > 0:
                    prev = f"c1h{j}s{st_i-1}_fin" if st_i else \
                        f"c1h{j-1}s{NST-1}_fin"
                    p.wait(tensor, prev)

                for kb in range(NKB):
                    if kb >= 2:
                        p.wait(tensor, f"{hq}_ex_{kb-2}")
                    elif st_i == 0 and kb < 2 and j + st_i > 0:
                        pass
                    for kt in (2 * kb, 2 * kb + 1):
                        inst = tensor.matmul(
                            pp[kt % 4][:],
                            kF.ap()[:, kt * 128:(kt + 1) * 128],
                            qF.ap()[:, q0:q0 + 512],
                            start=True, stop=True)
                    p.inc(inst, "t", 1, f"{hq}_lgb_{kb}")
                p.wait(tensor, f"{hq}_ex_{NKB-1}")
                ev = Efull.ap().rearrange("p (kt t) -> p kt t", kt=32)
                for kt in range(32):
                    tensor.matmul(pE[:], vt3[:, kt, 0:65], ev[:, kt, :],
                                  start=(kt == 0), stop=(kt == 31))
                    s = q0 - kt * 128 + 3968
                    inst = tensor.matmul(pB[:], vt3[:, kt, 0:64],
                                         G[:, s:s + 512],
                                         start=(kt == 0), stop=(kt == 31))
                p.inc(inst, "t", 1, f"{hq}_avd")
            p.block.tensor(tn_att)

            def sc_exp(scalar, hq=hq, NKB=NKB):
                ev = Efull.ap().rearrange("p (kt t) -> p kt t", kt=32)
                for kb in range(NKB):
                    p.wait(scalar, f"{hq}_lgb_{kb}")
                    for kt in (2 * kb, 2 * kb + 1):
                        inst = scalar.activation(out=ev[:, kt, :],
                                                 in_=pp[kt % 4][:], func=AF.Exp)
                    p.inc_reserved(inst, f"{hq}_ex_{kb}")
            p.block.scalar(sc_exp)

            def vec_rc(vector, hq=hq):
                p.wait(vector, f"{hq}_avd")
                p.inc_reserved(vector.reciprocal(r32[:], pE[64:65, :]), f"{hq}_rc")
            p.block.vector(vec_rc)

            def tn_rep(tensor, hq=hq):
                p.wait(tensor, f"{hq}_rc")
                p.inc(tensor.matmul(prep2[:], ones64[:], r32[:],
                                    start=True, stop=True),
                      "t", 1, f"{hq}_rp")
            p.block.tensor(tn_rep)

            def vec_att(vector, j=j, st_i=st_i, hq=hq, q0=q0):
                p.wait(vector, f"{hq}_rp")
                k = j * NST + st_i
                if k >= 2:
                    p.wait(vector, f"c1_ast_{k-2}")
                vector.tensor_copy(out=Rr32[:], in_=prep2[:])
                vector.tensor_tensor(out=tE32[:], in0=pE[0:64, :],
                                     in1=Rr32[:], op=ALU.mult)
                si = (st_i % 2) * 512
                p.inc_reserved(
                    vector.tensor_tensor(out=att_s[:, si:si + 512],
                                         in0=tE32[:], in1=pB[:], op=ALU.add),
                    f"{hq}_fin")
                ai = (st_i % 2) * 256
                for br in range(2):
                    for bc in range(16):
                        pbase2 = 32 * (bc % 4)
                        col = ai + (bc // 4) * 64 + br * 32
                        inst = vector.transpose(
                            out=atok.ap()[pbase2:pbase2 + 32, col:col + 32],
                            in_=att_s.ap()[br * 32:(br + 1) * 32,
                                           si + bc * 32:si + bc * 32 + 32])
                p.inc_reserved(inst, f"{hq}_tr")
            p.block.vector(vec_att)

            def st_att(sync, j=j, st_i=st_i, hq=hq, q0=q0):
                p.wait(sync, f"{hq}_tr")
                k = j * NST + st_i
                ai = (st_i % 2) * 256
                p.mark_dma(sync.dma_start(
                    out=bass.AP(attTok.tensor, j * 256 * C + q0 * 64,
                                [[64, 128], [8192, 4], [1, 64]]),
                    in_=bass.AP(atok, ai, [[512, 128], [64, 4], [1, 64]])),
                    "dma_c1s", f"c1_ast_{k}")
            p.block.sync(st_att)

        p.ev[f"c1_Gfree_{j}"] = p.ev[f"c1h{j}s{NST-1}_fin"]
        p.ev[f"{bq0}_avdone"] = p.ev[f"c1h{j}s{NST-1}_avd"]

    def fin(sync):
        p.wait(sync, f"c1_ast_{16*NST-1}")
    p.block.sync(fin)



def emit_C3(p):
    TC = 512
    NCH = T // TC
    attTok = p.dram["att_tok_d"]
    xatt = fm_view(p.dram["xatt_d"])
    abd = fm_view(p.dram["ab_d"])
    ab16d = fm_view(p.dram["ab16_d"])
    sv = p.sv32

    wo16, woev = p.load_weight_cast("c3_wo", p.dram["l1wo_t"], NCT, C, "dma_c3w",
                                    stage_cols=4096, dma_eng="gpsimd")

    atokc = p.sb("c3_atok", [128, 4 * C], BF16)
    ident = p.sb("c3_ident", [128, 128], BF16)
    sa16 = p.sb("c3_sa16", [128, NCT * TC], BF16)
    sx = p.sb("c3_sx", [128, NCT * TC], F32)
    t32 = p.sb("c3_t32", [128, NCT * TC], F32)
    tmp32 = p.sb("c3_tmp", [128, NCT * TC], F32)
    sc16 = p.sb("c3_sc16", [128, NCT * TC], BF16)
    sq16 = p.sb("c3_sq16", [128, NCT * TC], BF16)
    o16 = p.sb("c3_o16", [128, NCT * TC], BF16)
    rowbuf = p.sb("c3_rows", [1, 6 * TC], F32)

    psA = p.ps("c3_psA", [1, TC])
    psB = p.ps("c3_psB", [1, TC])
    prep = p.ps("c3_prep", [128, 2 * TC])
    po = [p.ps(f"c3_po{j}", [128, TC]) for j in range(2)]
    pt16 = [p.ps(f"c3_pt16{j}", [128, 128], BF16) for j in range(2)]

    def r3(buf):
        return buf.ap().rearrange("p (ct t) -> p ct t", ct=NCT)

    def gid(gpsimd):
        gpsimd.memset(ident[:], 0.0)
        import concourse.mybir as _mb
        inst = gpsimd.affine_select(out=ident[:], in_=ident[:],
                                    compare_op=_mb.AluOpType.not_equal,
                                    fill=1.0, base=0, pattern=[[-1, 128]],
                                    channel_multiplier=1)
        p.inc(inst, "g", 1, "c3_id")
    p.block.gpsimd(gid)

    for i in range(NCH):
        tsl = slice(i * TC, (i + 1) * TC)

        def ld(sync, i=i, tsl=tsl):
            if i > 0:
                p.wait(sync, f"c3_tp_{i-1}_{31}")
                p.wait(sync, f"c3_t_{i-1}")
            p.mark_dma(sync.dma_start(
                out=atokc.ap().rearrange("p (tt c) -> p tt c", tt=4),
                in_=attTok[i * TC:(i + 1) * TC, :].rearrange(
                    "(tt p) c -> p tt c", p=128)),
                "dma_c3", None)
            p.mark_dma(sync.dma_start(out=r3(sx), in_=xatt[:, :, tsl]),
                       "dma_c3", f"c3_x_{i}")
        p.block.sync(ld)

        for z in range(32):
            p.reserve("v", f"c3_tev_{i}_{z}")

        def tn_tp(tensor, i=i):
            p.wait(tensor, f"c3_x_{i}")
            if i == 0:
                p.wait(tensor, "c3_id")
            av = atokc.ap().rearrange("p (tt c) -> p tt c", tt=4)
            z = 0
            for tt in range(4):
                for ct in range(NCT):
                    if z >= 2:
                        p.wait(tensor, f"c3_tev_{i}_{z-2}")
                    elif i > 0:
                        p.wait(tensor, f"c3_tev_{i-1}_{30+z}")
                    inst = tensor.transpose(
                        pt16[z % 2][:],
                        av[:, tt, ct * 128:(ct + 1) * 128],
                        ident[:])
                    p.inc(inst, "t", 1, f"c3_tp_{i}_{z}")
                    z += 1
        p.block.tensor(tn_tp)

        def vec_tp(vector, i=i):
            sa = r3(sa16)
            z = 0
            for tt in range(4):
                for ct in range(NCT):
                    p.wait(vector, f"c3_tp_{i}_{z}")
                    p.inc_reserved(
                        vector.tensor_copy(
                            out=sa[:, ct, tt * 128:(tt + 1) * 128],
                            in_=pt16[z % 2][:]),
                        f"c3_tev_{i}_{z}")
                    z += 1
        p.block.vector(vec_tp)

        for ot in range(NCT):
            p.reserve("s", f"c3_ev_{i}_{ot}")

        def tn_mm(tensor, i=i):
            if i == 0:
                p.wait(tensor, woev)
            p.wait(tensor, f"c3_tev_{i}_{31}")
            wv = wo16.ap().rearrange("p (ct o) -> p ct o", ct=NCT)
            av = r3(sa16)
            for ot in range(NCT):
                if ot >= 2:
                    p.wait(tensor, f"c3_ev_{i}_{ot-2}")
                elif i > 0:
                    p.wait(tensor, f"c3_ev_{i-1}_{NCT-2+ot}")
                for ct in range(NCT):
                    inst = tensor.matmul(po[ot % 2][:], wv[:, ct, ot * 128:(ot + 1) * 128],
                                         av[:, ct, :],
                                         start=(ct == 0), stop=(ct == NCT - 1))
                p.inc(inst, "t", 1, f"c3_mm_{i}_{ot}")
        p.block.tensor(tn_mm)

        def sc_ev(scalar, i=i):
            bo, _ = _SV["l1_bo"]
            t3 = r3(t32)
            for ot in range(NCT):
                p.wait(scalar, f"c3_mm_{i}_{ot}")
                p.inc_reserved(scalar.activation(out=t3[:, ot, :], in_=po[ot % 2][:],
                                                 func=AF.Identity,
                                                 bias=sv[:, bo + ot:bo + ot + 1]),
                               f"c3_ev_{i}_{ot}")
        p.block.scalar(sc_ev)

        def vec_t(vector, i=i):
            p.wait(vector, f"c3_ev_{i}_{NCT-1}")
            p.inc(vector.tensor_tensor(out=t32[:], in0=t32[:], in1=sx[:], op=ALU.add),
                  "v", 1, f"c3_t_{i}")
        p.block.vector(vec_t)

        emit_stats(p, f"c3l{i}", t32, sc16, sq16, NCT, TC, psA, psB,
                   rowbuf, prep, [f"c3_t_{i}"], no_eps=True)

        def vec_app(vector, i=i):
            p.wait(vector, f"c3l{i}_prep")
            if i > 0:
                p.wait(vector, f"c3_st_{i-1}")
            Mb = prep.ap()[:, 0:TC].unsqueeze(1).broadcast_to([128, NCT, TC])
            Rb = prep.ap()[:, TC:2 * TC].unsqueeze(1).broadcast_to([128, NCT, TC])
            vector.tensor_tensor(out=r3(tmp32), in0=r3(t32), in1=Mb, op=ALU.subtract)
            vector.tensor_tensor(out=r3(tmp32), in0=r3(tmp32), in1=Rb, op=ALU.mult)
            p.inc(vector.tensor_copy(out=o16[:], in_=tmp32[:]), "v", 1, f"c3_app_{i}")
        p.block.vector(vec_app)

        def st(sync, i=i, tsl=tsl):
            p.wait(sync, f"c3_app_{i}")
            p.mark_dma(sync.dma_start(out=abd[:, :, tsl], in_=r3(tmp32)),
                       "dma_c3s", None)
            p.mark_dma(sync.dma_start(out=ab16d[:, :, tsl], in_=r3(o16)),
                       "dma_c3s", f"c3_st_{i}")
        p.block.sync(st)

    def fin(sync):
        p.wait(sync, f"c3_st_{NCH-1}")
    p.block.sync(fin)



def emit_C4(p):
    TC = 512
    NCH = T // TC
    ab16d = fm_view(p.dram["ab16_d"])
    abd = fm_view(p.dram["ab_d"])
    res2 = fm_view(p.dram["res2_d"])
    res3 = fm_view(p.dram["res3_d"])
    sv = p.sv32

    w116, w1ev = p.load_weight_cast("c4_w1", p.dram["l1w1_t"], NCT, C, "dma_c4w",
                                    stage_cols=4096, dma_eng="gpsimd")
    w216, w2ev = p.load_weight_cast("c4_w2", p.dram["l1w2_t"], NCT, C, "dma_c4w",
                                    stage_cols=4096, dma_eng="gpsimd")

    sa16 = p.sb("c4_sa16", [128, NCT * TC], BF16)
    sab = p.sb("c4_sab", [128, NCT * TC], F32)
    sr2 = p.sb("c4_sr2", [128, NCT * TC], F32)
    f116 = p.sb("c4_f116", [128, NCT * TC], BF16)
    t32 = p.sb("c4_t32", [128, NCT * TC], F32)
    tmp32 = p.sb("c4_tmp", [128, NCT * TC], F32)
    sc16 = p.sb("c4_sc16", [128, NCT * TC], BF16)
    sq16 = p.sb("c4_sq16", [128, NCT * TC], BF16)
    rowbuf = p.sb("c4_rows", [1, 6 * TC], F32)

    psA = p.ps("c4_psA", [1, TC])
    psB = p.ps("c4_psB", [1, TC])
    prep = p.ps("c4_prep", [128, 2 * TC])
    po = [p.ps(f"c4_po{j}", [128, TC]) for j in range(2)]

    def r3(buf):
        return buf.ap().rearrange("p (ct t) -> p ct t", ct=NCT)

    for i in range(NCH):
        tsl = slice(i * TC, (i + 1) * TC)

        def ld(sync, i=i, tsl=tsl):
            if i > 0:
                p.wait(sync, f"c4_f1mm_{i-1}_{NCT-1}")
                p.wait(sync, f"c4_out_{i-1}")
            p.mark_dma(sync.dma_start(out=r3(sa16), in_=ab16d[:, :, tsl]),
                       "dma_c4", None)
            p.mark_dma(sync.dma_start(out=r3(sab), in_=abd[:, :, tsl]),
                       "dma_c4", None)
            p.mark_dma(sync.dma_start(out=r3(sr2), in_=res2[:, :, tsl]),
                       "dma_c4", f"c4_x_{i}")
        p.block.sync(ld)

        for ot in range(NCT):
            p.reserve("s", f"c4_f1ev_{i}_{ot}")

        def tn_f1(tensor, i=i):
            p.wait(tensor, f"c4_x_{i}")
            if i == 0:
                p.wait(tensor, w1ev)
            wv = w116.ap().rearrange("p (ct o) -> p ct o", ct=NCT)
            av = r3(sa16)
            for ot in range(NCT):
                if ot >= 2:
                    p.wait(tensor, f"c4_f1ev_{i}_{ot-2}")
                elif i > 0:
                    p.wait(tensor, f"c4_f1ev_{i-1}_{NCT-2+ot}")
                for ct in range(NCT):
                    inst = tensor.matmul(po[ot % 2][:], wv[:, ct, ot * 128:(ot + 1) * 128],
                                         av[:, ct, :],
                                         start=(ct == 0), stop=(ct == NCT - 1))
                p.inc(inst, "t", 1, f"c4_f1mm_{i}_{ot}")
        p.block.tensor(tn_f1)

        def sc_f1(scalar, i=i):
            b1, _ = _SV["l1_b1"]
            fv = r3(f116)
            for ot in range(NCT):
                p.wait(scalar, f"c4_f1mm_{i}_{ot}")
                p.inc_reserved(scalar.activation(out=fv[:, ot, :], in_=po[ot % 2][:],
                                                 func=AF.Identity,
                                                 bias=sv[:, b1 + ot:b1 + ot + 1]),
                               f"c4_f1ev_{i}_{ot}")
        p.block.scalar(sc_f1)

        for ot in range(NCT):
            p.reserve("v", f"c4_f2ev_{i}_{ot}")

        def tn_f2(tensor, i=i):
            if i == 0:
                p.wait(tensor, w2ev)
            p.wait(tensor, f"c4_f1ev_{i}_{NCT-1}")
            wv = w216.ap().rearrange("p (ct o) -> p ct o", ct=NCT)
            fv = r3(f116)
            for ot in range(NCT):
                if ot >= 2:
                    p.wait(tensor, f"c4_f2ev_{i}_{ot-2}")
                elif i > 0:
                    p.wait(tensor, f"c4_f2ev_{i-1}_{NCT-2+ot}")
                for ct in range(NCT):
                    inst = tensor.matmul(po[ot % 2][:], wv[:, ct, ot * 128:(ot + 1) * 128],
                                         fv[:, ct, :],
                                         start=(ct == 0), stop=(ct == NCT - 1))
                p.inc(inst, "t", 1, f"c4_f2mm_{i}_{ot}")
        p.block.tensor(tn_f2)

        def vec_f2(vector, i=i):
            b2, _ = _SV["l1_b2"]
            t3 = r3(t32)
            ab3 = r3(sab)
            for ot in range(NCT):
                p.wait(vector, f"c4_f2mm_{i}_{ot}")
                p.inc_reserved(
                    vector.scalar_tensor_tensor(out=t3[:, ot, :], in0=po[ot % 2][:],
                                                scalar=sv[:, b2 + ot:b2 + ot + 1],
                                                in1=ab3[:, ot, :],
                                                op0=ALU.add, op1=ALU.add),
                    f"c4_f2ev_{i}_{ot}")
            p.ev[f"c4_t_{i}"] = p.ev[f"c4_f2ev_{i}_{NCT-1}"]
        p.block.vector(vec_f2)

        emit_stats(p, f"c4l{i}", t32, sc16, sq16, NCT, TC, psA, psB,
                   rowbuf, prep, [f"c4_t_{i}"], no_eps=True)

        def vec_app(vector, i=i):
            p.wait(vector, f"c4l{i}_prep")
            if i > 0:
                p.wait(vector, f"c4_st_{i-1}")
            Mb = prep.ap()[:, 0:TC].unsqueeze(1).broadcast_to([128, NCT, TC])
            Rb = prep.ap()[:, TC:2 * TC].unsqueeze(1).broadcast_to([128, NCT, TC])
            vector.tensor_tensor(out=r3(tmp32), in0=r3(t32), in1=Mb, op=ALU.subtract)
            vector.tensor_tensor(out=r3(tmp32), in0=r3(tmp32), in1=Rb, op=ALU.mult)
            p.inc(vector.tensor_tensor(out=tmp32[:], in0=tmp32[:], in1=sr2[:],
                                       op=ALU.add),
                  "v", 1, f"c4_out_{i}")
        p.block.vector(vec_app)

        def st(sync, i=i, tsl=tsl):
            p.wait(sync, f"c4_out_{i}")
            p.mark_dma(sync.dma_start(out=res3[:, :, tsl], in_=r3(tmp32)),
                       "dma_c4s", f"c4_st_{i}")
        p.block.sync(st)

    def fin(sync):
        p.wait(sync, f"c4_st_{NCH-1}")
    p.block.sync(fin)



DV = {"om_tmk": 0, "om_tmv": 8, "om_tmr": 16, "om_cmk": 24, "om_cmr": 32,
      "lam": 40, "eu": 48, "scr": 56}


def emit_derived(p):
    if getattr(p, "derived_done", False):
        return
    p.derived_done = True
    sv = p.sv32
    O = SVW

    def vec(vector):
        p.wait(vector, "pre_ready")
        for nm, src in [("om_tmk", "tmk"), ("om_tmv", "tmv"), ("om_tmr", "tmr"),
                        ("om_cmk", "cmk"), ("om_cmr", "cmr")]:
            o1, _ = _SV[src]
            inst = vector.tensor_scalar(out=sv[:, O + DV[nm]:O + DV[nm] + 8],
                                        in0=sv[:, o1:o1 + 8], scalar1=-1.0,
                                        scalar2=1.0, op0=ALU.mult, op1=ALU.add)
        p.inc(inst, "v", 1, "derived_v")
    p.block.vector(vec)

    def sc(scalar):
        p.wait(scalar, "pre_ready")
        td, _ = _SV["tdecay"]
        tf, _ = _SV["tfirst"]
        scalar.activation(out=sv[:, O + DV["scr"]:O + DV["scr"] + 8],
                          in_=sv[:, td:td + 8], func=AF.Exp)
        scalar.activation(out=sv[:, O + DV["lam"]:O + DV["lam"] + 8],
                          in_=sv[:, O + DV["scr"]:O + DV["scr"] + 8],
                          func=AF.Exp, scale=-1.0)
        inst = scalar.activation(out=sv[:, O + DV["eu"]:O + DV["eu"] + 8],
                                 in_=sv[:, tf:tf + 8], func=AF.Exp)
        p.inc(inst, "s", 1, "derived_s")
    p.block.scalar(sc)



def emit_A1(p):
    TC = 256
    NCH = T // TC
    xfm = fm_view(p.dram["x_fm"])
    x2v = fm_view(p.dram["x2_d"])
    xsv = fm_view(p.dram["xsum_d"])
    emit_derived(p)
    sv = p.sv32
    O = SVW

    wk16, wkev = p.load_weight_cast("a1_wk", p.dram["wk_t"], NCT, C, "dma_a1w",
                                    stage_cols=4096, dma_eng="gpsimd")
    wv16, wvev = p.load_weight_cast("a1_wv", p.dram["wv_t"], NCT, C, "dma_a1w",
                                    stage_cols=4096, dma_eng="gpsimd")
    wr16, wrev = p.load_weight_cast("a1_wr", p.dram["wr_t"], NCT, C, "dma_a1w",
                                    stage_cols=4096, dma_eng="gpsimd")
    wo16, woev = p.load_weight_cast("a1_wo", p.dram["wo_t"], NCT, C, "dma_a1w",
                                    stage_cols=4096, dma_eng="gpsimd")

    sx = p.sb("a1_sx", [128, NCT * TC], F32)
    tmp32 = p.sb("a1_tmp", [128, NCT * TC], F32)
    sc16 = p.sb("a1_sc16", [128, NCT * TC], BF16)
    sq16 = p.sb("a1_sq16", [128, NCT * TC], BF16)
    x1 = p.sb("a1_x1", [128, NCT * TC], F32)
    hpad = p.sb("a1_hpad", [128, NCT * (TC + 1)], F32)
    mixk = p.sb("a1_mixk", [128, NCT * TC], BF16)
    mixv = p.sb("a1_mixv", [128, NCT * TC], BF16)
    mixr = p.sb("a1_mixr", [128, NCT * TC], BF16)
    ek = p.sb("a1_ek", [128, NCT * TC], F32)
    v32 = p.sb("a1_v32", [128, NCT * TC], F32)
    sr = p.sb("a1_sr", [128, NCT * TC], F32)
    out32 = p.sb("a1_out", [128, NCT * TC], F32)
    Apad = p.sb("a1_Apad", [128, NCT * (TC + 1)], F32)
    Bpad = p.sb("a1_Bpad", [128, NCT * (TC + 1)], F32)
    rowbuf = p.sb("a1_rows", [1, 6 * TC], F32)

    psA = p.ps("a1_psA", [1, TC])
    psB = p.ps("a1_psB", [1, TC])
    prep = p.ps("a1_prep", [128, 2 * TC])
    pkv = p.ps("a1_pkv", [128, NCT * TC])

    hp = hpad.ap().rearrange("p (ct t) -> p ct t", ct=NCT)
    Av = Apad.ap().rearrange("p (ct t) -> p ct t", ct=NCT)
    Bv = Bpad.ap().rearrange("p (ct t) -> p ct t", ct=NCT)

    def init(vector):
        vector.memset(hpad[:], 0.0)
        vector.memset(Apad[:], 0.0)
        p.inc(vector.memset(Bpad[:], 0.0), "v", 1, "a1_init")
    p.block.vector(init)

    def r3(buf):
        return buf.ap().rearrange("p (ct t) -> p ct t", ct=NCT)

    for i in range(NCH):
        tsl = slice(i * TC, (i + 1) * TC)

        def ld(sync, i=i, tsl=tsl):
            if i > 0:
                p.wait(sync, f"a1_xs_{i-1}")
            p.mark_dma(sync.dma_start(out=r3(sx), in_=xfm[:, :, tsl]),
                       "dma_a1x", f"a1_x_{i}")
        p.block.sync(ld)

        emit_stats(p, f"a1o{i}", sx, sc16, sq16, NCT, TC, psA, psB,
                   rowbuf, prep, [f"a1_x_{i}"] + (["a1_init", "derived_v", "derived_s"]
                                                  if i == 0 else []),
                   double=True)

        def vec_x1(vector, i=i):
            p.wait(vector, f"a1o{i}_prep")
            Mb = prep.ap()[:, 0:TC].unsqueeze(1).broadcast_to([128, NCT, TC])
            Rb = prep.ap()[:, TC:2 * TC].unsqueeze(1).broadcast_to([128, NCT, TC])
            vector.tensor_tensor(out=r3(tmp32), in0=r3(sx), in1=Mb, op=ALU.subtract)
            vector.tensor_tensor(out=r3(tmp32), in0=r3(tmp32), in1=Rb, op=ALU.mult)
            g0, _ = _SV["ln0_w"]
            b0, _ = _SV["ln0_b"]
            x1v = r3(x1)
            t3 = r3(tmp32)
            for ct in range(NCT):
                inst = vector.tensor_scalar(out=x1v[:, ct, :], in0=t3[:, ct, :],
                                            scalar1=sv[:, g0 + ct:g0 + ct + 1],
                                            scalar2=sv[:, b0 + ct:b0 + ct + 1],
                                            op0=ALU.mult, op1=ALU.add)
            p.inc(inst, "v", 1, f"a1_x1_{i}")
        p.block.vector(vec_x1)

        emit_stats(p, f"a1l{i}", x1, sc16, sq16, NCT, TC, psA, psB,
                   rowbuf, prep, [f"a1_x1_{i}"])

        def vec_h(vector, i=i):
            p.wait(vector, f"a1l{i}_prep")
            Mb = prep.ap()[:, 0:TC].unsqueeze(1).broadcast_to([128, NCT, TC])
            Rb = prep.ap()[:, TC:2 * TC].unsqueeze(1).broadcast_to([128, NCT, TC])
            vector.tensor_tensor(out=r3(tmp32), in0=r3(x1), in1=Mb, op=ALU.subtract)
            vector.tensor_tensor(out=r3(tmp32), in0=r3(tmp32), in1=Rb, op=ALU.mult)
            g1, _ = _SV["ln1_w"]
            b1, _ = _SV["ln1_b"]
            t3 = r3(tmp32)
            for ct in range(NCT):
                vector.tensor_scalar(out=hp[:, ct, 1:TC + 1], in0=t3[:, ct, :],
                                     scalar1=sv[:, g1 + ct:g1 + ct + 1],
                                     scalar2=sv[:, b1 + ct:b1 + ct + 1],
                                     op0=ALU.mult, op1=ALU.add)
            for mbuf, tm, om in [(mixk, "tmk", "om_tmk"), (mixv, "tmv", "om_tmv"),
                                 (mixr, "tmr", "om_tmr")]:
                to, _ = _SV[tm]
                oo = O + DV[om]
                mv = r3(mbuf)
                for ct in range(NCT):
                    vector.tensor_scalar(out=t3[:, ct, :], in0=hp[:, ct, 0:TC],
                                         scalar1=sv[:, oo + ct:oo + ct + 1],
                                         scalar2=None, op0=ALU.mult)
                    vector.scalar_tensor_tensor(out=mv[:, ct, :], in0=hp[:, ct, 1:TC + 1],
                                                scalar=sv[:, to + ct:to + ct + 1],
                                                in1=t3[:, ct, :],
                                                op0=ALU.mult, op1=ALU.add)
            inst = vector.tensor_copy(out=hp[:, :, 0:1], in_=hp[:, :, TC:TC + 1])
            p.inc(inst, "v", 1, f"a1_mix_{i}")
        p.block.vector(vec_h)

        p.reserve("s", f"a1_ek_{i}")
        p.reserve("s", f"a1_v_{i}")
        p.reserve("s", f"a1_sr_{i}")

        def proj(tensor, lhsT16, rhs_buf, wait_evs, ev, i=i):
            for e in wait_evs:
                p.wait(tensor, e)
            wv = lhsT16.ap().rearrange("p (ct o) -> p ct o", ct=NCT)
            rv = r3(rhs_buf)
            pv = pkv.ap().rearrange("p (ot t) -> p ot t", ot=NCT)
            for ot in range(NCT):
                for ct in range(NCT):
                    inst = tensor.matmul(pv[:, ot, :], wv[:, ct, ot * 128:(ot + 1) * 128],
                                         rv[:, ct, :],
                                         start=(ct == 0), stop=(ct == NCT - 1))
            p.inc(inst, "t", 1, ev)

        def tn_k(tensor, i=i):
            evs = [f"a1_mix_{i}"]
            if i == 0:
                evs.append(wkev)
            else:
                evs.append(f"a1_x2_{i-1}")
            proj(tensor, wk16, mixk, evs, f"a1_kmm_{i}", i)
        p.block.tensor(tn_k)

        def sc_k(scalar, i=i):
            p.wait(scalar, f"a1_kmm_{i}")
            p.inc_reserved(scalar.activation(out=ek[:], in_=pkv[:], func=AF.Exp),
                           f"a1_ek_{i}")
        p.block.scalar(sc_k)

        def tn_v(tensor, i=i):
            evs = [f"a1_ek_{i}"] + ([wvev] if i == 0 else [])
            proj(tensor, wv16, mixv, evs, f"a1_vmm_{i}", i)
        p.block.tensor(tn_v)

        def sc_v(scalar, i=i):
            p.wait(scalar, f"a1_vmm_{i}")
            p.inc_reserved(scalar.copy(out=v32[:], in_=pkv[:]), f"a1_v_{i}")
        p.block.scalar(sc_v)

        def tn_r(tensor, i=i):
            evs = [f"a1_v_{i}"] + ([wrev] if i == 0 else [])
            proj(tensor, wr16, mixr, evs, f"a1_rmm_{i}", i)
        p.block.tensor(tn_r)

        def sc_r(scalar, i=i):
            p.wait(scalar, f"a1_rmm_{i}")
            p.inc_reserved(scalar.activation(out=sr[:], in_=pkv[:], func=AF.Sigmoid),
                           f"a1_sr_{i}")
        p.block.scalar(sc_r)

        def vec_wkv(vector, i=i):
            p.wait(vector, f"a1_ek_{i}")
            p.wait(vector, f"a1_v_{i}")
            if i > 0:
                p.wait(vector, f"a1_st_{i-1}")
            vector.tensor_tensor(out=out32[:], in0=ek[:], in1=v32[:], op=ALU.mult)
            lam = O + DV["lam"]
            eu = O + DV["eu"]
            o3 = r3(out32)
            e3 = r3(ek)
            for ct in range(NCT):
                vector.tensor_tensor_scan(
                    out=Av[:, ct, 1:TC + 1],
                    data0=sv[:, lam + ct:lam + ct + 1].broadcast_to([128, TC]),
                    data1=o3[:, ct, :], initial=Av[:, ct, 0:1],
                    op0=ALU.mult, op1=ALU.add)
            for ct in range(NCT):
                vector.tensor_tensor_scan(
                    out=Bv[:, ct, 1:TC + 1],
                    data0=sv[:, lam + ct:lam + ct + 1].broadcast_to([128, TC]),
                    data1=e3[:, ct, :], initial=Bv[:, ct, 0:1],
                    op0=ALU.mult, op1=ALU.add)
            n3 = r3(v32)
            for ct in range(NCT):
                vector.scalar_tensor_tensor(out=n3[:, ct, :], in0=o3[:, ct, :],
                                            scalar=sv[:, eu + ct:eu + ct + 1],
                                            in1=Av[:, ct, 0:TC],
                                            op0=ALU.mult, op1=ALU.add)
            t3 = r3(tmp32)
            for ct in range(NCT):
                inst = vector.scalar_tensor_tensor(out=t3[:, ct, :], in0=e3[:, ct, :],
                                                   scalar=sv[:, eu + ct:eu + ct + 1],
                                                   in1=Bv[:, ct, 0:TC],
                                                   op0=ALU.mult, op1=ALU.add)
            p.inc(inst, "v", 1, f"a1_den_{i}")
        p.block.vector(vec_wkv)

        def sc_rden(scalar, i=i):
            p.wait(scalar, f"a1_den_{i}")
            scalar.activation(out=tmp32[:], in_=tmp32[:], func=AF.Ln)
            p.inc(scalar.activation(out=tmp32[:], in_=tmp32[:], func=AF.Exp,
                                    scale=-1.0),
                  "s", 1, f"a1_rden_{i}")
        p.block.scalar(sc_rden)

        def vec_y(vector, i=i):
            p.wait(vector, f"a1_rden_{i}")
            p.wait(vector, f"a1_sr_{i}")
            vector.tensor_tensor(out=ek[:], in0=v32[:], in1=tmp32[:], op=ALU.mult)
            vector.tensor_tensor(out=mixk[:], in0=sr[:], in1=ek[:], op=ALU.mult)
            vector.tensor_copy(out=Av[:, :, 0:1], in_=Av[:, :, TC:TC + 1])
            p.inc(vector.tensor_copy(out=Bv[:, :, 0:1], in_=Bv[:, :, TC:TC + 1]),
                  "v", 1, f"a1_satt_{i}")
        p.block.vector(vec_y)

        def tn_wo(tensor, i=i):
            evs = [f"a1_satt_{i}", f"a1_sr_{i}"] + ([woev] if i == 0 else [])
            proj(tensor, wo16, mixk, evs, f"a1_womm_{i}", i)
        p.block.tensor(tn_wo)

        def vec_x2(vector, i=i):
            p.wait(vector, f"a1_womm_{i}")
            vector.tensor_tensor(out=out32[:], in0=pkv[:], in1=x1[:], op=ALU.add)
            p.inc(vector.tensor_tensor(out=ek[:], in0=out32[:], in1=sx[:], op=ALU.add),
                  "v", 1, f"a1_xs_{i}")
            p.ev[f"a1_x2_{i}"] = p.ev[f"a1_xs_{i}"]
        p.block.vector(vec_x2)

        def st(sync, i=i, tsl=tsl):
            p.wait(sync, f"a1_xs_{i}")
            p.mark_dma(sync.dma_start(out=x2v[:, :, tsl], in_=r3(out32)),
                       "dma_a1s", None)
            p.mark_dma(sync.dma_start(out=xsv[:, :, tsl], in_=r3(ek)),
                       "dma_a1s", f"a1_st_{i}")
        p.block.sync(st)

    def fin(sync):
        p.wait(sync, f"a1_st_{NCH-1}")
    p.block.sync(fin)



def emit_A2(p):
    TC = 512
    NCH = T // TC
    NQ = 4
    QH = NHT // NQ
    x2v = fm_view(p.dram["x2_d"])
    xsv = fm_view(p.dram["xsum_d"])
    res1 = fm_view(p.dram["res1_d"])
    wfk16d = p.dram["wfk16_d"]
    wfv16d = p.dram["wfv16_d"]
    emit_derived(p)
    sv = p.sv32
    O = SVW

    wfr16, wfrev = p.load_weight_cast("a2_wfr", p.dram["wfr_t"], NCT, C, "dma_a2w",
                                      stage_cols=4096, dma_eng="gpsimd")

    x2c = p.sb("a2_x2c", [128, NCT * TC], F32)
    tmp32 = p.sb("a2_tmp", [128, NCT * TC], F32)
    sc16 = p.sb("a2_sc16", [128, NCT * TC], BF16)
    sq16 = p.sb("a2_sq16", [128, NCT * TC], BF16)
    h2pad = p.sb("a2_h2pad", [128, NCT * (TC + 1)], F32)
    mixk2 = p.sb("a2_mixk2", [128, NCT * TC], BF16)
    mixr2 = p.sb("a2_mixr2", [128, NCT * TC], BF16)
    kf16 = p.sb("a2_kf16", [128, QH * TC], BF16)
    kv32 = p.sb("a2_kv32", [128, NCT * TC], F32)
    xs = p.sb("a2_xs", [128, NCT * TC], F32)
    wfkq = p.sb("a2_wfkq", [128, NCT * 1024], BF16)
    wfvq = p.sb("a2_wfvq", [128, QH * 1024], BF16)
    rowbuf = p.sb("a2_rows", [1, 6 * TC], F32)

    psA = p.ps("a2_psA", [1, TC])
    psB = p.ps("a2_psB", [1, TC])
    prep = p.ps("a2_prep", [128, 2 * TC])
    pkf = [p.ps(f"a2_pkf{j}", [128, TC]) for j in range(2)]
    pkv = [p.ps(f"a2_pkv{j}", [128, TC]) for j in range(2)]

    hp = h2pad.ap().rearrange("p (ct t) -> p ct t", ct=NCT)

    def r3(buf, n=NCT):
        return buf.ap().rearrange("p (ct t) -> p ct t", ct=n)

    def init(vector):
        p.inc(vector.memset(h2pad[:], 0.0), "v", 1, "a2_init")
    p.block.vector(init)

    def emit_qload(i, q):
        if i == 0:
            _, ev1 = p.load_weight_cast(f"a2fk{q}", p.dram["wfk_t"][:, q * 1024:(q + 1) * 1024],
                                        NCT, 1024, "dma_a2w", stage_cols=4096,
                                        dma_eng="gpsimd", w16=wfkq,
                                        halt_event=f"a2_kfmm_0_{q-1}" if q else None)
            _, ev2 = p.load_weight_cast(f"a2fv{q}", p.dram["wfv_t"][q * 1024:(q + 1) * 1024, :],
                                        QH, C, "dma_a2w", stage_cols=4096,
                                        dma_eng="gpsimd", w16=wfvq,
                                        halt_event=f"a2_kvmm_0_{q-1}" if q else None)
            p.ev[f"a2_fkld_{i}_{q}"] = p.ev[ev1]
            p.ev[f"a2_fvld_{i}_{q}"] = p.ev[ev2]

            def cst(gpsimd, q=q):
                p.wait(gpsimd, ev1)
                p.mark_dma(gpsimd.dma_start(out=wfk16d[:, q * 1024:(q + 1) * 1024]
                                            .rearrange("(ct p) o -> p ct o", p=128),
                                            in_=r3(wfkq)),
                           "dma_a2c", f"a2_fkcst_{q}")
                p.wait(gpsimd, ev2)
                p.mark_dma(gpsimd.dma_start(out=wfv16d[q * 1024:(q + 1) * 1024, :]
                                            .rearrange("(ct p) o -> p ct o", p=128),
                                            in_=r3(wfvq, QH)),
                           "dma_a2c", f"a2_fvcst_{q}")
            p.block.gpsimd(cst)
        else:
            def ldq(gpsimd, i=i, q=q):
                p.wait(gpsimd, f"a2_fkcst_{q}")
                p.wait(gpsimd, f"a2_fvcst_{q}")
                prev = (i, q - 1) if q > 0 else (i - 1, NQ - 1)
                p.wait(gpsimd, f"a2_kfmm_{prev[0]}_{prev[1]}")
                p.mark_dma(gpsimd.dma_start(
                    out=r3(wfkq),
                    in_=wfk16d[:, q * 1024:(q + 1) * 1024].rearrange(
                        "(ct p) o -> p ct o", p=128)),
                    "dma_a2w", f"a2_fkld_{i}_{q}")
                p.wait(gpsimd, f"a2_kvmm_{prev[0]}_{prev[1]}")
                p.mark_dma(gpsimd.dma_start(
                    out=r3(wfvq, QH),
                    in_=wfv16d[q * 1024:(q + 1) * 1024, :].rearrange(
                        "(ct p) o -> p ct o", p=128)),
                    "dma_a2w", f"a2_fvld_{i}_{q}")
            p.block.gpsimd(ldq)

    for i in range(NCH):
        tsl = slice(i * TC, (i + 1) * TC)

        def ld(sync, i=i, tsl=tsl):
            if i > 0:
                p.wait(sync, f"a2_mix_{i-1}")
                p.wait(sync, f"a2_st_{i-1}")
            p.mark_dma(sync.dma_start(out=r3(x2c), in_=x2v[:, :, tsl]),
                       "dma_a2x", f"a2_x2_{i}")
            if i > 0:
                p.wait(sync, f"a2_out_{i-1}")
            p.mark_dma(sync.dma_start(out=r3(xs), in_=xsv[:, :, tsl]),
                       "dma_a2x", f"a2_xs_{i}")
        p.block.sync(ld)

        emit_stats(p, f"a2l{i}", x2c, sc16, sq16, NCT, TC, psA, psB,
                   rowbuf, prep, [f"a2_x2_{i}"] + (["a2_init", "derived_v",
                                                    "derived_s"] if i == 0 else []))

        def vec_h(vector, i=i):
            p.wait(vector, f"a2l{i}_prep")
            Mb = prep.ap()[:, 0:TC].unsqueeze(1).broadcast_to([128, NCT, TC])
            Rb = prep.ap()[:, TC:2 * TC].unsqueeze(1).broadcast_to([128, NCT, TC])
            vector.tensor_tensor(out=r3(tmp32), in0=r3(x2c), in1=Mb, op=ALU.subtract)
            vector.tensor_tensor(out=r3(tmp32), in0=r3(tmp32), in1=Rb, op=ALU.mult)
            g2, _ = _SV["ln2_w"]
            b2, _ = _SV["ln2_b"]
            t3 = r3(tmp32)
            for ct in range(NCT):
                vector.tensor_scalar(out=hp[:, ct, 1:TC + 1], in0=t3[:, ct, :],
                                     scalar1=sv[:, g2 + ct:g2 + ct + 1],
                                     scalar2=sv[:, b2 + ct:b2 + ct + 1],
                                     op0=ALU.mult, op1=ALU.add)
            for mbuf, tm, om in [(mixk2, "cmk", "om_cmk"), (mixr2, "cmr", "om_cmr")]:
                to, _ = _SV[tm]
                oo = O + DV[om]
                mv = r3(mbuf)
                for ct in range(NCT):
                    vector.tensor_scalar(out=t3[:, ct, :], in0=hp[:, ct, 0:TC],
                                         scalar1=sv[:, oo + ct:oo + ct + 1],
                                         scalar2=None, op0=ALU.mult)
                    vector.scalar_tensor_tensor(out=mv[:, ct, :], in0=hp[:, ct, 1:TC + 1],
                                                scalar=sv[:, to + ct:to + ct + 1],
                                                in1=t3[:, ct, :],
                                                op0=ALU.mult, op1=ALU.add)
            inst = vector.tensor_copy(out=hp[:, :, 0:1], in_=hp[:, :, TC:TC + 1])
            p.inc(inst, "v", 1, f"a2_mix_{i}")
        p.block.vector(vec_h)

        for q in range(NQ):
            emit_qload(i, q)
            for ht in range(QH):
                p.reserve("s", f"a2_kfev_{i}_{q}_{ht}")

            def tn_kf(tensor, i=i, q=q):
                p.wait(tensor, f"a2_mix_{i}")
                p.wait(tensor, f"a2_fkld_{i}_{q}")
                wv = r3(wfkq)
                mv = r3(mixk2)
                for ht in range(QH):
                    if ht >= 2:
                        p.wait(tensor, f"a2_kfev_{i}_{q}_{ht-2}")
                    elif (i, q) != (0, 0):
                        pi, pq = (i, q - 1) if q else (i - 1, NQ - 1)
                        p.wait(tensor, f"a2_kfev_{pi}_{pq}_{QH-2+ht}")
                    for ct in range(NCT):
                        inst = tensor.matmul(pkf[ht % 2][:],
                                             wv[:, ct, ht * 128:(ht + 1) * 128],
                                             mv[:, ct, :],
                                             start=(ct == 0), stop=(ct == NCT - 1))
                    p.inc(inst, "t", 1, f"a2_kfg_{i}_{q}_{ht}")
                p.ev[f"a2_kfmm_{i}_{q}"] = p.ev[f"a2_kfg_{i}_{q}_{QH-1}"]
            p.block.tensor(tn_kf)

            def sc_kf(scalar, i=i, q=q):
                kv = r3(kf16, QH)
                for ht in range(QH):
                    p.wait(scalar, f"a2_kfg_{i}_{q}_{ht}")
                    scalar.activation(out=kv[:, ht, :], in_=pkf[ht % 2][:],
                                      func=AF.Relu)
                    p.inc_reserved(scalar.activation(out=kv[:, ht, :], in_=kv[:, ht, :],
                                                     func=AF.Square),
                                   f"a2_kfev_{i}_{q}_{ht}")
            p.block.scalar(sc_kf)

            for ct in range(NCT):
                p.reserve("v", f"a2_kvev_{i}_{q}_{ct}")

            def tn_kv(tensor, i=i, q=q):
                p.wait(tensor, f"a2_kfev_{i}_{q}_{QH-1}")
                p.wait(tensor, f"a2_fvld_{i}_{q}")
                wv = r3(wfvq, QH)
                kv = r3(kf16, QH)
                for ct in range(NCT):
                    if ct >= 2:
                        p.wait(tensor, f"a2_kvev_{i}_{q}_{ct-2}")
                    elif (i, q) != (0, 0):
                        pi, pq = (i, q - 1) if q else (i - 1, NQ - 1)
                        p.wait(tensor, f"a2_kvev_{pi}_{pq}_{NCT-2+ct}")
                    for ht in range(QH):
                        inst = tensor.matmul(pkv[ct % 2][:],
                                             wv[:, ht, ct * 128:(ct + 1) * 128],
                                             kv[:, ht, :],
                                             start=(ht == 0), stop=(ht == QH - 1))
                    p.inc(inst, "t", 1, f"a2_kvg_{i}_{q}_{ct}")
                p.ev[f"a2_kvmm_{i}_{q}"] = p.ev[f"a2_kvg_{i}_{q}_{NCT-1}"]
            p.block.tensor(tn_kv)

            def vec_kv(vector, i=i, q=q):
                k3 = r3(kv32)
                for ct in range(NCT):
                    p.wait(vector, f"a2_kvg_{i}_{q}_{ct}")
                    if q == 0:
                        inst = vector.tensor_copy(out=k3[:, ct, :], in_=pkv[ct % 2][:])
                    else:
                        inst = vector.tensor_tensor(out=k3[:, ct, :], in0=pkv[ct % 2][:],
                                                    in1=k3[:, ct, :], op=ALU.add)
                    p.inc_reserved(inst, f"a2_kvev_{i}_{q}_{ct}")
            p.block.vector(vec_kv)

        for ot in range(NCT):
            p.reserve("s", f"a2_srev_{i}_{ot}")

        def tn_r(tensor, i=i):
            p.wait(tensor, f"a2_mix_{i}")
            if i == 0:
                p.wait(tensor, wfrev)
            wv = r3(wfr16)
            mv = r3(mixr2)
            for ot in range(NCT):
                if ot >= 2:
                    p.wait(tensor, f"a2_srev_{i}_{ot-2}")
                else:
                    p.wait(tensor, f"a2_kfev_{i}_{NQ-1}_{QH-1}")
                for ct in range(NCT):
                    inst = tensor.matmul(pkf[ot % 2][:], wv[:, ct, ot * 128:(ot + 1) * 128],
                                         mv[:, ct, :],
                                         start=(ct == 0), stop=(ct == NCT - 1))
                p.inc(inst, "t", 1, f"a2_rmm_{i}_{ot}")
        p.block.tensor(tn_r)

        def sc_sr(scalar, i=i):
            t3 = r3(tmp32)
            for ot in range(NCT):
                p.wait(scalar, f"a2_rmm_{i}_{ot}")
                p.inc_reserved(scalar.activation(out=t3[:, ot, :], in_=pkf[ot % 2][:],
                                                 func=AF.Sigmoid),
                               f"a2_srev_{i}_{ot}")
        p.block.scalar(sc_sr)

        def vec_out(vector, i=i):
            p.wait(vector, f"a2_srev_{i}_{NCT-1}")
            p.wait(vector, f"a2_kvev_{i}_{NQ-1}_{NCT-1}")
            p.wait(vector, f"a2_xs_{i}")
            vector.tensor_tensor(out=kv32[:], in0=kv32[:], in1=tmp32[:], op=ALU.mult)
            p.inc(vector.tensor_tensor(out=x2c[:], in0=kv32[:], in1=xs[:], op=ALU.add),
                  "v", 1, f"a2_out_{i}")
        p.block.vector(vec_out)

        def st(sync, i=i, tsl=tsl):
            p.wait(sync, f"a2_out_{i}")
            p.mark_dma(sync.dma_start(out=res1[:, :, tsl], in_=r3(x2c)),
                       "dma_a2s", f"a2_st_{i}")
        p.block.sync(st)

    def fin(sync):
        p.wait(sync, f"a2_st_{NCH-1}")
    p.block.sync(fin)



def emit_B(p):
    nc = p.nc
    TC = 256
    NCH = T // TC
    res1 = fm_view(p.dram["res1_d"])
    res2 = fm_view(p.dram["res2_d"])

    w116, w1ev = p.load_weight_cast("b_w1", p.dram["l0f_w1"], NCT, H, "dma_b",
                                    stage_cols=4096)
    w216, w2ev = p.load_weight_cast("b_w2", p.dram["l0f_w2"], NHT, C, "dma_b",
                                    stage_cols=4096)

    sx = p.sb("b_sx", [128, NCT * TC], F32)
    tmp32 = p.sb("b_tmp", [128, NCT * TC], F32)
    sc16 = p.sb("b_sc16", [128, NCT * TC], BF16)
    sq16 = p.sb("b_sq16", [128, NCT * TC], BF16)
    h16 = p.sb("b_h16", [128, NHT * TC], BF16)
    rowbuf = p.sb("b_rows", [1, 6 * TC], F32)
    out32 = p.sb("b_out", [128, NCT * TC], F32)

    psA = p.ps("b_psA", [1, TC])
    psB = p.ps("b_psB", [1, TC])
    prep = p.ps("b_prep", [128, 2 * TC])
    ph = [p.ps(f"b_ph{i}", [128, TC]) for i in range(2)]
    po = [p.ps(f"b_po{i}", [128, TC]) for i in range(2)]

    sv16 = p.sv16
    sv = p.sv32

    for i in range(NCH):
        tsl = slice(i * TC, (i + 1) * TC)

        def ld(sync, i=i, tsl=tsl):
            if i >= 1:
                p.wait(sync, f"b_res_done_{i-1}")
            p.mark_dma(sync.dma_start(out=sx.ap().rearrange("p (ct t) -> p ct t", ct=NCT),
                                      in_=res1[:, :, tsl]),
                       "dma_b", f"b_x_{i}")
        p.block.sync(ld)

        if i == 0:
            pass

        prep_ev = emit_stats(p, f"bln{i}", sx, sc16, sq16, NCT, TC,
                             psA, psB, rowbuf, prep, [f"b_x_{i}"])

        def vec_ln(vector, i=i):
            p.wait(vector, f"bln{i}_prep")
            Mb = prep.ap()[:, 0:TC].unsqueeze(1).broadcast_to([128, NCT, TC])
            Rb = prep.ap()[:, TC:2 * TC].unsqueeze(1).broadcast_to([128, NCT, TC])
            xv = sx.ap().rearrange("p (ct t) -> p ct t", ct=NCT)
            tv = tmp32.ap().rearrange("p (ct t) -> p ct t", ct=NCT)
            ov = sq16.ap().rearrange("p (ct t) -> p ct t", ct=NCT)
            vector.tensor_tensor(out=tv, in0=xv, in1=Mb, op=ALU.subtract)
            p.inc(vector.tensor_tensor(out=ov, in0=tv, in1=Rb, op=ALU.mult),
                  "v", 1, f"b_xln_{i}")
        p.block.vector(vec_ln)

        for ht in range(NHT):
            p.reserve("s", f"b_hev_{i}_{ht}")
        for ct in range(NCT):
            p.reserve("v", f"b_oev_{i}_{ct}")
        p.ev[f"b_res_done_{i}"] = p.ev[f"b_oev_{i}_{NCT-1}"]

        def tn_h(tensor, i=i):
            p.wait(tensor, f"b_xln_{i}")
            if i == 0:
                p.wait(tensor, w1ev)
            xl = sq16.ap().rearrange("p (ct t) -> p ct t", ct=NCT)
            w1v = w116.ap().rearrange("p (ct h) -> p ct h", ct=NCT)
            for ht in range(NHT):
                if ht >= 2:
                    p.wait(tensor, f"b_hev_{i}_{ht-2}")
                elif i > 0:
                    p.wait(tensor, f"b_hev_{i-1}_{NHT-2+ht}")
                pb = ph[ht % 2]
                for ct in range(NCT):
                    inst = tensor.matmul(pb[:], w1v[:, ct, ht * 128:(ht + 1) * 128],
                                         xl[:, ct, :],
                                         start=(ct == 0), stop=(ct == NCT - 1))
                p.inc(inst, "t", 1, f"b_hmm_{i}_{ht}")
        p.block.tensor(tn_h)

        def sc_h(scalar, i=i):
            b1off = _SV["l0f_b1"][0]
            hv = h16.ap().rearrange("p (ht t) -> p ht t", ht=NHT)
            for ht in range(NHT):
                p.wait(scalar, f"b_hmm_{i}_{ht}")
                p.inc_reserved(scalar.activation(out=hv[:, ht, :], in_=ph[ht % 2][:],
                                                 func=AF.Relu,
                                                 bias=sv[:, b1off + ht:b1off + ht + 1]),
                               f"b_hev_{i}_{ht}")
        p.block.scalar(sc_h)

        def tn_o(tensor, i=i):
            if i == 0:
                p.wait(tensor, w2ev)
            hv = h16.ap().rearrange("p (ht t) -> p ht t", ht=NHT)
            w2v = w216.ap().rearrange("p (ht c) -> p ht c", ht=NHT)
            p.wait(tensor, f"b_hev_{i}_{NHT-1}")
            for ct in range(NCT):
                if ct >= 2:
                    p.wait(tensor, f"b_oev_{i}_{ct-2}")
                elif i > 0:
                    p.wait(tensor, f"b_oev_{i-1}_{NCT-2+ct}")
                pb = po[ct % 2]
                for ht in range(NHT):
                    inst = tensor.matmul(pb[:], w2v[:, ht, ct * 128:(ct + 1) * 128],
                                         hv[:, ht, :],
                                         start=(ht == 0), stop=(ht == NHT - 1))
                p.inc(inst, "t", 1, f"b_omm_{i}_{ct}")
        p.block.tensor(tn_o)

        def vec_o(vector, i=i):
            b2off = _SV["l0f_b2"][0]
            xv = sx.ap().rearrange("p (ct t) -> p ct t", ct=NCT)
            ov = out32.ap().rearrange("p (ct t) -> p ct t", ct=NCT)
            if i > 0:
                p.wait(vector, f"b_st_{i-1}")
            for ct in range(NCT):
                p.wait(vector, f"b_omm_{i}_{ct}")
                inst = vector.scalar_tensor_tensor(
                    out=ov[:, ct, :], in0=po[ct % 2][:],
                    scalar=sv[:, b2off + ct:b2off + ct + 1],
                    in1=xv[:, ct, :], op0=ALU.add, op1=ALU.add)
                p.inc_reserved(inst, f"b_oev_{i}_{ct}")
        p.block.vector(vec_o)

        def st(gpsimd, i=i, tsl=tsl):
            p.wait(gpsimd, f"b_res_done_{i}")
            p.mark_dma(gpsimd.dma_start(
                out=res2[:, :, tsl],
                in_=out32.ap().rearrange("p (ct t) -> p ct t", ct=NCT)),
                "dma_bs", f"b_st_{i}")
            p.ev[f"b_out_used_{i}"] = p.ev[f"b_st_{i}"]
        p.block.gpsimd(st)

    def fin(gpsimd):
        p.wait(gpsimd, f"b_st_{NCH-1}")
    p.block.gpsimd(fin)



def emit_D(p):
    TC = 512
    NCH = T // TC
    res3 = fm_view(p.dram["res3_d"])
    yfm = fm_view(p.dram["y_fm"])
    sv = p.sv32

    xc = p.sb("d_xc", [128, NCT * TC], F32)
    tmp32 = p.sb("d_tmp", [128, NCT * TC], F32)
    sc16 = p.sb("d_sc16", [128, NCT * TC], BF16)
    sq16 = p.sb("d_sq16", [128, NCT * TC], BF16)
    xln16 = sq16
    t216 = p.sb("d_t216", [128, NCT * TC], BF16)
    hA = p.sb("d_hA", [128, NHT * TC], BF16)
    hB = p.sb("d_hB", [128, NHT * TC], BF16)
    wbuf = p.sb("d_wbuf", [128, 16384], BF16)
    out32 = tmp32
    rowbuf = p.sb("d_rows", [1, 6 * TC], F32)

    psA = p.ps("d_psA", [1, TC])
    psB = p.ps("d_psB", [1, TC])
    prep = p.ps("d_prep", [128, 2 * TC])
    pm = [p.ps(f"d_pm{j}", [128, TC]) for j in range(2)]

    def r3(buf, n=NCT):
        return buf.ap().rearrange("p (ct t) -> p ct t", ct=n)

    BLOCKS = []
    for ob in range(4):
        BLOCKS.append(("f1", "f1_t", "f116_d", NCT, 1024, ob * 1024))
        BLOCKS.append(("f2", "f2_t", "f216_d", NCT, 1024, ob * 1024))
    for ob in range(8):
        BLOCKS.append(("h1", "fh1_t", "fh116_d", NHT, 512, ob * 512))
    for ob in range(8):
        BLOCKS.append(("h2", "fh2_t", "fh216_d", NHT, 512, ob * 512))
    for ob in range(2):
        BLOCKS.append(("w3", "f3_t", "f316_d", NHT, 512, ob * 512))
    NBLK = len(BLOCKS)

    evac_seq = {}

    def mm_bank_wait(tensor, i, g):
        if g >= 2:
            p.wait(tensor, evac_seq[i][g - 2])
        elif i > 0:
            p.wait(tensor, evac_seq[i - 1][len(evac_seq[i - 1]) - 2 + g])
        return pm[g % 2]

    def emit_wload(i, b):
        nm, src, cache, nin, nout, o0 = BLOCKS[b]
        ldev = f"d_wld_{i}_{b}"
        halt = (f"d_wmm_{i}_{b-1}" if b else f"d_wmm_{i-1}_{NBLK-1}") \
            if (i, b) != (0, 0) else None
        if i == 0:
            src_ap = p.dram[src][:, o0:o0 + nout]
            _, ev = p.load_weight_cast(f"d_{nm}{o0}", src_ap, nin, nout, "dma_dw",
                                       stage_cols=4096, dma_eng="gpsimd",
                                       w16=wbuf, halt_event=halt)
            p.ev[ldev] = p.ev[ev]

            def cst(gpsimd, b=b, cache=cache, nin=nin, nout=nout, o0=o0, ev=ev):
                p.wait(gpsimd, ev)
                p.mark_dma(gpsimd.dma_start(
                    out=p.dram[cache][:, o0:o0 + nout].rearrange(
                        "(ct p) o -> p ct o", p=128),
                    in_=wbuf.ap()[:, 0:nin * nout].rearrange(
                        "p (ct o) -> p ct o", ct=nin)),
                    "dma_dc", f"d_cst_{b}")
            p.block.gpsimd(cst)
        else:
            def ldq(gpsimd, i=i, b=b, cache=cache, nin=nin, nout=nout, o0=o0, halt=halt):
                p.wait(gpsimd, f"d_cst_{b}")
                p.wait(gpsimd, halt)
                p.mark_dma(gpsimd.dma_start(
                    out=wbuf.ap()[:, 0:nin * nout].rearrange(
                        "p (ct o) -> p ct o", ct=nin),
                    in_=p.dram[cache][:, o0:o0 + nout].rearrange(
                        "(ct p) o -> p ct o", p=128)),
                    "dma_dw", ldev)
            p.block.gpsimd(ldq)

    for i in range(NCH):
        tsl = slice(i * TC, (i + 1) * TC)
        evac_seq[i] = []
        gctr = [0]

        def ld(sync, i=i, tsl=tsl):
            if i > 0:
                p.wait(sync, f"d_out_{i-1}")
            p.mark_dma(sync.dma_start(out=r3(xc), in_=res3[:, :, tsl]),
                       "dma_d", f"d_x_{i}")
        p.block.sync(ld)

        emit_stats(p, f"dl{i}", xc, sc16, sq16, NCT, TC, psA, psB,
                   rowbuf, prep, [f"d_x_{i}"])

        def vec_app(vector, i=i):
            p.wait(vector, f"dl{i}_prep")
            if i > 0:
                p.wait(vector, f"d_st_{i-1}")
            Mb = prep.ap()[:, 0:TC].unsqueeze(1).broadcast_to([128, NCT, TC])
            Rb = prep.ap()[:, TC:2 * TC].unsqueeze(1).broadcast_to([128, NCT, TC])
            vector.tensor_tensor(out=r3(tmp32), in0=r3(xc), in1=Mb, op=ALU.subtract)
            vector.tensor_tensor(out=r3(tmp32), in0=r3(tmp32), in1=Rb, op=ALU.mult)
            p.inc(vector.tensor_copy(out=xln16[:], in_=tmp32[:]), "v", 1, f"d_xln_{i}")
        p.block.vector(vec_app)

        hAv = r3(hA, NHT)
        hBv = r3(hB, NHT)
        t2v = r3(t216, NCT)

        for ob in range(4):
            for half, nm in ((0, "f1"), (1, "f2")):
                b = ob * 2 + half
                emit_wload(i, b)
                for sub in range(8):
                    p.reserve("s", f"d_gev_{i}_{b}_{sub}")
                    evac_seq[i].append(f"d_gev_{i}_{b}_{sub}")

                def tn_g(tensor, i=i, b=b):
                    p.wait(tensor, f"d_wld_{i}_{b}")
                    p.wait(tensor, f"d_xln_{i}")
                    wv = wbuf.ap()[:, 0:NCT * 1024].rearrange("p (ct o) -> p ct o",
                                                              ct=NCT)
                    xv = r3(xln16)
                    for sub in range(8):
                        pb = mm_bank_wait(tensor, i, gctr[0]); gctr[0] += 1
                        for ct in range(NCT):
                            inst = tensor.matmul(
                                pb[:], wv[:, ct, sub * 128:(sub + 1) * 128],
                                xv[:, ct, :],
                                start=(ct == 0), stop=(ct == NCT - 1))
                        p.inc(inst, "t", 1, f"d_gmm_{i}_{b}_{sub}")
                    p.ev[f"d_wmm_{i}_{b}"] = p.ev[f"d_gmm_{i}_{b}_{7}"]
                p.block.tensor(tn_g)

                def sc_g(scalar, i=i, b=b, ob=ob, half=half):
                    s1o, _ = _SV["l1f_s1"]
                    s2o, _ = _SV["l1f_s2"]
                    for sub in range(8):
                        glob = ob * 8 + sub
                        p.wait(scalar, f"d_gmm_{i}_{b}_{sub}")
                        bank = (b * 8 + sub) % 2
                        if half == 0:
                            inst = scalar.activation(
                                out=hAv[:, glob, :], in_=pm[bank][:], func=AF.Silu,
                                scale=sv[:, s1o + glob:s1o + glob + 1])
                        else:
                            if i > 0 or ob > 0:
                                p.wait(scalar, f"d_hA_{i}_{(ob-1)*8+sub}" if ob
                                       else f"d_hA_{i-1}_{24+sub}")
                            inst = scalar.activation(
                                out=t2v[:, sub, :], in_=pm[bank][:], func=AF.Identity,
                                scale=sv[:, s2o + glob:s2o + glob + 1])
                        p.inc_reserved(inst, f"d_gev_{i}_{b}_{sub}")
                        if half == 1:
                            p.reserve("v", f"d_hA_{i}_{glob}")
                p.block.scalar(sc_g)

                if half == 1:
                    def vec_g(vector, i=i, b=b, ob=ob):
                        for sub in range(8):
                            glob = ob * 8 + sub
                            p.wait(vector, f"d_gev_{i}_{b}_{sub}")
                            p.inc_reserved(
                                vector.tensor_tensor(out=hAv[:, glob, :],
                                                     in0=hAv[:, glob, :],
                                                     in1=t2v[:, sub, :], op=ALU.mult),
                                f"d_hA_{i}_{glob}")
                    p.block.vector(vec_g)
        p.ev[f"d_hA_done_{i}"] = p.ev[f"d_hA_{i}_{NHT-1}"]

        for hph in range(2):
            srcv = hAv if hph == 0 else hBv
            dstv = hBv if hph == 0 else hAv
            svn = "l1f_hs1" if hph == 0 else "l1f_hs2"
            base_b = 8 + 8 * hph
            in_ev = f"d_hA_done_{i}" if hph == 0 else f"d_hB_done_{i}"
            for ob in range(8):
                b = base_b + ob
                emit_wload(i, b)
                for sub in range(4):
                    p.reserve("s", f"d_hev_{i}_{b}_{sub}")
                    evac_seq[i].append(f"d_hev_{i}_{b}_{sub}")

                def tn_h(tensor, i=i, b=b, srcv=srcv, in_ev=in_ev):
                    p.wait(tensor, f"d_wld_{i}_{b}")
                    p.wait(tensor, in_ev)
                    wv = wbuf.ap()[:, 0:NHT * 512].rearrange("p (ht o) -> p ht o",
                                                             ht=NHT)
                    for sub in range(4):
                        pb = mm_bank_wait(tensor, i, gctr[0]); gctr[0] += 1
                        for ht in range(NHT):
                            inst = tensor.matmul(
                                pb[:], wv[:, ht, sub * 128:(sub + 1) * 128],
                                srcv[:, ht, :],
                                start=(ht == 0), stop=(ht == NHT - 1))
                        p.inc(inst, "t", 1, f"d_hmm_{i}_{b}_{sub}")
                    p.ev[f"d_wmm_{i}_{b}"] = p.ev[f"d_hmm_{i}_{b}_{3}"]
                p.block.tensor(tn_h)

                def sc_h(scalar, i=i, b=b, ob=ob, dstv=dstv, svn=svn, base_b=base_b):
                    so, _ = _SV[svn]
                    for sub in range(4):
                        glob = ob * 4 + sub
                        p.wait(scalar, f"d_hmm_{i}_{b}_{sub}")
                        bank = (64 + (b - 8) * 4 + sub) % 2
                        p.inc_reserved(
                            scalar.activation(out=dstv[:, glob, :], in_=pm[bank][:],
                                              func=AF.Gelu,
                                              scale=sv[:, so + glob:so + glob + 1]),
                            f"d_hev_{i}_{b}_{sub}")
                p.block.scalar(sc_h)
            p.ev[f"d_hB_done_{i}" if hph == 0 else f"d_hC_done_{i}"] = \
                p.ev[f"d_hev_{i}_{base_b+7}_{3}"]

        for ob in range(2):
            b = 24 + ob
            emit_wload(i, b)
            for sub in range(4):
                p.reserve("v", f"d_w3ev_{i}_{ob*4+sub}")
                evac_seq[i].append(f"d_w3ev_{i}_{ob*4+sub}")

            def tn_w3(tensor, i=i, b=b):
                p.wait(tensor, f"d_wld_{i}_{b}")
                p.wait(tensor, f"d_hC_done_{i}")
                wv = wbuf.ap()[:, 0:NHT * 512].rearrange("p (ht o) -> p ht o", ht=NHT)
                for sub in range(4):
                    pb = mm_bank_wait(tensor, i, gctr[0]); gctr[0] += 1
                    for ht in range(NHT):
                        inst = tensor.matmul(
                            pb[:], wv[:, ht, sub * 128:(sub + 1) * 128],
                            hAv[:, ht, :],
                            start=(ht == 0), stop=(ht == NHT - 1))
                    p.inc(inst, "t", 1, f"d_w3mm_{i}_{(b-24)*4+sub}")
                p.ev[f"d_wmm_{i}_{b}"] = p.ev[f"d_w3mm_{i}_{(b-24)*4+3}"]
            p.block.tensor(tn_w3)

            def vec_w3(vector, i=i, b=b, ob=ob):
                s3o, _ = _SV["l1f_s3"]
                ov = r3(out32)
                xv = r3(xc)
                for sub in range(4):
                    glob = ob * 4 + sub
                    p.wait(vector, f"d_w3mm_{i}_{glob}")
                    if i > 0 and glob == 0:
                        p.wait(vector, f"d_st_{i-1}")
                    bank = (128 + glob) % 2
                    p.inc_reserved(
                        vector.scalar_tensor_tensor(out=ov[:, glob, :],
                                                    in0=pm[bank][:],
                                                    scalar=sv[:, s3o + glob:s3o + glob + 1],
                                                    in1=xv[:, glob, :],
                                                    op0=ALU.mult, op1=ALU.add),
                        f"d_w3ev_{i}_{glob}")
            p.block.vector(vec_w3)
        p.ev[f"d_out_{i}"] = p.ev[f"d_w3ev_{i}_{7}"]

        def st(sync, i=i, tsl=tsl):
            p.wait(sync, f"d_out_{i}")
            p.mark_dma(sync.dma_start(out=yfm[:, :, tsl], in_=r3(out32)),
                       "dma_ds", f"d_st_{i}")
        p.block.sync(st)

    def fin(sync):
        p.wait(sync, f"d_st_{NCH-1}")
    p.block.sync(fin)



def emit_prelude(p):
    nc = p.nc
    p.sv32 = p.sb("sv32", [128, SVW + 64], F32)
    p.sv16 = p.sb("sv16", [128, SVW], BF16)
    p.rv32 = p.sb("rv32", [1, RVW], F32)
    p.ones16 = p.sb("ones16", [128, 1], BF16)
    p.ones32 = p.sb("ones32", [1, 128], F32)
    p.get_stg(4096)

    def ld(sync):
        p.mark_dma(sync.dma_start(out=p.sv32[:, 0:SVW], in_=p.dram["smallvecs"][:]),
                   "dma_pre", "sv_ld")
        p.mark_dma(sync.dma_start(out=p.rv32[:], in_=p.dram["rowvecs"][:]),
                   "dma_pre", "rv_ld")
    p.block.sync(ld)

    def vec(vector):
        p.wait(vector, "sv_ld")
        p.wait(vector, "rv_ld")
        vector.memset(p.ones16[:], 1.0)
        vector.memset(p.ones32[:], 1.0)
        p.inc(vector.tensor_copy(out=p.sv16[:], in_=p.sv32[:, 0:SVW]), "v", 1, "sv16_ready")
        p.ev["pre_ready"] = ("v", p.cnt["v"])
    p.block.vector(vec)

    p.eps_ap = p.rv32.ap()[:, _RV["eps"][0]:_RV["eps"][0] + 1]

    def allwait(eng):
        p.wait(eng, "pre_ready")
    for n in ["sync", "vector", "scalar", "tensor", "gpsimd"]:
        getattr(p.block, n)(allwait)



def build_program(phases):
    nc = bass.Bass("TRN2", target_bir_lowering=False)
    ctx = ExitStack()
    dram = declare_dram(nc, phases)
    block = ctx.enter_context(nc.Block())
    p = P(nc, block, ctx, dram, phases)
    emit_prelude(p)
    for phi, ph in enumerate(phases):
        from contextlib import ExitStack as _ES
        p.phase_ctx = _ES()
        if ph == "A1":
            emit_A1(p)
        elif ph == "A2":
            emit_A2(p)
        elif ph == "B":
            emit_B(p)
        elif ph == "C0":
            emit_C0(p)
        elif ph == "C1":
            emit_C1(p)
        elif ph == "C3":
            emit_C3(p)
        elif ph == "C4":
            emit_C4(p)
        elif ph == "D":
            emit_D(p)
        else:
            raise NotImplementedError(ph)
        if phi < len(phases) - 1:
            p.barrier(f"ph_{ph}")
        p.phase_ctx.close()
        p.phase_ctx = None
    ctx.close()
    return nc


def run_phases(phase_groups, host, extra_outputs=()):
    acts = {}
    results = None
    for phases in phase_groups:
        nc = build_program(phases)
        in_map = {}
        pset = set(phases)
        for name, (shape, prod) in WEIGHTS.items():
            if prod in pset:
                in_map[name] = host[name]
        in_map["smallvecs"] = host["smallvecs"]
        in_map["rowvecs"] = host["rowvecs"]
        per_core = [dict(in_map) for _ in range(NCORE)]
        for name, (shape, dt, prod, cons) in ACTS.items():
            if name == "x_fm":
                if pset & {"A1", "A2"}:
                    for b in range(NCORE):
                        per_core[b]["x_fm"] = host["x_fm"][b]
                continue
            if (cons & pset) and prod not in pset:
                for b in range(NCORE):
                    per_core[b][name] = acts[name][b]
        res = run_bass_kernel_spmd(nc, per_core, core_ids=list(range(NCORE)))
        for b, r in enumerate(res.results):
            for k, v in r.items():
                acts.setdefault(k, [None] * NCORE)[b] = v
    return acts


FUSED = True


def kernel(**inputs):
    host = prep_inputs(inputs)
    groups = [ALL_PHASES] if FUSED else [[ph] for ph in ALL_PHASES]
    acts = run_phases(groups, host)
    out = np.stack([acts["y_fm"][b].T for b in range(NCORE)], axis=0)
    return np.ascontiguousarray(out.astype(np.float32))


# version 03 (final)
# speedup vs baseline: 1.1613x; 1.1597x over previous
import numpy as np
from contextlib import ExitStack
import concourse.bass as bass
import concourse.mybir as mybir
from concourse.bass_utils import run_bass_kernel_spmd

F32 = mybir.dt.float32
BF16 = mybir.dt.bfloat16
AF = mybir.ActivationFunctionType
ALU = mybir.AluOpType

C = 1024
T = 4096
NCT = 8
H = 4096
NHT = 32
NCORE = 8
EPS = 1e-5
NB_HEADS = 16
DH = 64


def _t(a):
    return np.ascontiguousarray(np.asarray(a, np.float32).T)


def _pp(v, n):
    return np.ascontiguousarray(np.asarray(v, np.float32).reshape(n, 128).T)


_SV = {}
_SV_W = [0]


def _sv_slot(name, width):
    if name not in _SV:
        _SV[name] = (_SV_W[0], width)
        _SV_W[0] += width
    return _SV[name]


for _n in ["ln0_w", "ln0_b", "ln1_w", "ln1_b", "ln2_w", "ln2_b",
           "tmk", "tmv", "tmr", "cmk", "cmr", "tdecay", "tfirst",
           "l0f_b2", "l1_bq", "l1_bk", "l1_bo", "l1_b1", "l1_b2", "l1f_s3"]:
    _sv_slot(_n, 8)
for _n in ["l0f_b1", "l1f_s1", "l1f_s2", "l1f_hs1", "l1f_hs2"]:
    _sv_slot(_n, 32)
SVW = _SV_W[0]

_RV = {"bv_row": (0, 1024), "eps": (1024, 1), "one": (1025, 1)}
RVW = 1026


def _relpos_bucket(rel, num_buckets=32, max_distance=128):
    import math
    n = -rel
    nb = num_buckets // 2
    ret = (n < 0).astype(np.int64) * nb
    n = np.abs(n)
    max_exact = nb // 2
    is_small = n < max_exact
    val_if_large = max_exact + (
        np.log(np.maximum(n, 1).astype(np.float64) / max_exact)
        / math.log(max_distance / max_exact) * (nb - max_exact)
    ).astype(np.int64)
    val_if_large = np.minimum(val_if_large, nb - 1)
    return ret + np.where(is_small, n, val_if_large)


def prep_inputs(inp):
    d = {}
    x = np.asarray(inp["L0_attn_x"], np.float32)
    d["x_fm"] = [np.ascontiguousarray(x[b].T) for b in range(NCORE)]

    for src, dst in [("L0_attn_Wk", "wk_t"), ("L0_attn_Wv", "wv_t"),
                     ("L0_attn_Wr", "wr_t"), ("L0_attn_Wo", "wo_t"),
                     ("L0_attn_Wfk", "wfk_t"), ("L0_attn_Wfr", "wfr_t"),
                     ("L0_attn_Wfv", "wfv_t"),
                     ("L1_attn_Wq", "l1wq_t"), ("L1_attn_Wk", "l1wk_t"),
                     ("L1_attn_Wv", "l1wv_t"), ("L1_attn_Wo", "l1wo_t"),
                     ("L1_attn_W1", "l1w1_t"), ("L1_attn_W2", "l1w2_t"),
                     ("L1_ffn_w1_w", "f1_t"), ("L1_ffn_w2_w", "f2_t"),
                     ("L1_ffn_h1_w", "fh1_t"), ("L1_ffn_h2_w", "fh2_t"),
                     ("L1_ffn_w3_w", "f3_t")]:
        d[dst] = _t(inp[src])
    d["l0f_w1"] = np.ascontiguousarray(np.asarray(inp["L0_ffn_w1"], np.float32))
    d["l0f_w2"] = np.ascontiguousarray(np.asarray(inp["L0_ffn_w2"], np.float32))

    sv = np.zeros((128, SVW), np.float32)

    def put(name, val, n):
        off, w = _SV[name]
        assert w == n
        sv[:, off:off + n] = _pp(val, n)

    put("ln0_w", inp["L0_attn_ln0_w"], 8); put("ln0_b", inp["L0_attn_ln0_b"], 8)
    put("ln1_w", inp["L0_attn_ln1_w"], 8); put("ln1_b", inp["L0_attn_ln1_b"], 8)
    put("ln2_w", inp["L0_attn_ln2_w"], 8); put("ln2_b", inp["L0_attn_ln2_b"], 8)
    put("tmk", inp["L0_attn_tmk"], 8); put("tmv", inp["L0_attn_tmv"], 8)
    put("tmr", inp["L0_attn_tmr"], 8)
    put("cmk", inp["L0_attn_cmk"], 8); put("cmr", inp["L0_attn_cmr"], 8)
    put("tdecay", inp["L0_attn_time_decay"], 8)
    put("tfirst", inp["L0_attn_time_first"], 8)
    put("l0f_b1", inp["L0_ffn_b1"], 32); put("l0f_b2", inp["L0_ffn_b2"], 8)
    put("l1_bq", inp["L1_attn_bq"], 8); put("l1_bk", inp["L1_attn_bk"], 8)
    put("l1_bo", inp["L1_attn_bo"], 8)
    put("l1_b1", inp["L1_attn_b1"], 8); put("l1_b2", inp["L1_attn_b2"], 8)
    put("l1f_s1", inp["L1_ffn_w1_s"], 32); put("l1f_s2", inp["L1_ffn_w2_s"], 32)
    put("l1f_hs1", inp["L1_ffn_h1_s"], 32); put("l1f_hs2", inp["L1_ffn_h2_s"], 32)
    put("l1f_s3", inp["L1_ffn_w3_s"], 8)
    d["smallvecs"] = sv

    rv = np.zeros((1, RVW), np.float32)
    rv[0, _RV["eps"][0]] = EPS
    rv[0, _RV["bv_row"][0]:_RV["bv_row"][0] + C] = np.asarray(inp["L1_attn_bv"], np.float32)
    rv[0, _RV["one"][0]] = 1.0
    d["rowvecs"] = rv

    rel_bias = np.asarray(inp["L1_attn_rel_bias"], np.float32)
    n = np.arange(8191)
    buckets = _relpos_bucket(4095 - n)
    d["g_master"] = np.ascontiguousarray(rel_bias[buckets, :].T)
    return d


WEIGHTS = {
    "wk_t": ([C, C], "A1"), "wv_t": ([C, C], "A1"), "wr_t": ([C, C], "A1"),
    "wo_t": ([C, C], "A1"),
    "wfk_t": ([C, H], "A2"), "wfv_t": ([H, C], "A2"), "wfr_t": ([C, C], "A2"),
    "l0f_w1": ([C, H], "B"), "l0f_w2": ([H, C], "B"),
    "l1wq_t": ([C, C], "C1"), "l1wk_t": ([C, C], "C1"), "l1wv_t": ([C, C], "C1"),
    "l1wo_t": ([C, C], "C3"), "l1w1_t": ([C, C], "C4"), "l1w2_t": ([C, C], "C4"),
    "f1_t": ([C, H], "D"), "f2_t": ([C, H], "D"),
    "fh1_t": ([H, H], "D"), "fh2_t": ([H, H], "D"), "f3_t": ([H, C], "D"),
    "g_master": ([16, 8191], "C0"),
}
ACTS = {
    "x_fm":    ([C, T], F32, None, {"A1", "A2"}),
    "x2_d":    ([C, T], F32, "A1", {"A2"}),
    "xsum_d":  ([C, T], F32, "A1", {"A2"}),
    "res1_d":  ([C, T], F32, "A2", {"B"}),
    "wfk16_d": ([C, H], BF16, "A2", {"A2"}),
    "wfv16_d": ([H, C], BF16, "A2", {"A2"}),
    "res2_d":  ([C, T], F32, "B", {"C0", "C4"}),
    "xatt_d":  ([C, T], F32, "C0", {"C3"}),
    "xatt16_d": ([C, T], BF16, "C0", {"C1"}),
    "g16_d":   ([16, 8191], BF16, "C0", {"C1"}),
    "att_tok_d": ([T, C], BF16, "C1", {"C3"}),
    "v_d":     ([T, C], BF16, "C1", {"C1"}),
    "ab_d":    ([C, T], F32, "C3", {"C4"}),
    "ab16_d":  ([C, T], BF16, "C3", {"C4"}),
    "res3_d":  ([C, T], F32, "C4", {"D"}),
    "f116_d":  ([C, H], BF16, "D", {"D"}),
    "f216_d":  ([C, H], BF16, "D", {"D"}),
    "fh116_d": ([H, H], BF16, "D", {"D"}),
    "fh216_d": ([H, H], BF16, "D", {"D"}),
    "f316_d":  ([H, C], BF16, "D", {"D"}),
    "y_fm":    ([C, T], F32, "D", set()),
}
ALL_PHASES = ["A1", "A2", "B", "C0", "C1", "C3", "C4", "D"]



class P:

    def __init__(self, nc, block, ctx, dram, phases):
        self.nc = nc
        self.block = block
        self.ctx = ctx
        self.dram = dram
        self.phases = phases
        self.cnt = {}
        self.sem = {}
        self.ev = {}

    def s(self, name):
        if name not in self.sem:
            self.sem[name] = self.ctx.enter_context(self.nc.semaphore(name))
            self.cnt[name] = 0
        return self.sem[name]

    def inc(self, inst, semname, n=1, event=None):
        sem = self.s(semname)
        inst.then_inc(sem, n)
        self.cnt[semname] += n
        if event is not None:
            self.ev[event] = (semname, self.cnt[semname])
        return self.cnt[semname]

    def mark_dma(self, inst, semname, event=None):
        return self.inc(inst, semname, 16, event)

    def reserve(self, semname, event, n=1):
        self.s(semname)
        self.cnt[semname] += n
        self.ev[event] = (semname, self.cnt[semname])

    def inc_reserved(self, inst, event):
        semname, _ = self.ev[event]
        inst.then_inc(self.sem[semname], 1)

    def wait(self, eng, event):
        semname, v = self.ev[event]
        eng.wait_ge(self.sem[semname], v)

    def wait_maybe(self, eng, event):
        if event in self.ev:
            self.wait(eng, event)

    def sb(self, name, shape, dtype, scope=None):
        ctx = scope or getattr(self, "phase_ctx", None) or self.ctx
        return ctx.enter_context(self.nc.sbuf_tensor(name, shape, dtype))

    def ps(self, name, shape, dtype=F32):
        ctx = getattr(self, "phase_ctx", None) or self.ctx
        return ctx.enter_context(self.nc.psum_tensor(name, shape, dtype))

    def barrier(self, tag):
        b = self.block
        names = ["sync", "vector", "scalar", "tensor", "gpsimd"]
        sem = self.s("bar")
        base = self.cnt["bar"]

        for n in names:
            def emit(eng, n=n):
                eng.sem_inc(sem, 1)
            getattr(b, n)(emit)
        self.cnt["bar"] += 5
        tgt = self.cnt["bar"]

        for n in names:
            def emit(eng):
                eng.wait_ge(sem, tgt)
            getattr(b, n)(emit)


    def get_stg(self, cols=8192):
        if not hasattr(self, "stg") or self.stg_cols < cols:
            assert not hasattr(self, "stg"), "allocate biggest staging first"
            self.stg = self.sb("wstg", [128, cols], F32, scope=self.ctx)
            self.stg_cols = cols
            self.stg_users = 0
        return self.stg

    def load_weight_cast(self, tag, w_ap, n_in_t, n_out, dma_sem, stage_cols=8192,
                         cast_eng="vector", w16=None, halt_event=None,
                         dma_eng="sync"):
        assert stage_cols % n_out == 0
        tps = stage_cols // n_out
        nstage = (n_in_t + tps - 1) // tps
        if w16 is None:
            w16 = self.sb(f"{tag}_16", [128, n_in_t * n_out], BF16)
        stg = self.get_stg(stage_cols)
        src = w_ap.rearrange("(ct p) o -> p ct o", p=128)
        ev_done = f"{tag}_cast_done"
        csem = "v" if cast_eng == "vector" else "g"
        prev_user = getattr(self, "stg_last_ev", None)
        for j in range(nstage):
            self.reserve(csem, f"{tag}_cast_{j}" if j < nstage - 1 else ev_done)
        self.stg_last_ev = ev_done

        def emit_sync(sync):
            for j in range(nstage):
                c0 = j * tps
                c1 = min(n_in_t, c0 + tps)
                if j == 0:
                    if prev_user is not None:
                        self.wait(sync, prev_user)
                    if halt_event is not None:
                        for he in ([halt_event] if isinstance(halt_event, str)
                                   else halt_event):
                            self.wait(sync, he)
                else:
                    self.wait(sync, f"{tag}_cast_{j-1}")
                self.mark_dma(
                    sync.dma_start(
                        out=stg.ap()[:, 0:(c1 - c0) * n_out].rearrange(
                            "p (ct o) -> p ct o", ct=c1 - c0),
                        in_=src[:, c0:c1, :]),
                    dma_sem, f"{tag}_ld_{j}")
        getattr(self.block, dma_eng)(emit_sync)

        def emit_cast(eng):
            for j in range(nstage):
                c0 = j * tps
                c1 = min(n_in_t, c0 + tps)
                self.wait(eng, f"{tag}_ld_{j}")
                ev = f"{tag}_cast_{j}" if j < nstage - 1 else ev_done
                self.inc_reserved(
                    eng.tensor_copy(
                        out=w16[:, c0 * n_out:c1 * n_out],
                        in_=stg[:, 0:(c1 - c0) * n_out]),
                    ev)
        getattr(self.block, cast_eng)(emit_cast)
        return w16, ev_done


def declare_dram(nc, phases):
    dram = {}
    pset = set(phases)
    for name, (shape, prod) in WEIGHTS.items():
        if prod in pset:
            dram[name] = nc.dram_tensor(name, shape, F32, kind="ExternalInput").ap()
    for name, (shape, dt, prod, cons) in ACTS.items():
        needed_in = bool(cons & pset) or (name == "x_fm" and bool(pset))
        produced = prod in pset
        if name == "x_fm":
            if pset & {"A1", "A2"}:
                dram[name] = nc.dram_tensor(name, shape, dt, kind="ExternalInput").ap()
            continue
        if produced and cons and cons <= {prod}:
            dram[name] = nc.dram_tensor(name, shape, dt, kind="Internal").ap()
        elif produced:
            dram[name] = nc.dram_tensor(name, shape, dt, kind="ExternalOutput").ap()
        elif cons & pset:
            dram[name] = nc.dram_tensor(name, shape, dt, kind="ExternalInput").ap()
    if pset:
        dram["smallvecs"] = nc.dram_tensor("smallvecs", [128, SVW], F32,
                                           kind="ExternalInput").ap()
        dram["rowvecs"] = nc.dram_tensor("rowvecs", [1, RVW], F32,
                                         kind="ExternalInput").ap()
    return dram


def fm_view(ap):
    return ap.rearrange("(ct p) t -> p ct t", p=128)



def emit_stats(p, tag, src32, src16, sq16, nct, tc, psA, psB, rowbuf,
               prep_ps, dep_events, double=False, no_eps=False):
    ncols = rowbuf.shape[1] // tc if hasattr(rowbuf, 'shape') else 6
    cs = [rowbuf[:, k * tc:(k + 1) * tc] for k in range(min(6, ncols))]

    def vec_cast(vector):
        for e in dep_events:
            p.wait(vector, e)
        p.inc(vector.tensor_copy(out=src16[:], in_=src32[:]), "v", 1, f"{tag}_c16")
    p.block.vector(vec_cast)

    def sc_sq(scalar):
        for e in dep_events:
            p.wait(scalar, e)
        p.inc(scalar.activation(out=sq16[:], in_=src32[:], func=AF.Square),
              "s", 1, f"{tag}_sq")
    p.block.scalar(sc_sq)

    def tn_stats(tensor):
        p.wait(tensor, f"{tag}_c16")
        p.wait(tensor, f"{tag}_sq")
        s16v = src16.ap().rearrange("p (ct t) -> p ct t", ct=nct)
        q16v = sq16.ap().rearrange("p (ct t) -> p ct t", ct=nct)
        for ct in range(nct):
            tensor.matmul(psA[:], p.ones16[:], s16v[:, ct, :],
                          start=(ct == 0), stop=(ct == nct - 1))
        for ct in range(nct):
            inst = tensor.matmul(psB[:], p.ones16[:], q16v[:, ct, :],
                                 start=(ct == 0), stop=(ct == nct - 1))
        p.inc(inst, "t", 1, f"{tag}_stats")
    p.block.tensor(tn_stats)

    def vec_rows(vector):
        p.wait(vector, f"{tag}_stats")
        vector.tensor_scalar_mul(cs[0], psA[:], 1.0 / C)
        vector.tensor_scalar_mul(cs[1], psB[:], 1.0 / C)
        vector.tensor_tensor(out=cs[2], in0=cs[0], in1=cs[0], op=ALU.mult)
        inst = vector.tensor_tensor(out=cs[1], in0=cs[1], in1=cs[2],
                                    op=ALU.subtract)
        if double:
            vector.tensor_scalar_add(cs[3], cs[1], EPS)
            vector.reciprocal(cs[4], cs[3])
            inst = vector.tensor_tensor(out=cs[2], in0=cs[1], in1=cs[4],
                                        op=ALU.mult)
        p.inc(inst, "v", 1, f"{tag}_var")
    p.block.vector(vec_rows)

    def sc_rstd(scalar):
        p.wait(scalar, f"{tag}_var")
        if double:
            scalar.activation(out=cs[3], in_=cs[3], func=AF.Ln)
            scalar.activation(out=cs[3], in_=cs[3], func=AF.Exp, scale=-0.5)
            scalar.activation(out=cs[5], in_=cs[2], func=AF.Ln, bias=p.eps_ap)
            inst = scalar.activation(out=cs[5], in_=cs[5], func=AF.Exp, scale=-0.5)
        else:
            scalar.activation(out=cs[2], in_=cs[1], func=AF.Ln,
                              bias=0.0 if no_eps else p.eps_ap)
            inst = scalar.activation(out=cs[1], in_=cs[2], func=AF.Exp, scale=-0.5)
        p.inc(inst, "s", 1, f"{tag}_rstd")
    p.block.scalar(sc_rstd)

    if double:
        def vec_r12(vector):
            p.wait(vector, f"{tag}_rstd")
            p.inc(vector.tensor_tensor(out=cs[1], in0=cs[3], in1=cs[5], op=ALU.mult),
                  "v", 1, f"{tag}_r12")
        p.block.vector(vec_r12)

    def tn_rep(tensor):
        p.wait(tensor, f"{tag}_r12" if double else f"{tag}_rstd")
        for o in range(0, 2 * tc, 512):
            e = min(2 * tc, o + 512)
            inst = tensor.matmul(prep_ps[:, o:e], p.ones32[:], rowbuf[:, o:e],
                                 start=True, stop=True)
        p.inc(inst, "t", 1, f"{tag}_prep")
    p.block.tensor(tn_rep)
    return f"{tag}_prep"



def emit_C0(p):
    TC = 512
    NCH = T // TC
    res2 = fm_view(p.dram["res2_d"])
    xatt = fm_view(p.dram["xatt_d"])
    xatt16 = fm_view(p.dram["xatt16_d"])

    sx = p.sb("c0_sx", [128, NCT * TC], F32)
    tmp32 = p.sb("c0_tmp", [128, NCT * TC], F32)
    sc16 = p.sb("c0_sc16", [128, NCT * TC], BF16)
    sq16 = p.sb("c0_sq16", [128, NCT * TC], BF16)
    o16 = p.sb("c0_o16", [128, NCT * TC], BF16)
    gm32 = p.sb("c0_gm32", [16, 8191], F32)
    gm16 = p.sb("c0_gm16", [16, 8191], BF16)
    rowbuf = p.sb("c0_rows", [1, 6 * TC], F32)

    psA = p.ps("c0_psA", [1, TC])
    psB = p.ps("c0_psB", [1, TC])
    prep = p.ps("c0_prep", [128, 2 * TC])

    def r3(buf):
        return buf.ap().rearrange("p (ct t) -> p ct t", ct=NCT)

    def gld(gpsimd):
        p.mark_dma(gpsimd.dma_start(out=gm32[:], in_=p.dram["g_master"][:]),
                   "dma_c0g", "c0_gld")
    p.block.gpsimd(gld)

    def gcast(vector):
        p.wait(vector, "c0_gld")
        p.inc(vector.tensor_copy(out=gm16[:], in_=gm32[:]), "v", 1, "c0_gcast")
    p.block.vector(gcast)

    def gst(gpsimd):
        p.wait(gpsimd, "c0_gcast")
        p.mark_dma(gpsimd.dma_start(out=p.dram["g16_d"][:], in_=gm16[:]),
                   "dma_c0g", "c0_gst")
    p.block.gpsimd(gst)

    for i in range(NCH):
        tsl = slice(i * TC, (i + 1) * TC)

        def ld(sync, i=i, tsl=tsl):
            if i > 0:
                p.wait(sync, f"c0_app_{i-1}")
            p.mark_dma(sync.dma_start(out=r3(sx), in_=res2[:, :, tsl]),
                       "dma_c0", f"c0_x_{i}")
        p.block.sync(ld)

        emit_stats(p, f"c0l{i}", sx, sc16, sq16, NCT, TC, psA, psB,
                   rowbuf, prep, [f"c0_x_{i}"])

        def vec_app(vector, i=i):
            p.wait(vector, f"c0l{i}_prep")
            if i > 0:
                p.wait(vector, f"c0_st_{i-1}")
            Mb = prep.ap()[:, 0:TC].unsqueeze(1).broadcast_to([128, NCT, TC])
            Rb = prep.ap()[:, TC:2 * TC].unsqueeze(1).broadcast_to([128, NCT, TC])
            vector.tensor_tensor(out=r3(tmp32), in0=r3(sx), in1=Mb, op=ALU.subtract)
            vector.tensor_tensor(out=r3(tmp32), in0=r3(tmp32), in1=Rb, op=ALU.mult)
            p.inc(vector.tensor_copy(out=o16[:], in_=tmp32[:]), "v", 1, f"c0_app_{i}")
        p.block.vector(vec_app)

        def st(sync, i=i, tsl=tsl):
            p.wait(sync, f"c0_app_{i}")
            p.mark_dma(sync.dma_start(out=xatt[:, :, tsl], in_=r3(tmp32)),
                       "dma_c0s", None)
            p.mark_dma(sync.dma_start(out=xatt16[:, :, tsl], in_=r3(o16)),
                       "dma_c0s", f"c0_st_{i}")
        p.block.sync(st)

    def fin(sync):
        p.wait(sync, f"c0_st_{NCH-1}")
        p.wait(sync, "c0_gst")
    p.block.sync(fin)



C1_DEBUG = 0
C1_NTC = 8

def emit_C1(p):
    GW = 8064
    NST = 8
    xatt16 = fm_view(p.dram["xatt16_d"])
    attTok = p.dram["att_tok_d"]
    vd = p.dram["v_d"]
    g16d = p.dram["g16_d"]
    sv = p.sv32

    qF = p.sb("c1_qF", [64, T], BF16)
    kF = p.sb("c1_kF", [64, T], BF16)
    vones = p.sb("c1_vones", [128, 32 * 66], BF16)
    att_s = p.sb("c1_att_s", [64, 2 * 512], BF16)
    atok = p.sb("c1_atok", [128, 2 * 4 * 64], BF16)
    Hb = p.sb("c1_H", [128, GW], BF16)
    G = p.sb("c1_G", [128, GW], BF16)
    Efull = p.sb("c1_Efull", [128, 32 * 512], BF16)
    ones64 = p.sb("c1_ones64", [1, 64], F32)
    xblk = p.sb("c1_xblk", [128, NCT * 256], BF16)
    xtt = p.sb("c1_xtt", [128, NCT * 128], BF16)
    wq16 = p.sb("c1_wq16", [128, NCT * C], BF16)
    wk16 = p.sb("c1_wk16", [128, NCT * C], BF16)
    wv16 = p.sb("c1_wv16", [128, NCT * C], BF16)
    vst = p.sb("c1_vst", [128, 2 * C], BF16)
    vb32 = p.sb("c1_vb32", [128, C], F32)
    r32 = p.sb("c1_r32", [1, 512], F32)
    Rr32 = p.sb("c1_Rr32", [64, 512], F32)
    tE32 = p.sb("c1_tE32", [64, 512], F32)

    pp = [p.ps(f"c1_pp{j}", [128, 512]) for j in range(4)]
    pE = p.ps("c1_pE", [65, 512])
    pB = p.ps("c1_pB", [64, 512])
    prep2 = p.ps("c1_prep2", [64, 512])
    vbr = p.ps("c1_vbr", [128, 512])

    def vinit(vector):
        vector.memset(ones64[:], 1.0)
        p.inc(vector.memset(vones[:], 1.0), "v", 1, "c1_vones_init")
    p.block.vector(vinit)

    wq16, wqev = p.load_weight_cast("c1_wq", p.dram["l1wq_t"], NCT, C, "dma_c1w",
                                    stage_cols=4096, dma_eng="gpsimd", w16=wq16)
    wk16, wkev = p.load_weight_cast("c1_wk", p.dram["l1wk_t"], NCT, C, "dma_c1w",
                                    stage_cols=4096, dma_eng="gpsimd", w16=wk16)
    wv16, wvev = p.load_weight_cast("c1_wv", p.dram["l1wv_t"], NCT, C, "dma_c1w",
                                    stage_cols=4096, dma_eng="gpsimd", w16=wv16)

    def tn_bv(tensor):
        p.wait(tensor, "pre_ready")
        bvo = _RV["bv_row"][0]
        p.inc(tensor.matmul(vbr[:], p.ones32[:], p.rv32[:, bvo:bvo + 512],
                            start=True, stop=True), "t", 1, "c1_bv0")
    p.block.tensor(tn_bv)

    def vec_bv0(vector):
        p.wait(vector, "c1_bv0")
        p.inc(vector.tensor_copy(out=vb32[:, 0:512], in_=vbr[:]), "v", 1, "c1_bve0")
    p.block.vector(vec_bv0)

    def tn_bv1(tensor):
        p.wait(tensor, "c1_bve0")
        bvo = _RV["bv_row"][0]
        p.inc(tensor.matmul(vbr[:], p.ones32[:], p.rv32[:, bvo + 512:bvo + 1024],
                            start=True, stop=True), "t", 1, "c1_bv1")
    p.block.tensor(tn_bv1)

    def vec_bv1(vector):
        p.wait(vector, "c1_bv1")
        p.inc(vector.tensor_copy(out=vb32[:, 512:1024], in_=vbr[:]), "v", 1, "c1_bv")
    p.block.vector(vec_bv1)

    for tt in range(32):
        p.reserve("v", f"c1_vev_{tt}")

        def ldv(sync, tt=tt):
            if tt:
                p.wait(sync, f"c1_vmm_{tt-1}")
            p.mark_dma(sync.dma_start(
                out=xtt.ap().rearrange("p (ct t) -> p ct t", ct=NCT),
                in_=xatt16[:, :, tt * 128:(tt + 1) * 128]),
                "dma_c1x", f"c1_xtt_{tt}")
        p.block.sync(ldv)

        def tn_v(tensor, tt=tt):
            p.wait(tensor, f"c1_xtt_{tt}")
            if tt == 0:
                p.wait(tensor, wvev)
            if tt >= 2:
                p.wait(tensor, f"c1_vev_{tt-2}")
            xv = xtt.ap().rearrange("p (ct t) -> p ct t", ct=NCT)
            wvw = wv16.ap().rearrange("p (ct o) -> p ct o", ct=NCT)
            for half in range(2):
                for ct in range(NCT):
                    inst = tensor.matmul(pp[2 * (tt % 2) + half][:], xv[:, ct, :],
                                         wvw[:, ct, half * 512:(half + 1) * 512],
                                         start=(ct == 0), stop=(ct == NCT - 1))
                p.inc(inst, "t", 1, f"c1_vmmh_{tt}_{half}")
            p.ev[f"c1_vmm_{tt}"] = p.ev[f"c1_vmmh_{tt}_{1}"]
        p.block.tensor(tn_v)

        def vec_v(vector, tt=tt):
            if tt == 0:
                p.wait(vector, "c1_bv")
            if tt >= 2:
                p.wait(vector, f"c1_vst_{tt-2}")
            for half in range(2):
                p.wait(vector, f"c1_vmmh_{tt}_{half}")
                inst = vector.tensor_tensor(
                    out=vst[:, (tt % 2) * C + half * 512:(tt % 2) * C + (half + 1) * 512],
                    in0=pp[2 * (tt % 2) + half][:],
                    in1=vb32[:, half * 512:(half + 1) * 512], op=ALU.add)
            p.inc_reserved(inst, f"c1_vev_{tt}")
        p.block.vector(vec_v)

        def stv(sync, tt=tt):
            p.wait(sync, f"c1_vev_{tt}")
            p.mark_dma(sync.dma_start(
                out=vd[tt * 128:(tt + 1) * 128, :],
                in_=vst[:, (tt % 2) * C:(tt % 2 + 1) * C]),
                "dma_c1s", f"c1_vst_{tt}")
        p.block.sync(stv)

    for j in range(16):
        bq0 = f"c1b{j}"

        def ldxb(sync, j=j):
            if j:
                p.wait(sync, f"c1b{j-1}_qkf")
            p.mark_dma(sync.dma_start(
                out=xblk.ap().rearrange("p (ct t) -> p ct t", ct=NCT),
                in_=xatt16[:, :, j * 256:(j + 1) * 256]),
                "dma_c1x", f"{bq0}_xb")
        p.block.sync(ldxb)

        def ldvf(sync, j=j):
            p.wait(sync, f"c1_vst_{31}")
            if j:
                p.wait(sync, f"c1b{j-1}_avdone")
            p.mark_dma(sync.dma_start(
                out=bass.AP(vones, 0, [[66 * 32, 128], [66, 32], [1, 64]]),
                in_=bass.AP(vd.tensor, j * 256 * C,
                            [[C, 8], [64, 16], [8 * C, 32], [1, 64]])),
                "dma_c1x", f"{bq0}_vf")
        p.block.sync(ldvf)

        for ck in range(16):
            p.reserve("s", f"{bq0}_qkev_{ck}")
        for ck in range(16):
            p.reserve("s", f"{bq0}_kkev_{ck}")

        def tn_qkf(tensor, j=j):
            p.wait(tensor, f"{bq0}_xb")
            if j == 0:
                p.wait(tensor, wqev)
                p.wait(tensor, wkev)
            xv = xblk.ap().rearrange("p (ct t) -> p ct t", ct=NCT)
            for jq, w16b in ((0, wq16), (1, wk16)):
                wvw = w16b.ap().rearrange("p (ct o) -> p ct o", ct=NCT)
                for ck in range(16):
                    oi = jq * 16 + ck
                    if oi >= 2:
                        pj, pck = divmod(oi - 2, 16)
                        p.wait(tensor, f"{bq0}_qkev_{pck}" if pj == 0
                               else f"{bq0}_kkev_{pck}")
                    elif j > 0:
                        p.wait(tensor, f"c1h{j-1}s{NST-1}_ex_{15}")
                    else:
                        p.wait(tensor, f"c1_vev_{31}")
                    for ct in range(NCT):
                        inst = tensor.matmul(
                            pp[oi % 2].ap()[0:64, 0:256],
                            wvw[:, ct, ck * 64:(ck + 1) * 64],
                            xv[:, ct, :],
                            start=(ct == 0), stop=(ct == NCT - 1))
                    p.inc(inst, "t", 1,
                          f"{bq0}_qkmm_{ck}" if jq == 0 else f"{bq0}_kkmm_{ck}")
        p.block.tensor(tn_qkf)

        def sc_qkf(scalar, j=j):
            bqo, _ = _SV["l1_bq"]
            bko, _ = _SV["l1_bk"]
            for jq, dstF, bo in ((0, qF, bqo), (1, kF, bko)):
                for ck in range(16):
                    p.wait(scalar, f"{bq0}_qkmm_{ck}" if jq == 0 else f"{bq0}_kkmm_{ck}")
                    oi = jq * 16 + ck
                    dst = bass.AP(dstF, ck, [[T, 64], [16, 256]])
                    bias_ap = sv[(ck % 2) * 64:(ck % 2) * 64 + 64,
                                 (bqo if jq == 0 else bko) + ck // 2:
                                 (bqo if jq == 0 else bko) + ck // 2 + 1]
                    p.inc_reserved(
                        scalar.activation(out=dst, in_=pp[oi % 2].ap()[0:64, 0:256],
                                          func=AF.Identity, bias=bias_ap),
                        f"{bq0}_qkev_{ck}" if jq == 0 else f"{bq0}_kkev_{ck}")
            p.ev[f"{bq0}_qkf"] = p.ev[f"{bq0}_kkev_{15}"]
        p.block.scalar(sc_qkf)

        def ldH(sync, j=j):
            if j > 0:
                p.wait(sync, f"c1_Gbuilt_{j-1}")
            if "c0_gst" in p.ev:
                p.wait(sync, "c0_gst")
            p.mark_dma(sync.dma_start(
                out=Hb[:],
                in_=bass.AP(g16d.tensor, j * 8191, [[1, 128], [1, GW]])),
                "dma_c1g", f"c1_H_{j}")
        p.block.sync(ldH)

        def vec_G(vector, j=j):
            p.wait(vector, f"c1_H_{j}")
            if j >= 1:
                p.wait(vector, f"c1_Gfree_{j-1}")
            p.inc(vector.tensor_copy(out=G[:], in_=Hb[:, ::-1]),
                  "v", 1, f"c1_Gbuilt_{j}")
        p.block.vector(vec_G)

        vt3 = vones.ap().rearrange("p (kt s) -> p kt s", kt=32)

        for st_i in range(NST):
            q0 = st_i * 512
            hq = f"c1h{j}s{st_i}"
            NKB = 16
            for kb in range(NKB):
                p.reserve("s", f"{hq}_ex_{kb}")
            p.reserve("v", f"{hq}_rc")
            p.reserve("v", f"{hq}_fin")
            p.reserve("v", f"{hq}_tr")

            def tn_att(tensor, j=j, st_i=st_i, q0=q0, hq=hq, NKB=NKB):
                if st_i == 0:
                    p.wait(tensor, f"c1_Gbuilt_{j}")
                    p.wait(tensor, f"{bq0}_vf")
                    p.wait(tensor, f"{bq0}_qkf")
                if j + st_i > 0:
                    prev = f"c1h{j}s{st_i-1}_fin" if st_i else \
                        f"c1h{j-1}s{NST-1}_fin"
                    p.wait(tensor, prev)

                for kb in range(NKB):
                    if kb >= 2:
                        p.wait(tensor, f"{hq}_ex_{kb-2}")
                    elif st_i == 0 and kb < 2 and j + st_i > 0:
                        pass
                    for kt in (2 * kb, 2 * kb + 1):
                        inst = tensor.matmul(
                            pp[kt % 4][:],
                            kF.ap()[:, kt * 128:(kt + 1) * 128],
                            qF.ap()[:, q0:q0 + 512],
                            start=True, stop=True)
                    p.inc(inst, "t", 1, f"{hq}_lgb_{kb}")
                p.wait(tensor, f"{hq}_ex_{NKB-1}")
                ev = Efull.ap().rearrange("p (kt t) -> p kt t", kt=32)
                for kt in range(32):
                    tensor.matmul(pE[:], vt3[:, kt, 0:65], ev[:, kt, :],
                                  start=(kt == 0), stop=(kt == 31))
                    s = q0 - kt * 128 + 3968
                    inst = tensor.matmul(pB[:], vt3[:, kt, 0:64],
                                         G[:, s:s + 512],
                                         start=(kt == 0), stop=(kt == 31))
                p.inc(inst, "t", 1, f"{hq}_avd")
            p.block.tensor(tn_att)

            def sc_exp(scalar, hq=hq, NKB=NKB):
                ev = Efull.ap().rearrange("p (kt t) -> p kt t", kt=32)
                for kb in range(NKB):
                    p.wait(scalar, f"{hq}_lgb_{kb}")
                    for kt in (2 * kb, 2 * kb + 1):
                        inst = scalar.activation(out=ev[:, kt, :],
                                                 in_=pp[kt % 4][:], func=AF.Exp)
                    p.inc_reserved(inst, f"{hq}_ex_{kb}")
            p.block.scalar(sc_exp)

            def vec_rc(vector, hq=hq):
                p.wait(vector, f"{hq}_avd")
                p.inc_reserved(vector.reciprocal(r32[:], pE[64:65, :]), f"{hq}_rc")
            p.block.vector(vec_rc)

            def tn_rep(tensor, hq=hq):
                p.wait(tensor, f"{hq}_rc")
                p.inc(tensor.matmul(prep2[:], ones64[:], r32[:],
                                    start=True, stop=True),
                      "t", 1, f"{hq}_rp")
            p.block.tensor(tn_rep)

            def vec_att(vector, j=j, st_i=st_i, hq=hq, q0=q0):
                p.wait(vector, f"{hq}_rp")
                k = j * NST + st_i
                if k >= 2:
                    p.wait(vector, f"c1_ast_{k-2}")
                vector.tensor_copy(out=Rr32[:], in_=prep2[:])
                vector.tensor_tensor(out=tE32[:], in0=pE[0:64, :],
                                     in1=Rr32[:], op=ALU.mult)
                si = (st_i % 2) * 512
                p.inc_reserved(
                    vector.tensor_tensor(out=att_s[:, si:si + 512],
                                         in0=tE32[:], in1=pB[:], op=ALU.add),
                    f"{hq}_fin")
                ai = (st_i % 2) * 256
                for br in range(2):
                    for bc in range(16):
                        pbase2 = 32 * (bc % 4)
                        col = ai + (bc // 4) * 64 + br * 32
                        inst = vector.transpose(
                            out=atok.ap()[pbase2:pbase2 + 32, col:col + 32],
                            in_=att_s.ap()[br * 32:(br + 1) * 32,
                                           si + bc * 32:si + bc * 32 + 32])
                p.inc_reserved(inst, f"{hq}_tr")
            p.block.vector(vec_att)

            def st_att(sync, j=j, st_i=st_i, hq=hq, q0=q0):
                p.wait(sync, f"{hq}_tr")
                k = j * NST + st_i
                ai = (st_i % 2) * 256
                p.mark_dma(sync.dma_start(
                    out=bass.AP(attTok.tensor, j * 256 * C + q0 * 64,
                                [[64, 128], [8192, 4], [1, 64]]),
                    in_=bass.AP(atok, ai, [[512, 128], [64, 4], [1, 64]])),
                    "dma_c1s", f"c1_ast_{k}")
            p.block.sync(st_att)

        p.ev[f"c1_Gfree_{j}"] = p.ev[f"c1h{j}s{NST-1}_fin"]
        p.ev[f"{bq0}_avdone"] = p.ev[f"c1h{j}s{NST-1}_avd"]

    def fin(sync):
        p.wait(sync, f"c1_ast_{16*NST-1}")
    p.block.sync(fin)



def emit_C3(p):
    TC = 512
    NCH = T // TC
    attTok = p.dram["att_tok_d"]
    xatt = fm_view(p.dram["xatt_d"])
    abd = fm_view(p.dram["ab_d"])
    ab16d = fm_view(p.dram["ab16_d"])
    sv = p.sv32

    wo16, woev = p.load_weight_cast("c3_wo", p.dram["l1wo_t"], NCT, C, "dma_c3w",
                                    stage_cols=4096, dma_eng="gpsimd")

    atokc = p.sb("c3_atok", [128, 4 * C], BF16)
    ident = p.sb("c3_ident", [128, 128], BF16)
    sa16 = p.sb("c3_sa16", [128, NCT * TC], BF16)
    sx = p.sb("c3_sx", [128, NCT * TC], F32)
    t32 = p.sb("c3_t32", [128, NCT * TC], F32)
    tmp32 = p.sb("c3_tmp", [128, NCT * TC], F32)
    sc16 = p.sb("c3_sc16", [128, NCT * TC], BF16)
    sq16 = p.sb("c3_sq16", [128, NCT * TC], BF16)
    o16 = p.sb("c3_o16", [128, NCT * TC], BF16)
    rowbuf = p.sb("c3_rows", [1, 6 * TC], F32)

    psA = p.ps("c3_psA", [1, TC])
    psB = p.ps("c3_psB", [1, TC])
    prep = p.ps("c3_prep", [128, 2 * TC])
    po = [p.ps(f"c3_po{j}", [128, TC]) for j in range(2)]
    pt16 = [p.ps(f"c3_pt16{j}", [128, 128], BF16) for j in range(2)]

    def r3(buf):
        return buf.ap().rearrange("p (ct t) -> p ct t", ct=NCT)

    def gid(gpsimd):
        gpsimd.memset(ident[:], 0.0)
        import concourse.mybir as _mb
        inst = gpsimd.affine_select(out=ident[:], in_=ident[:],
                                    compare_op=_mb.AluOpType.not_equal,
                                    fill=1.0, base=0, pattern=[[-1, 128]],
                                    channel_multiplier=1)
        p.inc(inst, "g", 1, "c3_id")
    p.block.gpsimd(gid)

    for i in range(NCH):
        tsl = slice(i * TC, (i + 1) * TC)

        def ld(sync, i=i, tsl=tsl):
            if i > 0:
                p.wait(sync, f"c3_tp_{i-1}_{31}")
                p.wait(sync, f"c3_t_{i-1}")
            p.mark_dma(sync.dma_start(
                out=atokc.ap().rearrange("p (tt c) -> p tt c", tt=4),
                in_=attTok[i * TC:(i + 1) * TC, :].rearrange(
                    "(tt p) c -> p tt c", p=128)),
                "dma_c3", None)
            p.mark_dma(sync.dma_start(out=r3(sx), in_=xatt[:, :, tsl]),
                       "dma_c3", f"c3_x_{i}")
        p.block.sync(ld)

        for z in range(32):
            p.reserve("v", f"c3_tev_{i}_{z}")

        def tn_tp(tensor, i=i):
            p.wait(tensor, f"c3_x_{i}")
            if i == 0:
                p.wait(tensor, "c3_id")
            av = atokc.ap().rearrange("p (tt c) -> p tt c", tt=4)
            z = 0
            for tt in range(4):
                for ct in range(NCT):
                    if z >= 2:
                        p.wait(tensor, f"c3_tev_{i}_{z-2}")
                    elif i > 0:
                        p.wait(tensor, f"c3_tev_{i-1}_{30+z}")
                    inst = tensor.transpose(
                        pt16[z % 2][:],
                        av[:, tt, ct * 128:(ct + 1) * 128],
                        ident[:])
                    p.inc(inst, "t", 1, f"c3_tp_{i}_{z}")
                    z += 1
        p.block.tensor(tn_tp)

        def vec_tp(vector, i=i):
            sa = r3(sa16)
            z = 0
            for tt in range(4):
                for ct in range(NCT):
                    p.wait(vector, f"c3_tp_{i}_{z}")
                    p.inc_reserved(
                        vector.tensor_copy(
                            out=sa[:, ct, tt * 128:(tt + 1) * 128],
                            in_=pt16[z % 2][:]),
                        f"c3_tev_{i}_{z}")
                    z += 1
        p.block.vector(vec_tp)

        for ot in range(NCT):
            p.reserve("s", f"c3_ev_{i}_{ot}")

        def tn_mm(tensor, i=i):
            if i == 0:
                p.wait(tensor, woev)
            p.wait(tensor, f"c3_tev_{i}_{31}")
            wv = wo16.ap().rearrange("p (ct o) -> p ct o", ct=NCT)
            av = r3(sa16)
            for ot in range(NCT):
                if ot >= 2:
                    p.wait(tensor, f"c3_ev_{i}_{ot-2}")
                elif i > 0:
                    p.wait(tensor, f"c3_ev_{i-1}_{NCT-2+ot}")
                for ct in range(NCT):
                    inst = tensor.matmul(po[ot % 2][:], wv[:, ct, ot * 128:(ot + 1) * 128],
                                         av[:, ct, :],
                                         start=(ct == 0), stop=(ct == NCT - 1))
                p.inc(inst, "t", 1, f"c3_mm_{i}_{ot}")
        p.block.tensor(tn_mm)

        def sc_ev(scalar, i=i):
            bo, _ = _SV["l1_bo"]
            t3 = r3(t32)
            for ot in range(NCT):
                p.wait(scalar, f"c3_mm_{i}_{ot}")
                p.inc_reserved(scalar.activation(out=t3[:, ot, :], in_=po[ot % 2][:],
                                                 func=AF.Identity,
                                                 bias=sv[:, bo + ot:bo + ot + 1]),
                               f"c3_ev_{i}_{ot}")
        p.block.scalar(sc_ev)

        def vec_t(vector, i=i):
            p.wait(vector, f"c3_ev_{i}_{NCT-1}")
            p.inc(vector.tensor_tensor(out=t32[:], in0=t32[:], in1=sx[:], op=ALU.add),
                  "v", 1, f"c3_t_{i}")
        p.block.vector(vec_t)

        emit_stats(p, f"c3l{i}", t32, sc16, sq16, NCT, TC, psA, psB,
                   rowbuf, prep, [f"c3_t_{i}"], no_eps=True)

        def vec_app(vector, i=i):
            p.wait(vector, f"c3l{i}_prep")
            if i > 0:
                p.wait(vector, f"c3_st_{i-1}")
            Mb = prep.ap()[:, 0:TC].unsqueeze(1).broadcast_to([128, NCT, TC])
            Rb = prep.ap()[:, TC:2 * TC].unsqueeze(1).broadcast_to([128, NCT, TC])
            vector.tensor_tensor(out=r3(tmp32), in0=r3(t32), in1=Mb, op=ALU.subtract)
            vector.tensor_tensor(out=r3(tmp32), in0=r3(tmp32), in1=Rb, op=ALU.mult)
            p.inc(vector.tensor_copy(out=o16[:], in_=tmp32[:]), "v", 1, f"c3_app_{i}")
        p.block.vector(vec_app)

        def st(sync, i=i, tsl=tsl):
            p.wait(sync, f"c3_app_{i}")
            p.mark_dma(sync.dma_start(out=abd[:, :, tsl], in_=r3(tmp32)),
                       "dma_c3s", None)
            p.mark_dma(sync.dma_start(out=ab16d[:, :, tsl], in_=r3(o16)),
                       "dma_c3s", f"c3_st_{i}")
        p.block.sync(st)

    def fin(sync):
        p.wait(sync, f"c3_st_{NCH-1}")
    p.block.sync(fin)



def emit_C4(p):
    TC = 512
    NCH = T // TC
    ab16d = fm_view(p.dram["ab16_d"])
    abd = fm_view(p.dram["ab_d"])
    res2 = fm_view(p.dram["res2_d"])
    res3 = fm_view(p.dram["res3_d"])
    sv = p.sv32

    w116, w1ev = p.load_weight_cast("c4_w1", p.dram["l1w1_t"], NCT, C, "dma_c4w",
                                    stage_cols=4096, dma_eng="gpsimd")
    w216, w2ev = p.load_weight_cast("c4_w2", p.dram["l1w2_t"], NCT, C, "dma_c4w",
                                    stage_cols=4096, dma_eng="gpsimd")

    sa16 = p.sb("c4_sa16", [128, NCT * TC], BF16)
    sab = p.sb("c4_sab", [128, NCT * TC], F32)
    sr2 = p.sb("c4_sr2", [128, NCT * TC], F32)
    f116 = p.sb("c4_f116", [128, NCT * TC], BF16)
    t32 = p.sb("c4_t32", [128, NCT * TC], F32)
    tmp32 = p.sb("c4_tmp", [128, NCT * TC], F32)
    sc16 = p.sb("c4_sc16", [128, NCT * TC], BF16)
    sq16 = p.sb("c4_sq16", [128, NCT * TC], BF16)
    rowbuf = p.sb("c4_rows", [1, 6 * TC], F32)

    psA = p.ps("c4_psA", [1, TC])
    psB = p.ps("c4_psB", [1, TC])
    prep = p.ps("c4_prep", [128, 2 * TC])
    po = [p.ps(f"c4_po{j}", [128, TC]) for j in range(2)]

    def r3(buf):
        return buf.ap().rearrange("p (ct t) -> p ct t", ct=NCT)

    for i in range(NCH):
        tsl = slice(i * TC, (i + 1) * TC)

        def ld(sync, i=i, tsl=tsl):
            if i > 0:
                p.wait(sync, f"c4_f1mm_{i-1}_{NCT-1}")
                p.wait(sync, f"c4_out_{i-1}")
            p.mark_dma(sync.dma_start(out=r3(sa16), in_=ab16d[:, :, tsl]),
                       "dma_c4", None)
            p.mark_dma(sync.dma_start(out=r3(sab), in_=abd[:, :, tsl]),
                       "dma_c4", None)
            p.mark_dma(sync.dma_start(out=r3(sr2), in_=res2[:, :, tsl]),
                       "dma_c4", f"c4_x_{i}")
        p.block.sync(ld)

        for ot in range(NCT):
            p.reserve("s", f"c4_f1ev_{i}_{ot}")

        def tn_f1(tensor, i=i):
            p.wait(tensor, f"c4_x_{i}")
            if i == 0:
                p.wait(tensor, w1ev)
            wv = w116.ap().rearrange("p (ct o) -> p ct o", ct=NCT)
            av = r3(sa16)
            for ot in range(NCT):
                if ot >= 2:
                    p.wait(tensor, f"c4_f1ev_{i}_{ot-2}")
                elif i > 0:
                    p.wait(tensor, f"c4_f1ev_{i-1}_{NCT-2+ot}")
                for ct in range(NCT):
                    inst = tensor.matmul(po[ot % 2][:], wv[:, ct, ot * 128:(ot + 1) * 128],
                                         av[:, ct, :],
                                         start=(ct == 0), stop=(ct == NCT - 1))
                p.inc(inst, "t", 1, f"c4_f1mm_{i}_{ot}")
        p.block.tensor(tn_f1)

        def sc_f1(scalar, i=i):
            b1, _ = _SV["l1_b1"]
            fv = r3(f116)
            for ot in range(NCT):
                p.wait(scalar, f"c4_f1mm_{i}_{ot}")
                p.inc_reserved(scalar.activation(out=fv[:, ot, :], in_=po[ot % 2][:],
                                                 func=AF.Identity,
                                                 bias=sv[:, b1 + ot:b1 + ot + 1]),
                               f"c4_f1ev_{i}_{ot}")
        p.block.scalar(sc_f1)

        for ot in range(NCT):
            p.reserve("v", f"c4_f2ev_{i}_{ot}")

        def tn_f2(tensor, i=i):
            if i == 0:
                p.wait(tensor, w2ev)
            p.wait(tensor, f"c4_f1ev_{i}_{NCT-1}")
            wv = w216.ap().rearrange("p (ct o) -> p ct o", ct=NCT)
            fv = r3(f116)
            for ot in range(NCT):
                if ot >= 2:
                    p.wait(tensor, f"c4_f2ev_{i}_{ot-2}")
                elif i > 0:
                    p.wait(tensor, f"c4_f2ev_{i-1}_{NCT-2+ot}")
                for ct in range(NCT):
                    inst = tensor.matmul(po[ot % 2][:], wv[:, ct, ot * 128:(ot + 1) * 128],
                                         fv[:, ct, :],
                                         start=(ct == 0), stop=(ct == NCT - 1))
                p.inc(inst, "t", 1, f"c4_f2mm_{i}_{ot}")
        p.block.tensor(tn_f2)

        def vec_f2(vector, i=i):
            b2, _ = _SV["l1_b2"]
            t3 = r3(t32)
            ab3 = r3(sab)
            for ot in range(NCT):
                p.wait(vector, f"c4_f2mm_{i}_{ot}")
                p.inc_reserved(
                    vector.scalar_tensor_tensor(out=t3[:, ot, :], in0=po[ot % 2][:],
                                                scalar=sv[:, b2 + ot:b2 + ot + 1],
                                                in1=ab3[:, ot, :],
                                                op0=ALU.add, op1=ALU.add),
                    f"c4_f2ev_{i}_{ot}")
            p.ev[f"c4_t_{i}"] = p.ev[f"c4_f2ev_{i}_{NCT-1}"]
        p.block.vector(vec_f2)

        emit_stats(p, f"c4l{i}", t32, sc16, sq16, NCT, TC, psA, psB,
                   rowbuf, prep, [f"c4_t_{i}"], no_eps=True)

        def vec_app(vector, i=i):
            p.wait(vector, f"c4l{i}_prep")
            if i > 0:
                p.wait(vector, f"c4_st_{i-1}")
            Mb = prep.ap()[:, 0:TC].unsqueeze(1).broadcast_to([128, NCT, TC])
            Rb = prep.ap()[:, TC:2 * TC].unsqueeze(1).broadcast_to([128, NCT, TC])
            vector.tensor_tensor(out=r3(tmp32), in0=r3(t32), in1=Mb, op=ALU.subtract)
            vector.tensor_tensor(out=r3(tmp32), in0=r3(tmp32), in1=Rb, op=ALU.mult)
            p.inc(vector.tensor_tensor(out=tmp32[:], in0=tmp32[:], in1=sr2[:],
                                       op=ALU.add),
                  "v", 1, f"c4_out_{i}")
        p.block.vector(vec_app)

        def st(sync, i=i, tsl=tsl):
            p.wait(sync, f"c4_out_{i}")
            p.mark_dma(sync.dma_start(out=res3[:, :, tsl], in_=r3(tmp32)),
                       "dma_c4s", f"c4_st_{i}")
        p.block.sync(st)

    def fin(sync):
        p.wait(sync, f"c4_st_{NCH-1}")
    p.block.sync(fin)



DV = {"om_tmk": 0, "om_tmv": 8, "om_tmr": 16, "om_cmk": 24, "om_cmr": 32,
      "lam": 40, "eu": 48, "scr": 56}


def emit_derived(p):
    if getattr(p, "derived_done", False):
        return
    p.derived_done = True
    sv = p.sv32
    O = SVW

    def vec(vector):
        p.wait(vector, "pre_ready")
        for nm, src in [("om_tmk", "tmk"), ("om_tmv", "tmv"), ("om_tmr", "tmr"),
                        ("om_cmk", "cmk"), ("om_cmr", "cmr")]:
            o1, _ = _SV[src]
            inst = vector.tensor_scalar(out=sv[:, O + DV[nm]:O + DV[nm] + 8],
                                        in0=sv[:, o1:o1 + 8], scalar1=-1.0,
                                        scalar2=1.0, op0=ALU.mult, op1=ALU.add)
        p.inc(inst, "v", 1, "derived_v")
    p.block.vector(vec)

    def sc(scalar):
        p.wait(scalar, "pre_ready")
        td, _ = _SV["tdecay"]
        tf, _ = _SV["tfirst"]
        scalar.activation(out=sv[:, O + DV["scr"]:O + DV["scr"] + 8],
                          in_=sv[:, td:td + 8], func=AF.Exp)
        scalar.activation(out=sv[:, O + DV["lam"]:O + DV["lam"] + 8],
                          in_=sv[:, O + DV["scr"]:O + DV["scr"] + 8],
                          func=AF.Exp, scale=-1.0)
        inst = scalar.activation(out=sv[:, O + DV["eu"]:O + DV["eu"] + 8],
                                 in_=sv[:, tf:tf + 8], func=AF.Exp)
        p.inc(inst, "s", 1, "derived_s")
    p.block.scalar(sc)



def emit_A1(p):
    TC = 256
    NCH = T // TC
    xfm = fm_view(p.dram["x_fm"])
    x2v = fm_view(p.dram["x2_d"])
    xsv = fm_view(p.dram["xsum_d"])
    emit_derived(p)
    sv = p.sv32
    O = SVW

    wk16, wkev = p.load_weight_cast("a1_wk", p.dram["wk_t"], NCT, C, "dma_a1w",
                                    stage_cols=4096, dma_eng="gpsimd")
    wv16, wvev = p.load_weight_cast("a1_wv", p.dram["wv_t"], NCT, C, "dma_a1w",
                                    stage_cols=4096, dma_eng="gpsimd")
    wr16, wrev = p.load_weight_cast("a1_wr", p.dram["wr_t"], NCT, C, "dma_a1w",
                                    stage_cols=4096, dma_eng="gpsimd")
    wo16, woev = p.load_weight_cast("a1_wo", p.dram["wo_t"], NCT, C, "dma_a1w",
                                    stage_cols=4096, dma_eng="gpsimd")

    sx = p.sb("a1_sx", [128, NCT * TC], F32)
    tmp32 = p.sb("a1_tmp", [128, NCT * TC], F32)
    sc16 = p.sb("a1_sc16", [128, NCT * TC], BF16)
    sq16 = p.sb("a1_sq16", [128, NCT * TC], BF16)
    x1 = p.sb("a1_x1", [128, NCT * TC], F32)
    hpad = p.sb("a1_hpad", [128, NCT * (TC + 1)], F32)
    mixk = p.sb("a1_mixk", [128, NCT * TC], BF16)
    mixv = p.sb("a1_mixv", [128, NCT * TC], BF16)
    mixr = p.sb("a1_mixr", [128, NCT * TC], BF16)
    ek = p.sb("a1_ek", [128, NCT * TC], F32)
    v32 = p.sb("a1_v32", [128, NCT * TC], F32)
    sr = p.sb("a1_sr", [128, NCT * TC], F32)
    out32 = p.sb("a1_out", [128, NCT * TC], F32)
    Apad = p.sb("a1_Apad", [128, NCT * (TC + 1)], F32)
    Bpad = p.sb("a1_Bpad", [128, NCT * (TC + 1)], F32)
    rowbuf = p.sb("a1_rows", [1, 6 * TC], F32)

    psA = p.ps("a1_psA", [1, TC])
    psB = p.ps("a1_psB", [1, TC])
    prep = p.ps("a1_prep", [128, 2 * TC])
    pkv = p.ps("a1_pkv", [128, NCT * TC])

    hp = hpad.ap().rearrange("p (ct t) -> p ct t", ct=NCT)
    Av = Apad.ap().rearrange("p (ct t) -> p ct t", ct=NCT)
    Bv = Bpad.ap().rearrange("p (ct t) -> p ct t", ct=NCT)

    def init(vector):
        vector.memset(hpad[:], 0.0)
        vector.memset(Apad[:], 0.0)
        p.inc(vector.memset(Bpad[:], 0.0), "v", 1, "a1_init")
    p.block.vector(init)

    def r3(buf):
        return buf.ap().rearrange("p (ct t) -> p ct t", ct=NCT)

    for i in range(NCH):
        tsl = slice(i * TC, (i + 1) * TC)

        def ld(sync, i=i, tsl=tsl):
            if i > 0:
                p.wait(sync, f"a1_xs_{i-1}")
            p.mark_dma(sync.dma_start(out=r3(sx), in_=xfm[:, :, tsl]),
                       "dma_a1x", f"a1_x_{i}")
        p.block.sync(ld)

        emit_stats(p, f"a1o{i}", sx, sc16, sq16, NCT, TC, psA, psB,
                   rowbuf, prep, [f"a1_x_{i}"] + (["a1_init", "derived_v", "derived_s"]
                                                  if i == 0 else []),
                   double=True)

        def vec_x1(vector, i=i):
            p.wait(vector, f"a1o{i}_prep")
            Mb = prep.ap()[:, 0:TC].unsqueeze(1).broadcast_to([128, NCT, TC])
            Rb = prep.ap()[:, TC:2 * TC].unsqueeze(1).broadcast_to([128, NCT, TC])
            vector.tensor_tensor(out=r3(tmp32), in0=r3(sx), in1=Mb, op=ALU.subtract)
            vector.tensor_tensor(out=r3(tmp32), in0=r3(tmp32), in1=Rb, op=ALU.mult)
            g0, _ = _SV["ln0_w"]
            b0, _ = _SV["ln0_b"]
            x1v = r3(x1)
            t3 = r3(tmp32)
            for ct in range(NCT):
                inst = vector.tensor_scalar(out=x1v[:, ct, :], in0=t3[:, ct, :],
                                            scalar1=sv[:, g0 + ct:g0 + ct + 1],
                                            scalar2=sv[:, b0 + ct:b0 + ct + 1],
                                            op0=ALU.mult, op1=ALU.add)
            p.inc(inst, "v", 1, f"a1_x1_{i}")
        p.block.vector(vec_x1)

        emit_stats(p, f"a1l{i}", x1, sc16, sq16, NCT, TC, psA, psB,
                   rowbuf, prep, [f"a1_x1_{i}"])

        def vec_h(vector, i=i):
            p.wait(vector, f"a1l{i}_prep")
            Mb = prep.ap()[:, 0:TC].unsqueeze(1).broadcast_to([128, NCT, TC])
            Rb = prep.ap()[:, TC:2 * TC].unsqueeze(1).broadcast_to([128, NCT, TC])
            vector.tensor_tensor(out=r3(tmp32), in0=r3(x1), in1=Mb, op=ALU.subtract)
            vector.tensor_tensor(out=r3(tmp32), in0=r3(tmp32), in1=Rb, op=ALU.mult)
            g1, _ = _SV["ln1_w"]
            b1, _ = _SV["ln1_b"]
            t3 = r3(tmp32)
            for ct in range(NCT):
                vector.tensor_scalar(out=hp[:, ct, 1:TC + 1], in0=t3[:, ct, :],
                                     scalar1=sv[:, g1 + ct:g1 + ct + 1],
                                     scalar2=sv[:, b1 + ct:b1 + ct + 1],
                                     op0=ALU.mult, op1=ALU.add)
            for mbuf, tm, om in [(mixk, "tmk", "om_tmk"), (mixv, "tmv", "om_tmv"),
                                 (mixr, "tmr", "om_tmr")]:
                to, _ = _SV[tm]
                oo = O + DV[om]
                mv = r3(mbuf)
                for ct in range(NCT):
                    vector.tensor_scalar(out=t3[:, ct, :], in0=hp[:, ct, 0:TC],
                                         scalar1=sv[:, oo + ct:oo + ct + 1],
                                         scalar2=None, op0=ALU.mult)
                    vector.scalar_tensor_tensor(out=mv[:, ct, :], in0=hp[:, ct, 1:TC + 1],
                                                scalar=sv[:, to + ct:to + ct + 1],
                                                in1=t3[:, ct, :],
                                                op0=ALU.mult, op1=ALU.add)
            inst = vector.tensor_copy(out=hp[:, :, 0:1], in_=hp[:, :, TC:TC + 1])
            p.inc(inst, "v", 1, f"a1_mix_{i}")
        p.block.vector(vec_h)

        p.reserve("s", f"a1_ek_{i}")
        p.reserve("s", f"a1_v_{i}")
        p.reserve("s", f"a1_sr_{i}")

        def proj(tensor, lhsT16, rhs_buf, wait_evs, ev, i=i):
            for e in wait_evs:
                p.wait(tensor, e)
            wv = lhsT16.ap().rearrange("p (ct o) -> p ct o", ct=NCT)
            rv = r3(rhs_buf)
            pv = pkv.ap().rearrange("p (ot t) -> p ot t", ot=NCT)
            for ot in range(NCT):
                for ct in range(NCT):
                    inst = tensor.matmul(pv[:, ot, :], wv[:, ct, ot * 128:(ot + 1) * 128],
                                         rv[:, ct, :],
                                         start=(ct == 0), stop=(ct == NCT - 1))
            p.inc(inst, "t", 1, ev)

        def tn_k(tensor, i=i):
            evs = [f"a1_mix_{i}"]
            if i == 0:
                evs.append(wkev)
            else:
                evs.append(f"a1_x2_{i-1}")
            proj(tensor, wk16, mixk, evs, f"a1_kmm_{i}", i)
        p.block.tensor(tn_k)

        def sc_k(scalar, i=i):
            p.wait(scalar, f"a1_kmm_{i}")
            p.inc_reserved(scalar.activation(out=ek[:], in_=pkv[:], func=AF.Exp),
                           f"a1_ek_{i}")
        p.block.scalar(sc_k)

        def tn_v(tensor, i=i):
            evs = [f"a1_ek_{i}"] + ([wvev] if i == 0 else [])
            proj(tensor, wv16, mixv, evs, f"a1_vmm_{i}", i)
        p.block.tensor(tn_v)

        def sc_v(scalar, i=i):
            p.wait(scalar, f"a1_vmm_{i}")
            p.inc_reserved(scalar.copy(out=v32[:], in_=pkv[:]), f"a1_v_{i}")
        p.block.scalar(sc_v)

        def tn_r(tensor, i=i):
            evs = [f"a1_v_{i}"] + ([wrev] if i == 0 else [])
            proj(tensor, wr16, mixr, evs, f"a1_rmm_{i}", i)
        p.block.tensor(tn_r)

        def sc_r(scalar, i=i):
            p.wait(scalar, f"a1_rmm_{i}")
            p.inc_reserved(scalar.activation(out=sr[:], in_=pkv[:], func=AF.Sigmoid),
                           f"a1_sr_{i}")
        p.block.scalar(sc_r)

        def vec_wkv(vector, i=i):
            p.wait(vector, f"a1_ek_{i}")
            p.wait(vector, f"a1_v_{i}")
            if i > 0:
                p.wait(vector, f"a1_st_{i-1}")
            vector.tensor_tensor(out=out32[:], in0=ek[:], in1=v32[:], op=ALU.mult)
            lam = O + DV["lam"]
            eu = O + DV["eu"]
            o3 = r3(out32)
            e3 = r3(ek)
            for ct in range(NCT):
                vector.tensor_tensor_scan(
                    out=Av[:, ct, 1:TC + 1],
                    data0=sv[:, lam + ct:lam + ct + 1].broadcast_to([128, TC]),
                    data1=o3[:, ct, :], initial=Av[:, ct, 0:1],
                    op0=ALU.mult, op1=ALU.add)
            for ct in range(NCT):
                vector.tensor_tensor_scan(
                    out=Bv[:, ct, 1:TC + 1],
                    data0=sv[:, lam + ct:lam + ct + 1].broadcast_to([128, TC]),
                    data1=e3[:, ct, :], initial=Bv[:, ct, 0:1],
                    op0=ALU.mult, op1=ALU.add)
            n3 = r3(v32)
            for ct in range(NCT):
                vector.scalar_tensor_tensor(out=n3[:, ct, :], in0=o3[:, ct, :],
                                            scalar=sv[:, eu + ct:eu + ct + 1],
                                            in1=Av[:, ct, 0:TC],
                                            op0=ALU.mult, op1=ALU.add)
            t3 = r3(tmp32)
            for ct in range(NCT):
                inst = vector.scalar_tensor_tensor(out=t3[:, ct, :], in0=e3[:, ct, :],
                                                   scalar=sv[:, eu + ct:eu + ct + 1],
                                                   in1=Bv[:, ct, 0:TC],
                                                   op0=ALU.mult, op1=ALU.add)
            p.inc(inst, "v", 1, f"a1_den_{i}")
        p.block.vector(vec_wkv)

        def sc_rden(scalar, i=i):
            p.wait(scalar, f"a1_den_{i}")
            scalar.activation(out=tmp32[:], in_=tmp32[:], func=AF.Ln)
            p.inc(scalar.activation(out=tmp32[:], in_=tmp32[:], func=AF.Exp,
                                    scale=-1.0),
                  "s", 1, f"a1_rden_{i}")
        p.block.scalar(sc_rden)

        def vec_y(vector, i=i):
            p.wait(vector, f"a1_rden_{i}")
            p.wait(vector, f"a1_sr_{i}")
            vector.tensor_tensor(out=ek[:], in0=v32[:], in1=tmp32[:], op=ALU.mult)
            vector.tensor_tensor(out=mixk[:], in0=sr[:], in1=ek[:], op=ALU.mult)
            vector.tensor_copy(out=Av[:, :, 0:1], in_=Av[:, :, TC:TC + 1])
            p.inc(vector.tensor_copy(out=Bv[:, :, 0:1], in_=Bv[:, :, TC:TC + 1]),
                  "v", 1, f"a1_satt_{i}")
        p.block.vector(vec_y)

        def tn_wo(tensor, i=i):
            evs = [f"a1_satt_{i}", f"a1_sr_{i}"] + ([woev] if i == 0 else [])
            proj(tensor, wo16, mixk, evs, f"a1_womm_{i}", i)
        p.block.tensor(tn_wo)

        def vec_x2(vector, i=i):
            p.wait(vector, f"a1_womm_{i}")
            vector.tensor_tensor(out=out32[:], in0=pkv[:], in1=x1[:], op=ALU.add)
            p.inc(vector.tensor_tensor(out=ek[:], in0=out32[:], in1=sx[:], op=ALU.add),
                  "v", 1, f"a1_xs_{i}")
            p.ev[f"a1_x2_{i}"] = p.ev[f"a1_xs_{i}"]
        p.block.vector(vec_x2)

        def st(sync, i=i, tsl=tsl):
            p.wait(sync, f"a1_xs_{i}")
            p.mark_dma(sync.dma_start(out=x2v[:, :, tsl], in_=r3(out32)),
                       "dma_a1s", None)
            p.mark_dma(sync.dma_start(out=xsv[:, :, tsl], in_=r3(ek)),
                       "dma_a1s", f"a1_st_{i}")
        p.block.sync(st)

    def fin(sync):
        p.wait(sync, f"a1_st_{NCH-1}")
    p.block.sync(fin)



def emit_A2(p):
    TC = 512
    NCH = T // TC
    NQ = 4
    QH = NHT // NQ
    x2v = fm_view(p.dram["x2_d"])
    xsv = fm_view(p.dram["xsum_d"])
    res1 = fm_view(p.dram["res1_d"])
    wfk16d = p.dram["wfk16_d"]
    wfv16d = p.dram["wfv16_d"]
    emit_derived(p)
    sv = p.sv32
    O = SVW

    wfr16, wfrev = p.load_weight_cast("a2_wfr", p.dram["wfr_t"], NCT, C, "dma_a2w",
                                      stage_cols=4096, dma_eng="gpsimd")

    x2c = p.sb("a2_x2c", [128, NCT * TC], F32)
    tmp32 = p.sb("a2_tmp", [128, NCT * TC], F32)
    sc16 = p.sb("a2_sc16", [128, NCT * TC], BF16)
    sq16 = p.sb("a2_sq16", [128, NCT * TC], BF16)
    h2pad = p.sb("a2_h2pad", [128, NCT * (TC + 1)], F32)
    mixk2 = p.sb("a2_mixk2", [128, NCT * TC], BF16)
    mixr2 = p.sb("a2_mixr2", [128, NCT * TC], BF16)
    kf16 = p.sb("a2_kf16", [128, QH * TC], BF16)
    kv32 = p.sb("a2_kv32", [128, NCT * TC], F32)
    xs = p.sb("a2_xs", [128, NCT * TC], F32)
    wfkq = p.sb("a2_wfkq", [128, NCT * 1024], BF16)
    wfvq = p.sb("a2_wfvq", [128, QH * 1024], BF16)
    rowbuf = p.sb("a2_rows", [1, 6 * TC], F32)

    psA = p.ps("a2_psA", [1, TC])
    psB = p.ps("a2_psB", [1, TC])
    prep = p.ps("a2_prep", [128, 2 * TC])
    pkf = [p.ps(f"a2_pkf{j}", [128, TC]) for j in range(2)]
    pkv = [p.ps(f"a2_pkv{j}", [128, TC]) for j in range(2)]

    hp = h2pad.ap().rearrange("p (ct t) -> p ct t", ct=NCT)

    def r3(buf, n=NCT):
        return buf.ap().rearrange("p (ct t) -> p ct t", ct=n)

    def init(vector):
        p.inc(vector.memset(h2pad[:], 0.0), "v", 1, "a2_init")
    p.block.vector(init)

    def emit_qload(i, q):
        if i == 0:
            _, ev1 = p.load_weight_cast(f"a2fk{q}", p.dram["wfk_t"][:, q * 1024:(q + 1) * 1024],
                                        NCT, 1024, "dma_a2w", stage_cols=4096,
                                        dma_eng="gpsimd", w16=wfkq,
                                        halt_event=f"a2_kfmm_0_{q-1}" if q else None)
            _, ev2 = p.load_weight_cast(f"a2fv{q}", p.dram["wfv_t"][q * 1024:(q + 1) * 1024, :],
                                        QH, C, "dma_a2w", stage_cols=4096,
                                        dma_eng="gpsimd", w16=wfvq,
                                        halt_event=f"a2_kvmm_0_{q-1}" if q else None)
            p.ev[f"a2_fkld_{i}_{q}"] = p.ev[ev1]
            p.ev[f"a2_fvld_{i}_{q}"] = p.ev[ev2]

            def cst(gpsimd, q=q):
                p.wait(gpsimd, ev1)
                p.mark_dma(gpsimd.dma_start(out=wfk16d[:, q * 1024:(q + 1) * 1024]
                                            .rearrange("(ct p) o -> p ct o", p=128),
                                            in_=r3(wfkq)),
                           "dma_a2c", f"a2_fkcst_{q}")
                p.wait(gpsimd, ev2)
                p.mark_dma(gpsimd.dma_start(out=wfv16d[q * 1024:(q + 1) * 1024, :]
                                            .rearrange("(ct p) o -> p ct o", p=128),
                                            in_=r3(wfvq, QH)),
                           "dma_a2c", f"a2_fvcst_{q}")
            p.block.gpsimd(cst)
        else:
            def ldq(gpsimd, i=i, q=q):
                p.wait(gpsimd, f"a2_fkcst_{q}")
                p.wait(gpsimd, f"a2_fvcst_{q}")
                prev = (i, q - 1) if q > 0 else (i - 1, NQ - 1)
                p.wait(gpsimd, f"a2_kfmm_{prev[0]}_{prev[1]}")
                p.mark_dma(gpsimd.dma_start(
                    out=r3(wfkq),
                    in_=wfk16d[:, q * 1024:(q + 1) * 1024].rearrange(
                        "(ct p) o -> p ct o", p=128)),
                    "dma_a2w", f"a2_fkld_{i}_{q}")
                p.wait(gpsimd, f"a2_kvmm_{prev[0]}_{prev[1]}")
                p.mark_dma(gpsimd.dma_start(
                    out=r3(wfvq, QH),
                    in_=wfv16d[q * 1024:(q + 1) * 1024, :].rearrange(
                        "(ct p) o -> p ct o", p=128)),
                    "dma_a2w", f"a2_fvld_{i}_{q}")
            p.block.gpsimd(ldq)

    for i in range(NCH):
        tsl = slice(i * TC, (i + 1) * TC)

        def ld(sync, i=i, tsl=tsl):
            if i > 0:
                p.wait(sync, f"a2_mix_{i-1}")
                p.wait(sync, f"a2_st_{i-1}")
            p.mark_dma(sync.dma_start(out=r3(x2c), in_=x2v[:, :, tsl]),
                       "dma_a2x", f"a2_x2_{i}")
            if i > 0:
                p.wait(sync, f"a2_out_{i-1}")
            p.mark_dma(sync.dma_start(out=r3(xs), in_=xsv[:, :, tsl]),
                       "dma_a2x", f"a2_xs_{i}")
        p.block.sync(ld)

        emit_stats(p, f"a2l{i}", x2c, sc16, sq16, NCT, TC, psA, psB,
                   rowbuf, prep, [f"a2_x2_{i}"] + (["a2_init", "derived_v",
                                                    "derived_s"] if i == 0 else []))

        def vec_h(vector, i=i):
            p.wait(vector, f"a2l{i}_prep")
            Mb = prep.ap()[:, 0:TC].unsqueeze(1).broadcast_to([128, NCT, TC])
            Rb = prep.ap()[:, TC:2 * TC].unsqueeze(1).broadcast_to([128, NCT, TC])
            vector.tensor_tensor(out=r3(tmp32), in0=r3(x2c), in1=Mb, op=ALU.subtract)
            vector.tensor_tensor(out=r3(tmp32), in0=r3(tmp32), in1=Rb, op=ALU.mult)
            g2, _ = _SV["ln2_w"]
            b2, _ = _SV["ln2_b"]
            t3 = r3(tmp32)
            for ct in range(NCT):
                vector.tensor_scalar(out=hp[:, ct, 1:TC + 1], in0=t3[:, ct, :],
                                     scalar1=sv[:, g2 + ct:g2 + ct + 1],
                                     scalar2=sv[:, b2 + ct:b2 + ct + 1],
                                     op0=ALU.mult, op1=ALU.add)
            for mbuf, tm, om in [(mixk2, "cmk", "om_cmk"), (mixr2, "cmr", "om_cmr")]:
                to, _ = _SV[tm]
                oo = O + DV[om]
                mv = r3(mbuf)
                for ct in range(NCT):
                    vector.tensor_scalar(out=t3[:, ct, :], in0=hp[:, ct, 0:TC],
                                         scalar1=sv[:, oo + ct:oo + ct + 1],
                                         scalar2=None, op0=ALU.mult)
                    vector.scalar_tensor_tensor(out=mv[:, ct, :], in0=hp[:, ct, 1:TC + 1],
                                                scalar=sv[:, to + ct:to + ct + 1],
                                                in1=t3[:, ct, :],
                                                op0=ALU.mult, op1=ALU.add)
            inst = vector.tensor_copy(out=hp[:, :, 0:1], in_=hp[:, :, TC:TC + 1])
            p.inc(inst, "v", 1, f"a2_mix_{i}")
        p.block.vector(vec_h)

        for q in range(NQ):
            emit_qload(i, q)
            for ht in range(QH):
                p.reserve("s", f"a2_kfev_{i}_{q}_{ht}")

            def tn_kf(tensor, i=i, q=q):
                p.wait(tensor, f"a2_mix_{i}")
                p.wait(tensor, f"a2_fkld_{i}_{q}")
                wv = r3(wfkq)
                mv = r3(mixk2)
                for ht in range(QH):
                    if ht >= 2:
                        p.wait(tensor, f"a2_kfev_{i}_{q}_{ht-2}")
                    elif (i, q) != (0, 0):
                        pi, pq = (i, q - 1) if q else (i - 1, NQ - 1)
                        p.wait(tensor, f"a2_kfev_{pi}_{pq}_{QH-2+ht}")
                    for ct in range(NCT):
                        inst = tensor.matmul(pkf[ht % 2][:],
                                             wv[:, ct, ht * 128:(ht + 1) * 128],
                                             mv[:, ct, :],
                                             start=(ct == 0), stop=(ct == NCT - 1))
                    p.inc(inst, "t", 1, f"a2_kfg_{i}_{q}_{ht}")
                p.ev[f"a2_kfmm_{i}_{q}"] = p.ev[f"a2_kfg_{i}_{q}_{QH-1}"]
            p.block.tensor(tn_kf)

            def sc_kf(scalar, i=i, q=q):
                kv = r3(kf16, QH)
                for ht in range(QH):
                    p.wait(scalar, f"a2_kfg_{i}_{q}_{ht}")
                    scalar.activation(out=kv[:, ht, :], in_=pkf[ht % 2][:],
                                      func=AF.Relu)
                    p.inc_reserved(scalar.activation(out=kv[:, ht, :], in_=kv[:, ht, :],
                                                     func=AF.Square),
                                   f"a2_kfev_{i}_{q}_{ht}")
            p.block.scalar(sc_kf)

            for ct in range(NCT):
                p.reserve("v", f"a2_kvev_{i}_{q}_{ct}")

            def tn_kv(tensor, i=i, q=q):
                p.wait(tensor, f"a2_kfev_{i}_{q}_{QH-1}")
                p.wait(tensor, f"a2_fvld_{i}_{q}")
                wv = r3(wfvq, QH)
                kv = r3(kf16, QH)
                for ct in range(NCT):
                    if ct >= 2:
                        p.wait(tensor, f"a2_kvev_{i}_{q}_{ct-2}")
                    elif (i, q) != (0, 0):
                        pi, pq = (i, q - 1) if q else (i - 1, NQ - 1)
                        p.wait(tensor, f"a2_kvev_{pi}_{pq}_{NCT-2+ct}")
                    for ht in range(QH):
                        inst = tensor.matmul(pkv[ct % 2][:],
                                             wv[:, ht, ct * 128:(ct + 1) * 128],
                                             kv[:, ht, :],
                                             start=(ht == 0), stop=(ht == QH - 1))
                    p.inc(inst, "t", 1, f"a2_kvg_{i}_{q}_{ct}")
                p.ev[f"a2_kvmm_{i}_{q}"] = p.ev[f"a2_kvg_{i}_{q}_{NCT-1}"]
            p.block.tensor(tn_kv)

            def vec_kv(vector, i=i, q=q):
                k3 = r3(kv32)
                for ct in range(NCT):
                    p.wait(vector, f"a2_kvg_{i}_{q}_{ct}")
                    if q == 0:
                        inst = vector.tensor_copy(out=k3[:, ct, :], in_=pkv[ct % 2][:])
                    else:
                        inst = vector.tensor_tensor(out=k3[:, ct, :], in0=pkv[ct % 2][:],
                                                    in1=k3[:, ct, :], op=ALU.add)
                    p.inc_reserved(inst, f"a2_kvev_{i}_{q}_{ct}")
            p.block.vector(vec_kv)

        for ot in range(NCT):
            p.reserve("s", f"a2_srev_{i}_{ot}")

        def tn_r(tensor, i=i):
            p.wait(tensor, f"a2_mix_{i}")
            if i == 0:
                p.wait(tensor, wfrev)
            wv = r3(wfr16)
            mv = r3(mixr2)
            for ot in range(NCT):
                if ot >= 2:
                    p.wait(tensor, f"a2_srev_{i}_{ot-2}")
                else:
                    p.wait(tensor, f"a2_kfev_{i}_{NQ-1}_{QH-1}")
                for ct in range(NCT):
                    inst = tensor.matmul(pkf[ot % 2][:], wv[:, ct, ot * 128:(ot + 1) * 128],
                                         mv[:, ct, :],
                                         start=(ct == 0), stop=(ct == NCT - 1))
                p.inc(inst, "t", 1, f"a2_rmm_{i}_{ot}")
        p.block.tensor(tn_r)

        def sc_sr(scalar, i=i):
            t3 = r3(tmp32)
            for ot in range(NCT):
                p.wait(scalar, f"a2_rmm_{i}_{ot}")
                p.inc_reserved(scalar.activation(out=t3[:, ot, :], in_=pkf[ot % 2][:],
                                                 func=AF.Sigmoid),
                               f"a2_srev_{i}_{ot}")
        p.block.scalar(sc_sr)

        def vec_out(vector, i=i):
            p.wait(vector, f"a2_srev_{i}_{NCT-1}")
            p.wait(vector, f"a2_kvev_{i}_{NQ-1}_{NCT-1}")
            p.wait(vector, f"a2_xs_{i}")
            vector.tensor_tensor(out=kv32[:], in0=kv32[:], in1=tmp32[:], op=ALU.mult)
            p.inc(vector.tensor_tensor(out=x2c[:], in0=kv32[:], in1=xs[:], op=ALU.add),
                  "v", 1, f"a2_out_{i}")
        p.block.vector(vec_out)

        def st(sync, i=i, tsl=tsl):
            p.wait(sync, f"a2_out_{i}")
            p.mark_dma(sync.dma_start(out=res1[:, :, tsl], in_=r3(x2c)),
                       "dma_a2s", f"a2_st_{i}")
        p.block.sync(st)

    def fin(sync):
        p.wait(sync, f"a2_st_{NCH-1}")
    p.block.sync(fin)



def emit_B(p):
    nc = p.nc
    TC = 256
    NCH = T // TC
    res1 = fm_view(p.dram["res1_d"])
    res2 = fm_view(p.dram["res2_d"])

    w116, w1ev = p.load_weight_cast("b_w1", p.dram["l0f_w1"], NCT, H, "dma_b",
                                    stage_cols=4096)
    w216, w2ev = p.load_weight_cast("b_w2", p.dram["l0f_w2"], NHT, C, "dma_b",
                                    stage_cols=4096)

    sx = p.sb("b_sx", [128, NCT * TC], F32)
    tmp32 = p.sb("b_tmp", [128, NCT * TC], F32)
    sc16 = p.sb("b_sc16", [128, NCT * TC], BF16)
    sq16 = p.sb("b_sq16", [128, NCT * TC], BF16)
    h16 = p.sb("b_h16", [128, NHT * TC], BF16)
    rowbuf = p.sb("b_rows", [1, 6 * TC], F32)
    out32 = p.sb("b_out", [128, NCT * TC], F32)

    psA = p.ps("b_psA", [1, TC])
    psB = p.ps("b_psB", [1, TC])
    prep = p.ps("b_prep", [128, 2 * TC])
    ph = [p.ps(f"b_ph{i}", [128, TC]) for i in range(2)]
    po = [p.ps(f"b_po{i}", [128, TC]) for i in range(2)]

    sv16 = p.sv16
    sv = p.sv32

    for i in range(NCH):
        tsl = slice(i * TC, (i + 1) * TC)

        def ld(sync, i=i, tsl=tsl):
            if i >= 1:
                p.wait(sync, f"b_res_done_{i-1}")
            p.mark_dma(sync.dma_start(out=sx.ap().rearrange("p (ct t) -> p ct t", ct=NCT),
                                      in_=res1[:, :, tsl]),
                       "dma_b", f"b_x_{i}")
        p.block.sync(ld)

        if i == 0:
            pass

        prep_ev = emit_stats(p, f"bln{i}", sx, sc16, sq16, NCT, TC,
                             psA, psB, rowbuf, prep, [f"b_x_{i}"])

        def vec_ln(vector, i=i):
            p.wait(vector, f"bln{i}_prep")
            Mb = prep.ap()[:, 0:TC].unsqueeze(1).broadcast_to([128, NCT, TC])
            Rb = prep.ap()[:, TC:2 * TC].unsqueeze(1).broadcast_to([128, NCT, TC])
            xv = sx.ap().rearrange("p (ct t) -> p ct t", ct=NCT)
            tv = tmp32.ap().rearrange("p (ct t) -> p ct t", ct=NCT)
            ov = sq16.ap().rearrange("p (ct t) -> p ct t", ct=NCT)
            vector.tensor_tensor(out=tv, in0=xv, in1=Mb, op=ALU.subtract)
            p.inc(vector.tensor_tensor(out=ov, in0=tv, in1=Rb, op=ALU.mult),
                  "v", 1, f"b_xln_{i}")
        p.block.vector(vec_ln)

        for ht in range(NHT):
            p.reserve("s", f"b_hev_{i}_{ht}")
        for ct in range(NCT):
            p.reserve("v", f"b_oev_{i}_{ct}")
        p.ev[f"b_res_done_{i}"] = p.ev[f"b_oev_{i}_{NCT-1}"]

        def tn_h(tensor, i=i):
            p.wait(tensor, f"b_xln_{i}")
            if i == 0:
                p.wait(tensor, w1ev)
            xl = sq16.ap().rearrange("p (ct t) -> p ct t", ct=NCT)
            w1v = w116.ap().rearrange("p (ct h) -> p ct h", ct=NCT)
            for ht in range(NHT):
                if ht >= 2:
                    p.wait(tensor, f"b_hev_{i}_{ht-2}")
                elif i > 0:
                    p.wait(tensor, f"b_hev_{i-1}_{NHT-2+ht}")
                pb = ph[ht % 2]
                for ct in range(NCT):
                    inst = tensor.matmul(pb[:], w1v[:, ct, ht * 128:(ht + 1) * 128],
                                         xl[:, ct, :],
                                         start=(ct == 0), stop=(ct == NCT - 1))
                p.inc(inst, "t", 1, f"b_hmm_{i}_{ht}")
        p.block.tensor(tn_h)

        def sc_h(scalar, i=i):
            b1off = _SV["l0f_b1"][0]
            hv = h16.ap().rearrange("p (ht t) -> p ht t", ht=NHT)
            for ht in range(NHT):
                p.wait(scalar, f"b_hmm_{i}_{ht}")
                p.inc_reserved(scalar.activation(out=hv[:, ht, :], in_=ph[ht % 2][:],
                                                 func=AF.Relu,
                                                 bias=sv[:, b1off + ht:b1off + ht + 1]),
                               f"b_hev_{i}_{ht}")
        p.block.scalar(sc_h)

        def tn_o(tensor, i=i):
            if i == 0:
                p.wait(tensor, w2ev)
            hv = h16.ap().rearrange("p (ht t) -> p ht t", ht=NHT)
            w2v = w216.ap().rearrange("p (ht c) -> p ht c", ht=NHT)
            p.wait(tensor, f"b_hev_{i}_{NHT-1}")
            for ct in range(NCT):
                if ct >= 2:
                    p.wait(tensor, f"b_oev_{i}_{ct-2}")
                elif i > 0:
                    p.wait(tensor, f"b_oev_{i-1}_{NCT-2+ct}")
                pb = po[ct % 2]
                for ht in range(NHT):
                    inst = tensor.matmul(pb[:], w2v[:, ht, ct * 128:(ct + 1) * 128],
                                         hv[:, ht, :],
                                         start=(ht == 0), stop=(ht == NHT - 1))
                p.inc(inst, "t", 1, f"b_omm_{i}_{ct}")
        p.block.tensor(tn_o)

        def vec_o(vector, i=i):
            b2off = _SV["l0f_b2"][0]
            xv = sx.ap().rearrange("p (ct t) -> p ct t", ct=NCT)
            ov = out32.ap().rearrange("p (ct t) -> p ct t", ct=NCT)
            if i > 0:
                p.wait(vector, f"b_st_{i-1}")
            for ct in range(NCT):
                p.wait(vector, f"b_omm_{i}_{ct}")
                inst = vector.scalar_tensor_tensor(
                    out=ov[:, ct, :], in0=po[ct % 2][:],
                    scalar=sv[:, b2off + ct:b2off + ct + 1],
                    in1=xv[:, ct, :], op0=ALU.add, op1=ALU.add)
                p.inc_reserved(inst, f"b_oev_{i}_{ct}")
        p.block.vector(vec_o)

        def st(gpsimd, i=i, tsl=tsl):
            p.wait(gpsimd, f"b_res_done_{i}")
            p.mark_dma(gpsimd.dma_start(
                out=res2[:, :, tsl],
                in_=out32.ap().rearrange("p (ct t) -> p ct t", ct=NCT)),
                "dma_bs", f"b_st_{i}")
            p.ev[f"b_out_used_{i}"] = p.ev[f"b_st_{i}"]
        p.block.gpsimd(st)

    def fin(gpsimd):
        p.wait(gpsimd, f"b_st_{NCH-1}")
    p.block.gpsimd(fin)



def emit_D(p):
    TC = 512
    NCH = T // TC
    res3 = fm_view(p.dram["res3_d"])
    yfm = fm_view(p.dram["y_fm"])
    sv = p.sv32

    xc = p.sb("d_xc", [128, NCT * TC], F32)
    tmp32 = p.sb("d_tmp", [128, NCT * TC], F32)
    sq16 = p.sb("d_sq16", [128, NCT * TC], BF16)
    xln16 = sq16
    t216 = p.sb("d_t216", [128, NCT * TC], BF16)
    sc16 = t216
    hA = p.sb("d_hA", [128, NHT * TC], BF16)
    hB = p.sb("d_hB", [128, NHT * TC], BF16)
    wbuf2 = p.sb("d_wbuf", [128, 2 * 16384], BF16)
    out32 = tmp32
    rowbuf = p.sb("d_rows", [1, 3 * TC], F32)

    psA = p.ps("d_psA", [1, TC])
    psB = p.ps("d_psB", [1, TC])
    prep = p.ps("d_prep", [128, 2 * TC])
    pm = [p.ps(f"d_pm{j}", [128, TC]) for j in range(2)]

    def r3(buf, n=NCT):
        return buf.ap().rearrange("p (ct t) -> p ct t", ct=n)

    BLOCKS = []
    for ob in range(4):
        BLOCKS.append(("f1", "f1_t", "f116_d", NCT, 1024, ob * 1024))
        BLOCKS.append(("f2", "f2_t", "f216_d", NCT, 1024, ob * 1024))
    for ob in range(8):
        BLOCKS.append(("h1", "fh1_t", "fh116_d", NHT, 512, ob * 512))
    for ob in range(8):
        BLOCKS.append(("h2", "fh2_t", "fh216_d", NHT, 512, ob * 512))
    for ob in range(2):
        BLOCKS.append(("w3", "f3_t", "f316_d", NHT, 512, ob * 512))
    NBLK = len(BLOCKS)

    evac_seq = {}

    def mm_bank_wait(tensor, i, g):
        if g >= 2:
            p.wait(tensor, evac_seq[i][g - 2])
        elif i > 0:
            p.wait(tensor, evac_seq[i - 1][len(evac_seq[i - 1]) - 2 + g])
        return pm[g % 2]

    def wbuf_half(b):
        return wbuf2.ap()[:, (b % 2) * 16384:(b % 2) * 16384 + 16384]

    def emit_wload(i, b):
        nm, src, cache, nin, nout, o0 = BLOCKS[b]
        ldev = f"d_wld_{i}_{b}"
        if b >= 2:
            halts = [f"d_wmm_{i}_{b-2}"]
        elif i > 0:
            halts = [f"d_wmm_{i-1}_{NBLK-2+b}"]
        else:
            halts = []
        if i == 0:
            if b >= 2:
                halts.append(f"d_cst_{b-2}")
            src_ap = p.dram[src][:, o0:o0 + nout]
            _, ev = p.load_weight_cast(f"d_{nm}{o0}", src_ap, nin, nout, "dma_dw",
                                       stage_cols=4096, dma_eng="gpsimd",
                                       w16=wbuf_half(b), halt_event=halts or None)
            p.ev[ldev] = p.ev[ev]

            def cst(gpsimd, b=b, cache=cache, nin=nin, nout=nout, o0=o0, ev=ev):
                p.wait(gpsimd, ev)
                p.mark_dma(gpsimd.dma_start(
                    out=p.dram[cache][:, o0:o0 + nout].rearrange(
                        "(ct p) o -> p ct o", p=128),
                    in_=wbuf_half(b)[:, 0:nin * nout].rearrange(
                        "p (ct o) -> p ct o", ct=nin)),
                    "dma_dc", f"d_cst_{b}")
            p.block.gpsimd(cst)
        else:
            def ldq(gpsimd, i=i, b=b, cache=cache, nin=nin, nout=nout, o0=o0,
                    halts=tuple(halts)):
                p.wait(gpsimd, f"d_cst_{b}")
                for he in halts:
                    p.wait(gpsimd, he)
                p.mark_dma(gpsimd.dma_start(
                    out=wbuf_half(b)[:, 0:nin * nout].rearrange(
                        "p (ct o) -> p ct o", ct=nin),
                    in_=p.dram[cache][:, o0:o0 + nout].rearrange(
                        "(ct p) o -> p ct o", p=128)),
                    "dma_dw", ldev)
            p.block.gpsimd(ldq)

    for i in range(NCH):
        tsl = slice(i * TC, (i + 1) * TC)
        evac_seq[i] = []
        gctr = [0]

        def ld(sync, i=i, tsl=tsl):
            if i > 0:
                p.wait(sync, f"d_out_{i-1}")
            p.mark_dma(sync.dma_start(out=r3(xc), in_=res3[:, :, tsl]),
                       "dma_d", f"d_x_{i}")
        p.block.sync(ld)

        emit_stats(p, f"dl{i}", xc, sc16, sq16, NCT, TC, psA, psB,
                   rowbuf, prep, [f"d_x_{i}"])

        def vec_app(vector, i=i):
            p.wait(vector, f"dl{i}_prep")
            if i > 0:
                p.wait(vector, f"d_st_{i-1}")
            Mb = prep.ap()[:, 0:TC].unsqueeze(1).broadcast_to([128, NCT, TC])
            Rb = prep.ap()[:, TC:2 * TC].unsqueeze(1).broadcast_to([128, NCT, TC])
            vector.tensor_tensor(out=r3(tmp32), in0=r3(xc), in1=Mb, op=ALU.subtract)
            vector.tensor_tensor(out=r3(tmp32), in0=r3(tmp32), in1=Rb, op=ALU.mult)
            p.inc(vector.tensor_copy(out=xln16[:], in_=tmp32[:]), "v", 1, f"d_xln_{i}")
        p.block.vector(vec_app)

        hAv = r3(hA, NHT)
        hBv = r3(hB, NHT)
        t2v = r3(t216, NCT)

        for ob in range(4):
            for half, nm in ((0, "f1"), (1, "f2")):
                b = ob * 2 + half
                emit_wload(i, b)
                for sub in range(8):
                    p.reserve("s", f"d_gev_{i}_{b}_{sub}")
                    evac_seq[i].append(f"d_gev_{i}_{b}_{sub}")

                def tn_g(tensor, i=i, b=b):
                    p.wait(tensor, f"d_wld_{i}_{b}")
                    p.wait(tensor, f"d_xln_{i}")
                    wv = wbuf_half(b)[:, 0:NCT * 1024].rearrange("p (ct o) -> p ct o",
                                                                 ct=NCT)
                    xv = r3(xln16)
                    for sub in range(8):
                        pb = mm_bank_wait(tensor, i, gctr[0]); gctr[0] += 1
                        for ct in range(NCT):
                            inst = tensor.matmul(
                                pb[:], wv[:, ct, sub * 128:(sub + 1) * 128],
                                xv[:, ct, :],
                                start=(ct == 0), stop=(ct == NCT - 1))
                        p.inc(inst, "t", 1, f"d_gmm_{i}_{b}_{sub}")
                    p.ev[f"d_wmm_{i}_{b}"] = p.ev[f"d_gmm_{i}_{b}_{7}"]
                p.block.tensor(tn_g)

                def sc_g(scalar, i=i, b=b, ob=ob, half=half):
                    s1o, _ = _SV["l1f_s1"]
                    s2o, _ = _SV["l1f_s2"]
                    for sub in range(8):
                        glob = ob * 8 + sub
                        p.wait(scalar, f"d_gmm_{i}_{b}_{sub}")
                        bank = (b * 8 + sub) % 2
                        if half == 0:
                            inst = scalar.activation(
                                out=hAv[:, glob, :], in_=pm[bank][:], func=AF.Silu,
                                scale=sv[:, s1o + glob:s1o + glob + 1])
                        else:
                            if i > 0 or ob > 0:
                                p.wait(scalar, f"d_hA_{i}_{(ob-1)*8+sub}" if ob
                                       else f"d_hA_{i-1}_{24+sub}")
                            inst = scalar.activation(
                                out=t2v[:, sub, :], in_=pm[bank][:], func=AF.Identity,
                                scale=sv[:, s2o + glob:s2o + glob + 1])
                        p.inc_reserved(inst, f"d_gev_{i}_{b}_{sub}")
                        if half == 1:
                            p.reserve("v", f"d_hA_{i}_{glob}")
                p.block.scalar(sc_g)

                if half == 1:
                    def vec_g(vector, i=i, b=b, ob=ob):
                        for sub in range(8):
                            glob = ob * 8 + sub
                            p.wait(vector, f"d_gev_{i}_{b}_{sub}")
                            p.inc_reserved(
                                vector.tensor_tensor(out=hAv[:, glob, :],
                                                     in0=hAv[:, glob, :],
                                                     in1=t2v[:, sub, :], op=ALU.mult),
                                f"d_hA_{i}_{glob}")
                    p.block.vector(vec_g)
        p.ev[f"d_hA_done_{i}"] = p.ev[f"d_hA_{i}_{NHT-1}"]

        for hph in range(2):
            srcv = hAv if hph == 0 else hBv
            dstv = hBv if hph == 0 else hAv
            svn = "l1f_hs1" if hph == 0 else "l1f_hs2"
            base_b = 8 + 8 * hph
            in_ev = f"d_hA_done_{i}" if hph == 0 else f"d_hB_done_{i}"
            for ob in range(8):
                b = base_b + ob
                emit_wload(i, b)
                for sub in range(4):
                    p.reserve("s", f"d_hev_{i}_{b}_{sub}")
                    evac_seq[i].append(f"d_hev_{i}_{b}_{sub}")

                def tn_h(tensor, i=i, b=b, srcv=srcv, in_ev=in_ev):
                    p.wait(tensor, f"d_wld_{i}_{b}")
                    p.wait(tensor, in_ev)
                    wv = wbuf_half(b)[:, 0:NHT * 512].rearrange("p (ht o) -> p ht o",
                                                                ht=NHT)
                    for sub in range(4):
                        pb = mm_bank_wait(tensor, i, gctr[0]); gctr[0] += 1
                        for ht in range(NHT):
                            inst = tensor.matmul(
                                pb[:], wv[:, ht, sub * 128:(sub + 1) * 128],
                                srcv[:, ht, :],
                                start=(ht == 0), stop=(ht == NHT - 1))
                        p.inc(inst, "t", 1, f"d_hmm_{i}_{b}_{sub}")
                    p.ev[f"d_wmm_{i}_{b}"] = p.ev[f"d_hmm_{i}_{b}_{3}"]
                p.block.tensor(tn_h)

                def sc_h(scalar, i=i, b=b, ob=ob, dstv=dstv, svn=svn, base_b=base_b):
                    so, _ = _SV[svn]
                    for sub in range(4):
                        glob = ob * 4 + sub
                        p.wait(scalar, f"d_hmm_{i}_{b}_{sub}")
                        bank = (64 + (b - 8) * 4 + sub) % 2
                        p.inc_reserved(
                            scalar.activation(out=dstv[:, glob, :], in_=pm[bank][:],
                                              func=AF.Gelu,
                                              scale=sv[:, so + glob:so + glob + 1]),
                            f"d_hev_{i}_{b}_{sub}")
                p.block.scalar(sc_h)
            p.ev[f"d_hB_done_{i}" if hph == 0 else f"d_hC_done_{i}"] = \
                p.ev[f"d_hev_{i}_{base_b+7}_{3}"]

        for ob in range(2):
            b = 24 + ob
            emit_wload(i, b)
            for sub in range(4):
                p.reserve("v", f"d_w3ev_{i}_{ob*4+sub}")
                evac_seq[i].append(f"d_w3ev_{i}_{ob*4+sub}")

            def tn_w3(tensor, i=i, b=b):
                p.wait(tensor, f"d_wld_{i}_{b}")
                p.wait(tensor, f"d_hC_done_{i}")
                wv = wbuf_half(b)[:, 0:NHT * 512].rearrange("p (ht o) -> p ht o", ht=NHT)
                for sub in range(4):
                    pb = mm_bank_wait(tensor, i, gctr[0]); gctr[0] += 1
                    for ht in range(NHT):
                        inst = tensor.matmul(
                            pb[:], wv[:, ht, sub * 128:(sub + 1) * 128],
                            hAv[:, ht, :],
                            start=(ht == 0), stop=(ht == NHT - 1))
                    p.inc(inst, "t", 1, f"d_w3mm_{i}_{(b-24)*4+sub}")
                p.ev[f"d_wmm_{i}_{b}"] = p.ev[f"d_w3mm_{i}_{(b-24)*4+3}"]
            p.block.tensor(tn_w3)

            def vec_w3(vector, i=i, b=b, ob=ob):
                s3o, _ = _SV["l1f_s3"]
                ov = r3(out32)
                xv = r3(xc)
                for sub in range(4):
                    glob = ob * 4 + sub
                    p.wait(vector, f"d_w3mm_{i}_{glob}")
                    if i > 0 and glob == 0:
                        p.wait(vector, f"d_st_{i-1}")
                    bank = (128 + glob) % 2
                    p.inc_reserved(
                        vector.scalar_tensor_tensor(out=ov[:, glob, :],
                                                    in0=pm[bank][:],
                                                    scalar=sv[:, s3o + glob:s3o + glob + 1],
                                                    in1=xv[:, glob, :],
                                                    op0=ALU.mult, op1=ALU.add),
                        f"d_w3ev_{i}_{glob}")
            p.block.vector(vec_w3)
        p.ev[f"d_out_{i}"] = p.ev[f"d_w3ev_{i}_{7}"]

        def st(sync, i=i, tsl=tsl):
            p.wait(sync, f"d_out_{i}")
            p.mark_dma(sync.dma_start(out=yfm[:, :, tsl], in_=r3(out32)),
                       "dma_ds", f"d_st_{i}")
        p.block.sync(st)

    def fin(sync):
        p.wait(sync, f"d_st_{NCH-1}")
    p.block.sync(fin)



def emit_prelude(p):
    nc = p.nc
    p.sv32 = p.sb("sv32", [128, SVW + 64], F32)
    p.sv16 = p.sb("sv16", [128, SVW], BF16)
    p.rv32 = p.sb("rv32", [1, RVW], F32)
    p.ones16 = p.sb("ones16", [128, 1], BF16)
    p.ones32 = p.sb("ones32", [1, 128], F32)
    p.get_stg(4096)

    def ld(sync):
        p.mark_dma(sync.dma_start(out=p.sv32[:, 0:SVW], in_=p.dram["smallvecs"][:]),
                   "dma_pre", "sv_ld")
        p.mark_dma(sync.dma_start(out=p.rv32[:], in_=p.dram["rowvecs"][:]),
                   "dma_pre", "rv_ld")
    p.block.sync(ld)

    def vec(vector):
        p.wait(vector, "sv_ld")
        p.wait(vector, "rv_ld")
        vector.memset(p.ones16[:], 1.0)
        vector.memset(p.ones32[:], 1.0)
        p.inc(vector.tensor_copy(out=p.sv16[:], in_=p.sv32[:, 0:SVW]), "v", 1, "sv16_ready")
        p.ev["pre_ready"] = ("v", p.cnt["v"])
    p.block.vector(vec)

    p.eps_ap = p.rv32.ap()[:, _RV["eps"][0]:_RV["eps"][0] + 1]

    def allwait(eng):
        p.wait(eng, "pre_ready")
    for n in ["sync", "vector", "scalar", "tensor", "gpsimd"]:
        getattr(p.block, n)(allwait)



def build_program(phases):
    nc = bass.Bass("TRN2", target_bir_lowering=False)
    ctx = ExitStack()
    dram = declare_dram(nc, phases)
    block = ctx.enter_context(nc.Block())
    p = P(nc, block, ctx, dram, phases)
    emit_prelude(p)
    for phi, ph in enumerate(phases):
        from contextlib import ExitStack as _ES
        p.phase_ctx = _ES()
        if ph == "A1":
            emit_A1(p)
        elif ph == "A2":
            emit_A2(p)
        elif ph == "B":
            emit_B(p)
        elif ph == "C0":
            emit_C0(p)
        elif ph == "C1":
            emit_C1(p)
        elif ph == "C3":
            emit_C3(p)
        elif ph == "C4":
            emit_C4(p)
        elif ph == "D":
            emit_D(p)
        else:
            raise NotImplementedError(ph)
        if phi < len(phases) - 1:
            p.barrier(f"ph_{ph}")
        p.phase_ctx.close()
        p.phase_ctx = None
    ctx.close()
    return nc


def run_phases(phase_groups, host, extra_outputs=()):
    acts = {}
    results = None
    for phases in phase_groups:
        nc = build_program(phases)
        in_map = {}
        pset = set(phases)
        for name, (shape, prod) in WEIGHTS.items():
            if prod in pset:
                in_map[name] = host[name]
        in_map["smallvecs"] = host["smallvecs"]
        in_map["rowvecs"] = host["rowvecs"]
        per_core = [dict(in_map) for _ in range(NCORE)]
        for name, (shape, dt, prod, cons) in ACTS.items():
            if name == "x_fm":
                if pset & {"A1", "A2"}:
                    for b in range(NCORE):
                        per_core[b]["x_fm"] = host["x_fm"][b]
                continue
            if (cons & pset) and prod not in pset:
                for b in range(NCORE):
                    per_core[b][name] = acts[name][b]
        res = run_bass_kernel_spmd(nc, per_core, core_ids=list(range(NCORE)))
        for b, r in enumerate(res.results):
            for k, v in r.items():
                acts.setdefault(k, [None] * NCORE)[b] = v
    return acts


FUSED = True


def kernel(**inputs):
    host = prep_inputs(inputs)
    groups = [ALL_PHASES] if FUSED else [[ph] for ph in ALL_PHASES]
    acts = run_phases(groups, host)
    out = np.stack([acts["y_fm"][b].T for b in range(NCORE)], axis=0)
    return np.ascontiguousarray(out.astype(np.float32))


# version 04
# speedup vs baseline: 1.1690x; 1.0066x over previous
import numpy as np
from contextlib import ExitStack
import concourse.bass as bass
import concourse.mybir as mybir
from concourse.bass_utils import run_bass_kernel_spmd

F32 = mybir.dt.float32
BF16 = mybir.dt.bfloat16
AF = mybir.ActivationFunctionType
ALU = mybir.AluOpType

C = 1024
T = 4096
NCT = 8
H = 4096
NHT = 32
NCORE = 8
EPS = 1e-5
NB_HEADS = 16
DH = 64


def _t(a):
    return np.ascontiguousarray(np.asarray(a, np.float32).T)


def _pp(v, n):
    return np.ascontiguousarray(np.asarray(v, np.float32).reshape(n, 128).T)


_SV = {}
_SV_W = [0]


def _sv_slot(name, width):
    if name not in _SV:
        _SV[name] = (_SV_W[0], width)
        _SV_W[0] += width
    return _SV[name]


for _n in ["ln0_w", "ln0_b", "ln1_w", "ln1_b", "ln2_w", "ln2_b",
           "tmk", "tmv", "tmr", "cmk", "cmr", "tdecay", "tfirst",
           "l0f_b2", "l1_bq", "l1_bk", "l1_bo", "l1_b1", "l1_b2", "l1f_s3"]:
    _sv_slot(_n, 8)
for _n in ["l0f_b1", "l1f_s1", "l1f_s2", "l1f_hs1", "l1f_hs2"]:
    _sv_slot(_n, 32)
SVW = _SV_W[0]

_RV = {"bv_row": (0, 1024), "eps": (1024, 1), "one": (1025, 1)}
RVW = 1026


def _relpos_bucket(rel, num_buckets=32, max_distance=128):
    import math
    n = -rel
    nb = num_buckets // 2
    ret = (n < 0).astype(np.int64) * nb
    n = np.abs(n)
    max_exact = nb // 2
    is_small = n < max_exact
    val_if_large = max_exact + (
        np.log(np.maximum(n, 1).astype(np.float64) / max_exact)
        / math.log(max_distance / max_exact) * (nb - max_exact)
    ).astype(np.int64)
    val_if_large = np.minimum(val_if_large, nb - 1)
    return ret + np.where(is_small, n, val_if_large)


def prep_inputs(inp):
    d = {}
    x = np.asarray(inp["L0_attn_x"], np.float32)
    d["x_fm"] = [np.ascontiguousarray(x[b].T) for b in range(NCORE)]

    for src, dst in [("L0_attn_Wk", "wk_t"), ("L0_attn_Wv", "wv_t"),
                     ("L0_attn_Wr", "wr_t"), ("L0_attn_Wo", "wo_t"),
                     ("L0_attn_Wfk", "wfk_t"), ("L0_attn_Wfr", "wfr_t"),
                     ("L0_attn_Wfv", "wfv_t"),
                     ("L1_attn_Wq", "l1wq_t"), ("L1_attn_Wk", "l1wk_t"),
                     ("L1_attn_Wv", "l1wv_t"), ("L1_attn_Wo", "l1wo_t"),
                     ("L1_attn_W1", "l1w1_t"), ("L1_attn_W2", "l1w2_t"),
                     ("L1_ffn_w1_w", "f1_t"), ("L1_ffn_w2_w", "f2_t"),
                     ("L1_ffn_h1_w", "fh1_t"), ("L1_ffn_h2_w", "fh2_t"),
                     ("L1_ffn_w3_w", "f3_t")]:
        d[dst] = _t(inp[src])
    d["l0f_w1"] = np.ascontiguousarray(np.asarray(inp["L0_ffn_w1"], np.float32))
    d["l0f_w2"] = np.ascontiguousarray(np.asarray(inp["L0_ffn_w2"], np.float32))

    sv = np.zeros((128, SVW), np.float32)

    def put(name, val, n):
        off, w = _SV[name]
        assert w == n
        sv[:, off:off + n] = _pp(val, n)

    put("ln0_w", inp["L0_attn_ln0_w"], 8); put("ln0_b", inp["L0_attn_ln0_b"], 8)
    put("ln1_w", inp["L0_attn_ln1_w"], 8); put("ln1_b", inp["L0_attn_ln1_b"], 8)
    put("ln2_w", inp["L0_attn_ln2_w"], 8); put("ln2_b", inp["L0_attn_ln2_b"], 8)
    put("tmk", inp["L0_attn_tmk"], 8); put("tmv", inp["L0_attn_tmv"], 8)
    put("tmr", inp["L0_attn_tmr"], 8)
    put("cmk", inp["L0_attn_cmk"], 8); put("cmr", inp["L0_attn_cmr"], 8)
    put("tdecay", inp["L0_attn_time_decay"], 8)
    put("tfirst", inp["L0_attn_time_first"], 8)
    put("l0f_b1", inp["L0_ffn_b1"], 32); put("l0f_b2", inp["L0_ffn_b2"], 8)
    put("l1_bq", inp["L1_attn_bq"], 8); put("l1_bk", inp["L1_attn_bk"], 8)
    put("l1_bo", inp["L1_attn_bo"], 8)
    put("l1_b1", inp["L1_attn_b1"], 8); put("l1_b2", inp["L1_attn_b2"], 8)
    put("l1f_s1", inp["L1_ffn_w1_s"], 32); put("l1f_s2", inp["L1_ffn_w2_s"], 32)
    put("l1f_hs1", inp["L1_ffn_h1_s"], 32); put("l1f_hs2", inp["L1_ffn_h2_s"], 32)
    put("l1f_s3", inp["L1_ffn_w3_s"], 8)
    d["smallvecs"] = sv

    rv = np.zeros((1, RVW), np.float32)
    rv[0, _RV["eps"][0]] = EPS
    rv[0, _RV["bv_row"][0]:_RV["bv_row"][0] + C] = np.asarray(inp["L1_attn_bv"], np.float32)
    rv[0, _RV["one"][0]] = 1.0
    d["rowvecs"] = rv

    rel_bias = np.asarray(inp["L1_attn_rel_bias"], np.float32)
    n = np.arange(8191)
    buckets = _relpos_bucket(4095 - n)
    d["g_master"] = np.ascontiguousarray(rel_bias[buckets, :].T)
    return d


WEIGHTS = {
    "wk_t": ([C, C], "A1"), "wv_t": ([C, C], "A1"), "wr_t": ([C, C], "A1"),
    "wo_t": ([C, C], "A1"),
    "wfk_t": ([C, H], "A2"), "wfv_t": ([H, C], "A2"), "wfr_t": ([C, C], "A2"),
    "l0f_w1": ([C, H], "B"), "l0f_w2": ([H, C], "B"),
    "l1wq_t": ([C, C], "C1"), "l1wk_t": ([C, C], "C1"), "l1wv_t": ([C, C], "C1"),
    "l1wo_t": ([C, C], "C3"), "l1w1_t": ([C, C], "C4"), "l1w2_t": ([C, C], "C4"),
    "f1_t": ([C, H], "D"), "f2_t": ([C, H], "D"),
    "fh1_t": ([H, H], "D"), "fh2_t": ([H, H], "D"), "f3_t": ([H, C], "D"),
    "g_master": ([16, 8191], "C0"),
}
ACTS = {
    "x_fm":    ([C, T], F32, None, {"A1", "A2"}),
    "x2_d":    ([C, T], F32, "A1", {"A2"}),
    "xsum_d":  ([C, T], F32, "A1", {"A2"}),
    "res1_d":  ([C, T], F32, "A2", {"B"}),
    "wfk16_d": ([C, H], BF16, "A2", {"A2"}),
    "wfv16_d": ([H, C], BF16, "A2", {"A2"}),
    "res2_d":  ([C, T], F32, "B", {"C0", "C4"}),
    "xatt_d":  ([C, T], F32, "C0", {"C3"}),
    "xatt16_d": ([C, T], BF16, "C0", {"C1"}),
    "g16_d":   ([16, 8191], BF16, "C0", {"C1"}),
    "att_tok_d": ([T, C], BF16, "C1", {"C3"}),
    "v_d":     ([T, C], BF16, "C1", {"C1"}),
    "ab_d":    ([C, T], F32, "C3", {"C4"}),
    "ab16_d":  ([C, T], BF16, "C3", {"C4"}),
    "res3_d":  ([C, T], F32, "C4", {"D"}),
    "f116_d":  ([C, H], BF16, "D", {"D"}),
    "f216_d":  ([C, H], BF16, "D", {"D"}),
    "fh116_d": ([H, H], BF16, "D", {"D"}),
    "fh216_d": ([H, H], BF16, "D", {"D"}),
    "f316_d":  ([H, C], BF16, "D", {"D"}),
    "y_fm":    ([C, T], F32, "D", set()),
}
ALL_PHASES = ["A1", "A2", "B", "C0", "C1", "C3", "C4", "D"]



class P:

    def __init__(self, nc, block, ctx, dram, phases):
        self.nc = nc
        self.block = block
        self.ctx = ctx
        self.dram = dram
        self.phases = phases
        self.cnt = {}
        self.sem = {}
        self.ev = {}

    def s(self, name):
        if name not in self.sem:
            self.sem[name] = self.ctx.enter_context(self.nc.semaphore(name))
            self.cnt[name] = 0
        return self.sem[name]

    def inc(self, inst, semname, n=1, event=None):
        sem = self.s(semname)
        inst.then_inc(sem, n)
        self.cnt[semname] += n
        if event is not None:
            self.ev[event] = (semname, self.cnt[semname])
        return self.cnt[semname]

    def mark_dma(self, inst, semname, event=None):
        return self.inc(inst, semname, 16, event)

    def reserve(self, semname, event, n=1):
        self.s(semname)
        self.cnt[semname] += n
        self.ev[event] = (semname, self.cnt[semname])

    def inc_reserved(self, inst, event):
        semname, _ = self.ev[event]
        inst.then_inc(self.sem[semname], 1)

    def wait(self, eng, event):
        semname, v = self.ev[event]
        eng.wait_ge(self.sem[semname], v)

    def wait_maybe(self, eng, event):
        if event in self.ev:
            self.wait(eng, event)

    def sb(self, name, shape, dtype, scope=None):
        ctx = scope or getattr(self, "phase_ctx", None) or self.ctx
        return ctx.enter_context(self.nc.sbuf_tensor(name, shape, dtype))

    def ps(self, name, shape, dtype=F32):
        ctx = getattr(self, "phase_ctx", None) or self.ctx
        return ctx.enter_context(self.nc.psum_tensor(name, shape, dtype))

    def barrier(self, tag):
        b = self.block
        names = ["sync", "vector", "scalar", "tensor", "gpsimd"]
        sem = self.s("bar")
        base = self.cnt["bar"]

        for n in names:
            def emit(eng, n=n):
                eng.sem_inc(sem, 1)
            getattr(b, n)(emit)
        self.cnt["bar"] += 5
        tgt = self.cnt["bar"]

        for n in names:
            def emit(eng):
                eng.wait_ge(sem, tgt)
            getattr(b, n)(emit)


    def get_stg(self, cols=8192):
        if not hasattr(self, "stg") or self.stg_cols < cols:
            assert not hasattr(self, "stg"), "allocate biggest staging first"
            self.stg = self.sb("wstg", [128, cols], F32, scope=self.ctx)
            self.stg_cols = cols
            self.stg_users = 0
        return self.stg

    def load_weight_cast(self, tag, w_ap, n_in_t, n_out, dma_sem, stage_cols=8192,
                         cast_eng="vector", w16=None, halt_event=None,
                         dma_eng="sync"):
        assert stage_cols % n_out == 0
        tps = stage_cols // n_out
        nstage = (n_in_t + tps - 1) // tps
        if w16 is None:
            w16 = self.sb(f"{tag}_16", [128, n_in_t * n_out], BF16)
        stg = self.get_stg(stage_cols)
        src = w_ap.rearrange("(ct p) o -> p ct o", p=128)
        ev_done = f"{tag}_cast_done"
        csem = "v" if cast_eng == "vector" else "g"
        prev_user = getattr(self, "stg_last_ev", None)
        for j in range(nstage):
            self.reserve(csem, f"{tag}_cast_{j}" if j < nstage - 1 else ev_done)
        self.stg_last_ev = ev_done

        def emit_sync(sync):
            for j in range(nstage):
                c0 = j * tps
                c1 = min(n_in_t, c0 + tps)
                if j == 0:
                    if prev_user is not None:
                        self.wait(sync, prev_user)
                    if halt_event is not None:
                        for he in ([halt_event] if isinstance(halt_event, str)
                                   else halt_event):
                            self.wait(sync, he)
                else:
                    self.wait(sync, f"{tag}_cast_{j-1}")
                self.mark_dma(
                    sync.dma_start(
                        out=stg.ap()[:, 0:(c1 - c0) * n_out].rearrange(
                            "p (ct o) -> p ct o", ct=c1 - c0),
                        in_=src[:, c0:c1, :]),
                    dma_sem, f"{tag}_ld_{j}")
        getattr(self.block, dma_eng)(emit_sync)

        def emit_cast(eng):
            for j in range(nstage):
                c0 = j * tps
                c1 = min(n_in_t, c0 + tps)
                self.wait(eng, f"{tag}_ld_{j}")
                ev = f"{tag}_cast_{j}" if j < nstage - 1 else ev_done
                self.inc_reserved(
                    eng.tensor_copy(
                        out=w16[:, c0 * n_out:c1 * n_out],
                        in_=stg[:, 0:(c1 - c0) * n_out]),
                    ev)
        getattr(self.block, cast_eng)(emit_cast)
        return w16, ev_done


def declare_dram(nc, phases):
    dram = {}
    pset = set(phases)
    for name, (shape, prod) in WEIGHTS.items():
        if prod in pset:
            dram[name] = nc.dram_tensor(name, shape, F32, kind="ExternalInput").ap()
    for name, (shape, dt, prod, cons) in ACTS.items():
        needed_in = bool(cons & pset) or (name == "x_fm" and bool(pset))
        produced = prod in pset
        if name == "x_fm":
            if pset & {"A1", "A2"}:
                dram[name] = nc.dram_tensor(name, shape, dt, kind="ExternalInput").ap()
            continue
        if produced and cons and cons <= {prod}:
            dram[name] = nc.dram_tensor(name, shape, dt, kind="Internal").ap()
        elif produced:
            dram[name] = nc.dram_tensor(name, shape, dt, kind="ExternalOutput").ap()
        elif cons & pset:
            dram[name] = nc.dram_tensor(name, shape, dt, kind="ExternalInput").ap()
    if pset:
        dram["smallvecs"] = nc.dram_tensor("smallvecs", [128, SVW], F32,
                                           kind="ExternalInput").ap()
        dram["rowvecs"] = nc.dram_tensor("rowvecs", [1, RVW], F32,
                                         kind="ExternalInput").ap()
    return dram


def fm_view(ap):
    return ap.rearrange("(ct p) t -> p ct t", p=128)



def emit_stats(p, tag, src32, src16, sq16, nct, tc, psA, psB, rowbuf,
               prep_ps, dep_events, double=False, no_eps=False):
    ncols = rowbuf.shape[1] // tc if hasattr(rowbuf, 'shape') else 6
    cs = [rowbuf[:, k * tc:(k + 1) * tc] for k in range(min(6, ncols))]

    def vec_cast(vector):
        for e in dep_events:
            p.wait(vector, e)
        p.inc(vector.tensor_copy(out=src16[:], in_=src32[:]), "v", 1, f"{tag}_c16")
    p.block.vector(vec_cast)

    def sc_sq(scalar):
        for e in dep_events:
            p.wait(scalar, e)
        p.inc(scalar.activation(out=sq16[:], in_=src32[:], func=AF.Square),
              "s", 1, f"{tag}_sq")
    p.block.scalar(sc_sq)

    def tn_stats(tensor):
        p.wait(tensor, f"{tag}_c16")
        p.wait(tensor, f"{tag}_sq")
        s16v = src16.ap().rearrange("p (ct t) -> p ct t", ct=nct)
        q16v = sq16.ap().rearrange("p (ct t) -> p ct t", ct=nct)
        for ct in range(nct):
            tensor.matmul(psA[:], p.ones16[:], s16v[:, ct, :],
                          start=(ct == 0), stop=(ct == nct - 1))
        for ct in range(nct):
            inst = tensor.matmul(psB[:], p.ones16[:], q16v[:, ct, :],
                                 start=(ct == 0), stop=(ct == nct - 1))
        p.inc(inst, "t", 1, f"{tag}_stats")
    p.block.tensor(tn_stats)

    def vec_rows(vector):
        p.wait(vector, f"{tag}_stats")
        vector.tensor_scalar_mul(cs[0], psA[:], 1.0 / C)
        vector.tensor_scalar_mul(cs[1], psB[:], 1.0 / C)
        vector.tensor_tensor(out=cs[2], in0=cs[0], in1=cs[0], op=ALU.mult)
        inst = vector.tensor_tensor(out=cs[1], in0=cs[1], in1=cs[2],
                                    op=ALU.subtract)
        if double:
            vector.tensor_scalar_add(cs[3], cs[1], EPS)
            vector.reciprocal(cs[4], cs[3])
            inst = vector.tensor_tensor(out=cs[2], in0=cs[1], in1=cs[4],
                                        op=ALU.mult)
        p.inc(inst, "v", 1, f"{tag}_var")
    p.block.vector(vec_rows)

    def sc_rstd(scalar):
        p.wait(scalar, f"{tag}_var")
        if double:
            scalar.activation(out=cs[3], in_=cs[3], func=AF.Ln)
            scalar.activation(out=cs[3], in_=cs[3], func=AF.Exp, scale=-0.5)
            scalar.activation(out=cs[5], in_=cs[2], func=AF.Ln, bias=p.eps_ap)
            inst = scalar.activation(out=cs[5], in_=cs[5], func=AF.Exp, scale=-0.5)
        else:
            scalar.activation(out=cs[2], in_=cs[1], func=AF.Ln,
                              bias=0.0 if no_eps else p.eps_ap)
            inst = scalar.activation(out=cs[1], in_=cs[2], func=AF.Exp, scale=-0.5)
        p.inc(inst, "s", 1, f"{tag}_rstd")
    p.block.scalar(sc_rstd)

    if double:
        def vec_r12(vector):
            p.wait(vector, f"{tag}_rstd")
            p.inc(vector.tensor_tensor(out=cs[1], in0=cs[3], in1=cs[5], op=ALU.mult),
                  "v", 1, f"{tag}_r12")
        p.block.vector(vec_r12)

    def tn_rep(tensor):
        p.wait(tensor, f"{tag}_r12" if double else f"{tag}_rstd")
        for o in range(0, 2 * tc, 512):
            e = min(2 * tc, o + 512)
            inst = tensor.matmul(prep_ps[:, o:e], p.ones32[:], rowbuf[:, o:e],
                                 start=True, stop=True)
        p.inc(inst, "t", 1, f"{tag}_prep")
    p.block.tensor(tn_rep)
    return f"{tag}_prep"



def emit_C0(p):
    TC = 512
    NCH = T // TC
    res2 = fm_view(p.dram["res2_d"])
    xatt = fm_view(p.dram["xatt_d"])
    xatt16 = fm_view(p.dram["xatt16_d"])

    sx = p.sb("c0_sx", [128, NCT * TC], F32)
    tmp32 = p.sb("c0_tmp", [128, NCT * TC], F32)
    sc16 = p.sb("c0_sc16", [128, NCT * TC], BF16)
    sq16 = p.sb("c0_sq16", [128, NCT * TC], BF16)
    o16 = p.sb("c0_o16", [128, NCT * TC], BF16)
    gm32 = p.sb("c0_gm32", [16, 8191], F32)
    gm16 = p.sb("c0_gm16", [16, 8191], BF16)
    rowbuf = p.sb("c0_rows", [1, 6 * TC], F32)

    psA = p.ps("c0_psA", [1, TC])
    psB = p.ps("c0_psB", [1, TC])
    prep = p.ps("c0_prep", [128, 2 * TC])

    def r3(buf):
        return buf.ap().rearrange("p (ct t) -> p ct t", ct=NCT)

    def gld(gpsimd):
        p.mark_dma(gpsimd.dma_start(out=gm32[:], in_=p.dram["g_master"][:]),
                   "dma_c0g", "c0_gld")
    p.block.gpsimd(gld)

    def gcast(vector):
        p.wait(vector, "c0_gld")
        p.inc(vector.tensor_copy(out=gm16[:], in_=gm32[:]), "v", 1, "c0_gcast")
    p.block.vector(gcast)

    def gst(gpsimd):
        p.wait(gpsimd, "c0_gcast")
        p.mark_dma(gpsimd.dma_start(out=p.dram["g16_d"][:], in_=gm16[:]),
                   "dma_c0g", "c0_gst")
    p.block.gpsimd(gst)

    for i in range(NCH):
        tsl = slice(i * TC, (i + 1) * TC)

        def ld(sync, i=i, tsl=tsl):
            if i > 0:
                p.wait(sync, f"c0_app_{i-1}")
            p.mark_dma(sync.dma_start(out=r3(sx), in_=res2[:, :, tsl]),
                       "dma_c0", f"c0_x_{i}")
        p.block.sync(ld)

        emit_stats(p, f"c0l{i}", sx, sc16, sq16, NCT, TC, psA, psB,
                   rowbuf, prep, [f"c0_x_{i}"])

        def vec_app(vector, i=i):
            p.wait(vector, f"c0l{i}_prep")
            if i > 0:
                p.wait(vector, f"c0_st_{i-1}")
            Mb = prep.ap()[:, 0:TC].unsqueeze(1).broadcast_to([128, NCT, TC])
            Rb = prep.ap()[:, TC:2 * TC].unsqueeze(1).broadcast_to([128, NCT, TC])
            vector.tensor_tensor(out=r3(tmp32), in0=r3(sx), in1=Mb, op=ALU.subtract)
            vector.tensor_tensor(out=r3(tmp32), in0=r3(tmp32), in1=Rb, op=ALU.mult)
            p.inc(vector.tensor_copy(out=o16[:], in_=tmp32[:]), "v", 1, f"c0_app_{i}")
        p.block.vector(vec_app)

        def st(sync, i=i, tsl=tsl):
            p.wait(sync, f"c0_app_{i}")
            p.mark_dma(sync.dma_start(out=xatt[:, :, tsl], in_=r3(tmp32)),
                       "dma_c0s", None)
            p.mark_dma(sync.dma_start(out=xatt16[:, :, tsl], in_=r3(o16)),
                       "dma_c0s", f"c0_st_{i}")
        p.block.sync(st)

    def fin(sync):
        p.wait(sync, f"c0_st_{NCH-1}")
        p.wait(sync, "c0_gst")
    p.block.sync(fin)



C1_DEBUG = 0
C1_NTC = 8

def emit_C1(p):
    GW = 8064
    NST = 8
    xatt16 = fm_view(p.dram["xatt16_d"])
    attTok = p.dram["att_tok_d"]
    vd = p.dram["v_d"]
    g16d = p.dram["g16_d"]
    sv = p.sv32

    qF = p.sb("c1_qF", [64, T], BF16)
    kF = p.sb("c1_kF", [64, T], BF16)
    vones = p.sb("c1_vones", [128, 32 * 66], BF16)
    att_s = p.sb("c1_att_s", [64, 2 * 512], BF16)
    atok = p.sb("c1_atok", [128, 2 * 4 * 64], BF16)
    Hb = p.sb("c1_H", [128, GW], BF16)
    G = p.sb("c1_G", [128, GW], BF16)
    Efull = p.sb("c1_Efull", [128, 32 * 512], BF16)
    ones64 = p.sb("c1_ones64", [1, 64], F32)
    xblk = p.sb("c1_xblk", [128, NCT * 256], BF16)
    xtt = p.sb("c1_xtt", [128, NCT * 128], BF16)
    wq16 = p.sb("c1_wq16", [128, NCT * C], BF16)
    wk16 = p.sb("c1_wk16", [128, NCT * C], BF16)
    wv16 = p.sb("c1_wv16", [128, NCT * C], BF16)
    vst = p.sb("c1_vst", [128, 2 * C], BF16)
    vb32 = p.sb("c1_vb32", [128, C], F32)
    r32 = p.sb("c1_r32", [1, 512], F32)
    Rr32 = p.sb("c1_Rr32", [64, 512], F32)
    tE32 = p.sb("c1_tE32", [64, 512], F32)

    pp = [p.ps(f"c1_pp{j}", [128, 512]) for j in range(4)]
    pE = p.ps("c1_pE", [65, 512])
    pB = p.ps("c1_pB", [64, 512])
    prep2 = p.ps("c1_prep2", [64, 512])
    vbr = p.ps("c1_vbr", [128, 512])

    def vinit(vector):
        vector.memset(ones64[:], 1.0)
        p.inc(vector.memset(vones[:], 1.0), "v", 1, "c1_vones_init")
    p.block.vector(vinit)

    wq16, wqev = p.load_weight_cast("c1_wq", p.dram["l1wq_t"], NCT, C, "dma_c1w",
                                    stage_cols=4096, dma_eng="gpsimd", w16=wq16)
    wk16, wkev = p.load_weight_cast("c1_wk", p.dram["l1wk_t"], NCT, C, "dma_c1w",
                                    stage_cols=4096, dma_eng="gpsimd", w16=wk16)
    wv16, wvev = p.load_weight_cast("c1_wv", p.dram["l1wv_t"], NCT, C, "dma_c1w",
                                    stage_cols=4096, dma_eng="gpsimd", w16=wv16)

    def tn_bv(tensor):
        p.wait(tensor, "pre_ready")
        bvo = _RV["bv_row"][0]
        p.inc(tensor.matmul(vbr[:], p.ones32[:], p.rv32[:, bvo:bvo + 512],
                            start=True, stop=True), "t", 1, "c1_bv0")
    p.block.tensor(tn_bv)

    def vec_bv0(vector):
        p.wait(vector, "c1_bv0")
        p.inc(vector.tensor_copy(out=vb32[:, 0:512], in_=vbr[:]), "v", 1, "c1_bve0")
    p.block.vector(vec_bv0)

    def tn_bv1(tensor):
        p.wait(tensor, "c1_bve0")
        bvo = _RV["bv_row"][0]
        p.inc(tensor.matmul(vbr[:], p.ones32[:], p.rv32[:, bvo + 512:bvo + 1024],
                            start=True, stop=True), "t", 1, "c1_bv1")
    p.block.tensor(tn_bv1)

    def vec_bv1(vector):
        p.wait(vector, "c1_bv1")
        p.inc(vector.tensor_copy(out=vb32[:, 512:1024], in_=vbr[:]), "v", 1, "c1_bv")
    p.block.vector(vec_bv1)

    for tt in range(32):
        p.reserve("v", f"c1_vev_{tt}")

        def ldv(sync, tt=tt):
            if tt:
                p.wait(sync, f"c1_vmm_{tt-1}")
            p.mark_dma(sync.dma_start(
                out=xtt.ap().rearrange("p (ct t) -> p ct t", ct=NCT),
                in_=xatt16[:, :, tt * 128:(tt + 1) * 128]),
                "dma_c1x", f"c1_xtt_{tt}")
        p.block.sync(ldv)

        def tn_v(tensor, tt=tt):
            p.wait(tensor, f"c1_xtt_{tt}")
            if tt == 0:
                p.wait(tensor, wvev)
            if tt >= 2:
                p.wait(tensor, f"c1_vev_{tt-2}")
            xv = xtt.ap().rearrange("p (ct t) -> p ct t", ct=NCT)
            wvw = wv16.ap().rearrange("p (ct o) -> p ct o", ct=NCT)
            for half in range(2):
                for ct in range(NCT):
                    inst = tensor.matmul(pp[2 * (tt % 2) + half][:], xv[:, ct, :],
                                         wvw[:, ct, half * 512:(half + 1) * 512],
                                         start=(ct == 0), stop=(ct == NCT - 1))
                p.inc(inst, "t", 1, f"c1_vmmh_{tt}_{half}")
            p.ev[f"c1_vmm_{tt}"] = p.ev[f"c1_vmmh_{tt}_{1}"]
        p.block.tensor(tn_v)

        def vec_v(vector, tt=tt):
            if tt == 0:
                p.wait(vector, "c1_bv")
            if tt >= 2:
                p.wait(vector, f"c1_vst_{tt-2}")
            for half in range(2):
                p.wait(vector, f"c1_vmmh_{tt}_{half}")
                inst = vector.tensor_tensor(
                    out=vst[:, (tt % 2) * C + half * 512:(tt % 2) * C + (half + 1) * 512],
                    in0=pp[2 * (tt % 2) + half][:],
                    in1=vb32[:, half * 512:(half + 1) * 512], op=ALU.add)
            p.inc_reserved(inst, f"c1_vev_{tt}")
        p.block.vector(vec_v)

        def stv(sync, tt=tt):
            p.wait(sync, f"c1_vev_{tt}")
            p.mark_dma(sync.dma_start(
                out=vd[tt * 128:(tt + 1) * 128, :],
                in_=vst[:, (tt % 2) * C:(tt % 2 + 1) * C]),
                "dma_c1s", f"c1_vst_{tt}")
        p.block.sync(stv)

    for j in range(16):
        bq0 = f"c1b{j}"

        def ldxb(sync, j=j):
            if j:
                p.wait(sync, f"c1b{j-1}_qkf")
            p.mark_dma(sync.dma_start(
                out=xblk.ap().rearrange("p (ct t) -> p ct t", ct=NCT),
                in_=xatt16[:, :, j * 256:(j + 1) * 256]),
                "dma_c1x", f"{bq0}_xb")
        p.block.sync(ldxb)

        def ldvf(sync, j=j):
            p.wait(sync, f"c1_vst_{31}")
            if j:
                p.wait(sync, f"c1b{j-1}_avdone")
            p.mark_dma(sync.dma_start(
                out=bass.AP(vones, 0, [[66 * 32, 128], [66, 32], [1, 64]]),
                in_=bass.AP(vd.tensor, j * 256 * C,
                            [[C, 8], [64, 16], [8 * C, 32], [1, 64]])),
                "dma_c1x", f"{bq0}_vf")
        p.block.sync(ldvf)

        for ck in range(16):
            p.reserve("s", f"{bq0}_qkev_{ck}")
        for ck in range(16):
            p.reserve("s", f"{bq0}_kkev_{ck}")

        def tn_qkf(tensor, j=j):
            p.wait(tensor, f"{bq0}_xb")
            if j == 0:
                p.wait(tensor, wqev)
                p.wait(tensor, wkev)
            xv = xblk.ap().rearrange("p (ct t) -> p ct t", ct=NCT)
            for jq, w16b in ((0, wq16), (1, wk16)):
                wvw = w16b.ap().rearrange("p (ct o) -> p ct o", ct=NCT)
                for ck in range(16):
                    oi = jq * 16 + ck
                    if oi >= 2:
                        pj, pck = divmod(oi - 2, 16)
                        p.wait(tensor, f"{bq0}_qkev_{pck}" if pj == 0
                               else f"{bq0}_kkev_{pck}")
                    elif j > 0:
                        p.wait(tensor, f"c1h{j-1}s{NST-1}_ex_{15}")
                    else:
                        p.wait(tensor, f"c1_vev_{31}")
                    for ct in range(NCT):
                        inst = tensor.matmul(
                            pp[oi % 2].ap()[0:64, 0:256],
                            wvw[:, ct, ck * 64:(ck + 1) * 64],
                            xv[:, ct, :],
                            start=(ct == 0), stop=(ct == NCT - 1))
                    p.inc(inst, "t", 1,
                          f"{bq0}_qkmm_{ck}" if jq == 0 else f"{bq0}_kkmm_{ck}")
        p.block.tensor(tn_qkf)

        def sc_qkf(scalar, j=j):
            bqo, _ = _SV["l1_bq"]
            bko, _ = _SV["l1_bk"]
            for jq, dstF, bo in ((0, qF, bqo), (1, kF, bko)):
                for ck in range(16):
                    p.wait(scalar, f"{bq0}_qkmm_{ck}" if jq == 0 else f"{bq0}_kkmm_{ck}")
                    oi = jq * 16 + ck
                    dst = bass.AP(dstF, ck, [[T, 64], [16, 256]])
                    bias_ap = sv[(ck % 2) * 64:(ck % 2) * 64 + 64,
                                 (bqo if jq == 0 else bko) + ck // 2:
                                 (bqo if jq == 0 else bko) + ck // 2 + 1]
                    p.inc_reserved(
                        scalar.activation(out=dst, in_=pp[oi % 2].ap()[0:64, 0:256],
                                          func=AF.Identity, bias=bias_ap),
                        f"{bq0}_qkev_{ck}" if jq == 0 else f"{bq0}_kkev_{ck}")
            p.ev[f"{bq0}_qkf"] = p.ev[f"{bq0}_kkev_{15}"]
        p.block.scalar(sc_qkf)

        def ldH(sync, j=j):
            if j > 0:
                p.wait(sync, f"c1_Gbuilt_{j-1}")
            if "c0_gst" in p.ev:
                p.wait(sync, "c0_gst")
            p.mark_dma(sync.dma_start(
                out=Hb[:],
                in_=bass.AP(g16d.tensor, j * 8191, [[1, 128], [1, GW]])),
                "dma_c1g", f"c1_H_{j}")
        p.block.sync(ldH)

        def vec_G(vector, j=j):
            p.wait(vector, f"c1_H_{j}")
            if j >= 1:
                p.wait(vector, f"c1_Gfree_{j-1}")
            p.inc(vector.tensor_copy(out=G[:], in_=Hb[:, ::-1]),
                  "v", 1, f"c1_Gbuilt_{j}")
        p.block.vector(vec_G)

        vt3 = vones.ap().rearrange("p (kt s) -> p kt s", kt=32)

        for st_i in range(NST):
            q0 = st_i * 512
            hq = f"c1h{j}s{st_i}"
            NKB = 16
            for kb in range(NKB):
                p.reserve("s", f"{hq}_ex_{kb}")
            p.reserve("v", f"{hq}_rc")
            p.reserve("v", f"{hq}_fin")
            p.reserve("v", f"{hq}_tr")

            def tn_att(tensor, j=j, st_i=st_i, q0=q0, hq=hq, NKB=NKB):
                if st_i == 0:
                    p.wait(tensor, f"c1_Gbuilt_{j}")
                    p.wait(tensor, f"{bq0}_vf")
                    p.wait(tensor, f"{bq0}_qkf")
                for kb in range(NKB):
                    if kb >= 2:
                        p.wait(tensor, f"{hq}_ex_{kb-2}")
                    elif j + st_i > 0:
                        pv_ = f"c1h{j}s{st_i-1}" if st_i else f"c1h{j-1}s{NST-1}"
                        p.wait(tensor, f"{pv_}_ex_{14+kb}")
                    for kt in (2 * kb, 2 * kb + 1):
                        inst = tensor.matmul(
                            pp[kt % 4][:],
                            kF.ap()[:, kt * 128:(kt + 1) * 128],
                            qF.ap()[:, q0:q0 + 512],
                            start=True, stop=True)
                    p.inc(inst, "t", 1, f"{hq}_lgb_{kb}")
                if j + st_i > 0:
                    prev = f"c1h{j}s{st_i-1}_fin" if st_i else \
                        f"c1h{j-1}s{NST-1}_fin"
                    p.wait(tensor, prev)
                p.wait(tensor, f"{hq}_ex_{NKB-1}")
                ev = Efull.ap().rearrange("p (kt t) -> p kt t", kt=32)
                for kt in range(32):
                    tensor.matmul(pE[:], vt3[:, kt, 0:65], ev[:, kt, :],
                                  start=(kt == 0), stop=(kt == 31))
                    s = q0 - kt * 128 + 3968
                    inst = tensor.matmul(pB[:], vt3[:, kt, 0:64],
                                         G[:, s:s + 512],
                                         start=(kt == 0), stop=(kt == 31))
                p.inc(inst, "t", 1, f"{hq}_avd")
            p.block.tensor(tn_att)

            def sc_exp(scalar, hq=hq, NKB=NKB):
                ev = Efull.ap().rearrange("p (kt t) -> p kt t", kt=32)
                for kb in range(NKB):
                    p.wait(scalar, f"{hq}_lgb_{kb}")
                    for kt in (2 * kb, 2 * kb + 1):
                        inst = scalar.activation(out=ev[:, kt, :],
                                                 in_=pp[kt % 4][:], func=AF.Exp)
                    p.inc_reserved(inst, f"{hq}_ex_{kb}")
            p.block.scalar(sc_exp)

            def vec_rc(vector, hq=hq):
                p.wait(vector, f"{hq}_avd")
                p.inc_reserved(vector.reciprocal(r32[:], pE[64:65, :]), f"{hq}_rc")
            p.block.vector(vec_rc)

            def tn_rep(tensor, hq=hq):
                p.wait(tensor, f"{hq}_rc")
                p.inc(tensor.matmul(prep2[:], ones64[:], r32[:],
                                    start=True, stop=True),
                      "t", 1, f"{hq}_rp")
            p.block.tensor(tn_rep)

            def vec_att(vector, j=j, st_i=st_i, hq=hq, q0=q0):
                p.wait(vector, f"{hq}_rp")
                k = j * NST + st_i
                if k >= 2:
                    p.wait(vector, f"c1_ast_{k-2}")
                vector.tensor_copy(out=Rr32[:], in_=prep2[:])
                vector.tensor_tensor(out=tE32[:], in0=pE[0:64, :],
                                     in1=Rr32[:], op=ALU.mult)
                si = (st_i % 2) * 512
                p.inc_reserved(
                    vector.tensor_tensor(out=att_s[:, si:si + 512],
                                         in0=tE32[:], in1=pB[:], op=ALU.add),
                    f"{hq}_fin")
                ai = (st_i % 2) * 256
                for br in range(2):
                    for bc in range(16):
                        pbase2 = 32 * (bc % 4)
                        col = ai + (bc // 4) * 64 + br * 32
                        inst = vector.transpose(
                            out=atok.ap()[pbase2:pbase2 + 32, col:col + 32],
                            in_=att_s.ap()[br * 32:(br + 1) * 32,
                                           si + bc * 32:si + bc * 32 + 32])
                p.inc_reserved(inst, f"{hq}_tr")
            p.block.vector(vec_att)

            def st_att(sync, j=j, st_i=st_i, hq=hq, q0=q0):
                p.wait(sync, f"{hq}_tr")
                k = j * NST + st_i
                ai = (st_i % 2) * 256
                p.mark_dma(sync.dma_start(
                    out=bass.AP(attTok.tensor, j * 256 * C + q0 * 64,
                                [[64, 128], [8192, 4], [1, 64]]),
                    in_=bass.AP(atok, ai, [[512, 128], [64, 4], [1, 64]])),
                    "dma_c1s", f"c1_ast_{k}")
            p.block.sync(st_att)

        p.ev[f"c1_Gfree_{j}"] = p.ev[f"c1h{j}s{NST-1}_fin"]
        p.ev[f"{bq0}_avdone"] = p.ev[f"c1h{j}s{NST-1}_avd"]

    def fin(sync):
        p.wait(sync, f"c1_ast_{16*NST-1}")
    p.block.sync(fin)



def emit_C3(p):
    TC = 512
    NCH = T // TC
    attTok = p.dram["att_tok_d"]
    xatt = fm_view(p.dram["xatt_d"])
    abd = fm_view(p.dram["ab_d"])
    ab16d = fm_view(p.dram["ab16_d"])
    sv = p.sv32

    wo16, woev = p.load_weight_cast("c3_wo", p.dram["l1wo_t"], NCT, C, "dma_c3w",
                                    stage_cols=4096, dma_eng="gpsimd")

    atokc = p.sb("c3_atok", [128, 4 * C], BF16)
    ident = p.sb("c3_ident", [128, 128], BF16)
    sa16 = p.sb("c3_sa16", [128, NCT * TC], BF16)
    sx = p.sb("c3_sx", [128, NCT * TC], F32)
    t32 = p.sb("c3_t32", [128, NCT * TC], F32)
    tmp32 = p.sb("c3_tmp", [128, NCT * TC], F32)
    sc16 = p.sb("c3_sc16", [128, NCT * TC], BF16)
    sq16 = p.sb("c3_sq16", [128, NCT * TC], BF16)
    o16 = p.sb("c3_o16", [128, NCT * TC], BF16)
    rowbuf = p.sb("c3_rows", [1, 6 * TC], F32)

    psA = p.ps("c3_psA", [1, TC])
    psB = p.ps("c3_psB", [1, TC])
    prep = p.ps("c3_prep", [128, 2 * TC])
    po = [p.ps(f"c3_po{j}", [128, TC]) for j in range(2)]
    pt16 = [p.ps(f"c3_pt16{j}", [128, 128], BF16) for j in range(2)]

    def r3(buf):
        return buf.ap().rearrange("p (ct t) -> p ct t", ct=NCT)

    def gid(gpsimd):
        gpsimd.memset(ident[:], 0.0)
        import concourse.mybir as _mb
        inst = gpsimd.affine_select(out=ident[:], in_=ident[:],
                                    compare_op=_mb.AluOpType.not_equal,
                                    fill=1.0, base=0, pattern=[[-1, 128]],
                                    channel_multiplier=1)
        p.inc(inst, "g", 1, "c3_id")
    p.block.gpsimd(gid)

    for i in range(NCH):
        tsl = slice(i * TC, (i + 1) * TC)

        def ld(sync, i=i, tsl=tsl):
            if i > 0:
                p.wait(sync, f"c3_tp_{i-1}_{31}")
                p.wait(sync, f"c3_t_{i-1}")
            p.mark_dma(sync.dma_start(
                out=atokc.ap().rearrange("p (tt c) -> p tt c", tt=4),
                in_=attTok[i * TC:(i + 1) * TC, :].rearrange(
                    "(tt p) c -> p tt c", p=128)),
                "dma_c3", None)
            p.mark_dma(sync.dma_start(out=r3(sx), in_=xatt[:, :, tsl]),
                       "dma_c3", f"c3_x_{i}")
        p.block.sync(ld)

        for z in range(32):
            p.reserve("v", f"c3_tev_{i}_{z}")

        def tn_tp(tensor, i=i):
            p.wait(tensor, f"c3_x_{i}")
            if i == 0:
                p.wait(tensor, "c3_id")
            av = atokc.ap().rearrange("p (tt c) -> p tt c", tt=4)
            z = 0
            for tt in range(4):
                for ct in range(NCT):
                    if z >= 2:
                        p.wait(tensor, f"c3_tev_{i}_{z-2}")
                    elif i > 0:
                        p.wait(tensor, f"c3_tev_{i-1}_{30+z}")
                    inst = tensor.transpose(
                        pt16[z % 2][:],
                        av[:, tt, ct * 128:(ct + 1) * 128],
                        ident[:])
                    p.inc(inst, "t", 1, f"c3_tp_{i}_{z}")
                    z += 1
        p.block.tensor(tn_tp)

        def vec_tp(vector, i=i):
            sa = r3(sa16)
            z = 0
            for tt in range(4):
                for ct in range(NCT):
                    p.wait(vector, f"c3_tp_{i}_{z}")
                    p.inc_reserved(
                        vector.tensor_copy(
                            out=sa[:, ct, tt * 128:(tt + 1) * 128],
                            in_=pt16[z % 2][:]),
                        f"c3_tev_{i}_{z}")
                    z += 1
        p.block.vector(vec_tp)

        for ot in range(NCT):
            p.reserve("s", f"c3_ev_{i}_{ot}")

        def tn_mm(tensor, i=i):
            if i == 0:
                p.wait(tensor, woev)
            p.wait(tensor, f"c3_tev_{i}_{31}")
            wv = wo16.ap().rearrange("p (ct o) -> p ct o", ct=NCT)
            av = r3(sa16)
            for ot in range(NCT):
                if ot >= 2:
                    p.wait(tensor, f"c3_ev_{i}_{ot-2}")
                elif i > 0:
                    p.wait(tensor, f"c3_ev_{i-1}_{NCT-2+ot}")
                for ct in range(NCT):
                    inst = tensor.matmul(po[ot % 2][:], wv[:, ct, ot * 128:(ot + 1) * 128],
                                         av[:, ct, :],
                                         start=(ct == 0), stop=(ct == NCT - 1))
                p.inc(inst, "t", 1, f"c3_mm_{i}_{ot}")
        p.block.tensor(tn_mm)

        def sc_ev(scalar, i=i):
            bo, _ = _SV["l1_bo"]
            t3 = r3(t32)
            for ot in range(NCT):
                p.wait(scalar, f"c3_mm_{i}_{ot}")
                p.inc_reserved(scalar.activation(out=t3[:, ot, :], in_=po[ot % 2][:],
                                                 func=AF.Identity,
                                                 bias=sv[:, bo + ot:bo + ot + 1]),
                               f"c3_ev_{i}_{ot}")
        p.block.scalar(sc_ev)

        def vec_t(vector, i=i):
            p.wait(vector, f"c3_ev_{i}_{NCT-1}")
            p.inc(vector.tensor_tensor(out=t32[:], in0=t32[:], in1=sx[:], op=ALU.add),
                  "v", 1, f"c3_t_{i}")
        p.block.vector(vec_t)

        emit_stats(p, f"c3l{i}", t32, sc16, sq16, NCT, TC, psA, psB,
                   rowbuf, prep, [f"c3_t_{i}"], no_eps=True)

        def vec_app(vector, i=i):
            p.wait(vector, f"c3l{i}_prep")
            if i > 0:
                p.wait(vector, f"c3_st_{i-1}")
            Mb = prep.ap()[:, 0:TC].unsqueeze(1).broadcast_to([128, NCT, TC])
            Rb = prep.ap()[:, TC:2 * TC].unsqueeze(1).broadcast_to([128, NCT, TC])
            vector.tensor_tensor(out=r3(tmp32), in0=r3(t32), in1=Mb, op=ALU.subtract)
            vector.tensor_tensor(out=r3(tmp32), in0=r3(tmp32), in1=Rb, op=ALU.mult)
            p.inc(vector.tensor_copy(out=o16[:], in_=tmp32[:]), "v", 1, f"c3_app_{i}")
        p.block.vector(vec_app)

        def st(sync, i=i, tsl=tsl):
            p.wait(sync, f"c3_app_{i}")
            p.mark_dma(sync.dma_start(out=abd[:, :, tsl], in_=r3(tmp32)),
                       "dma_c3s", None)
            p.mark_dma(sync.dma_start(out=ab16d[:, :, tsl], in_=r3(o16)),
                       "dma_c3s", f"c3_st_{i}")
        p.block.sync(st)

    def fin(sync):
        p.wait(sync, f"c3_st_{NCH-1}")
    p.block.sync(fin)



def emit_C4(p):
    TC = 512
    NCH = T // TC
    ab16d = fm_view(p.dram["ab16_d"])
    abd = fm_view(p.dram["ab_d"])
    res2 = fm_view(p.dram["res2_d"])
    res3 = fm_view(p.dram["res3_d"])
    sv = p.sv32

    w116, w1ev = p.load_weight_cast("c4_w1", p.dram["l1w1_t"], NCT, C, "dma_c4w",
                                    stage_cols=4096, dma_eng="gpsimd")
    w216, w2ev = p.load_weight_cast("c4_w2", p.dram["l1w2_t"], NCT, C, "dma_c4w",
                                    stage_cols=4096, dma_eng="gpsimd")

    sa16 = p.sb("c4_sa16", [128, NCT * TC], BF16)
    sab = p.sb("c4_sab", [128, NCT * TC], F32)
    sr2 = p.sb("c4_sr2", [128, NCT * TC], F32)
    f116 = p.sb("c4_f116", [128, NCT * TC], BF16)
    t32 = p.sb("c4_t32", [128, NCT * TC], F32)
    tmp32 = p.sb("c4_tmp", [128, NCT * TC], F32)
    sc16 = p.sb("c4_sc16", [128, NCT * TC], BF16)
    sq16 = p.sb("c4_sq16", [128, NCT * TC], BF16)
    rowbuf = p.sb("c4_rows", [1, 6 * TC], F32)

    psA = p.ps("c4_psA", [1, TC])
    psB = p.ps("c4_psB", [1, TC])
    prep = p.ps("c4_prep", [128, 2 * TC])
    po = [p.ps(f"c4_po{j}", [128, TC]) for j in range(2)]

    def r3(buf):
        return buf.ap().rearrange("p (ct t) -> p ct t", ct=NCT)

    for i in range(NCH):
        tsl = slice(i * TC, (i + 1) * TC)

        def ld(sync, i=i, tsl=tsl):
            if i > 0:
                p.wait(sync, f"c4_f1mm_{i-1}_{NCT-1}")
                p.wait(sync, f"c4_out_{i-1}")
            p.mark_dma(sync.dma_start(out=r3(sa16), in_=ab16d[:, :, tsl]),
                       "dma_c4", None)
            p.mark_dma(sync.dma_start(out=r3(sab), in_=abd[:, :, tsl]),
                       "dma_c4", None)
            p.mark_dma(sync.dma_start(out=r3(sr2), in_=res2[:, :, tsl]),
                       "dma_c4", f"c4_x_{i}")
        p.block.sync(ld)

        for ot in range(NCT):
            p.reserve("s", f"c4_f1ev_{i}_{ot}")

        def tn_f1(tensor, i=i):
            p.wait(tensor, f"c4_x_{i}")
            if i == 0:
                p.wait(tensor, w1ev)
            wv = w116.ap().rearrange("p (ct o) -> p ct o", ct=NCT)
            av = r3(sa16)
            for ot in range(NCT):
                if ot >= 2:
                    p.wait(tensor, f"c4_f1ev_{i}_{ot-2}")
                elif i > 0:
                    p.wait(tensor, f"c4_f1ev_{i-1}_{NCT-2+ot}")
                for ct in range(NCT):
                    inst = tensor.matmul(po[ot % 2][:], wv[:, ct, ot * 128:(ot + 1) * 128],
                                         av[:, ct, :],
                                         start=(ct == 0), stop=(ct == NCT - 1))
                p.inc(inst, "t", 1, f"c4_f1mm_{i}_{ot}")
        p.block.tensor(tn_f1)

        def sc_f1(scalar, i=i):
            b1, _ = _SV["l1_b1"]
            fv = r3(f116)
            for ot in range(NCT):
                p.wait(scalar, f"c4_f1mm_{i}_{ot}")
                p.inc_reserved(scalar.activation(out=fv[:, ot, :], in_=po[ot % 2][:],
                                                 func=AF.Identity,
                                                 bias=sv[:, b1 + ot:b1 + ot + 1]),
                               f"c4_f1ev_{i}_{ot}")
        p.block.scalar(sc_f1)

        for ot in range(NCT):
            p.reserve("v", f"c4_f2ev_{i}_{ot}")

        def tn_f2(tensor, i=i):
            if i == 0:
                p.wait(tensor, w2ev)
            p.wait(tensor, f"c4_f1ev_{i}_{NCT-1}")
            wv = w216.ap().rearrange("p (ct o) -> p ct o", ct=NCT)
            fv = r3(f116)
            for ot in range(NCT):
                if ot >= 2:
                    p.wait(tensor, f"c4_f2ev_{i}_{ot-2}")
                elif i > 0:
                    p.wait(tensor, f"c4_f2ev_{i-1}_{NCT-2+ot}")
                for ct in range(NCT):
                    inst = tensor.matmul(po[ot % 2][:], wv[:, ct, ot * 128:(ot + 1) * 128],
                                         fv[:, ct, :],
                                         start=(ct == 0), stop=(ct == NCT - 1))
                p.inc(inst, "t", 1, f"c4_f2mm_{i}_{ot}")
        p.block.tensor(tn_f2)

        def vec_f2(vector, i=i):
            b2, _ = _SV["l1_b2"]
            t3 = r3(t32)
            ab3 = r3(sab)
            for ot in range(NCT):
                p.wait(vector, f"c4_f2mm_{i}_{ot}")
                p.inc_reserved(
                    vector.scalar_tensor_tensor(out=t3[:, ot, :], in0=po[ot % 2][:],
                                                scalar=sv[:, b2 + ot:b2 + ot + 1],
                                                in1=ab3[:, ot, :],
                                                op0=ALU.add, op1=ALU.add),
                    f"c4_f2ev_{i}_{ot}")
            p.ev[f"c4_t_{i}"] = p.ev[f"c4_f2ev_{i}_{NCT-1}"]
        p.block.vector(vec_f2)

        emit_stats(p, f"c4l{i}", t32, sc16, sq16, NCT, TC, psA, psB,
                   rowbuf, prep, [f"c4_t_{i}"], no_eps=True)

        def vec_app(vector, i=i):
            p.wait(vector, f"c4l{i}_prep")
            if i > 0:
                p.wait(vector, f"c4_st_{i-1}")
            Mb = prep.ap()[:, 0:TC].unsqueeze(1).broadcast_to([128, NCT, TC])
            Rb = prep.ap()[:, TC:2 * TC].unsqueeze(1).broadcast_to([128, NCT, TC])
            vector.tensor_tensor(out=r3(tmp32), in0=r3(t32), in1=Mb, op=ALU.subtract)
            vector.tensor_tensor(out=r3(tmp32), in0=r3(tmp32), in1=Rb, op=ALU.mult)
            p.inc(vector.tensor_tensor(out=tmp32[:], in0=tmp32[:], in1=sr2[:],
                                       op=ALU.add),
                  "v", 1, f"c4_out_{i}")
        p.block.vector(vec_app)

        def st(sync, i=i, tsl=tsl):
            p.wait(sync, f"c4_out_{i}")
            p.mark_dma(sync.dma_start(out=res3[:, :, tsl], in_=r3(tmp32)),
                       "dma_c4s", f"c4_st_{i}")
        p.block.sync(st)

    def fin(sync):
        p.wait(sync, f"c4_st_{NCH-1}")
    p.block.sync(fin)



DV = {"om_tmk": 0, "om_tmv": 8, "om_tmr": 16, "om_cmk": 24, "om_cmr": 32,
      "lam": 40, "eu": 48, "scr": 56}


def emit_derived(p):
    if getattr(p, "derived_done", False):
        return
    p.derived_done = True
    sv = p.sv32
    O = SVW

    def vec(vector):
        p.wait(vector, "pre_ready")
        for nm, src in [("om_tmk", "tmk"), ("om_tmv", "tmv"), ("om_tmr", "tmr"),
                        ("om_cmk", "cmk"), ("om_cmr", "cmr")]:
            o1, _ = _SV[src]
            inst = vector.tensor_scalar(out=sv[:, O + DV[nm]:O + DV[nm] + 8],
                                        in0=sv[:, o1:o1 + 8], scalar1=-1.0,
                                        scalar2=1.0, op0=ALU.mult, op1=ALU.add)
        p.inc(inst, "v", 1, "derived_v")
    p.block.vector(vec)

    def sc(scalar):
        p.wait(scalar, "pre_ready")
        td, _ = _SV["tdecay"]
        tf, _ = _SV["tfirst"]
        scalar.activation(out=sv[:, O + DV["scr"]:O + DV["scr"] + 8],
                          in_=sv[:, td:td + 8], func=AF.Exp)
        scalar.activation(out=sv[:, O + DV["lam"]:O + DV["lam"] + 8],
                          in_=sv[:, O + DV["scr"]:O + DV["scr"] + 8],
                          func=AF.Exp, scale=-1.0)
        inst = scalar.activation(out=sv[:, O + DV["eu"]:O + DV["eu"] + 8],
                                 in_=sv[:, tf:tf + 8], func=AF.Exp)
        p.inc(inst, "s", 1, "derived_s")
    p.block.scalar(sc)



def emit_A1(p):
    TC = 256
    NCH = T // TC
    xfm = fm_view(p.dram["x_fm"])
    x2v = fm_view(p.dram["x2_d"])
    xsv = fm_view(p.dram["xsum_d"])
    emit_derived(p)
    sv = p.sv32
    O = SVW

    wk16, wkev = p.load_weight_cast("a1_wk", p.dram["wk_t"], NCT, C, "dma_a1w",
                                    stage_cols=4096, dma_eng="gpsimd")
    wv16, wvev = p.load_weight_cast("a1_wv", p.dram["wv_t"], NCT, C, "dma_a1w",
                                    stage_cols=4096, dma_eng="gpsimd")
    wr16, wrev = p.load_weight_cast("a1_wr", p.dram["wr_t"], NCT, C, "dma_a1w",
                                    stage_cols=4096, dma_eng="gpsimd")
    wo16, woev = p.load_weight_cast("a1_wo", p.dram["wo_t"], NCT, C, "dma_a1w",
                                    stage_cols=4096, dma_eng="gpsimd")

    sx = p.sb("a1_sx", [128, NCT * TC], F32)
    tmp32 = p.sb("a1_tmp", [128, NCT * TC], F32)
    sc16 = p.sb("a1_sc16", [128, NCT * TC], BF16)
    sq16 = p.sb("a1_sq16", [128, NCT * TC], BF16)
    x1 = p.sb("a1_x1", [128, NCT * TC], F32)
    hpad = p.sb("a1_hpad", [128, NCT * (TC + 1)], F32)
    mixk = p.sb("a1_mixk", [128, NCT * TC], BF16)
    mixv = p.sb("a1_mixv", [128, NCT * TC], BF16)
    mixr = p.sb("a1_mixr", [128, NCT * TC], BF16)
    ek = p.sb("a1_ek", [128, NCT * TC], F32)
    v32 = p.sb("a1_v32", [128, NCT * TC], F32)
    sr = p.sb("a1_sr", [128, NCT * TC], F32)
    out32 = p.sb("a1_out", [128, NCT * TC], F32)
    Apad = p.sb("a1_Apad", [128, NCT * (TC + 1)], F32)
    Bpad = p.sb("a1_Bpad", [128, NCT * (TC + 1)], F32)
    rowbuf = p.sb("a1_rows", [1, 6 * TC], F32)

    psA = p.ps("a1_psA", [1, TC])
    psB = p.ps("a1_psB", [1, TC])
    prep = p.ps("a1_prep", [128, 2 * TC])
    pkv = p.ps("a1_pkv", [128, NCT * TC])

    hp = hpad.ap().rearrange("p (ct t) -> p ct t", ct=NCT)
    Av = Apad.ap().rearrange("p (ct t) -> p ct t", ct=NCT)
    Bv = Bpad.ap().rearrange("p (ct t) -> p ct t", ct=NCT)

    def init(vector):
        vector.memset(hpad[:], 0.0)
        vector.memset(Apad[:], 0.0)
        p.inc(vector.memset(Bpad[:], 0.0), "v", 1, "a1_init")
    p.block.vector(init)

    def r3(buf):
        return buf.ap().rearrange("p (ct t) -> p ct t", ct=NCT)

    for i in range(NCH):
        tsl = slice(i * TC, (i + 1) * TC)

        def ld(sync, i=i, tsl=tsl):
            if i > 0:
                p.wait(sync, f"a1_xs_{i-1}")
            p.mark_dma(sync.dma_start(out=r3(sx), in_=xfm[:, :, tsl]),
                       "dma_a1x", f"a1_x_{i}")
        p.block.sync(ld)

        emit_stats(p, f"a1o{i}", sx, sc16, sq16, NCT, TC, psA, psB,
                   rowbuf, prep, [f"a1_x_{i}"] + (["a1_init", "derived_v", "derived_s"]
                                                  if i == 0 else []),
                   double=True)

        def vec_x1(vector, i=i):
            p.wait(vector, f"a1o{i}_prep")
            Mb = prep.ap()[:, 0:TC].unsqueeze(1).broadcast_to([128, NCT, TC])
            Rb = prep.ap()[:, TC:2 * TC].unsqueeze(1).broadcast_to([128, NCT, TC])
            vector.tensor_tensor(out=r3(tmp32), in0=r3(sx), in1=Mb, op=ALU.subtract)
            vector.tensor_tensor(out=r3(tmp32), in0=r3(tmp32), in1=Rb, op=ALU.mult)
            g0, _ = _SV["ln0_w"]
            b0, _ = _SV["ln0_b"]
            x1v = r3(x1)
            t3 = r3(tmp32)
            for ct in range(NCT):
                inst = vector.tensor_scalar(out=x1v[:, ct, :], in0=t3[:, ct, :],
                                            scalar1=sv[:, g0 + ct:g0 + ct + 1],
                                            scalar2=sv[:, b0 + ct:b0 + ct + 1],
                                            op0=ALU.mult, op1=ALU.add)
            p.inc(inst, "v", 1, f"a1_x1_{i}")
        p.block.vector(vec_x1)

        emit_stats(p, f"a1l{i}", x1, sc16, sq16, NCT, TC, psA, psB,
                   rowbuf, prep, [f"a1_x1_{i}"])

        def vec_h(vector, i=i):
            p.wait(vector, f"a1l{i}_prep")
            Mb = prep.ap()[:, 0:TC].unsqueeze(1).broadcast_to([128, NCT, TC])
            Rb = prep.ap()[:, TC:2 * TC].unsqueeze(1).broadcast_to([128, NCT, TC])
            vector.tensor_tensor(out=r3(tmp32), in0=r3(x1), in1=Mb, op=ALU.subtract)
            vector.tensor_tensor(out=r3(tmp32), in0=r3(tmp32), in1=Rb, op=ALU.mult)
            g1, _ = _SV["ln1_w"]
            b1, _ = _SV["ln1_b"]
            t3 = r3(tmp32)
            for ct in range(NCT):
                vector.tensor_scalar(out=hp[:, ct, 1:TC + 1], in0=t3[:, ct, :],
                                     scalar1=sv[:, g1 + ct:g1 + ct + 1],
                                     scalar2=sv[:, b1 + ct:b1 + ct + 1],
                                     op0=ALU.mult, op1=ALU.add)
            for mbuf, tm, om in [(mixk, "tmk", "om_tmk"), (mixv, "tmv", "om_tmv"),
                                 (mixr, "tmr", "om_tmr")]:
                to, _ = _SV[tm]
                oo = O + DV[om]
                mv = r3(mbuf)
                for ct in range(NCT):
                    vector.tensor_scalar(out=t3[:, ct, :], in0=hp[:, ct, 0:TC],
                                         scalar1=sv[:, oo + ct:oo + ct + 1],
                                         scalar2=None, op0=ALU.mult)
                    vector.scalar_tensor_tensor(out=mv[:, ct, :], in0=hp[:, ct, 1:TC + 1],
                                                scalar=sv[:, to + ct:to + ct + 1],
                                                in1=t3[:, ct, :],
                                                op0=ALU.mult, op1=ALU.add)
            inst = vector.tensor_copy(out=hp[:, :, 0:1], in_=hp[:, :, TC:TC + 1])
            p.inc(inst, "v", 1, f"a1_mix_{i}")
        p.block.vector(vec_h)

        p.reserve("s", f"a1_ek_{i}")
        p.reserve("s", f"a1_v_{i}")
        p.reserve("s", f"a1_sr_{i}")

        def proj(tensor, lhsT16, rhs_buf, wait_evs, ev, i=i):
            for e in wait_evs:
                p.wait(tensor, e)
            wv = lhsT16.ap().rearrange("p (ct o) -> p ct o", ct=NCT)
            rv = r3(rhs_buf)
            pv = pkv.ap().rearrange("p (ot t) -> p ot t", ot=NCT)
            for ot in range(NCT):
                for ct in range(NCT):
                    inst = tensor.matmul(pv[:, ot, :], wv[:, ct, ot * 128:(ot + 1) * 128],
                                         rv[:, ct, :],
                                         start=(ct == 0), stop=(ct == NCT - 1))
            p.inc(inst, "t", 1, ev)

        def tn_k(tensor, i=i):
            evs = [f"a1_mix_{i}"]
            if i == 0:
                evs.append(wkev)
            else:
                evs.append(f"a1_x2_{i-1}")
            proj(tensor, wk16, mixk, evs, f"a1_kmm_{i}", i)
        p.block.tensor(tn_k)

        def sc_k(scalar, i=i):
            p.wait(scalar, f"a1_kmm_{i}")
            p.inc_reserved(scalar.activation(out=ek[:], in_=pkv[:], func=AF.Exp),
                           f"a1_ek_{i}")
        p.block.scalar(sc_k)

        def tn_v(tensor, i=i):
            evs = [f"a1_ek_{i}"] + ([wvev] if i == 0 else [])
            proj(tensor, wv16, mixv, evs, f"a1_vmm_{i}", i)
        p.block.tensor(tn_v)

        def sc_v(scalar, i=i):
            p.wait(scalar, f"a1_vmm_{i}")
            p.inc_reserved(scalar.copy(out=v32[:], in_=pkv[:]), f"a1_v_{i}")
        p.block.scalar(sc_v)

        def tn_r(tensor, i=i):
            evs = [f"a1_v_{i}"] + ([wrev] if i == 0 else [])
            proj(tensor, wr16, mixr, evs, f"a1_rmm_{i}", i)
        p.block.tensor(tn_r)

        def sc_r(scalar, i=i):
            p.wait(scalar, f"a1_rmm_{i}")
            p.inc_reserved(scalar.activation(out=sr[:], in_=pkv[:], func=AF.Sigmoid),
                           f"a1_sr_{i}")
        p.block.scalar(sc_r)

        def vec_wkv(vector, i=i):
            p.wait(vector, f"a1_ek_{i}")
            p.wait(vector, f"a1_v_{i}")
            if i > 0:
                p.wait(vector, f"a1_st_{i-1}")
            vector.tensor_tensor(out=out32[:], in0=ek[:], in1=v32[:], op=ALU.mult)
            lam = O + DV["lam"]
            eu = O + DV["eu"]
            o3 = r3(out32)
            e3 = r3(ek)
            for ct in range(NCT):
                vector.tensor_tensor_scan(
                    out=Av[:, ct, 1:TC + 1],
                    data0=sv[:, lam + ct:lam + ct + 1].broadcast_to([128, TC]),
                    data1=o3[:, ct, :], initial=Av[:, ct, 0:1],
                    op0=ALU.mult, op1=ALU.add)
            for ct in range(NCT):
                vector.tensor_tensor_scan(
                    out=Bv[:, ct, 1:TC + 1],
                    data0=sv[:, lam + ct:lam + ct + 1].broadcast_to([128, TC]),
                    data1=e3[:, ct, :], initial=Bv[:, ct, 0:1],
                    op0=ALU.mult, op1=ALU.add)
            n3 = r3(v32)
            for ct in range(NCT):
                vector.scalar_tensor_tensor(out=n3[:, ct, :], in0=o3[:, ct, :],
                                            scalar=sv[:, eu + ct:eu + ct + 1],
                                            in1=Av[:, ct, 0:TC],
                                            op0=ALU.mult, op1=ALU.add)
            t3 = r3(tmp32)
            for ct in range(NCT):
                inst = vector.scalar_tensor_tensor(out=t3[:, ct, :], in0=e3[:, ct, :],
                                                   scalar=sv[:, eu + ct:eu + ct + 1],
                                                   in1=Bv[:, ct, 0:TC],
                                                   op0=ALU.mult, op1=ALU.add)
            p.inc(inst, "v", 1, f"a1_den_{i}")
        p.block.vector(vec_wkv)

        def sc_rden(scalar, i=i):
            p.wait(scalar, f"a1_den_{i}")
            scalar.activation(out=tmp32[:], in_=tmp32[:], func=AF.Ln)
            p.inc(scalar.activation(out=tmp32[:], in_=tmp32[:], func=AF.Exp,
                                    scale=-1.0),
                  "s", 1, f"a1_rden_{i}")
        p.block.scalar(sc_rden)

        def vec_y(vector, i=i):
            p.wait(vector, f"a1_rden_{i}")
            p.wait(vector, f"a1_sr_{i}")
            vector.tensor_tensor(out=ek[:], in0=v32[:], in1=tmp32[:], op=ALU.mult)
            vector.tensor_tensor(out=mixk[:], in0=sr[:], in1=ek[:], op=ALU.mult)
            vector.tensor_copy(out=Av[:, :, 0:1], in_=Av[:, :, TC:TC + 1])
            p.inc(vector.tensor_copy(out=Bv[:, :, 0:1], in_=Bv[:, :, TC:TC + 1]),
                  "v", 1, f"a1_satt_{i}")
        p.block.vector(vec_y)

        def tn_wo(tensor, i=i):
            evs = [f"a1_satt_{i}", f"a1_sr_{i}"] + ([woev] if i == 0 else [])
            proj(tensor, wo16, mixk, evs, f"a1_womm_{i}", i)
        p.block.tensor(tn_wo)

        def vec_x2(vector, i=i):
            p.wait(vector, f"a1_womm_{i}")
            vector.tensor_tensor(out=out32[:], in0=pkv[:], in1=x1[:], op=ALU.add)
            p.inc(vector.tensor_tensor(out=ek[:], in0=out32[:], in1=sx[:], op=ALU.add),
                  "v", 1, f"a1_xs_{i}")
            p.ev[f"a1_x2_{i}"] = p.ev[f"a1_xs_{i}"]
        p.block.vector(vec_x2)

        def st(sync, i=i, tsl=tsl):
            p.wait(sync, f"a1_xs_{i}")
            p.mark_dma(sync.dma_start(out=x2v[:, :, tsl], in_=r3(out32)),
                       "dma_a1s", None)
            p.mark_dma(sync.dma_start(out=xsv[:, :, tsl], in_=r3(ek)),
                       "dma_a1s", f"a1_st_{i}")
        p.block.sync(st)

    def fin(sync):
        p.wait(sync, f"a1_st_{NCH-1}")
    p.block.sync(fin)



def emit_A2(p):
    TC = 512
    NCH = T // TC
    NQ = 4
    QH = NHT // NQ
    x2v = fm_view(p.dram["x2_d"])
    xsv = fm_view(p.dram["xsum_d"])
    res1 = fm_view(p.dram["res1_d"])
    wfk16d = p.dram["wfk16_d"]
    wfv16d = p.dram["wfv16_d"]
    emit_derived(p)
    sv = p.sv32
    O = SVW

    wfr16, wfrev = p.load_weight_cast("a2_wfr", p.dram["wfr_t"], NCT, C, "dma_a2w",
                                      stage_cols=4096, dma_eng="gpsimd")

    x2c = p.sb("a2_x2c", [128, NCT * TC], F32)
    tmp32 = p.sb("a2_tmp", [128, NCT * TC], F32)
    sc16 = p.sb("a2_sc16", [128, NCT * TC], BF16)
    sq16 = p.sb("a2_sq16", [128, NCT * TC], BF16)
    h2pad = p.sb("a2_h2pad", [128, NCT * (TC + 1)], F32)
    mixk2 = p.sb("a2_mixk2", [128, NCT * TC], BF16)
    mixr2 = p.sb("a2_mixr2", [128, NCT * TC], BF16)
    kf16 = p.sb("a2_kf16", [128, QH * TC], BF16)
    kv32 = p.sb("a2_kv32", [128, NCT * TC], F32)
    xs = p.sb("a2_xs", [128, NCT * TC], F32)
    wfkq = p.sb("a2_wfkq", [128, NCT * 1024], BF16)
    wfvq = p.sb("a2_wfvq", [128, QH * 1024], BF16)
    rowbuf = p.sb("a2_rows", [1, 6 * TC], F32)

    psA = p.ps("a2_psA", [1, TC])
    psB = p.ps("a2_psB", [1, TC])
    prep = p.ps("a2_prep", [128, 2 * TC])
    pkf = [p.ps(f"a2_pkf{j}", [128, TC]) for j in range(2)]
    pkv = [p.ps(f"a2_pkv{j}", [128, TC]) for j in range(2)]

    hp = h2pad.ap().rearrange("p (ct t) -> p ct t", ct=NCT)

    def r3(buf, n=NCT):
        return buf.ap().rearrange("p (ct t) -> p ct t", ct=n)

    def init(vector):
        p.inc(vector.memset(h2pad[:], 0.0), "v", 1, "a2_init")
    p.block.vector(init)

    def emit_qload(i, q):
        if i == 0:
            _, ev1 = p.load_weight_cast(f"a2fk{q}", p.dram["wfk_t"][:, q * 1024:(q + 1) * 1024],
                                        NCT, 1024, "dma_a2w", stage_cols=4096,
                                        dma_eng="gpsimd", w16=wfkq,
                                        halt_event=f"a2_kfmm_0_{q-1}" if q else None)
            _, ev2 = p.load_weight_cast(f"a2fv{q}", p.dram["wfv_t"][q * 1024:(q + 1) * 1024, :],
                                        QH, C, "dma_a2w", stage_cols=4096,
                                        dma_eng="gpsimd", w16=wfvq,
                                        halt_event=f"a2_kvmm_0_{q-1}" if q else None)
            p.ev[f"a2_fkld_{i}_{q}"] = p.ev[ev1]
            p.ev[f"a2_fvld_{i}_{q}"] = p.ev[ev2]

            def cst(gpsimd, q=q):
                p.wait(gpsimd, ev1)
                p.mark_dma(gpsimd.dma_start(out=wfk16d[:, q * 1024:(q + 1) * 1024]
                                            .rearrange("(ct p) o -> p ct o", p=128),
                                            in_=r3(wfkq)),
                           "dma_a2c", f"a2_fkcst_{q}")
                p.wait(gpsimd, ev2)
                p.mark_dma(gpsimd.dma_start(out=wfv16d[q * 1024:(q + 1) * 1024, :]
                                            .rearrange("(ct p) o -> p ct o", p=128),
                                            in_=r3(wfvq, QH)),
                           "dma_a2c", f"a2_fvcst_{q}")
            p.block.gpsimd(cst)
        else:
            def ldq(gpsimd, i=i, q=q):
                p.wait(gpsimd, f"a2_fkcst_{q}")
                p.wait(gpsimd, f"a2_fvcst_{q}")
                prev = (i, q - 1) if q > 0 else (i - 1, NQ - 1)
                p.wait(gpsimd, f"a2_kfmm_{prev[0]}_{prev[1]}")
                p.mark_dma(gpsimd.dma_start(
                    out=r3(wfkq),
                    in_=wfk16d[:, q * 1024:(q + 1) * 1024].rearrange(
                        "(ct p) o -> p ct o", p=128)),
                    "dma_a2w", f"a2_fkld_{i}_{q}")
                p.wait(gpsimd, f"a2_kvmm_{prev[0]}_{prev[1]}")
                p.mark_dma(gpsimd.dma_start(
                    out=r3(wfvq, QH),
                    in_=wfv16d[q * 1024:(q + 1) * 1024, :].rearrange(
                        "(ct p) o -> p ct o", p=128)),
                    "dma_a2w", f"a2_fvld_{i}_{q}")
            p.block.gpsimd(ldq)

    for i in range(NCH):
        tsl = slice(i * TC, (i + 1) * TC)

        def ld(sync, i=i, tsl=tsl):
            if i > 0:
                p.wait(sync, f"a2_mix_{i-1}")
                p.wait(sync, f"a2_st_{i-1}")
            p.mark_dma(sync.dma_start(out=r3(x2c), in_=x2v[:, :, tsl]),
                       "dma_a2x", f"a2_x2_{i}")
            if i > 0:
                p.wait(sync, f"a2_out_{i-1}")
            p.mark_dma(sync.dma_start(out=r3(xs), in_=xsv[:, :, tsl]),
                       "dma_a2x", f"a2_xs_{i}")
        p.block.sync(ld)

        emit_stats(p, f"a2l{i}", x2c, sc16, sq16, NCT, TC, psA, psB,
                   rowbuf, prep, [f"a2_x2_{i}"] + (["a2_init", "derived_v",
                                                    "derived_s"] if i == 0 else []))

        def vec_h(vector, i=i):
            p.wait(vector, f"a2l{i}_prep")
            Mb = prep.ap()[:, 0:TC].unsqueeze(1).broadcast_to([128, NCT, TC])
            Rb = prep.ap()[:, TC:2 * TC].unsqueeze(1).broadcast_to([128, NCT, TC])
            vector.tensor_tensor(out=r3(tmp32), in0=r3(x2c), in1=Mb, op=ALU.subtract)
            vector.tensor_tensor(out=r3(tmp32), in0=r3(tmp32), in1=Rb, op=ALU.mult)
            g2, _ = _SV["ln2_w"]
            b2, _ = _SV["ln2_b"]
            t3 = r3(tmp32)
            for ct in range(NCT):
                vector.tensor_scalar(out=hp[:, ct, 1:TC + 1], in0=t3[:, ct, :],
                                     scalar1=sv[:, g2 + ct:g2 + ct + 1],
                                     scalar2=sv[:, b2 + ct:b2 + ct + 1],
                                     op0=ALU.mult, op1=ALU.add)
            for mbuf, tm, om in [(mixk2, "cmk", "om_cmk"), (mixr2, "cmr", "om_cmr")]:
                to, _ = _SV[tm]
                oo = O + DV[om]
                mv = r3(mbuf)
                for ct in range(NCT):
                    vector.tensor_scalar(out=t3[:, ct, :], in0=hp[:, ct, 0:TC],
                                         scalar1=sv[:, oo + ct:oo + ct + 1],
                                         scalar2=None, op0=ALU.mult)
                    vector.scalar_tensor_tensor(out=mv[:, ct, :], in0=hp[:, ct, 1:TC + 1],
                                                scalar=sv[:, to + ct:to + ct + 1],
                                                in1=t3[:, ct, :],
                                                op0=ALU.mult, op1=ALU.add)
            inst = vector.tensor_copy(out=hp[:, :, 0:1], in_=hp[:, :, TC:TC + 1])
            p.inc(inst, "v", 1, f"a2_mix_{i}")
        p.block.vector(vec_h)

        for q in range(NQ):
            emit_qload(i, q)
            for ht in range(QH):
                p.reserve("s", f"a2_kfev_{i}_{q}_{ht}")

            def tn_kf(tensor, i=i, q=q):
                p.wait(tensor, f"a2_mix_{i}")
                p.wait(tensor, f"a2_fkld_{i}_{q}")
                wv = r3(wfkq)
                mv = r3(mixk2)
                for ht in range(QH):
                    if ht >= 2:
                        p.wait(tensor, f"a2_kfev_{i}_{q}_{ht-2}")
                    elif (i, q) != (0, 0):
                        pi, pq = (i, q - 1) if q else (i - 1, NQ - 1)
                        p.wait(tensor, f"a2_kfev_{pi}_{pq}_{QH-2+ht}")
                    for ct in range(NCT):
                        inst = tensor.matmul(pkf[ht % 2][:],
                                             wv[:, ct, ht * 128:(ht + 1) * 128],
                                             mv[:, ct, :],
                                             start=(ct == 0), stop=(ct == NCT - 1))
                    p.inc(inst, "t", 1, f"a2_kfg_{i}_{q}_{ht}")
                p.ev[f"a2_kfmm_{i}_{q}"] = p.ev[f"a2_kfg_{i}_{q}_{QH-1}"]
            p.block.tensor(tn_kf)

            def sc_kf(scalar, i=i, q=q):
                kv = r3(kf16, QH)
                for ht in range(QH):
                    p.wait(scalar, f"a2_kfg_{i}_{q}_{ht}")
                    scalar.activation(out=kv[:, ht, :], in_=pkf[ht % 2][:],
                                      func=AF.Relu)
                    p.inc_reserved(scalar.activation(out=kv[:, ht, :], in_=kv[:, ht, :],
                                                     func=AF.Square),
                                   f"a2_kfev_{i}_{q}_{ht}")
            p.block.scalar(sc_kf)

            for ct in range(NCT):
                p.reserve("v", f"a2_kvev_{i}_{q}_{ct}")

            def tn_kv(tensor, i=i, q=q):
                p.wait(tensor, f"a2_kfev_{i}_{q}_{QH-1}")
                p.wait(tensor, f"a2_fvld_{i}_{q}")
                wv = r3(wfvq, QH)
                kv = r3(kf16, QH)
                for ct in range(NCT):
                    if ct >= 2:
                        p.wait(tensor, f"a2_kvev_{i}_{q}_{ct-2}")
                    elif (i, q) != (0, 0):
                        pi, pq = (i, q - 1) if q else (i - 1, NQ - 1)
                        p.wait(tensor, f"a2_kvev_{pi}_{pq}_{NCT-2+ct}")
                    for ht in range(QH):
                        inst = tensor.matmul(pkv[ct % 2][:],
                                             wv[:, ht, ct * 128:(ct + 1) * 128],
                                             kv[:, ht, :],
                                             start=(ht == 0), stop=(ht == QH - 1))
                    p.inc(inst, "t", 1, f"a2_kvg_{i}_{q}_{ct}")
                p.ev[f"a2_kvmm_{i}_{q}"] = p.ev[f"a2_kvg_{i}_{q}_{NCT-1}"]
            p.block.tensor(tn_kv)

            def vec_kv(vector, i=i, q=q):
                k3 = r3(kv32)
                for ct in range(NCT):
                    p.wait(vector, f"a2_kvg_{i}_{q}_{ct}")
                    if q == 0:
                        inst = vector.tensor_copy(out=k3[:, ct, :], in_=pkv[ct % 2][:])
                    else:
                        inst = vector.tensor_tensor(out=k3[:, ct, :], in0=pkv[ct % 2][:],
                                                    in1=k3[:, ct, :], op=ALU.add)
                    p.inc_reserved(inst, f"a2_kvev_{i}_{q}_{ct}")
            p.block.vector(vec_kv)

        for ot in range(NCT):
            p.reserve("s", f"a2_srev_{i}_{ot}")

        def tn_r(tensor, i=i):
            p.wait(tensor, f"a2_mix_{i}")
            if i == 0:
                p.wait(tensor, wfrev)
            wv = r3(wfr16)
            mv = r3(mixr2)
            for ot in range(NCT):
                if ot >= 2:
                    p.wait(tensor, f"a2_srev_{i}_{ot-2}")
                else:
                    p.wait(tensor, f"a2_kfev_{i}_{NQ-1}_{QH-1}")
                for ct in range(NCT):
                    inst = tensor.matmul(pkf[ot % 2][:], wv[:, ct, ot * 128:(ot + 1) * 128],
                                         mv[:, ct, :],
                                         start=(ct == 0), stop=(ct == NCT - 1))
                p.inc(inst, "t", 1, f"a2_rmm_{i}_{ot}")
        p.block.tensor(tn_r)

        def sc_sr(scalar, i=i):
            t3 = r3(tmp32)
            for ot in range(NCT):
                p.wait(scalar, f"a2_rmm_{i}_{ot}")
                p.inc_reserved(scalar.activation(out=t3[:, ot, :], in_=pkf[ot % 2][:],
                                                 func=AF.Sigmoid),
                               f"a2_srev_{i}_{ot}")
        p.block.scalar(sc_sr)

        def vec_out(vector, i=i):
            p.wait(vector, f"a2_srev_{i}_{NCT-1}")
            p.wait(vector, f"a2_kvev_{i}_{NQ-1}_{NCT-1}")
            p.wait(vector, f"a2_xs_{i}")
            vector.tensor_tensor(out=kv32[:], in0=kv32[:], in1=tmp32[:], op=ALU.mult)
            p.inc(vector.tensor_tensor(out=x2c[:], in0=kv32[:], in1=xs[:], op=ALU.add),
                  "v", 1, f"a2_out_{i}")
        p.block.vector(vec_out)

        def st(sync, i=i, tsl=tsl):
            p.wait(sync, f"a2_out_{i}")
            p.mark_dma(sync.dma_start(out=res1[:, :, tsl], in_=r3(x2c)),
                       "dma_a2s", f"a2_st_{i}")
        p.block.sync(st)

    def fin(sync):
        p.wait(sync, f"a2_st_{NCH-1}")
    p.block.sync(fin)



def emit_B(p):
    nc = p.nc
    TC = 256
    NCH = T // TC
    res1 = fm_view(p.dram["res1_d"])
    res2 = fm_view(p.dram["res2_d"])

    w116, w1ev = p.load_weight_cast("b_w1", p.dram["l0f_w1"], NCT, H, "dma_b",
                                    stage_cols=4096)
    w216, w2ev = p.load_weight_cast("b_w2", p.dram["l0f_w2"], NHT, C, "dma_b",
                                    stage_cols=4096)

    sx = p.sb("b_sx", [128, NCT * TC], F32)
    tmp32 = p.sb("b_tmp", [128, NCT * TC], F32)
    sc16 = p.sb("b_sc16", [128, NCT * TC], BF16)
    sq16 = p.sb("b_sq16", [128, NCT * TC], BF16)
    h16 = p.sb("b_h16", [128, NHT * TC], BF16)
    rowbuf = p.sb("b_rows", [1, 6 * TC], F32)
    out32 = p.sb("b_out", [128, NCT * TC], F32)

    psA = p.ps("b_psA", [1, TC])
    psB = p.ps("b_psB", [1, TC])
    prep = p.ps("b_prep", [128, 2 * TC])
    ph = [p.ps(f"b_ph{i}", [128, TC]) for i in range(2)]
    po = [p.ps(f"b_po{i}", [128, TC]) for i in range(2)]

    sv16 = p.sv16
    sv = p.sv32

    for i in range(NCH):
        tsl = slice(i * TC, (i + 1) * TC)

        def ld(sync, i=i, tsl=tsl):
            if i >= 1:
                p.wait(sync, f"b_res_done_{i-1}")
            p.mark_dma(sync.dma_start(out=sx.ap().rearrange("p (ct t) -> p ct t", ct=NCT),
                                      in_=res1[:, :, tsl]),
                       "dma_b", f"b_x_{i}")
        p.block.sync(ld)

        if i == 0:
            pass

        prep_ev = emit_stats(p, f"bln{i}", sx, sc16, sq16, NCT, TC,
                             psA, psB, rowbuf, prep, [f"b_x_{i}"])

        def vec_ln(vector, i=i):
            p.wait(vector, f"bln{i}_prep")
            Mb = prep.ap()[:, 0:TC].unsqueeze(1).broadcast_to([128, NCT, TC])
            Rb = prep.ap()[:, TC:2 * TC].unsqueeze(1).broadcast_to([128, NCT, TC])
            xv = sx.ap().rearrange("p (ct t) -> p ct t", ct=NCT)
            tv = tmp32.ap().rearrange("p (ct t) -> p ct t", ct=NCT)
            ov = sq16.ap().rearrange("p (ct t) -> p ct t", ct=NCT)
            vector.tensor_tensor(out=tv, in0=xv, in1=Mb, op=ALU.subtract)
            p.inc(vector.tensor_tensor(out=ov, in0=tv, in1=Rb, op=ALU.mult),
                  "v", 1, f"b_xln_{i}")
        p.block.vector(vec_ln)

        for ht in range(NHT):
            p.reserve("s", f"b_hev_{i}_{ht}")
        for ct in range(NCT):
            p.reserve("v", f"b_oev_{i}_{ct}")
        p.ev[f"b_res_done_{i}"] = p.ev[f"b_oev_{i}_{NCT-1}"]

        def tn_h(tensor, i=i):
            p.wait(tensor, f"b_xln_{i}")
            if i == 0:
                p.wait(tensor, w1ev)
            xl = sq16.ap().rearrange("p (ct t) -> p ct t", ct=NCT)
            w1v = w116.ap().rearrange("p (ct h) -> p ct h", ct=NCT)
            for ht in range(NHT):
                if ht >= 2:
                    p.wait(tensor, f"b_hev_{i}_{ht-2}")
                elif i > 0:
                    p.wait(tensor, f"b_hev_{i-1}_{NHT-2+ht}")
                pb = ph[ht % 2]
                for ct in range(NCT):
                    inst = tensor.matmul(pb[:], w1v[:, ct, ht * 128:(ht + 1) * 128],
                                         xl[:, ct, :],
                                         start=(ct == 0), stop=(ct == NCT - 1))
                p.inc(inst, "t", 1, f"b_hmm_{i}_{ht}")
        p.block.tensor(tn_h)

        def sc_h(scalar, i=i):
            b1off = _SV["l0f_b1"][0]
            hv = h16.ap().rearrange("p (ht t) -> p ht t", ht=NHT)
            for ht in range(NHT):
                p.wait(scalar, f"b_hmm_{i}_{ht}")
                p.inc_reserved(scalar.activation(out=hv[:, ht, :], in_=ph[ht % 2][:],
                                                 func=AF.Relu,
                                                 bias=sv[:, b1off + ht:b1off + ht + 1]),
                               f"b_hev_{i}_{ht}")
        p.block.scalar(sc_h)

        def tn_o(tensor, i=i):
            if i == 0:
                p.wait(tensor, w2ev)
            hv = h16.ap().rearrange("p (ht t) -> p ht t", ht=NHT)
            w2v = w216.ap().rearrange("p (ht c) -> p ht c", ht=NHT)
            p.wait(tensor, f"b_hev_{i}_{NHT-1}")
            for ct in range(NCT):
                if ct >= 2:
                    p.wait(tensor, f"b_oev_{i}_{ct-2}")
                elif i > 0:
                    p.wait(tensor, f"b_oev_{i-1}_{NCT-2+ct}")
                pb = po[ct % 2]
                for ht in range(NHT):
                    inst = tensor.matmul(pb[:], w2v[:, ht, ct * 128:(ct + 1) * 128],
                                         hv[:, ht, :],
                                         start=(ht == 0), stop=(ht == NHT - 1))
                p.inc(inst, "t", 1, f"b_omm_{i}_{ct}")
        p.block.tensor(tn_o)

        def vec_o(vector, i=i):
            b2off = _SV["l0f_b2"][0]
            xv = sx.ap().rearrange("p (ct t) -> p ct t", ct=NCT)
            ov = out32.ap().rearrange("p (ct t) -> p ct t", ct=NCT)
            if i > 0:
                p.wait(vector, f"b_st_{i-1}")
            for ct in range(NCT):
                p.wait(vector, f"b_omm_{i}_{ct}")
                inst = vector.scalar_tensor_tensor(
                    out=ov[:, ct, :], in0=po[ct % 2][:],
                    scalar=sv[:, b2off + ct:b2off + ct + 1],
                    in1=xv[:, ct, :], op0=ALU.add, op1=ALU.add)
                p.inc_reserved(inst, f"b_oev_{i}_{ct}")
        p.block.vector(vec_o)

        def st(gpsimd, i=i, tsl=tsl):
            p.wait(gpsimd, f"b_res_done_{i}")
            p.mark_dma(gpsimd.dma_start(
                out=res2[:, :, tsl],
                in_=out32.ap().rearrange("p (ct t) -> p ct t", ct=NCT)),
                "dma_bs", f"b_st_{i}")
            p.ev[f"b_out_used_{i}"] = p.ev[f"b_st_{i}"]
        p.block.gpsimd(st)

    def fin(gpsimd):
        p.wait(gpsimd, f"b_st_{NCH-1}")
    p.block.gpsimd(fin)



def emit_D(p):
    TC = 512
    NCH = T // TC
    res3 = fm_view(p.dram["res3_d"])
    yfm = fm_view(p.dram["y_fm"])
    sv = p.sv32

    xc = p.sb("d_xc", [128, NCT * TC], F32)
    tmp32 = p.sb("d_tmp", [128, NCT * TC], F32)
    sq16 = p.sb("d_sq16", [128, NCT * TC], BF16)
    xln16 = sq16
    t216 = p.sb("d_t216", [128, NCT * TC], BF16)
    sc16 = t216
    hA = p.sb("d_hA", [128, NHT * TC], BF16)
    hB = p.sb("d_hB", [128, NHT * TC], BF16)
    wbuf2 = p.sb("d_wbuf", [128, 2 * 16384], BF16)
    out32 = tmp32
    rowbuf = p.sb("d_rows", [1, 3 * TC], F32)

    psA = p.ps("d_psA", [1, TC])
    psB = p.ps("d_psB", [1, TC])
    prep = p.ps("d_prep", [128, 2 * TC])
    pm = [p.ps(f"d_pm{j}", [128, TC]) for j in range(2)]

    def r3(buf, n=NCT):
        return buf.ap().rearrange("p (ct t) -> p ct t", ct=n)

    BLOCKS = []
    for ob in range(4):
        BLOCKS.append(("f1", "f1_t", "f116_d", NCT, 1024, ob * 1024))
        BLOCKS.append(("f2", "f2_t", "f216_d", NCT, 1024, ob * 1024))
    for ob in range(8):
        BLOCKS.append(("h1", "fh1_t", "fh116_d", NHT, 512, ob * 512))
    for ob in range(8):
        BLOCKS.append(("h2", "fh2_t", "fh216_d", NHT, 512, ob * 512))
    for ob in range(2):
        BLOCKS.append(("w3", "f3_t", "f316_d", NHT, 512, ob * 512))
    NBLK = len(BLOCKS)

    evac_seq = {}

    def mm_bank_wait(tensor, i, g):
        if g >= 2:
            p.wait(tensor, evac_seq[i][g - 2])
        elif i > 0:
            p.wait(tensor, evac_seq[i - 1][len(evac_seq[i - 1]) - 2 + g])
        return pm[g % 2]

    def wbuf_half(b):
        return wbuf2.ap()[:, (b % 2) * 16384:(b % 2) * 16384 + 16384]

    def emit_wload(i, b):
        nm, src, cache, nin, nout, o0 = BLOCKS[b]
        ldev = f"d_wld_{i}_{b}"
        if b >= 2:
            halts = [f"d_wmm_{i}_{b-2}"]
        elif i > 0:
            halts = [f"d_wmm_{i-1}_{NBLK-2+b}"]
        else:
            halts = []
        if i == 0:
            if b >= 2:
                halts.append(f"d_cst_{b-2}")
            src_ap = p.dram[src][:, o0:o0 + nout]
            _, ev = p.load_weight_cast(f"d_{nm}{o0}", src_ap, nin, nout, "dma_dw",
                                       stage_cols=4096, dma_eng="gpsimd",
                                       w16=wbuf_half(b), halt_event=halts or None)
            p.ev[ldev] = p.ev[ev]

            def cst(gpsimd, b=b, cache=cache, nin=nin, nout=nout, o0=o0, ev=ev):
                p.wait(gpsimd, ev)
                p.mark_dma(gpsimd.dma_start(
                    out=p.dram[cache][:, o0:o0 + nout].rearrange(
                        "(ct p) o -> p ct o", p=128),
                    in_=wbuf_half(b)[:, 0:nin * nout].rearrange(
                        "p (ct o) -> p ct o", ct=nin)),
                    "dma_dc", f"d_cst_{b}")
            p.block.gpsimd(cst)
        else:
            def ldq(gpsimd, i=i, b=b, cache=cache, nin=nin, nout=nout, o0=o0,
                    halts=tuple(halts)):
                p.wait(gpsimd, f"d_cst_{b}")
                for he in halts:
                    p.wait(gpsimd, he)
                p.mark_dma(gpsimd.dma_start(
                    out=wbuf_half(b)[:, 0:nin * nout].rearrange(
                        "p (ct o) -> p ct o", ct=nin),
                    in_=p.dram[cache][:, o0:o0 + nout].rearrange(
                        "(ct p) o -> p ct o", p=128)),
                    "dma_dw", ldev)
            p.block.gpsimd(ldq)

    for i in range(NCH):
        tsl = slice(i * TC, (i + 1) * TC)
        evac_seq[i] = []
        gctr = [0]

        def ld(sync, i=i, tsl=tsl):
            if i > 0:
                p.wait(sync, f"d_out_{i-1}")
            p.mark_dma(sync.dma_start(out=r3(xc), in_=res3[:, :, tsl]),
                       "dma_d", f"d_x_{i}")
        p.block.sync(ld)

        emit_stats(p, f"dl{i}", xc, sc16, sq16, NCT, TC, psA, psB,
                   rowbuf, prep, [f"d_x_{i}"])

        def vec_app(vector, i=i):
            p.wait(vector, f"dl{i}_prep")
            if i > 0:
                p.wait(vector, f"d_st_{i-1}")
            Mb = prep.ap()[:, 0:TC].unsqueeze(1).broadcast_to([128, NCT, TC])
            Rb = prep.ap()[:, TC:2 * TC].unsqueeze(1).broadcast_to([128, NCT, TC])
            vector.tensor_tensor(out=r3(tmp32), in0=r3(xc), in1=Mb, op=ALU.subtract)
            vector.tensor_tensor(out=r3(tmp32), in0=r3(tmp32), in1=Rb, op=ALU.mult)
            p.inc(vector.tensor_copy(out=xln16[:], in_=tmp32[:]), "v", 1, f"d_xln_{i}")
        p.block.vector(vec_app)

        hAv = r3(hA, NHT)
        hBv = r3(hB, NHT)
        t2v = r3(t216, NCT)

        for ob in range(4):
            for half, nm in ((0, "f1"), (1, "f2")):
                b = ob * 2 + half
                emit_wload(i, b)
                for sub in range(8):
                    p.reserve("s", f"d_gev_{i}_{b}_{sub}")
                    evac_seq[i].append(f"d_gev_{i}_{b}_{sub}")

                def tn_g(tensor, i=i, b=b):
                    p.wait(tensor, f"d_wld_{i}_{b}")
                    p.wait(tensor, f"d_xln_{i}")
                    wv = wbuf_half(b)[:, 0:NCT * 1024].rearrange("p (ct o) -> p ct o",
                                                                 ct=NCT)
                    xv = r3(xln16)
                    for sub in range(8):
                        pb = mm_bank_wait(tensor, i, gctr[0]); gctr[0] += 1
                        for ct in range(NCT):
                            inst = tensor.matmul(
                                pb[:], wv[:, ct, sub * 128:(sub + 1) * 128],
                                xv[:, ct, :],
                                start=(ct == 0), stop=(ct == NCT - 1))
                        p.inc(inst, "t", 1, f"d_gmm_{i}_{b}_{sub}")
                    p.ev[f"d_wmm_{i}_{b}"] = p.ev[f"d_gmm_{i}_{b}_{7}"]
                p.block.tensor(tn_g)

                def sc_g(scalar, i=i, b=b, ob=ob, half=half):
                    s1o, _ = _SV["l1f_s1"]
                    s2o, _ = _SV["l1f_s2"]
                    for sub in range(8):
                        glob = ob * 8 + sub
                        p.wait(scalar, f"d_gmm_{i}_{b}_{sub}")
                        bank = (b * 8 + sub) % 2
                        if half == 0:
                            inst = scalar.activation(
                                out=hAv[:, glob, :], in_=pm[bank][:], func=AF.Silu,
                                scale=sv[:, s1o + glob:s1o + glob + 1])
                        else:
                            if i > 0 or ob > 0:
                                p.wait(scalar, f"d_hA_{i}_{(ob-1)*8+sub}" if ob
                                       else f"d_hA_{i-1}_{24+sub}")
                            inst = scalar.activation(
                                out=t2v[:, sub, :], in_=pm[bank][:], func=AF.Identity,
                                scale=sv[:, s2o + glob:s2o + glob + 1])
                        p.inc_reserved(inst, f"d_gev_{i}_{b}_{sub}")
                        if half == 1:
                            p.reserve("v", f"d_hA_{i}_{glob}")
                p.block.scalar(sc_g)

                if half == 1:
                    def vec_g(vector, i=i, b=b, ob=ob):
                        for sub in range(8):
                            glob = ob * 8 + sub
                            p.wait(vector, f"d_gev_{i}_{b}_{sub}")
                            p.inc_reserved(
                                vector.tensor_tensor(out=hAv[:, glob, :],
                                                     in0=hAv[:, glob, :],
                                                     in1=t2v[:, sub, :], op=ALU.mult),
                                f"d_hA_{i}_{glob}")
                    p.block.vector(vec_g)
        p.ev[f"d_hA_done_{i}"] = p.ev[f"d_hA_{i}_{NHT-1}"]

        for hph in range(2):
            srcv = hAv if hph == 0 else hBv
            dstv = hBv if hph == 0 else hAv
            svn = "l1f_hs1" if hph == 0 else "l1f_hs2"
            base_b = 8 + 8 * hph
            in_ev = f"d_hA_done_{i}" if hph == 0 else f"d_hB_done_{i}"
            for ob in range(8):
                b = base_b + ob
                emit_wload(i, b)
                for sub in range(4):
                    p.reserve("s", f"d_hev_{i}_{b}_{sub}")
                    evac_seq[i].append(f"d_hev_{i}_{b}_{sub}")

                def tn_h(tensor, i=i, b=b, srcv=srcv, in_ev=in_ev):
                    p.wait(tensor, f"d_wld_{i}_{b}")
                    p.wait(tensor, in_ev)
                    wv = wbuf_half(b)[:, 0:NHT * 512].rearrange("p (ht o) -> p ht o",
                                                                ht=NHT)
                    for sub in range(4):
                        pb = mm_bank_wait(tensor, i, gctr[0]); gctr[0] += 1
                        for ht in range(NHT):
                            inst = tensor.matmul(
                                pb[:], wv[:, ht, sub * 128:(sub + 1) * 128],
                                srcv[:, ht, :],
                                start=(ht == 0), stop=(ht == NHT - 1))
                        p.inc(inst, "t", 1, f"d_hmm_{i}_{b}_{sub}")
                    p.ev[f"d_wmm_{i}_{b}"] = p.ev[f"d_hmm_{i}_{b}_{3}"]
                p.block.tensor(tn_h)

                def sc_h(scalar, i=i, b=b, ob=ob, dstv=dstv, svn=svn, base_b=base_b):
                    so, _ = _SV[svn]
                    for sub in range(4):
                        glob = ob * 4 + sub
                        p.wait(scalar, f"d_hmm_{i}_{b}_{sub}")
                        bank = (64 + (b - 8) * 4 + sub) % 2
                        p.inc_reserved(
                            scalar.activation(out=dstv[:, glob, :], in_=pm[bank][:],
                                              func=AF.Gelu,
                                              scale=sv[:, so + glob:so + glob + 1]),
                            f"d_hev_{i}_{b}_{sub}")
                p.block.scalar(sc_h)
            p.ev[f"d_hB_done_{i}" if hph == 0 else f"d_hC_done_{i}"] = \
                p.ev[f"d_hev_{i}_{base_b+7}_{3}"]

        for ob in range(2):
            b = 24 + ob
            emit_wload(i, b)
            for sub in range(4):
                p.reserve("v", f"d_w3ev_{i}_{ob*4+sub}")
                evac_seq[i].append(f"d_w3ev_{i}_{ob*4+sub}")

            def tn_w3(tensor, i=i, b=b):
                p.wait(tensor, f"d_wld_{i}_{b}")
                p.wait(tensor, f"d_hC_done_{i}")
                wv = wbuf_half(b)[:, 0:NHT * 512].rearrange("p (ht o) -> p ht o", ht=NHT)
                for sub in range(4):
                    pb = mm_bank_wait(tensor, i, gctr[0]); gctr[0] += 1
                    for ht in range(NHT):
                        inst = tensor.matmul(
                            pb[:], wv[:, ht, sub * 128:(sub + 1) * 128],
                            hAv[:, ht, :],
                            start=(ht == 0), stop=(ht == NHT - 1))
                    p.inc(inst, "t", 1, f"d_w3mm_{i}_{(b-24)*4+sub}")
                p.ev[f"d_wmm_{i}_{b}"] = p.ev[f"d_w3mm_{i}_{(b-24)*4+3}"]
            p.block.tensor(tn_w3)

            def vec_w3(vector, i=i, b=b, ob=ob):
                s3o, _ = _SV["l1f_s3"]
                ov = r3(out32)
                xv = r3(xc)
                for sub in range(4):
                    glob = ob * 4 + sub
                    p.wait(vector, f"d_w3mm_{i}_{glob}")
                    if i > 0 and glob == 0:
                        p.wait(vector, f"d_st_{i-1}")
                    bank = (128 + glob) % 2
                    p.inc_reserved(
                        vector.scalar_tensor_tensor(out=ov[:, glob, :],
                                                    in0=pm[bank][:],
                                                    scalar=sv[:, s3o + glob:s3o + glob + 1],
                                                    in1=xv[:, glob, :],
                                                    op0=ALU.mult, op1=ALU.add),
                        f"d_w3ev_{i}_{glob}")
            p.block.vector(vec_w3)
        p.ev[f"d_out_{i}"] = p.ev[f"d_w3ev_{i}_{7}"]

        def st(sync, i=i, tsl=tsl):
            p.wait(sync, f"d_out_{i}")
            p.mark_dma(sync.dma_start(out=yfm[:, :, tsl], in_=r3(out32)),
                       "dma_ds", f"d_st_{i}")
        p.block.sync(st)

    def fin(sync):
        p.wait(sync, f"d_st_{NCH-1}")
    p.block.sync(fin)



def emit_prelude(p):
    nc = p.nc
    p.sv32 = p.sb("sv32", [128, SVW + 64], F32)
    p.sv16 = p.sb("sv16", [128, SVW], BF16)
    p.rv32 = p.sb("rv32", [1, RVW], F32)
    p.ones16 = p.sb("ones16", [128, 1], BF16)
    p.ones32 = p.sb("ones32", [1, 128], F32)
    p.get_stg(4096)

    def ld(sync):
        p.mark_dma(sync.dma_start(out=p.sv32[:, 0:SVW], in_=p.dram["smallvecs"][:]),
                   "dma_pre", "sv_ld")
        p.mark_dma(sync.dma_start(out=p.rv32[:], in_=p.dram["rowvecs"][:]),
                   "dma_pre", "rv_ld")
    p.block.sync(ld)

    def vec(vector):
        p.wait(vector, "sv_ld")
        p.wait(vector, "rv_ld")
        vector.memset(p.ones16[:], 1.0)
        vector.memset(p.ones32[:], 1.0)
        p.inc(vector.tensor_copy(out=p.sv16[:], in_=p.sv32[:, 0:SVW]), "v", 1, "sv16_ready")
        p.ev["pre_ready"] = ("v", p.cnt["v"])
    p.block.vector(vec)

    p.eps_ap = p.rv32.ap()[:, _RV["eps"][0]:_RV["eps"][0] + 1]

    def allwait(eng):
        p.wait(eng, "pre_ready")
    for n in ["sync", "vector", "scalar", "tensor", "gpsimd"]:
        getattr(p.block, n)(allwait)



def build_program(phases):
    nc = bass.Bass("TRN2", target_bir_lowering=False)
    ctx = ExitStack()
    dram = declare_dram(nc, phases)
    block = ctx.enter_context(nc.Block())
    p = P(nc, block, ctx, dram, phases)
    emit_prelude(p)
    for phi, ph in enumerate(phases):
        from contextlib import ExitStack as _ES
        p.phase_ctx = _ES()
        if ph == "A1":
            emit_A1(p)
        elif ph == "A2":
            emit_A2(p)
        elif ph == "B":
            emit_B(p)
        elif ph == "C0":
            emit_C0(p)
        elif ph == "C1":
            emit_C1(p)
        elif ph == "C3":
            emit_C3(p)
        elif ph == "C4":
            emit_C4(p)
        elif ph == "D":
            emit_D(p)
        else:
            raise NotImplementedError(ph)
        if phi < len(phases) - 1:
            p.barrier(f"ph_{ph}")
        p.phase_ctx.close()
        p.phase_ctx = None
    ctx.close()
    return nc


def run_phases(phase_groups, host, extra_outputs=()):
    acts = {}
    results = None
    for phases in phase_groups:
        nc = build_program(phases)
        in_map = {}
        pset = set(phases)
        for name, (shape, prod) in WEIGHTS.items():
            if prod in pset:
                in_map[name] = host[name]
        in_map["smallvecs"] = host["smallvecs"]
        in_map["rowvecs"] = host["rowvecs"]
        per_core = [dict(in_map) for _ in range(NCORE)]
        for name, (shape, dt, prod, cons) in ACTS.items():
            if name == "x_fm":
                if pset & {"A1", "A2"}:
                    for b in range(NCORE):
                        per_core[b]["x_fm"] = host["x_fm"][b]
                continue
            if (cons & pset) and prod not in pset:
                for b in range(NCORE):
                    per_core[b][name] = acts[name][b]
        res = run_bass_kernel_spmd(nc, per_core, core_ids=list(range(NCORE)))
        for b, r in enumerate(res.results):
            for k, v in r.items():
                acts.setdefault(k, [None] * NCORE)[b] = v
    return acts


FUSED = True


def kernel(**inputs):
    host = prep_inputs(inputs)
    groups = [ALL_PHASES] if FUSED else [[ph] for ph in ALL_PHASES]
    acts = run_phases(groups, host)
    out = np.stack([acts["y_fm"][b].T for b in range(NCORE)], axis=0)
    return np.ascontiguousarray(out.astype(np.float32))
